# version 01
# baseline (speedup 1.0000x reference)
import math
from contextlib import ExitStack

import numpy as np
import ml_dtypes

import concourse.bass as bass
import concourse.mybir as mybir
from concourse.bass_utils import run_bass_kernel_spmd

F32 = mybir.dt.float32
BF16 = mybir.dt.bfloat16
AF = mybir.ActivationFunctionType
AL = mybir.AluOpType
AX = mybir.AxisListType
bfloat16 = ml_dtypes.bfloat16

NCORES = 8
D = 2048
NH = 16
DH = 128
HARM = 64
SEQ = 2048
SOWN = 1024
TC = 8
DC = 16
E = 8
HE = 1024
HCN = 8
OCN = 16
SM_SCALE = 1.0 / math.sqrt(DH)
LN_EPS = 1e-5
DN_EPS = 1e-6
BETA_MIN, BETA_MAX = 0.8, 0.9995


class Sem:
    def __init__(self, handle):
        self.h = handle
        self.n = 0

    def tick(self, k=1):
        self.n += k
        return self.n


class Slot:

    def __init__(self, ap=None):
        self.ap = ap
        self.wt = None
        self.rd = []

    def wwait(self):
        w = list(self.rd)
        if self.wt is not None:
            w.append(self.wt)
        return w

    def rwait(self):
        return [self.wt] if self.wt is not None else []

    def note_w(self, sem):
        self.wt = (sem, sem.n)
        self.rd = []

    def note_r(self, sem):
        self.rd.append((sem, sem.n))


def _dedup(waits):
    best = {}
    for (s, v) in waits:
        if s is None:
            continue
        if s not in best or v > best[s]:
            best[s] = v
    return [(s, v) for s, v in best.items()]


class Arena:
    def __init__(self, t, base, width):
        self.t = t
        self.base = base
        self.width = width
        self.top = base

    def alloc(self, shape):
        n = 1
        for s in shape[1:]:
            n *= s
        off = self.top
        self.top += n
        assert self.top <= self.base + self.width, \
            f"arena overflow: {self.top} > {self.base + self.width}"
        ap = self.t[0:shape[0], off:off + n]
        if len(shape) == 3:
            ap = ap.rearrange("p (a b) -> p a b", a=shape[1])
        elif len(shape) == 4:
            ap = ap.rearrange("p (a b c) -> p a b c", a=shape[1], b=shape[2])
        return ap

    def salloc(self, shape):
        return Slot(self.alloc(shape))

    def mark(self):
        return self.top

    def reset(self, mark=None):
        self.top = self.base if mark is None else mark


class MK:
    def __init__(self, stop_after="full"):
        self.stop_after = stop_after
        self.nc = bass.Bass("TRN2", target_bir_lowering=False)
        self.ops = {k: [] for k in ("tensor", "vector", "scalar", "gpsimd", "sync")}

    def emit(self, eng, fn, waits=(), inc=None):
        waits = tuple((s.h, v) for (s, v) in _dedup(waits))

        def thunk(e, fn=fn, waits=waits, inc=inc):
            for (h, v) in waits:
                e.wait_ge(h, v)
            ins = fn(e)
            if inc is not None:
                ins.then_inc(inc[0].h, inc[1])

        self.ops[eng].append(thunk)

    def t_mm(self, dst, lhsT, rhs, start, stop, waits=()):
        self.pe.tick()
        self.emit("tensor",
                  lambda e, d=dst, l=lhsT, r=rhs, st=start, sp=stop:
                  e.matmul(d, l, r, start=st, stop=sp),
                  waits=waits, inc=(self.pe, 1))
        return self.pe.n

    def gmm(self, dst, banks, parts, rd_slots=(), extra_waits=()):
        w = list(extra_waits)
        for b in banks:
            w += b.wwait()
        for s in rd_slots:
            w += s.rwait()
        n = len(parts)
        for i, (l, r) in enumerate(parts):
            self.t_mm(dst, l, r, i == 0, i == n - 1, waits=w if i == 0 else ())
        for b in banks:
            b.note_w(self.pe)
        for s in rd_slots:
            s.note_r(self.pe)
        return self.pe.n

    def ev(self, eng, fn, rd=(), wr=(), extra_waits=()):
        sem = {"vector": self.ve, "scalar": self.se, "gpsimd": self.ge}[eng]
        w = list(extra_waits)
        for s in rd:
            w += s.rwait()
        for s in wr:
            w += s.wwait()
        sem.tick()
        self.emit(eng, fn, waits=w, inc=(sem, 1))
        for s in rd:
            s.note_r(sem)
        for s in wr:
            s.note_w(sem)
        return sem.n

    def dma(self, eng, qsem, fn, rd=(), wr=(), extra_waits=()):
        w = list(extra_waits) + list(getattr(self, "fence", ()))
        for s in rd:
            w += s.rwait()
        for s in wr:
            w += s.wwait()
        qsem.tick(16)
        self.emit(eng, fn, waits=w, inc=(qsem, 16))
        for s in rd:
            s.note_r(qsem)
        for s in wr:
            s.note_w(qsem)
        return qsem.n

    def barrier(self):
        pts = [(self.pe, self.pe.n), (self.ve, self.ve.n), (self.se, self.se.n),
               (self.ge, self.ge.n)]
        for eng, own in (("tensor", self.pe), ("vector", self.ve),
                         ("scalar", self.se), ("gpsimd", self.ge)):
            w = [(s, v) for (s, v) in pts if s is not own]
            self.emit(eng, lambda e: e.nop(), waits=w)
        self.fence = pts

    def build(self):
        nc = self.nc
        self.DT = {}

        def din(name, shape, dtype=F32):
            self.DT[name] = nc.dram_tensor(name, list(shape), dtype,
                                           kind="ExternalInput").ap()

        din("x_own", [SOWN, D]); din("x_part", [SOWN, D])
        for nm in ("q", "k", "v", "o"):
            din(f"basis_{nm}", [HARM, D])
            din(f"phase_{nm}", [D, HARM])
            din(f"amp_{nm}", [D, HARM])
        for nm in ("Wq", "Wk", "Wv", "Wo"):
            din(nm, [D, D])
        din("Wg", [D, NH]); din("bg", [NH, 1]); din("bo_pk", [128, OCN])
        din("Wg1", [E, D]); din("Wg2", [E, E])
        din("Wup", [E, HE, D]); din("bup_pk", [128, E * HCN])
        din("Wdown", [E, D, HE]); din("bdown", [8, D])
        din("id_bf", [128, 128], BF16); din("id_f32", [128, 128], F32)
        din("ones_bf", [128, 128], BF16); din("ones_f32", [128, 128], F32)
        din("swap_bf", [128, 128], BF16); din("tri_f32", [128, 128], F32)
        din("rope_cos", [128, SOWN], BF16); din("rope_sin", [128, SOWN], BF16)
        din("cvec", [128, 8], F32); din("selmask", [128, 8], F32)
        din("hsel_bf", [NH, NH * 128], BF16)
        din("csel_bf", [128, NH * NH], BF16)

        self.y_out = nc.dram_tensor("y_out", [SOWN, D], F32, kind="ExternalOutput").ap()
        self.wup_bf = nc.dram_tensor("wup_bf", [E, HE, D], BF16).ap()
        self.wdown_bf = nc.dram_tensor("wdown_bf", [E, D, HE], BF16).ap()
        self.wpj_bf = {nm: nc.dram_tensor(f"w{nm}_bf", [D, D], BF16).ap()
                       for nm in ("q", "k", "v", "o")}
        self.cc_in = nc.dram_tensor("cc_in", [128, NH * 129], BF16).ap()
        self.cc_out = nc.dram_tensor("cc_out", [NCORES, 128, NH * 129], BF16).ap()

        with ExitStack() as stk:
            sb = lambda n, sh, dt: stk.enter_context(nc.sbuf_tensor(n, list(sh), dt))
            psal = lambda n, sh: stk.enter_context(nc.psum_tensor(n, list(sh), F32))

            self.xr_t = sb("xr", [128, DC, SOWN], F32)
            self.xl_t = sb("xl", [128, DC, SOWN], BF16)
            self.id_bf = sb("c_id_bf", [128, 128], BF16)
            self.id_f32 = sb("c_id_f32", [128, 128], F32)
            self.ones_bf = sb("c_ones_bf", [128, 128], BF16)
            self.ones_f32 = sb("c_ones_f32", [128, 128], F32)
            self.swap_bf = sb("c_swap_bf", [128, 128], BF16)
            self.tri_f32 = sb("c_tri_f32", [128, 128], F32)
            self.rope_cos = sb("c_rope_cos", [128, SOWN], BF16)
            self.rope_sin = sb("c_rope_sin", [128, SOWN], BF16)
            self.cvec = sb("c_cvec", [128, 8], F32)
            self.selmask = sb("c_selmask", [128, 8], F32)
            self.bup_sb = sb("c_bup", [128, E * HCN], F32)
            self.bdown_sb = sb("c_bdown", [8, D], BF16)
            self.bo_sb = sb("c_bo", [128, OCN], F32)
            self.bg_sb = sb("c_bg", [NH, 1], F32)
            self.wg_sb = sb("c_wg", [128, DC, NH], BF16)
            self.wg1T_sb = sb("c_wg1t", [128, DC, E], BF16)
            self.wg2T_sb = sb("c_wg2t", [E, E], BF16)
            self.hsel = sb("c_hsel", [NH, NH * 128], BF16)
            self.csel = sb("c_csel", [128, NH * NH], BF16)

            self.ARB_W = 42864
            self.ARF_W = 3112
            self.arb_t = sb("arb", [128, self.ARB_W], BF16)
            self.arf_t = sb("arf", [128, self.ARF_W], F32)

            self.Qt = [psal(f"q{i}", [128, 1024]) for i in range(4)]
            self.bank = [Slot(self.Qt[i // 2][:, (i % 2) * 512:(i % 2) * 512 + 512])
                         for i in range(8)]

            for nm in ("pe", "ve", "se", "ge", "dsy", "dsc", "dgp", "ccs", "outs"):
                setattr(self, nm, Sem(stk.enter_context(nc.semaphore(nm))))

            ARF = Arena(self.arf_t, 0, self.ARF_W)
            self.mcell = ARF.salloc([NH, 16])
            self.dn = ARF.salloc([NH, SOWN])
            self.mneg = ARF.salloc([128, NH])
            self.stat = ARF.salloc([128, 8])
            self.arf_scr = ARF.mark()
            self.ARF = ARF
            self.ARB = Arena(self.arb_t, 0, self.ARB_W)

            self.xr = Slot(self.xr_t)
            self.xl = Slot(self.xl_t)
            self.wg_s = Slot(self.wg_sb)
            self.wg1T = Slot(self.wg1T_sb)
            self.wg2T = Slot(self.wg2T_sb)

            self.p_prep()
            self.p_load_ln1()
            if self.stop_after == "ln1":
                self.p_finish(); self.playback(stk); return nc
            if self.stop_after == "ln1x":
                self.dbg = {
                    "xl0": (Slot(self.xl_t[:, 0, :]), 0, 0),
                    "xl1": (Slot(self.xl_t[:, 1, :]), 128, 0),
                    "xp0": (Slot(self.xp_ln.ap[:, 0, :]), 256, 0),
                    "st": (self.dbg_st, 384, 0),
                    "xlntm": (self.dbg_xlntm, 512, 0),
                    "xtm": (self.dbg_xtm, 640, 0),
                }
                self.p_dump(); self.playback(stk); return nc
            self.p_attn_a()
            if self.stop_after == "attn_a_dbg":
                self.p_dump(); self.playback(stk); return nc
            if self.stop_after != "attn_a":
                self.p_ln_fm()
                self.p_ffn()
                if self.stop_after == "ffn1_dbg":
                    self.p_dump(); self.playback(stk); return nc
                if self.stop_after != "ffn1":
                    self.p_ln_fm()
                    self.p_attn_b()
                    if self.stop_after == "attn_b_dbg":
                        self.p_dump(); self.playback(stk); return nc
                    if self.stop_after != "attn_b":
                        self.p_ln_fm()
                        self.p_ffn()
            self.p_finish()
            self.playback(stk)
            return nc

    def qpair(self, j):
        return self.Qt[j], [self.bank[2 * j], self.bank[2 * j + 1]]

    def p_prep(self):
        DT = self.DT
        for e_ in range(E):
            self.dma("gpsimd", self.dgp, lambda e, i=e_:
                     e.dma_start(out=self.wup_bf[i], in_=DT["Wup"][i]))
        self.t_prep_up = self.dgp.n
        for e_ in range(E):
            self.dma("gpsimd", self.dgp, lambda e, i=e_:
                     e.dma_start(out=self.wdown_bf[i], in_=DT["Wdown"][i]))
        self.t_prep_down = self.dgp.n
        for nm, key in (("Wq", "q"), ("Wk", "k"), ("Wv", "v"), ("Wo", "o")):
            for hh in range(2):
                self.dma("gpsimd", self.dgp, lambda e, s=DT[nm], d=self.wpj_bf[key], i=hh:
                         e.dma_start(out=d[i * 1024:(i + 1) * 1024, :],
                                     in_=s[i * 1024:(i + 1) * 1024, :]))
        self.t_prep_pj = self.dgp.n
        for dc in range(DC):
            self.dma("gpsimd", self.dgp, lambda e, d=dc:
                     e.dma_start(out=self.wg_sb[:, d, :],
                                 in_=DT["Wg"][d * 128:(d + 1) * 128, :]),
                     wr=[self.wg_s])
        for nm, dst in (("id_bf", self.id_bf), ("id_f32", self.id_f32),
                        ("ones_bf", self.ones_bf), ("ones_f32", self.ones_f32),
                        ("swap_bf", self.swap_bf), ("tri_f32", self.tri_f32),
                        ("rope_cos", self.rope_cos), ("rope_sin", self.rope_sin),
                        ("cvec", self.cvec), ("selmask", self.selmask),
                        ("bup_pk", self.bup_sb), ("bo_pk", self.bo_sb),
                        ("bg", self.bg_sb), ("hsel_bf", self.hsel),
                        ("csel_bf", self.csel)):
            self.dma("scalar", self.dsc, lambda e, s=DT[nm], d=dst:
                     e.dma_start(out=d[:, :], in_=s[:, :]))
        self.dma("gpsimd", self.dgp, lambda e:
                 e.dma_start(out=self.bdown_sb[:, :], in_=DT["bdown"][:, :]))
        self.t_bdown = self.dgp.n
        self.t_consts = self.dsc.n
        self.wc = [(self.dsc, self.t_consts)]

    def p_load_ln1(self):
        DT = self.DT
        ARF, ARB = self.ARF, self.ARB
        ARF.reset(self.arf_scr)
        x_tm = ARF.salloc([128, D])
        xln_tm = ARB.salloc([128, D])
        self.dbg_xtm = x_tm
        self.dbg_xlntm = xln_tm
        self.dbg_st = self.stat
        self.xp_base = ARB.mark()
        self.xp_ln = ARB.salloc([128, DC, SOWN])
        self.arb_m0 = ARB.mark()
        st = self.stat

        for src, out_fm, raw in ((DT["x_own"], self.xl, self.xr),
                                 (DT["x_part"], self.xp_ln, None)):
            for tci in range(TC):
                self.dma("sync", self.dsy, lambda e, i=tci, s=src:
                         e.dma_start(out=x_tm.ap[:, :], in_=s[i * 128:(i + 1) * 128, :]),
                         wr=[x_tm], extra_waits=self.wc)
                self.ev("vector", lambda e:
                        e.reduce_sum(st.ap[:, 0:1], x_tm.ap[:, :], axis=AX.X),
                        rd=[x_tm], wr=[st])
                self.ev("scalar", lambda e:
                        e.activation(xln_tm.ap[:, :], x_tm.ap[:, :], AF.Square,
                                     accum_out=st.ap[:, 1:2]),
                        rd=[x_tm], wr=[xln_tm, st])

                self.ev("vector", lambda e: e.tensor_scalar(
                    st.ap[:, 2:3], st.ap[:, 0:1], 1.0 / D, None, AL.mult),
                    rd=[st], wr=[st])
                self.ev("vector", lambda e: e.tensor_scalar(
                    st.ap[:, 1:2], st.ap[:, 1:2], 1.0 / D, None, AL.mult),
                    rd=[st], wr=[st])
                self.ev("vector", lambda e: e.scalar_tensor_tensor(
                    st.ap[:, 3:4], st.ap[:, 2:3], st.ap[:, 2:3],
                    st.ap[:, 1:2], AL.mult, AL.subtract),
                    rd=[st], wr=[st])
                self.ev("vector", lambda e: e.tensor_scalar(
                    st.ap[:, 3:4], st.ap[:, 3:4], -1.0, None, AL.mult),
                    rd=[st], wr=[st])
                self.ev("vector", lambda e: e.tensor_scalar(
                    st.ap[:, 3:4], st.ap[:, 3:4], LN_EPS, None, AL.add),
                    rd=[st], wr=[st])
                self.ev("scalar", lambda e:
                        e.activation(st.ap[:, 3:4], st.ap[:, 3:4], AF.Ln),
                        rd=[st], wr=[st])
                self.ev("scalar", lambda e:
                        e.activation(st.ap[:, 3:4], st.ap[:, 3:4], AF.Exp, scale=-0.5),
                        rd=[st], wr=[st])
                self.ev("vector", lambda e:
                        e.tensor_scalar(xln_tm.ap[:, :], x_tm.ap[:, :],
                                        st.ap[:, 2:3], st.ap[:, 3:4],
                                        AL.subtract, AL.mult),
                        rd=[x_tm, st], wr=[xln_tm])
                for dc in range(DC):
                    bk = self.bank[dc % 4]
                    self.gmm(bk.ap[:, 0:128], [bk],
                             [(xln_tm.ap[:, dc * 128:(dc + 1) * 128], self.id_bf[:, :])],
                             rd_slots=[xln_tm])
                    self.ev("vector", lambda e, q=bk.ap, d=dc, i=tci, o=out_fm:
                            e.tensor_copy(o.ap[:, d, i * 128:(i + 1) * 128], q[:, 0:128]),
                            rd=[bk], wr=[out_fm])
                if raw is not None:
                    for dc in range(DC):
                        bk = self.bank[4 + dc % 4]
                        self.gmm(bk.ap[:, 0:128], [bk],
                                 [(x_tm.ap[:, dc * 128:(dc + 1) * 128], self.id_f32[:, :])],
                                 rd_slots=[x_tm])
                        self.ev("vector", lambda e, q=bk.ap, d=dc, i=tci, o=raw:
                                e.tensor_copy(o.ap[:, d, i * 128:(i + 1) * 128],
                                              q[:, 0:128]),
                                rd=[bk], wr=[raw])
        self.barrier()

    def p_ln_fm(self):
        ARF, ARB = self.ARF, self.ARB
        ARF.reset(self.arf_scr)
        ARB.reset()
        xc = ARB.salloc([128, DC, SOWN])
        xq = ARB.salloc([128, DC, SOWN])
        mr_bf = ARB.salloc([1, 2 * SOWN])
        row_f = ARB.salloc([1, 2 * SOWN])
        msq = ARB.salloc([1, SOWN])
        tmp = ARF.salloc([128, 512])
        Q2, b2 = self.qpair(2)
        Q3, b3 = self.qpair(3)

        for dc in range(DC):
            self.ev("vector", lambda e, d=dc:
                    e.tensor_copy(xc.ap[:, d, :], self.xr_t[:, d, :]),
                    rd=[self.xr], wr=[xc])
            self.ev("scalar", lambda e, d=dc:
                    e.activation(xq.ap[:, d, :], xc.ap[:, d, :], AF.Square),
                    rd=[xc], wr=[xq])
        for Qrow, brow, srcs in ((Q2, b2, xc), (Q3, b3, xq)):
            for blk in range(2):
                bk = brow[blk]
                w = bk.wwait() + srcs.rwait()
                for dc in range(DC):
                    self.t_mm(Qrow[0:1, blk * 512:(blk + 1) * 512],
                              self.ones_bf[:, 0:1],
                              srcs.ap[:, dc, blk * 512:(blk + 1) * 512],
                              dc == 0, dc == DC - 1, waits=w if dc == 0 else ())
                bk.note_w(self.pe)
                srcs.note_r(self.pe)

        b2s = Slot(None)
        b2s.wt = b2[1].wt
        self.ev("vector", lambda e: e.tensor_scalar(
            row_f.ap[0:1, 0:SOWN], Q2[0:1, 0:1024], 1.0 / D, None, AL.mult),
            rd=[b2[0], b2[1]], wr=[row_f])
        self.ev("vector", lambda e: e.tensor_scalar(
            row_f.ap[0:1, SOWN:2 * SOWN], Q3[0:1, 0:1024], 1.0 / D, None, AL.mult),
            rd=[b3[0], b3[1]], wr=[row_f])
        self.ev("vector", lambda e: e.tensor_tensor(
            msq.ap[0:1, :], row_f.ap[0:1, 0:SOWN], row_f.ap[0:1, 0:SOWN], AL.mult),
            rd=[row_f], wr=[msq])
        self.ev("vector", lambda e: e.tensor_tensor(
            row_f.ap[0:1, SOWN:2 * SOWN], row_f.ap[0:1, SOWN:2 * SOWN],
            msq.ap[0:1, :], AL.subtract),
            rd=[msq, row_f], wr=[row_f])
        self.ev("vector", lambda e: e.tensor_scalar(
            row_f.ap[0:1, SOWN:2 * SOWN], row_f.ap[0:1, SOWN:2 * SOWN],
            LN_EPS, None, AL.add),
            rd=[row_f], wr=[row_f])
        self.ev("scalar", lambda e: e.activation(
            row_f.ap[0:1, SOWN:2 * SOWN], row_f.ap[0:1, SOWN:2 * SOWN], AF.Ln),
            rd=[row_f], wr=[row_f])
        self.ev("scalar", lambda e: e.activation(
            row_f.ap[0:1, SOWN:2 * SOWN], row_f.ap[0:1, SOWN:2 * SOWN],
            AF.Exp, scale=-0.5),
            rd=[row_f], wr=[row_f])
        self.ev("vector", lambda e: e.tensor_copy(mr_bf.ap[0:1, :], row_f.ap[0:1, :]),
                rd=[row_f], wr=[mr_bf])

        for blk in range(2):
            self.gmm(Q3[:, 0:512], [b3[0]],
                     [(self.ones_bf[0:1, :], mr_bf.ap[0:1, blk * 512:(blk + 1) * 512])],
                     rd_slots=[mr_bf])
            self.gmm(Q3[:, 512:1024], [b3[1]],
                     [(self.ones_bf[0:1, :],
                       mr_bf.ap[0:1, SOWN + blk * 512:SOWN + (blk + 1) * 512])],
                     rd_slots=[mr_bf])
            for dc in range(DC):
                self.ev("vector", lambda e, d=dc, b=blk: e.tensor_tensor(
                    tmp.ap[:, :], self.xr_t[:, d, b * 512:(b + 1) * 512],
                    Q3[:, 0:512], AL.subtract),
                    rd=[self.xr, b3[0]], wr=[tmp])
                self.ev("vector", lambda e, d=dc, b=blk: e.tensor_tensor(
                    self.xl_t[:, d, b * 512:(b + 1) * 512], tmp.ap[:, :],
                    Q3[:, 512:1024], AL.mult),
                    rd=[tmp, b3[1]], wr=[self.xl])
        self.barrier()

    def p_attn_a(self):
        ARB, ARF = self.ARB, self.ARF
        DT = self.DT
        ARB.reset(self.arb_m0)
        bas_nat = ARB.salloc([64, D])
        w_nat = ARB.salloc([128, DC, HARM])
        basisT = {nm: ARB.salloc([128, DC, HARM]) for nm in ("q", "k", "v", "o")}
        wT = {nm: ARB.salloc([64, D]) for nm in ("q", "k", "v", "o")}
        res_q = ARB.salloc([64, SOWN])
        res_k = ARB.salloc([64, SEQ])
        res_v = ARB.salloc([64, SEQ])
        reso_sb = ARB.salloc([64, SOWN])
        dnr_bf = ARB.salloc([NH, SOWN])
        ARF.reset(self.arf_scr)
        ph_nat = ARF.salloc([128, HARM])
        am_nat = ARF.salloc([128, HARM])

        HS = Arena(self.arb_t, self.xp_base, DC * SOWN)
        mcell, mneg, dn = self.mcell, self.mneg, self.dn
        xlkeep = ARB.salloc([128, SOWN])
        self.ev("vector", lambda e: e.tensor_copy(xlkeep.ap[:, :], self.xl_t[:, 0, :]),
                rd=[self.xl], wr=[xlkeep])
        self.xlkeep = xlkeep

        for nm in ("q", "k", "v", "o"):
            self.dma("gpsimd", self.dgp, lambda e, s=DT[f"basis_{nm}"]:
                     e.dma_start(out=bas_nat.ap[:, :], in_=s[:, :]), wr=[bas_nat])
            for dc in range(DC):
                bk = self.bank[4]
                self.gmm(bk.ap[:, 0:64], [bk],
                         [(bas_nat.ap[:, dc * 128:(dc + 1) * 128],
                           self.id_bf[0:64, 0:64])],
                         rd_slots=[bas_nat])
                self.ev("vector", lambda e, q=bk.ap, d=dc, t=basisT[nm].ap:
                        e.tensor_copy(t[:, d, :], q[:, 0:64]),
                        rd=[bk], wr=[basisT[nm]])
            for oc in range(DC):
                self.dma("sync", self.dsy, lambda e, s=DT[f"phase_{nm}"], o=oc:
                         e.dma_start(out=ph_nat.ap[:, :], in_=s[o * 128:(o + 1) * 128, :]),
                         wr=[ph_nat], extra_waits=self.wc)
                self.dma("sync", self.dsy, lambda e, s=DT[f"amp_{nm}"], o=oc:
                         e.dma_start(out=am_nat.ap[:, :], in_=s[o * 128:(o + 1) * 128, :]),
                         wr=[am_nat])
                self.ev("scalar", lambda e:
                        e.activation(ph_nat.ap[:, :], ph_nat.ap[:, :], AF.Sin,
                                     bias=self.cvec[:, 0:1]),
                        rd=[ph_nat], wr=[ph_nat])
                self.ev("vector", lambda e, o=oc:
                        e.tensor_tensor(w_nat.ap[:, o, :], ph_nat.ap[:, :],
                                        am_nat.ap[:, :], AL.mult),
                        rd=[ph_nat, am_nat], wr=[w_nat])
            for oc in range(DC):
                bk = self.bank[5]
                self.gmm(bk.ap[0:64, 0:128], [bk],
                         [(w_nat.ap[:, oc, :], self.id_bf[:, :])], rd_slots=[w_nat])
                self.ev("vector", lambda e, q=bk.ap, o=oc, t=wT[nm].ap:
                        e.tensor_copy(t[:, o * 128:(o + 1) * 128], q[0:64, 0:128]),
                        rd=[bk], wr=[wT[nm]])

        def resonance(bt, src, n_tok, base, dst):
            for blk in range(n_tok // 512):
                bk = self.bank[6 + blk % 2]
                self.gmm(bk.ap[0:64, :], [bk],
                         [(bt.ap[:, dc, :], src.ap[:, dc, blk * 512:(blk + 1) * 512])
                          for dc in range(DC)],
                         rd_slots=[bt, src])
                self.ev("vector", lambda e, q=bk.ap, b=blk, d=dst.ap, bc=base:
                        e.tensor_copy(d[:, bc + b * 512:bc + (b + 1) * 512], q[0:64, :]),
                        rd=[bk], wr=[dst])
        resonance(basisT["q"], self.xl, SOWN, 0, res_q)
        resonance(basisT["k"], self.xl, SOWN, 0, res_k)
        resonance(basisT["k"], self.xp_ln, SOWN, SOWN, res_k)
        resonance(basisT["v"], self.xl, SOWN, 0, res_v)
        resonance(basisT["v"], self.xp_ln, SOWN, SOWN, res_v)

        kh = HS.salloc([128, SEQ])
        qh = HS.salloc([128, SOWN])
        vfm = HS.salloc([128, SEQ])
        vtm = HS.salloc([128, DC, 128])
        q2 = HS.salloc([128, SOWN])
        k2 = HS.salloc([128, SEQ])
        ptb = [HS.salloc([128, SOWN]) for _ in range(2)]
        brow = HS.salloc([NH, SOWN])
        qn2 = ARF.salloc([NH, SOWN])
        y_raw = self.xl

        for h in range(NH):
            def expand(dst, wt_s, res_s, ncol, qj, eng):
                Qx, bx = self.qpair(qj)
                for blk in range(ncol // 512):
                    bk = bx[blk % 2]
                    self.gmm(Qx[:, (blk % 2) * 512:(blk % 2) * 512 + 512], [bk],
                             [(wt_s.ap[:, h * 128:(h + 1) * 128],
                               res_s.ap[:, blk * 512:(blk + 1) * 512])],
                             rd_slots=[wt_s, res_s])
                    if eng == "scalar":
                        self.ev("scalar", lambda e, q=bk.ap, b=blk, d=dst.ap:
                                e.activation(d[:, b * 512:(b + 1) * 512], q, AF.Copy),
                                rd=[bk], wr=[dst])
                    else:
                        self.ev("vector", lambda e, q=bk.ap, b=blk, d=dst.ap:
                                e.tensor_copy(d[:, b * 512:(b + 1) * 512], q),
                                rd=[bk], wr=[dst])
            expand(qh, wT["q"], res_q, SOWN, 0, "scalar")
            expand(kh, wT["k"], res_k, SEQ, 1, "scalar")
            expand(vfm, wT["v"], res_v, SEQ, 0, "vector")
            for kc in range(DC):
                bk = self.bank[2 + kc % 2]
                self.gmm(bk.ap[:, 0:128], [bk],
                         [(vfm.ap[:, kc * 128:(kc + 1) * 128], self.id_bf[:, :])],
                         rd_slots=[vfm])
                self.ev("vector", lambda e, q=bk.ap, c=kc:
                        e.tensor_copy(vtm.ap[:, c, :], q[:, 0:128]),
                        rd=[bk], wr=[vtm])
            self.ev("scalar", lambda e: e.activation(q2.ap[:, :], qh.ap[:, :], AF.Square),
                    rd=[qh], wr=[q2])
            self.ev("scalar", lambda e: e.activation(k2.ap[:, :], kh.ap[:, :], AF.Square),
                    rd=[kh], wr=[k2])
            for blk in range(4):
                bk = self.bank[4]
                self.gmm(bk.ap[0:NH, :], [bk],
                         [(self.ones_bf[:, 0:NH],
                           k2.ap[:, blk * 512:(blk + 1) * 512])],
                         rd_slots=[k2])
                self.ev("vector", lambda e, b=blk, q=bk.ap:
                        e.reduce_max(mcell.ap[0:NH, b:b + 1],
                                     q[0:NH, 0:512], axis=AX.X),
                        rd=[bk], wr=[mcell])
            self.ev("vector", lambda e: e.reduce_max(
                mcell.ap[0:NH, 8:9], mcell.ap[0:NH, 0:4], axis=AX.X),
                rd=[mcell], wr=[mcell])
            self.ev("vector", lambda e: e.tensor_scalar(
                mcell.ap[0:NH, 9:10], mcell.ap[0:NH, 8:9], SM_SCALE * SM_SCALE,
                None, AL.mult),
                rd=[mcell], wr=[mcell])
            for blk in range(2):
                bk = self.bank[4]
                self.gmm(bk.ap[0:NH, :], [bk],
                         [(self.ones_bf[:, 0:NH],
                           q2.ap[:, blk * 512:(blk + 1) * 512])],
                         rd_slots=[q2])
                self.ev("vector", lambda e, b=blk, q=bk.ap: e.tensor_copy(
                    qn2.ap[0:NH, b * 512:(b + 1) * 512], q[0:NH, 0:512]),
                    rd=[bk], wr=[qn2])
            self.ev("vector", lambda e: e.tensor_scalar(
                qn2.ap[0:NH, :], qn2.ap[0:NH, :], mcell.ap[0:NH, 9:10],
                None, AL.mult),
                rd=[qn2, mcell], wr=[qn2])
            self.ev("scalar", lambda e: e.activation(
                qn2.ap[0:NH, :], qn2.ap[0:NH, :], AF.Ln),
                rd=[qn2], wr=[qn2])
            self.ev("scalar", lambda e: e.activation(
                qn2.ap[0:NH, :], qn2.ap[0:NH, :], AF.Exp, scale=0.5),
                rd=[qn2], wr=[qn2])
            self.ev("vector", lambda e: e.tensor_scalar(
                qn2.ap[0:NH, :], qn2.ap[0:NH, :], -1.0, None, AL.mult),
                rd=[qn2], wr=[qn2])
            self.ev("vector", lambda e: e.tensor_scalar(
                qn2.ap[0:NH, :], qn2.ap[0:NH, :], 78.0, None, AL.add),
                rd=[qn2], wr=[qn2])
            self.ev("vector", lambda e: e.tensor_scalar(
                brow.ap[0:NH, :], qn2.ap[0:NH, :], 1.0 / SM_SCALE, None, AL.mult),
                rd=[qn2], wr=[brow])

            av_b = [self.bank[4], self.bank[5]]
            dn_b = self.bank[6]
            for kc in range(DC):
                qj = kc % 2
                Qx, bx = self.qpair(qj)
                sw = (bx[0].wwait() + bx[1].wwait() + kh.rwait() + qh.rwait() +
                      brow.rwait())
                for qb in range(2):
                    self.t_mm(Qx[:, qb * 512:(qb + 1) * 512],
                              kh.ap[:, kc * 128:(kc + 1) * 128],
                              qh.ap[:, qb * 512:(qb + 1) * 512],
                              True, False, waits=sw if qb == 0 else ())
                    self.t_mm(Qx[:, qb * 512:(qb + 1) * 512],
                              self.ones_bf[0:1, :],
                              brow.ap[0:1, qb * 512:(qb + 1) * 512],
                              False, True)
                bx[0].note_w(self.pe)
                bx[1].note_w(self.pe)
                kh.note_r(self.pe)
                qh.note_r(self.pe)
                brow.note_r(self.pe)
                pt = ptb[kc % 2]
                self.ev("scalar", lambda e, s=Qx, p=pt.ap:
                        e.activation(p[:, :], s[:, 0:1024], AF.Exp, scale=SM_SCALE),
                        rd=[bx[0], bx[1]], wr=[pt])
                w_av = (av_b[0].wwait() + av_b[1].wwait() +
                        pt.rwait() + vtm.rwait()) if kc == 0 else pt.rwait()
                if h == 0 and kc == 0:
                    w_av = w_av + dn_b.wwait() + self.bank[7].wwait()
                for qb in range(2):
                    self.t_mm(av_b[qb].ap, vtm.ap[:, kc, :],
                              pt.ap[:, qb * 512:(qb + 1) * 512], kc == 0, kc == DC - 1,
                              waits=w_av if qb == 0 else ())
                    self.t_mm(self.Qt[3][0:NH, qb * 512:(qb + 1) * 512],
                              self.csel[:, h * NH:(h + 1) * NH],
                              pt.ap[:, qb * 512:(qb + 1) * 512],
                              h == 0 and kc == 0,
                              h == NH - 1 and kc == DC - 1)
                pt.note_r(self.pe)
            av_b[0].note_w(self.pe)
            av_b[1].note_w(self.pe)
            vtm.note_r(self.pe)
            for qb in range(2):
                self.ev("vector", lambda e, b=qb, hh=h: e.tensor_copy(
                    y_raw.ap[:, hh, b * 512:(b + 1) * 512], av_b[b].ap),
                    rd=[av_b[qb]], wr=[y_raw])

        dn_b.note_w(self.pe)
        self.bank[7].note_w(self.pe)
        self.ev("vector", lambda e:
                e.tensor_copy(dn.ap[0:NH, :], self.Qt[3][0:NH, 0:1024]),
                rd=[dn_b, self.bank[7]], wr=[dn])
        self.ev("vector", lambda e: e.reciprocal(dn.ap[:, :], dn.ap[:, :]),
                rd=[dn], wr=[dn])
        self.ev("vector", lambda e: e.tensor_copy(dnr_bf.ap[:, :], dn.ap[:, :]),
                rd=[dn], wr=[dnr_bf])

        ro_b = [self.bank[2], self.bank[3]]
        w_ro = ro_b[0].wwait() + ro_b[1].wwait()
        for h in range(NH):
            for qb in range(2):
                bk = self.bank[0]
                self.gmm(bk.ap, [bk],
                         [(self.hsel[:, h * 128:(h + 1) * 128],
                           dnr_bf.ap[0:NH, qb * 512:(qb + 1) * 512])],
                         rd_slots=[dnr_bf])
                self.ev("vector", lambda e, q=bk.ap, hh=h, b=qb: e.tensor_tensor(
                    y_raw.ap[:, hh, b * 512:(b + 1) * 512],
                    y_raw.ap[:, hh, b * 512:(b + 1) * 512], q, AL.mult),
                    rd=[bk, y_raw], wr=[y_raw])
                self.t_mm(self.Qt[1][0:64, qb * 512:(qb + 1) * 512],
                          basisT["o"].ap[:, h, :],
                          y_raw.ap[:, h, qb * 512:(qb + 1) * 512],
                          h == 0, h == NH - 1,
                          waits=(w_ro + y_raw.rwait()) if h == 0 else y_raw.rwait())
        ro_b[0].note_w(self.pe)
        ro_b[1].note_w(self.pe)
        y_raw.note_r(self.pe)
        for qb in range(2):
            self.ev("vector", lambda e, b=qb: e.tensor_copy(
                reso_sb.ap[:, b * 512:(b + 1) * 512],
                self.Qt[1][0:64, b * 512:(b + 1) * 512]),
                rd=[ro_b[qb]], wr=[reso_sb])

        for oc in range(OCN):
            for qb in range(2):
                bk = self.bank[4 + (oc * 2 + qb) % 2]
                self.gmm(bk.ap, [bk],
                         [(wT["o"].ap[:, oc * 128:(oc + 1) * 128],
                           reso_sb.ap[:, qb * 512:(qb + 1) * 512])],
                         rd_slots=[reso_sb])
                self.ev("vector", lambda e, q=bk.ap, o=oc, b=qb: e.tensor_tensor(
                    self.xr_t[:, o, b * 512:(b + 1) * 512], q,
                    self.xr_t[:, o, b * 512:(b + 1) * 512], AL.add),
                    rd=[bk, self.xr], wr=[self.xr])
        self.dbg = {
            "dn": (dn, 0, 0),
            "mcell": (mcell, 16, 0),
            "mneg": (mneg, 32, 0),
            "yraw0": (Slot(y_raw.ap[:, 0, :]), 160, 0),
            "resq": (res_q, 288, 0),
            "resk": (res_k, 352, 0),
            "qh": (qh, 416, 0),
            "kh": (kh, 544, 0),
            "wtq": (wT["q"], 672, 0),
            "dnrbf": (dnr_bf, 736, 0),
            "reso": (reso_sb, 752, 0),
            "basisTq": (Slot(basisT["q"].ap[:, 0, :]), 816, 0),
            "xl0": (self.xlkeep, 880, 0),
        }
        self.barrier()

    def p_dump(self):
        for nm, (slot, row, col) in self.dbg.items():
            ap = slot.ap
            np_, nf = ap.shape[0], 1
            sh = list(ap.shape)
            nf = 1
            for s_ in sh[1:]:
                nf *= s_
            self.dma("gpsimd", self.dgp, lambda e, a=ap, r=row, c2=col, p=sh[0], f=nf:
                     e.dma_start(out=self.y_out[r:r + p, c2:c2 + f], in_=a),
                     rd=[slot])
        self.emit("gpsimd", lambda e: e.nop(), waits=[(self.dgp, self.dgp.n)])

    def p_ffn(self):
        ARB, ARF = self.ARB, self.ARF
        DT = self.DT
        ARB.reset()
        ARF.reset(self.arf_scr)
        srep = ARB.salloc([128, E, 512])
        slab = [ARB.salloc([128, 1024]) for _ in range(2)]
        husk = ARB.salloc([128, 512])
        sco = ARB.salloc([E, SOWN])
        arb_gate = ARB.mark()
        t1 = ARB.salloc([E, SOWN])
        stb = ARB.salloc([128, E])
        wg1n = ARB.salloc([E, D])
        lg_f = ARF.salloc([E, SOWN])
        lgt = ARF.salloc([128, E])
        gsc = ARF.salloc([128, 8])
        bank = self.bank

        self.dma("gpsimd", self.dgp, lambda e:
                 e.dma_start(out=wg1n.ap[:, :], in_=DT["Wg1"][:, :]), wr=[wg1n])
        for dc in range(DC):
            bk = bank[0]
            self.gmm(bk.ap[:, 0:8], [bk],
                     [(wg1n.ap[:, dc * 128:(dc + 1) * 128], self.id_bf[0:8, 0:8])],
                     rd_slots=[wg1n])
            self.ev("vector", lambda e, q=bk.ap, d=dc:
                    e.tensor_copy(self.wg1T_sb[:, d, :], q[:, 0:8]),
                    rd=[bk], wr=[self.wg1T])
        self.dma("gpsimd", self.dgp, lambda e:
                 e.dma_start(out=stb.ap[0:8, 0:8], in_=DT["Wg2"][:, :]), wr=[stb])
        bk = bank[0]
        self.gmm(bk.ap[0:8, 0:8], [bk], [(stb.ap[0:8, 0:8], self.id_bf[0:8, 0:8])],
                 rd_slots=[stb])
        self.ev("vector", lambda e, q=bk.ap: e.tensor_copy(self.wg2T_sb[:, :],
                                                           q[0:8, 0:8]),
                rd=[bk], wr=[self.wg2T])

        for b in range(2):
            bk = bank[2 + b]
            w = bk.wwait() + self.wg1T.rwait() + self.xl.rwait()
            for dc in range(DC):
                self.t_mm(self.Qt[1][0:E, b * 512:(b + 1) * 512],
                          self.wg1T_sb[:, dc, :],
                          self.xl_t[:, dc, b * 512:(b + 1) * 512],
                          dc == 0, dc == DC - 1, waits=w if dc == 0 else ())
            bk.note_w(self.pe)
        self.wg1T.note_r(self.pe)
        self.xl.note_r(self.pe)
        self.ev("scalar", lambda e: e.activation(t1.ap[:, :], self.Qt[1][0:E, 0:1024],
                                                 AF.Tanh),
                rd=[bank[2], bank[3]], wr=[t1])
        for b in range(2):
            bk = bank[2 + b]
            self.gmm(self.Qt[1][0:E, b * 512:(b + 1) * 512], [bk],
                     [(self.wg2T_sb[:, :], t1.ap[:, b * 512:(b + 1) * 512])],
                     rd_slots=[self.wg2T, t1])
        self.ev("vector", lambda e: e.tensor_copy(lg_f.ap[:, :], self.Qt[1][0:E, 0:1024]),
                rd=[bank[2], bank[3]], wr=[lg_f])

        for tci in range(TC):
            bk = bank[0]
            self.gmm(bk.ap[:, 0:8], [bk],
                     [(lg_f.ap[:, tci * 128:(tci + 1) * 128], self.id_f32[0:8, 0:8])],
                     rd_slots=[lg_f])
            self.ev("vector", lambda e, q=bk.ap: e.tensor_copy(lgt.ap[:, :], q[:, 0:8]),
                    rd=[bk], wr=[lgt])
            self.ev("vector", lambda e: e.reduce_max(gsc.ap[:, 0:1], lgt.ap[:, :],
                                                     axis=AX.X),
                    rd=[lgt], wr=[gsc])
            self.ev("vector", lambda e: e.tensor_scalar(
                stb.ap[:, 0:E], lgt.ap[:, :], gsc.ap[:, 0:1], None, AL.is_ge),
                rd=[lgt, gsc], wr=[stb])
            self.ev("vector", lambda e: e.scalar_tensor_tensor(
                lgt.ap[:, :], stb.ap[:, 0:E], -1e30, lgt.ap[:, :], AL.mult, AL.add),
                rd=[stb, lgt], wr=[lgt])
            self.ev("vector", lambda e: e.reduce_max(gsc.ap[:, 1:2], lgt.ap[:, :],
                                                     axis=AX.X),
                    rd=[lgt], wr=[gsc])
            self.ev("vector", lambda e: e.tensor_scalar(
                lgt.ap[:, :], lgt.ap[:, :], gsc.ap[:, 1:2], None, AL.is_ge),
                rd=[lgt, gsc], wr=[lgt])
            self.ev("vector", lambda e: e.tensor_tensor(
                gsc.ap[:, 2:3], gsc.ap[:, 0:1], gsc.ap[:, 1:2], AL.subtract),
                rd=[gsc], wr=[gsc])
            self.ev("scalar", lambda e: e.activation(gsc.ap[:, 2:3], gsc.ap[:, 2:3],
                                                     AF.Sigmoid),
                    rd=[gsc], wr=[gsc])
            self.ev("vector", lambda e: e.tensor_scalar(
                gsc.ap[:, 3:4], gsc.ap[:, 2:3], -1.0, None, AL.mult),
                rd=[gsc], wr=[gsc])
            self.ev("vector", lambda e: e.tensor_scalar(
                gsc.ap[:, 3:4], gsc.ap[:, 3:4], 1.0, None, AL.add),
                rd=[gsc], wr=[gsc])
            self.ev("vector", lambda e: e.tensor_scalar(
                stb.ap[:, 0:E], stb.ap[:, 0:E], gsc.ap[:, 2:3], None, AL.mult),
                rd=[stb, gsc], wr=[stb])
            self.ev("vector", lambda e: e.scalar_tensor_tensor(
                stb.ap[:, 0:E], lgt.ap[:, :], gsc.ap[:, 3:4], stb.ap[:, 0:E],
                AL.mult, AL.add),
                rd=[lgt, gsc, stb], wr=[stb])
            bk = bank[1]
            self.gmm(bk.ap[0:8, 0:128], [bk], [(stb.ap[:, 0:E], self.id_bf[:, :])],
                     rd_slots=[stb])
            self.ev("vector", lambda e, q=bk.ap, i=tci:
                    e.tensor_copy(sco.ap[:, i * 128:(i + 1) * 128], q[0:8, 0:128]),
                    rd=[bk], wr=[sco])

        self.barrier()
        ARB.reset(arb_gate)
        hbuf = ARB.salloc([128, E, HCN, 512])
        for tt in range(2):
            for e_ in range(E):
                bk = bank[7]
                self.gmm(bk.ap, [bk],
                         [(self.hsel[0:8, e_ * 128:(e_ + 1) * 128],
                           sco.ap[0:8, tt * 512:(tt + 1) * 512])],
                         rd_slots=[sco])
                self.ev("vector", lambda e, q=bk.ap, i=e_:
                        e.tensor_copy(srep.ap[:, i, :], q),
                        rd=[bk], wr=[srep])
            for e_ in range(E):
                for dc in range(DC):
                    sla = slab[dc % 2]
                    self.dma("sync", self.dsy, lambda e, i=e_, d=dc, s=sla:
                             e.dma_start(out=s.ap[:, :],
                                         in_=self.wup_bf[i][:, d * 128:(d + 1) * 128],
                                         transpose=True),
                             wr=[sla], extra_waits=[(self.dgp, self.t_prep_up)])
                    for hc in range(HCN):
                        bk = bank[hc]
                        w = (bk.wwait() + sla.rwait() + self.xl.rwait()) if dc == 0 \
                            else (sla.rwait() if hc == 0 else ())
                        self.t_mm(bk.ap, sla.ap[:, hc * 128:(hc + 1) * 128],
                                  self.xl_t[:, dc, tt * 512:(tt + 1) * 512],
                                  dc == 0, dc == DC - 1, waits=w)
                        if dc == DC - 1:
                            bk.note_w(self.pe)
                    sla.note_r(self.pe)
                self.xl.note_r(self.pe)
                for hc in range(HCN):
                    bk = bank[hc]
                    self.ev("scalar", lambda e, q=bk.ap, i=e_, c=hc: e.activation(
                        husk.ap[:, :], q, AF.Silu,
                        bias=self.bup_sb[:, i * HCN + c:i * HCN + c + 1]),
                        rd=[bk], wr=[husk])
                    self.ev("vector", lambda e, i=e_, c=hc: e.tensor_tensor(
                        hbuf.ap[:, i, c, :], husk.ap[:, :], srep.ap[:, i, :], AL.mult),
                        rd=[husk, srep], wr=[hbuf])
            for og in range(2):
                for oc2 in range(8):
                    oc = og * 8 + oc2
                    bk = bank[oc2]
                    self.t_mm(bk.ap, self.bdown_sb[0:8, oc * 128:(oc + 1) * 128],
                              sco.ap[:, tt * 512:(tt + 1) * 512], True, False,
                              waits=bk.wwait() + sco.rwait() +
                              [(self.dgp, self.t_bdown)])
                sco.note_r(self.pe)
                for e_ in range(E):
                    for hc in range(HCN):
                        sla = slab[(e_ * HCN + hc) % 2]
                        self.dma("sync", self.dsy, lambda e, i=e_, c=hc, g=og, s=sla:
                                 e.dma_start(
                                     out=s.ap[:, :],
                                     in_=self.wdown_bf[i][g * 1024:(g + 1) * 1024,
                                                          c * 128:(c + 1) * 128],
                                     transpose=True),
                                 wr=[sla], extra_waits=[(self.dgp, self.t_prep_down)])
                        last = (e_ == E - 1 and hc == HCN - 1)
                        for oc2 in range(8):
                            self.t_mm(bank[oc2].ap, sla.ap[:, oc2 * 128:(oc2 + 1) * 128],
                                      hbuf.ap[:, e_, hc, :], False, last,
                                      waits=(sla.rwait() + hbuf.rwait())
                                      if oc2 == 0 else ())
                        sla.note_r(self.pe)
                hbuf.note_r(self.pe)
                for oc2 in range(8):
                    bank[oc2].note_w(self.pe)
                    oc = og * 8 + oc2
                    self.ev("vector", lambda e, q=bank[oc2].ap, o=oc, b=tt:
                            e.tensor_tensor(
                                self.xr_t[:, o, b * 512:(b + 1) * 512], q,
                                self.xr_t[:, o, b * 512:(b + 1) * 512], AL.add),
                            rd=[bank[oc2], self.xr], wr=[self.xr])
        self.dbg = {
            "sco": (sco, 0, 0),
            "lgf": (lg_f, 8, 0),
            "hbuf00": (Slot(hbuf.ap[:, 0, 0, :]), 16, 0),
            "xl0": (Slot(self.xl_t[:, 0, :]), 144, 0),
            "srep0": (Slot(srep.ap[:, 0, :]), 272, 0),
        }
        self.barrier()

    def p_attn_b(self):
        ARB, ARF = self.ARB, self.ARF
        ARB.reset()
        ARF.reset(self.arf_scr)
        phq = ARB.salloc([128, NH, SOWN])
        phk = ARB.salloc([128, NH, SOWN])
        arb_m1 = ARB.mark()
        PA = Arena(self.arb_t, arb_m1, self.ARB_W - arb_m1)
        pslab = [PA.salloc([128, 512]) for _ in range(2)]
        cgb = PA.salloc([NH, SOWN])
        cbf = PA.salloc([NH, SOWN])
        zrow = PA.salloc([NH, 128])
        pa_perm = PA.mark()
        exf = PA.salloc([128, 512])
        rbuf = [PA.salloc([128, SOWN]) for _ in range(2)]
        swb = PA.salloc([128, 512])
        ta = PA.salloc([128, 512])
        rot = ta
        beta = ARF.salloc([NH, SOWN])
        ctm = ARF.salloc([128, TC * NH])
        rctm = ARF.salloc([128, TC * NH])
        gcol = ARF.salloc([128, 1])
        oex = ARF.salloc([NH, TC])
        svf = ARF.salloc([128, 128])
        bank = self.bank
        dn = self.dn
        ynum = phk

        bkg = [bank[0], bank[1]]
        for b in range(2):
            w = bkg[b].wwait() + self.wg_s.rwait() + self.xl.rwait()
            for dc in range(DC):
                self.t_mm(self.Qt[0][0:NH, b * 512:(b + 1) * 512],
                          self.wg_sb[:, dc, :],
                          self.xl_t[:, dc, b * 512:(b + 1) * 512], dc == 0, dc == DC - 1,
                          waits=w if dc == 0 else ())
            bkg[b].note_w(self.pe)
        self.wg_s.note_r(self.pe)
        self.ev("scalar", lambda e: e.activation(beta.ap[:, :], self.Qt[0][0:NH, 0:1024],
                                                 AF.Sigmoid, bias=self.bg_sb[:, 0:1]),
                rd=[bkg[0], bkg[1]], wr=[beta], extra_waits=self.wc)
        self.ev("vector", lambda e: e.tensor_scalar(
            beta.ap[:, :], beta.ap[:, :], BETA_MAX, None, AL.min),
            rd=[beta], wr=[beta])
        self.ev("vector", lambda e: e.tensor_scalar(
            beta.ap[:, :], beta.ap[:, :], BETA_MIN, None, AL.max),
            rd=[beta], wr=[beta])
        self.ev("vector", lambda e: e.memset(zrow.ap[:, :], 0.0), wr=[zrow])
        for c in range(TC):
            self.ev("vector", lambda e, i=c: e.tensor_tensor_scan(
                beta.ap[:, i * 128:(i + 1) * 128], beta.ap[:, i * 128:(i + 1) * 128],
                zrow.ap[:, :], 1.0, AL.mult, AL.add),
                rd=[beta, zrow], wr=[beta])
        cvar = beta
        self.ev("vector", lambda e: e.tensor_copy(cbf.ap[:, :], cvar.ap[:, :]),
                rd=[cvar], wr=[cbf])
        for c in range(TC):
            bk = bank[2]
            self.gmm(bk.ap[:, 0:NH], [bk],
                     [(cvar.ap[:, c * 128:(c + 1) * 128], self.id_f32[0:NH, 0:NH])],
                     rd_slots=[cvar])
            self.ev("vector", lambda e, q=bk.ap, i=c:
                    e.tensor_copy(ctm.ap[:, i * NH:(i + 1) * NH], q[:, 0:NH]),
                    rd=[bk], wr=[ctm])
        self.ev("vector", lambda e: e.reciprocal(rctm.ap[:, :], ctm.ap[:, :]),
                rd=[ctm], wr=[rctm])
        self.ev("vector", lambda e: e.tensor_tensor_scan(
            oex.ap[:, 0:TC], cvar.ap[:, 127::128], zrow.ap[:, 0:TC],
            1.0, AL.mult, AL.add),
            rd=[cvar, zrow], wr=[oex])
        self.ev("vector", lambda e: e.tensor_copy(exf.ap[0:NH, 0:TC - 1],
                                                  oex.ap[:, 0:TC - 1]),
                rd=[oex], wr=[exf])
        self.ev("vector", lambda e: e.tensor_copy(oex.ap[:, 1:TC],
                                                  exf.ap[0:NH, 0:TC - 1]),
                rd=[exf], wr=[oex])
        self.ev("vector", lambda e: e.memset(oex.ap[:, 0:1], 1.0), wr=[oex])
        for c in range(TC):
            self.ev("vector", lambda e, i=c: e.tensor_scalar(
                cgb.ap[:, i * 128:(i + 1) * 128], cvar.ap[:, i * 128:(i + 1) * 128],
                oex.ap[:, i:i + 1], None, AL.mult),
                rd=[cvar, oex], wr=[cgb])

        def proj_rope(wdram, phi):
            for og in range(4):
                for dc in range(DC):
                    sla = pslab[dc % 2]
                    self.dma("scalar", self.dsc, lambda e, g=og, d=dc, s=sla:
                             e.dma_start(out=s.ap[:, :],
                                         in_=wdram[d * 128:(d + 1) * 128,
                                                   g * 512:(g + 1) * 512]),
                             wr=[sla], extra_waits=[(self.dgp, self.t_prep_pj)])
                    for oc4 in range(4):
                        Qx, bx = self.qpair(oc4)
                        w = (bx[0].wwait() + bx[1].wwait() + sla.rwait() +
                             self.xl.rwait()) if dc == 0 \
                            else (sla.rwait() if oc4 == 0 else ())
                        for b in range(2):
                            self.t_mm(Qx[:, b * 512:(b + 1) * 512],
                                      sla.ap[:, oc4 * 128:(oc4 + 1) * 128],
                                      self.xl_t[:, dc, b * 512:(b + 1) * 512],
                                      dc == 0, dc == DC - 1,
                                      waits=w if b == 0 else ())
                        if dc == DC - 1:
                            bx[0].note_w(self.pe)
                            bx[1].note_w(self.pe)
                    sla.note_r(self.pe)
                self.xl.note_r(self.pe)
                for pair in range(2):
                    for i2 in range(2):
                        oc4 = pair * 2 + i2
                        Qx, bx = self.qpair(oc4)
                        self.ev("scalar", lambda e, s=Qx, r=rbuf[i2].ap:
                                e.activation(r[:, :], s[:, 0:1024], AF.Copy),
                                rd=[bx[0], bx[1]], wr=[rbuf[i2]])
                    for i2 in range(2):
                        oc4 = pair * 2 + i2
                        h = og * 4 + oc4
                        rb = rbuf[i2]
                        Qx, bx = self.qpair(i2)
                        for b in range(2):
                            self.t_mm(Qx[:, b * 512:(b + 1) * 512], self.swap_bf[:, :],
                                      rb.ap[:, b * 512:(b + 1) * 512], True, True,
                                      waits=(bx[0].wwait() + bx[1].wwait() + rb.rwait())
                                      if b == 0 else ())
                        bx[0].note_w(self.pe)
                        bx[1].note_w(self.pe)
                        rb.note_r(self.pe)
                        for b in range(2):
                            sl = b * 512
                            self.ev("vector", lambda e, s=Qx, o=sl: e.tensor_copy(
                                swb.ap[:, :], s[:, o:o + 512]),
                                rd=[bx[b]], wr=[swb])
                            self.ev("vector", lambda e, r=rb.ap, o=sl: e.tensor_tensor(
                                ta.ap[:, :], r[:, o:o + 512],
                                self.rope_cos[:, o:o + 512], AL.mult),
                                rd=[rb], wr=[ta], extra_waits=self.wc)
                            self.ev("vector", lambda e, o=sl: e.tensor_tensor(
                                swb.ap[:, :], swb.ap[:, :],
                                self.rope_sin[:, o:o + 512], AL.mult),
                                rd=[swb], wr=[swb])
                            self.ev("vector", lambda e: e.tensor_tensor(
                                rot.ap[:, :], ta.ap[:, :], swb.ap[:, :], AL.add),
                                rd=[swb], wr=[rot])
                            self.ev("scalar", lambda e: e.activation(
                                exf.ap[:, :], rot.ap[:, :], AF.Exp),
                                rd=[rot], wr=[exf])
                            self.ev("scalar", lambda e: e.activation(
                                rot.ap[:, :], rot.ap[:, :], AF.Relu),
                                rd=[rot], wr=[rot])
                            self.ev("vector", lambda e, hh=h, p=phi, o=sl:
                                    e.scalar_tensor_tensor(
                                        p.ap[:, hh, o:o + 512], rot.ap[:, :], 1.0,
                                        exf.ap[:, :], AL.add, AL.min),
                                    rd=[rot, exf], wr=[phi])
        proj_rope(self.wpj_bf["q"], phq)
        proj_rope(self.wpj_bf["k"], phk)

        self.barrier()
        PA.reset(pa_perm)
        Abf = PA.salloc([128, 128])
        ktb = PA.salloc([128, 128])
        vtb = PA.salloc([128, 128])
        qgb = PA.salloc([128, 128])
        sbf = PA.salloc([128, 128])
        zrep = PA.salloc([128, NH])
        dtf = ARF.salloc([128, 128])
        pa_scan = PA.mark()
        sendb = PA.salloc([128, NH, 129])
        vgrp = PA.salloc([128, 4, SOWN])
        p_crep, p_A = bank[0], bank[1]
        p_kT, p_Sc = bank[2], bank[3]
        p_nm, p_z = bank[4], bank[5]
        p_dn, p_vT = bank[6], bank[7]

        self.ev("vector", lambda e: e.memset(dn.ap[:, :], DN_EPS), wr=[dn])

        for vg in range(4):
            for dc in range(DC):
                sla = pslab[dc % 2]
                self.dma("scalar", self.dsc, lambda e, g=vg, d=dc, s=sla:
                         e.dma_start(out=s.ap[:, :],
                                     in_=self.wpj_bf["v"][d * 128:(d + 1) * 128,
                                                          g * 512:(g + 1) * 512]),
                         wr=[sla], extra_waits=[(self.dgp, self.t_prep_pj)])
                for oc4 in range(4):
                    Qx, bx = self.qpair(oc4)
                    w = (bx[0].wwait() + bx[1].wwait() + sla.rwait() +
                         self.xl.rwait()) if dc == 0 \
                        else (sla.rwait() if oc4 == 0 else ())
                    for b in range(2):
                        self.t_mm(Qx[:, b * 512:(b + 1) * 512],
                                  sla.ap[:, oc4 * 128:(oc4 + 1) * 128],
                                  self.xl_t[:, dc, b * 512:(b + 1) * 512],
                                  dc == 0, dc == DC - 1, waits=w if b == 0 else ())
                    if dc == DC - 1:
                        bx[0].note_w(self.pe)
                        bx[1].note_w(self.pe)
                sla.note_r(self.pe)
            for oc4 in range(4):
                Qx, bx = self.qpair(oc4)
                self.ev("scalar", lambda e, s=Qx, i=oc4:
                        e.activation(vgrp.ap[:, i, :], s[:, 0:1024], AF.Copy),
                        rd=[bx[0], bx[1]], wr=[vgrp])

            for hl in range(4):
                h = vg * 4 + hl
                self.ev("vector", lambda e: e.memset(svf.ap[:, :], 0.0), wr=[svf])
                self.ev("vector", lambda e: e.memset(sbf.ap[:, :], 0.0), wr=[sbf])
                self.ev("vector", lambda e: e.memset(zrep.ap[:, :], 0.0), wr=[zrep])
                for c in range(TC):
                    self.gmm(p_crep.ap[:, 0:128], [p_crep],
                             [(self.hsel[:, h * 128:(h + 1) * 128],
                               cbf.ap[0:NH, c * 128:(c + 1) * 128])],
                             rd_slots=[cbf])
                    self.ev("vector", lambda e: e.tensor_copy(
                        gcol.ap[:, 0:1], p_crep.ap[:, 127:128]),
                        rd=[p_crep], wr=[gcol])
                    self.ev("vector", lambda e, hh=h, cc=c: e.scalar_tensor_tensor(
                        dtf.ap[:, :], p_crep.ap[:, 0:128],
                        rctm.ap[:, cc * NH + hh:cc * NH + hh + 1],
                        self.tri_f32[:, :], AL.mult, AL.mult),
                        rd=[p_crep, rctm], wr=[dtf])
                    self.gmm(p_A.ap[:, 0:128], [p_A],
                             [(phk.ap[:, h, c * 128:(c + 1) * 128],
                               phq.ap[:, h, c * 128:(c + 1) * 128])],
                             rd_slots=[phk, phq])
                    self.ev("vector", lambda e: e.tensor_tensor(
                        Abf.ap[:, :], p_A.ap[:, 0:128], dtf.ap[:, :], AL.mult),
                        rd=[p_A, dtf], wr=[Abf])
                    self.gmm(p_kT.ap[:, 0:128], [p_kT],
                             [(phk.ap[:, h, c * 128:(c + 1) * 128], self.id_bf[:, :])],
                             rd_slots=[phk])
                    self.ev("vector", lambda e: e.tensor_scalar(
                        ktb.ap[:, :], p_kT.ap[:, 0:128], dtf.ap[:, 127:128],
                        None, AL.mult),
                        rd=[p_kT, dtf], wr=[ktb])
                    self.gmm(p_vT.ap[:, 0:128], [p_vT],
                             [(vgrp.ap[:, hl, c * 128:(c + 1) * 128], self.id_bf[:, :])],
                             rd_slots=[vgrp])
                    self.ev("vector", lambda e: e.tensor_copy(vtb.ap[:, :],
                                                              p_vT.ap[:, 0:128]),
                            rd=[p_vT], wr=[vtb])
                    self.ev("vector", lambda e, hh=h, cc=c: e.tensor_tensor(
                        qgb.ap[:, :], phq.ap[:, hh, cc * 128:(cc + 1) * 128],
                        p_crep.ap[:, 0:128], AL.mult),
                        rd=[phq, p_crep], wr=[qgb])
                    self.gmm(p_nm.ap[:, 0:128], [p_nm],
                             [(vtb.ap[:, :], Abf.ap[:, :]),
                              (sbf.ap[:, :], qgb.ap[:, :])],
                             rd_slots=[vtb, Abf, sbf, qgb])
                    self.ev("vector", lambda e, hh=h, cc=c: e.tensor_copy(
                        ynum.ap[:, hh, cc * 128:(cc + 1) * 128], p_nm.ap[:, 0:128]),
                        rd=[p_nm], wr=[ynum])
                    self.gmm(p_dn.ap[0:NH, 0:128], [p_dn],
                             [(self.csel[:, h * NH:(h + 1) * NH], Abf.ap[:, :]),
                              (zrep.ap[:, :], qgb.ap[:, :])],
                             rd_slots=[Abf, qgb, zrep])
                    self.ev("vector", lambda e, cc=c: e.tensor_tensor(
                        dn.ap[0:NH, cc * 128:(cc + 1) * 128],
                        dn.ap[0:NH, cc * 128:(cc + 1) * 128],
                        p_dn.ap[0:NH, 0:128], AL.add),
                        rd=[p_dn, dn], wr=[dn])
                    self.gmm(p_Sc.ap[:, 0:128], [p_Sc],
                             [(ktb.ap[:, :], vtb.ap[:, :])],
                             rd_slots=[ktb, vtb])
                    self.ev("vector", lambda e: e.scalar_tensor_tensor(
                        svf.ap[:, :], svf.ap[:, :], gcol.ap[:, 0:1],
                        p_Sc.ap[:, 0:128], AL.mult, AL.add),
                        rd=[p_Sc, gcol, svf], wr=[svf])
                    if c == TC - 1:
                        self.ev("vector", lambda e, hh=h: e.tensor_copy(
                            sendb.ap[:, hh, 0:128], svf.ap[:, :]),
                            rd=[svf], wr=[sendb])
                    else:
                        self.ev("vector", lambda e: e.tensor_copy(sbf.ap[:, :],
                                                                  svf.ap[:, :]),
                                rd=[svf], wr=[sbf])
                    self.gmm(p_z.ap[:, 0:NH], [p_z],
                             [(ktb.ap[:, :], self.ones_bf[:, 0:NH])],
                             rd_slots=[ktb])
                    self.ev("vector", lambda e, hh=h: e.scalar_tensor_tensor(
                        zrep.ap[:, hh:hh + 1], zrep.ap[:, hh:hh + 1],
                        gcol.ap[:, 0:1], p_z.ap[:, 0:1], AL.mult, AL.add),
                        rd=[p_z, gcol, zrep], wr=[zrep])
                self.ev("vector", lambda e, hh=h: e.tensor_copy(
                    sendb.ap[:, hh, 128:129], zrep.ap[:, hh:hh + 1]),
                    rd=[zrep], wr=[sendb])

        cc_in3 = self.cc_in.rearrange("p (a b) -> p a b", a=NH)
        self.dma("scalar", self.dsc, lambda e:
                 e.dma_start(out=cc_in3, in_=sendb.ap), rd=[sendb])
        t_ccin = self.dsc.n
        self.ccs.tick()
        self.emit("gpsimd", lambda e: e.collective_compute(
            "AllGather", AL.bypass, replica_groups=[list(range(NCORES))],
            ins=[self.cc_in.opt()], outs=[self.cc_out.opt()]),
            waits=[(self.dsc, t_ccin)], inc=(self.ccs, 1))
        t_cc = self.ccs.n

        PA.reset(pa_scan)
        sinb = PA.salloc([128, NH, 128])
        zin_c = PA.salloc([128, NH])
        rkb = PA.salloc([128, NH, 129])
        dnr = cbf
        self.ev("vector", lambda e: e.memset(sinb.ap[:, :, :], 0.0), wr=[sinb],
                extra_waits=[(self.dsc, self.dsc.n)] + sendb.rd)
        self.ev("vector", lambda e: e.memset(zin_c.ap[:, :], 0.0), wr=[zin_c])
        for r in range(NCORES):
            rb = rkb
            self.dma("sync", self.dsy, lambda e, i=r, d=rb:
                     e.dma_start(out=d.ap[:, :, :], in_=self.cc_out[i].rearrange(
                         "p (a b) -> p a b", a=NH)),
                     wr=[rb], extra_waits=[(self.ccs, t_cc)])
            self.ev("vector", lambda e, i=r, s=rb.ap: e.scalar_tensor_tensor(
                sinb.ap[:, :, :], s[:, :, 0:128], self.selmask[:, i:i + 1],
                sinb.ap[:, :, :], AL.mult, AL.add),
                rd=[rb, sinb], wr=[sinb])
            self.ev("vector", lambda e, i=r, s=rb.ap: e.scalar_tensor_tensor(
                zin_c.ap[:, :], s[:, :, 128], self.selmask[:, i:i + 1],
                zin_c.ap[:, :], AL.mult, AL.add),
                rd=[rb, zin_c], wr=[zin_c])

        zinsel = PA.salloc([128, NH])
        for h in range(NH):
            self.ev("vector", lambda e: e.memset(zinsel.ap[:, :], 0.0), wr=[zinsel])
            self.ev("vector", lambda e, hh=h: e.tensor_copy(
                zinsel.ap[:, hh:hh + 1], zin_c.ap[:, hh:hh + 1]),
                rd=[zin_c], wr=[zinsel])
            for c in range(TC):
                self.gmm(p_crep.ap[:, 0:128], [p_crep],
                         [(self.hsel[:, h * 128:(h + 1) * 128],
                           cgb.ap[0:NH, c * 128:(c + 1) * 128])],
                         rd_slots=[cgb])
                self.ev("vector", lambda e, hh=h, cc=c: e.tensor_tensor(
                    qgb.ap[:, :], phq.ap[:, hh, cc * 128:(cc + 1) * 128],
                    p_crep.ap[:, 0:128], AL.mult),
                    rd=[phq, p_crep], wr=[qgb])
                self.gmm(p_nm.ap[:, 0:128], [p_nm],
                         [(sinb.ap[:, h, :], qgb.ap[:, :])],
                         rd_slots=[sinb, qgb])
                self.ev("vector", lambda e, hh=h, cc=c: e.tensor_tensor(
                    ynum.ap[:, hh, cc * 128:(cc + 1) * 128],
                    ynum.ap[:, hh, cc * 128:(cc + 1) * 128],
                    p_nm.ap[:, 0:128], AL.add),
                    rd=[p_nm, ynum], wr=[ynum])
                self.gmm(p_dn.ap[0:NH, 0:128], [p_dn],
                         [(zinsel.ap[:, :], qgb.ap[:, :])],
                         rd_slots=[zinsel, qgb])
                self.ev("vector", lambda e, cc=c: e.tensor_tensor(
                    dn.ap[0:NH, cc * 128:(cc + 1) * 128],
                    dn.ap[0:NH, cc * 128:(cc + 1) * 128],
                    p_dn.ap[0:NH, 0:128], AL.add),
                    rd=[p_dn, dn], wr=[dn])

        self.ev("vector", lambda e: e.reciprocal(dn.ap[:, :], dn.ap[:, :]),
                rd=[dn], wr=[dn])
        self.ev("vector", lambda e: e.tensor_copy(dnr.ap[:, :], dn.ap[:, :]),
                rd=[dn], wr=[dnr])
        for h in range(NH):
            for b in range(2):
                bk = bank[0]
                self.gmm(bk.ap, [bk],
                         [(self.hsel[:, h * 128:(h + 1) * 128],
                           dnr.ap[0:NH, b * 512:(b + 1) * 512])],
                         rd_slots=[dnr])
                self.ev("vector", lambda e, hh=h, bb=b: e.tensor_tensor(
                    ynum.ap[:, hh, bb * 512:(bb + 1) * 512],
                    ynum.ap[:, hh, bb * 512:(bb + 1) * 512],
                    self.Qt[0][:, 0:512], AL.mult),
                    rd=[bk, ynum], wr=[ynum])
        for og in range(4):
            for dc in range(DC):
                sla = pslab[dc % 2]
                self.dma("scalar", self.dsc, lambda e, g=og, d=dc, s=sla:
                         e.dma_start(out=s.ap[:, :],
                                     in_=self.wpj_bf["o"][d * 128:(d + 1) * 128,
                                                          g * 512:(g + 1) * 512]),
                         wr=[sla], extra_waits=[(self.dgp, self.t_prep_pj)])
                for oc4 in range(4):
                    Qx, bx = self.qpair(oc4)
                    w = (bx[0].wwait() + bx[1].wwait() + sla.rwait() + ynum.rwait()) \
                        if dc == 0 else (sla.rwait() if oc4 == 0 else ())
                    for b in range(2):
                        self.t_mm(Qx[:, b * 512:(b + 1) * 512],
                                  sla.ap[:, oc4 * 128:(oc4 + 1) * 128],
                                  ynum.ap[:, dc, b * 512:(b + 1) * 512],
                                  dc == 0, dc == DC - 1, waits=w if b == 0 else ())
                    if dc == DC - 1:
                        bx[0].note_w(self.pe)
                        bx[1].note_w(self.pe)
                sla.note_r(self.pe)
            ynum.note_r(self.pe)
            for oc4 in range(4):
                Qx, bx = self.qpair(oc4)
                oc = og * 4 + oc4
                for b in range(2):
                    self.ev("vector", lambda e, s=Qx, o=oc, bb=b:
                            e.scalar_tensor_tensor(
                                self.xr_t[:, o, bb * 512:(bb + 1) * 512],
                                s[:, bb * 512:(bb + 1) * 512],
                                self.bo_sb[:, o:o + 1],
                                self.xr_t[:, o, bb * 512:(bb + 1) * 512],
                                AL.add, AL.add),
                            rd=[bx[b], self.xr], wr=[self.xr])
        self.dbg = {
            "beta_c": (beta, 0, 0),
            "cbf": (cbf, 16, 0),
            "cgb": (cgb, 32, 0),
            "dn": (dn, 48, 0),
            "phq0": (Slot(phq.ap[:, 0, :]), 64, 0),
            "phk0": (Slot(phk.ap[:, 0, :]), 192, 0),
            "ynum0": (Slot(ynum.ap[:, 0, :]), 320, 0),
            "sinb0": (Slot(sinb.ap[:, 0, :]), 448, 0),
            "sendb0": (Slot(sendb.ap[:, 0, :]), 576, 0),
            "ynum1": (Slot(ynum.ap[:, 4, :]), 704, 0),
            "ynum8": (Slot(ynum.ap[:, 12, :]), 832, 0),
        }
        self.barrier()

    def p_finish(self):
        ARF = self.ARF
        ARF.reset(self.arf_scr)
        ot = ARF.salloc([128, D])
        for tci in range(TC):
            for dc in range(DC):
                bk = self.bank[dc % 4]
                self.gmm(bk.ap[:, 0:128], [bk],
                         [(self.xr_t[:, dc, tci * 128:(tci + 1) * 128],
                           self.id_f32[:, :])],
                         rd_slots=[self.xr])
                self.ev("vector", lambda e, q=bk.ap, d=dc:
                        e.tensor_copy(ot.ap[:, d * 128:(d + 1) * 128], q[:, 0:128]),
                        rd=[bk], wr=[ot])
            self.dma("sync", self.outs, lambda e, i=tci:
                     e.dma_start(out=self.y_out[i * 128:(i + 1) * 128, :],
                                 in_=ot.ap[:, :]),
                     rd=[ot])
        self.emit("sync", lambda e: e.nop(), waits=[(self.outs, self.outs.n)])

    def playback(self, stk):
        block = stk.enter_context(self.nc.Block())

        @block.tensor
        def _(e):
            for f in self.ops["tensor"]:
                f(e)

        @block.vector
        def _(e):
            for f in self.ops["vector"]:
                f(e)

        @block.scalar
        def _(e):
            for f in self.ops["scalar"]:
                f(e)

        @block.gpsimd
        def _(e):
            for f in self.ops["gpsimd"]:
                f(e)

        @block.sync
        def _(e):
            for f in self.ops["sync"]:
                f(e)



def _make_consts(core_id):
    j = core_id % 2
    I = np.eye(128)
    ones = np.ones((128, 128))
    swap = np.zeros((128, 128))
    for m in range(128):
        swap[(m + 64) % 128, m] = 1.0
    tri = (np.arange(128)[:, None] <= np.arange(128)[None, :]).astype(np.float64)
    pos = np.arange(j * SOWN, (j + 1) * SOWN, dtype=np.float64)
    inv_freq = 1.0 / (10000.0 ** (np.arange(64, dtype=np.float64) / 64.0))
    fr = inv_freq[:, None] * pos[None, :]
    cos = np.cos(fr)
    sin = np.sin(fr)
    rope_cos = np.concatenate([cos, cos], axis=0)
    rope_sin = np.concatenate([-sin, sin], axis=0)
    cv = np.zeros((128, 8))
    cv[:, 0] = np.pi / 2
    sel = np.zeros((128, 8))
    if j == 1:
        sel[:, core_id - 1] = 1.0
    hsel = np.zeros((16, 16 * 128))
    for h in range(16):
        hsel[h, h * 128:(h + 1) * 128] = 1.0
    csel = np.zeros((128, 16 * 16))
    for h in range(16):
        csel[:, h * 16 + h] = 1.0
    return {
        "id_bf": I.astype(bfloat16), "id_f32": I.astype(np.float32),
        "ones_bf": ones.astype(bfloat16), "ones_f32": ones.astype(np.float32),
        "swap_bf": swap.astype(bfloat16), "tri_f32": tri.astype(np.float32),
        "rope_cos": rope_cos.astype(bfloat16), "rope_sin": rope_sin.astype(bfloat16),
        "cvec": cv.astype(np.float32), "selmask": sel.astype(np.float32),
        "hsel_bf": hsel.astype(bfloat16), "csel_bf": csel.astype(bfloat16),
    }


_NC_CACHE = {}


def _get_nc(stop_after):
    if stop_after not in _NC_CACHE:
        _NC_CACHE[stop_after] = MK(stop_after).build()
    return _NC_CACHE[stop_after]


def run_dev(inputs, stop_after="full"):
    nc = _get_nc(stop_after)
    x = np.ascontiguousarray(np.asarray(inputs["L0_attn_a_hidden_states"], np.float32))
    g = lambda n: np.ascontiguousarray(np.asarray(inputs[n], np.float32))
    shared = {}
    for nm in ("q", "k", "v", "o"):
        shared[f"basis_{nm}"] = g(f"L0_attn_a_basis_{nm}")
        shared[f"phase_{nm}"] = g(f"L0_attn_a_phase_{nm}")
        shared[f"amp_{nm}"] = g(f"L0_attn_a_amp_{nm}")
    for nm in ("Wq", "Wk", "Wv", "Wg", "Wo"):
        shared[nm] = g(f"L0_attn_b_{nm}")
    shared["bg"] = g("L0_attn_b_bg").reshape(NH, 1)
    shared["bo_pk"] = np.ascontiguousarray(
        g("L0_attn_b_bo").reshape(OCN, 128).T)
    shared["Wg1"] = g("L0_ffn_Wg1")
    shared["Wg2"] = g("L0_ffn_Wg2")
    shared["Wup"] = g("L0_ffn_Wup")
    shared["bup_pk"] = np.ascontiguousarray(
        g("L0_ffn_bup").reshape(E, HCN, 128).transpose(2, 0, 1).reshape(128, E * HCN))
    shared["Wdown"] = g("L0_ffn_Wdown")
    shared["bdown"] = g("L0_ffn_bdown")

    in_maps = []
    for c in range(NCORES):
        b, j = c // 2, c % 2
        m = dict(shared)
        m["x_own"] = np.ascontiguousarray(x[b, j * SOWN:(j + 1) * SOWN])
        m["x_part"] = np.ascontiguousarray(x[b, (1 - j) * SOWN:(2 - j) * SOWN])
        m.update(_make_consts(c))
        in_maps.append(m)

    res = run_bass_kernel_spmd(nc, in_maps, core_ids=list(range(NCORES)))
    out = np.empty((4, SEQ, D), np.float32)
    for c in range(NCORES):
        b, j = c // 2, c % 2
        out[b, j * SOWN:(j + 1) * SOWN] = res.results[c]["y_out"]
    return out


def kernel(**inputs):
    return run_dev(inputs, stop_after="full")


# version 02
# speedup vs baseline: 1.0251x; 1.0251x over previous
import math
from contextlib import ExitStack

import numpy as np
import ml_dtypes

import concourse.bass as bass
import concourse.mybir as mybir
from concourse.bass_utils import run_bass_kernel_spmd

F32 = mybir.dt.float32
BF16 = mybir.dt.bfloat16
AF = mybir.ActivationFunctionType
AL = mybir.AluOpType
AX = mybir.AxisListType
bfloat16 = ml_dtypes.bfloat16

NCORES = 8
D = 2048
NH = 16
DH = 128
HARM = 64
SEQ = 2048
SOWN = 1024
TC = 8
DC = 16
E = 8
HE = 1024
HCN = 8
OCN = 16
SM_SCALE = 1.0 / math.sqrt(DH)
LN_EPS = 1e-5
DN_EPS = 1e-6
BETA_MIN, BETA_MAX = 0.8, 0.9995


class Sem:
    def __init__(self, handle):
        self.h = handle
        self.n = 0

    def tick(self, k=1):
        self.n += k
        return self.n


class Slot:

    def __init__(self, ap=None):
        self.ap = ap
        self.wt = None
        self.rd = []

    def wwait(self):
        w = list(self.rd)
        if self.wt is not None:
            w.append(self.wt)
        return w

    def rwait(self):
        return [self.wt] if self.wt is not None else []

    def note_w(self, sem):
        self.wt = (sem, sem.n)
        self.rd = []

    def note_r(self, sem):
        self.rd.append((sem, sem.n))


def _dedup(waits):
    best = {}
    for (s, v) in waits:
        if s is None:
            continue
        if s not in best or v > best[s]:
            best[s] = v
    return [(s, v) for s, v in best.items()]


class Arena:
    def __init__(self, t, base, width):
        self.t = t
        self.base = base
        self.width = width
        self.top = base

    def alloc(self, shape):
        n = 1
        for s in shape[1:]:
            n *= s
        off = self.top
        self.top += n
        assert self.top <= self.base + self.width, \
            f"arena overflow: {self.top} > {self.base + self.width}"
        ap = self.t[0:shape[0], off:off + n]
        if len(shape) == 3:
            ap = ap.rearrange("p (a b) -> p a b", a=shape[1])
        elif len(shape) == 4:
            ap = ap.rearrange("p (a b c) -> p a b c", a=shape[1], b=shape[2])
        return ap

    def salloc(self, shape):
        return Slot(self.alloc(shape))

    def mark(self):
        return self.top

    def reset(self, mark=None):
        self.top = self.base if mark is None else mark


class MK:
    def __init__(self, stop_after="full"):
        self.stop_after = stop_after
        self.nc = bass.Bass("TRN2", target_bir_lowering=False)
        self.ops = {k: [] for k in ("tensor", "vector", "scalar", "gpsimd", "sync")}

    def emit(self, eng, fn, waits=(), inc=None):
        waits = tuple((s.h, v) for (s, v) in _dedup(waits))

        def thunk(e, fn=fn, waits=waits, inc=inc):
            for (h, v) in waits:
                e.wait_ge(h, v)
            ins = fn(e)
            if inc is not None:
                ins.then_inc(inc[0].h, inc[1])

        self.ops[eng].append(thunk)

    def t_mm(self, dst, lhsT, rhs, start, stop, waits=()):
        self.pe.tick()
        self.emit("tensor",
                  lambda e, d=dst, l=lhsT, r=rhs, st=start, sp=stop:
                  e.matmul(d, l, r, start=st, stop=sp),
                  waits=waits, inc=(self.pe, 1))
        return self.pe.n

    def gmm(self, dst, banks, parts, rd_slots=(), extra_waits=()):
        w = list(extra_waits)
        for b in banks:
            w += b.wwait()
        for s in rd_slots:
            w += s.rwait()
        n = len(parts)
        for i, (l, r) in enumerate(parts):
            self.t_mm(dst, l, r, i == 0, i == n - 1, waits=w if i == 0 else ())
        for b in banks:
            b.note_w(self.pe)
        for s in rd_slots:
            s.note_r(self.pe)
        return self.pe.n

    def ev(self, eng, fn, rd=(), wr=(), extra_waits=()):
        sem = {"vector": self.ve, "scalar": self.se, "gpsimd": self.ge}[eng]
        w = list(extra_waits)
        for s in rd:
            w += s.rwait()
        for s in wr:
            w += s.wwait()
        sem.tick()
        self.emit(eng, fn, waits=w, inc=(sem, 1))
        for s in rd:
            s.note_r(sem)
        for s in wr:
            s.note_w(sem)
        return sem.n

    def dma(self, eng, qsem, fn, rd=(), wr=(), extra_waits=()):
        w = list(extra_waits) + list(getattr(self, "fence", ()))
        for s in rd:
            w += s.rwait()
        for s in wr:
            w += s.wwait()
        qsem.tick(16)
        self.emit(eng, fn, waits=w, inc=(qsem, 16))
        for s in rd:
            s.note_r(qsem)
        for s in wr:
            s.note_w(qsem)
        return qsem.n

    def barrier(self):
        pts = [(self.pe, self.pe.n), (self.ve, self.ve.n), (self.se, self.se.n),
               (self.ge, self.ge.n)]
        for eng, own in (("tensor", self.pe), ("vector", self.ve),
                         ("scalar", self.se), ("gpsimd", self.ge)):
            w = [(s, v) for (s, v) in pts if s is not own]
            self.emit(eng, lambda e: e.nop(), waits=w)
        self.fence = pts

    def build(self):
        nc = self.nc
        self.DT = {}

        def din(name, shape, dtype=F32):
            self.DT[name] = nc.dram_tensor(name, list(shape), dtype,
                                           kind="ExternalInput").ap()

        din("x_own", [SOWN, D]); din("x_part", [SOWN, D])
        for nm in ("q", "k", "v", "o"):
            din(f"basis_{nm}", [HARM, D])
            din(f"phase_{nm}", [D, HARM])
            din(f"amp_{nm}", [D, HARM])
        for nm in ("Wq", "Wk", "Wv", "Wo"):
            din(nm, [D, D])
        din("Wg", [D, NH]); din("bg", [NH, 1]); din("bo_pk", [128, OCN])
        din("Wg1", [E, D]); din("Wg2", [E, E])
        din("Wup", [E, HE, D]); din("bup_pk", [128, E * HCN])
        din("Wdown", [E, D, HE]); din("bdown", [8, D])
        din("id_bf", [128, 128], BF16); din("id_f32", [128, 128], F32)
        din("ones_bf", [128, 128], BF16); din("ones_f32", [128, 128], F32)
        din("swap_bf", [128, 128], BF16); din("tri_f32", [128, 128], F32)
        din("rope_cos", [128, SOWN], BF16); din("rope_sin", [128, SOWN], BF16)
        din("cvec", [128, 8], F32); din("selmask", [128, 8], F32)
        din("hsel_bf", [NH, NH * 128], BF16)
        din("csel_bf", [128, NH * NH], BF16)

        self.y_out = nc.dram_tensor("y_out", [SOWN, D], F32, kind="ExternalOutput").ap()
        self.wup_bf = nc.dram_tensor("wup_bf", [E, HE, D], BF16).ap()
        self.wdown_bf = nc.dram_tensor("wdown_bf", [E, D, HE], BF16).ap()
        self.wpj_bf = {nm: nc.dram_tensor(f"w{nm}_bf", [D, D], BF16).ap()
                       for nm in ("q", "k", "v", "o")}
        self.bas_bf = {nm: nc.dram_tensor(f"bas{nm}_bf", [HARM, D], BF16).ap()
                       for nm in ("q", "k", "v", "o")}
        self.cc_in = nc.dram_tensor("cc_in", [128, NH * 129], BF16).ap()
        self.cc_out = nc.dram_tensor("cc_out", [NCORES, 128, NH * 129], BF16).ap()

        with ExitStack() as stk:
            sb = lambda n, sh, dt: stk.enter_context(nc.sbuf_tensor(n, list(sh), dt))
            psal = lambda n, sh: stk.enter_context(nc.psum_tensor(n, list(sh), F32))

            self.xr_t = sb("xr", [128, DC, SOWN], F32)
            self.xl_t = sb("xl", [128, DC, SOWN], BF16)
            self.id_bf = sb("c_id_bf", [128, 128], BF16)
            self.id_f32 = sb("c_id_f32", [128, 128], F32)
            self.ones_bf = sb("c_ones_bf", [128, 128], BF16)
            self.ones_f32 = sb("c_ones_f32", [128, 128], F32)
            self.swap_bf = sb("c_swap_bf", [128, 128], BF16)
            self.tri_f32 = sb("c_tri_f32", [128, 128], F32)
            self.rope_cos = sb("c_rope_cos", [128, SOWN], BF16)
            self.rope_sin = sb("c_rope_sin", [128, SOWN], BF16)
            self.cvec = sb("c_cvec", [128, 8], F32)
            self.selmask = sb("c_selmask", [128, 8], F32)
            self.bup_sb = sb("c_bup", [128, E * HCN], F32)
            self.bdown_sb = sb("c_bdown", [8, D], BF16)
            self.bo_sb = sb("c_bo", [128, OCN], F32)
            self.bg_sb = sb("c_bg", [NH, 1], F32)
            self.wg_sb = sb("c_wg", [128, DC, NH], BF16)
            self.wg1T_sb = sb("c_wg1t", [128, DC, E], BF16)
            self.wg2T_sb = sb("c_wg2t", [E, E], BF16)
            self.hsel = sb("c_hsel", [NH, NH * 128], BF16)
            self.csel = sb("c_csel", [128, NH * NH], BF16)

            self.ARB_W = 42864
            self.ARF_W = 3112
            self.arb_t = sb("arb", [128, self.ARB_W], BF16)
            self.arf_t = sb("arf", [128, self.ARF_W], F32)
            arb_addr = nc.lookup_mloc(self.arb_t).addr
            self.xf1_t = nc.alloc_sbuf_tensor_at(
                "xf1ov", [128, 2064], F32,
                offset=arb_addr + (self.ARB_W - 4128) * 2)

            self.Qt = [psal(f"q{i}", [128, 1024]) for i in range(4)]
            self.bank = [Slot(self.Qt[i // 2][:, (i % 2) * 512:(i % 2) * 512 + 512])
                         for i in range(8)]

            for nm in ("pe", "ve", "se", "ge", "dsy", "dsc", "dgp", "ccs", "outs"):
                setattr(self, nm, Sem(stk.enter_context(nc.semaphore(nm))))

            ARF = Arena(self.arf_t, 0, self.ARF_W)
            self.mcell = ARF.salloc([NH, 16])
            self.dn = ARF.salloc([NH, SOWN])
            self.mneg = ARF.salloc([128, NH])
            self.stat = ARF.salloc([128, 8])
            self.arf_scr = ARF.mark()
            self.ARF = ARF
            self.ARB = Arena(self.arb_t, 0, self.ARB_W)

            self.xr = Slot(self.xr_t)
            self.xl = Slot(self.xl_t)
            self.wg_s = Slot(self.wg_sb)
            self.wg1T = Slot(self.wg1T_sb)
            self.wg2T = Slot(self.wg2T_sb)

            self.p_prep()
            self.p_load_ln1()
            if self.stop_after == "ln1":
                self.p_finish(); self.playback(stk); return nc
            if self.stop_after == "ln1x":
                self.dbg = {
                    "xl0": (Slot(self.xl_t[:, 0, :]), 0, 0),
                    "xl1": (Slot(self.xl_t[:, 1, :]), 128, 0),
                    "xp0": (Slot(self.xp_ln.ap[:, 0, :]), 256, 0),
                    "st": (self.dbg_st, 384, 0),
                    "xlntm": (self.dbg_xlntm, 512, 0),
                    "xtm": (self.dbg_xtm, 640, 0),
                }
                self.p_dump(); self.playback(stk); return nc
            self.p_attn_a()
            if self.stop_after == "attn_a_dbg":
                self.p_dump(); self.playback(stk); return nc
            if self.stop_after != "attn_a":
                self.p_ln_fm()
                self.p_ffn()
                if self.stop_after == "ffn1_dbg":
                    self.p_dump(); self.playback(stk); return nc
                if self.stop_after != "ffn1":
                    self.p_ln_fm()
                    self.p_attn_b()
                    if self.stop_after == "attn_b_dbg":
                        self.p_dump(); self.playback(stk); return nc
                    if self.stop_after != "attn_b":
                        self.p_ln_fm()
                        self.p_ffn()
            self.p_finish()
            self.playback(stk)
            return nc

    def qpair(self, j):
        return self.Qt[j], [self.bank[2 * j], self.bank[2 * j + 1]]

    def p_prep(self):
        DT = self.DT
        for nm in ("q", "k", "v", "o"):
            self.dma("gpsimd", self.dgp, lambda e, s=DT[f"basis_{nm}"],
                     d=self.bas_bf[nm]:
                     e.dma_start(out=d[:, :], in_=s[:, :]))
        self.t_prep_bas = self.dgp.n
        for dc in range(DC):
            self.dma("gpsimd", self.dgp, lambda e, d=dc:
                     e.dma_start(out=self.wg_sb[:, d, :],
                                 in_=DT["Wg"][d * 128:(d + 1) * 128, :]),
                     wr=[self.wg_s])
        self.dma("gpsimd", self.dgp, lambda e:
                 e.dma_start(out=self.bdown_sb[:, :], in_=DT["bdown"][:, :]))
        self.t_bdown = self.dgp.n
        for e_ in range(E):
            self.dma("gpsimd", self.dgp, lambda e, i=e_:
                     e.dma_start(out=self.wup_bf[i], in_=DT["Wup"][i]))
        self.t_prep_up = self.dgp.n
        for e_ in range(E):
            self.dma("gpsimd", self.dgp, lambda e, i=e_:
                     e.dma_start(out=self.wdown_bf[i], in_=DT["Wdown"][i]))
        self.t_prep_down = self.dgp.n
        for nm, key in (("Wq", "q"), ("Wk", "k"), ("Wv", "v"), ("Wo", "o")):
            for hh in range(2):
                self.dma("gpsimd", self.dgp, lambda e, s=DT[nm], d=self.wpj_bf[key], i=hh:
                         e.dma_start(out=d[i * 1024:(i + 1) * 1024, :],
                                     in_=s[i * 1024:(i + 1) * 1024, :]))
        self.t_prep_pj = self.dgp.n
        for nm, dst in (("id_bf", self.id_bf), ("id_f32", self.id_f32),
                        ("ones_bf", self.ones_bf), ("ones_f32", self.ones_f32),
                        ("swap_bf", self.swap_bf), ("tri_f32", self.tri_f32),
                        ("rope_cos", self.rope_cos), ("rope_sin", self.rope_sin),
                        ("cvec", self.cvec), ("selmask", self.selmask),
                        ("bup_pk", self.bup_sb), ("bo_pk", self.bo_sb),
                        ("bg", self.bg_sb), ("hsel_bf", self.hsel),
                        ("csel_bf", self.csel)):
            self.dma("scalar", self.dsc, lambda e, s=DT[nm], d=dst:
                     e.dma_start(out=d[:, :], in_=s[:, :]))
        self.t_consts = self.dsc.n
        self.wc = [(self.dsc, self.t_consts)]

    def p_load_ln1(self):
        DT = self.DT
        ARF, ARB = self.ARF, self.ARB
        ARF.reset(self.arf_scr)
        xf = [ARF.salloc([128, D]), Slot(self.xf1_t[0:128, 0:2048])]
        stp = [Slot(self.xf1_t[0:128, 2048:2056]),
               Slot(self.xf1_t[0:128, 2056:2064])]
        self.xp_base = ARB.mark()
        self.xp_ln = ARB.salloc([128, DC, SOWN])
        self.arb_m0 = ARB.mark()
        xnb = [ARB.salloc([128, D]) for _ in range(2)]
        xdump = ARB.salloc([128, D])
        self.dbg_xtm = xf[0]
        self.dbg_xlntm = xnb[0]
        self.dbg_st = stp[0]

        jobs = [("own", tci) for tci in range(TC)] + \
               [("part", tci) for tci in range(TC)]
        for j, (kind, tci) in enumerate(jobs):
            p = j % 2
            src = DT["x_own"] if kind == "own" else DT["x_part"]
            x, st, xn = xf[p], stp[p], xnb[p]
            qeng, qsem = (("sync", self.dsy), ("scalar", self.dsc))[p]
            xa, sa, na = x.ap, st.ap, xn.ap
            self.dma(qeng, qsem, lambda e, i=tci, s=src, d=xa:
                     e.dma_start(out=d[:, :], in_=s[i * 128:(i + 1) * 128, :]),
                     wr=[x])
            self.ev("scalar", lambda e, xa=xa, sa=sa:
                    e.activation(xdump.ap[:, :], xa[:, :], AF.Copy,
                                 accum_out=sa[:, 0:1]),
                    rd=[x], wr=[st])
            self.ev("scalar", lambda e, xa=xa, sa=sa:
                    e.activation(xdump.ap[:, :], xa[:, :], AF.Square,
                                 accum_out=sa[:, 1:2]),
                    rd=[x], wr=[st])
            self.ev("vector", lambda e, sa=sa: e.tensor_scalar(
                sa[:, 2:3], sa[:, 0:1], 1.0 / D, None, AL.mult),
                rd=[st], wr=[st])
            self.ev("vector", lambda e, sa=sa: e.tensor_tensor(
                sa[:, 3:4], sa[:, 2:3], sa[:, 2:3], AL.mult),
                rd=[st], wr=[st])
            self.ev("vector", lambda e, sa=sa: e.scalar_tensor_tensor(
                sa[:, 4:5], sa[:, 1:2], 1.0 / D, sa[:, 3:4],
                AL.mult, AL.subtract),
                rd=[st], wr=[st])
            self.ev("scalar", lambda e, sa=sa:
                    e.activation(sa[:, 4:5], sa[:, 4:5], AF.Ln,
                                 bias=self.cvec[:, 1:2]),
                    rd=[st], wr=[st],
                    extra_waits=self.wc if j < 2 else ())
            self.ev("scalar", lambda e, sa=sa:
                    e.activation(sa[:, 4:5], sa[:, 4:5], AF.Exp, scale=-0.5),
                    rd=[st], wr=[st])
            self.ev("vector", lambda e, xa=xa, sa=sa, na=na:
                    e.tensor_scalar(na[:, :], xa[:, :],
                                    sa[:, 2:3], sa[:, 4:5],
                                    AL.subtract, AL.mult),
                    rd=[x, st], wr=[xn])
            out_fm = self.xl if kind == "own" else self.xp_ln
            boff = 0 if kind == "own" else 4 * p
            for g in range(4):
                bk = self.bank[boff + g]
                w = bk.wwait() + xn.rwait() + (self.wc if j < 2 and g == 0 else [])
                for dcl in range(4):
                    dc = g * 4 + dcl
                    self.t_mm(bk.ap[:, dcl * 128:(dcl + 1) * 128],
                              xn.ap[:, dc * 128:(dc + 1) * 128], self.id_bf[:, :],
                              True, True, waits=w if dcl == 0 else ())
                bk.note_w(self.pe)
                xn.note_r(self.pe)
                self.ev("vector", lambda e, q=bk.ap, g2=g, i=tci, o=out_fm:
                        e.tensor_copy(
                            o.ap[:, g2 * 4:(g2 + 1) * 4, i * 128:(i + 1) * 128],
                            q[:, 0:512].rearrange("p (a b) -> p a b", a=4)),
                        rd=[bk], wr=[out_fm])
            if kind == "own":
                for g in range(4):
                    bk = self.bank[4 + g]
                    w = bk.wwait() + x.rwait()
                    for dcl in range(4):
                        dc = g * 4 + dcl
                        self.t_mm(bk.ap[:, dcl * 128:(dcl + 1) * 128],
                                  x.ap[:, dc * 128:(dc + 1) * 128],
                                  self.id_f32[:, :],
                                  True, True, waits=w if dcl == 0 else ())
                    bk.note_w(self.pe)
                    x.note_r(self.pe)
                    self.ev("scalar", lambda e, q=bk.ap, g2=g, i=tci:
                            e.activation(
                                self.xr_t[:, g2 * 4:(g2 + 1) * 4,
                                          i * 128:(i + 1) * 128],
                                q[:, 0:512].rearrange("p (a b) -> p a b", a=4),
                                AF.Copy),
                            rd=[bk], wr=[self.xr])
        self.barrier()

    def p_ln_fm(self):
        ARF, ARB = self.ARF, self.ARB
        ARF.reset(self.arf_scr)
        ARB.reset()
        xc = ARB.salloc([128, DC, SOWN])
        xq = ARB.salloc([128, DC, SOWN])
        mr_bf = ARB.salloc([1, 2 * SOWN])
        row_f = ARB.salloc([1, 2 * SOWN])
        msq = ARB.salloc([1, SOWN])
        tmp = ARF.salloc([128, 512])
        Q2, b2 = self.qpair(2)
        Q3, b3 = self.qpair(3)

        for dc in range(DC):
            self.ev("vector", lambda e, d=dc:
                    e.tensor_copy(xc.ap[:, d, :], self.xr_t[:, d, :]),
                    rd=[self.xr], wr=[xc])
            self.ev("scalar", lambda e, d=dc:
                    e.activation(xq.ap[:, d, :], xc.ap[:, d, :], AF.Square),
                    rd=[xc], wr=[xq])
        for Qrow, brow, srcs in ((Q2, b2, xc), (Q3, b3, xq)):
            for blk in range(2):
                bk = brow[blk]
                w = bk.wwait() + srcs.rwait()
                for dc in range(DC):
                    self.t_mm(Qrow[0:1, blk * 512:(blk + 1) * 512],
                              self.ones_bf[:, 0:1],
                              srcs.ap[:, dc, blk * 512:(blk + 1) * 512],
                              dc == 0, dc == DC - 1, waits=w if dc == 0 else ())
                bk.note_w(self.pe)
                srcs.note_r(self.pe)

        b2s = Slot(None)
        b2s.wt = b2[1].wt
        self.ev("vector", lambda e: e.tensor_scalar(
            row_f.ap[0:1, 0:SOWN], Q2[0:1, 0:1024], 1.0 / D, None, AL.mult),
            rd=[b2[0], b2[1]], wr=[row_f])
        self.ev("vector", lambda e: e.tensor_scalar(
            row_f.ap[0:1, SOWN:2 * SOWN], Q3[0:1, 0:1024], 1.0 / D, None, AL.mult),
            rd=[b3[0], b3[1]], wr=[row_f])
        self.ev("vector", lambda e: e.tensor_tensor(
            msq.ap[0:1, :], row_f.ap[0:1, 0:SOWN], row_f.ap[0:1, 0:SOWN], AL.mult),
            rd=[row_f], wr=[msq])
        self.ev("vector", lambda e: e.tensor_tensor(
            row_f.ap[0:1, SOWN:2 * SOWN], row_f.ap[0:1, SOWN:2 * SOWN],
            msq.ap[0:1, :], AL.subtract),
            rd=[msq, row_f], wr=[row_f])
        self.ev("vector", lambda e: e.tensor_scalar(
            row_f.ap[0:1, SOWN:2 * SOWN], row_f.ap[0:1, SOWN:2 * SOWN],
            LN_EPS, None, AL.add),
            rd=[row_f], wr=[row_f])
        self.ev("scalar", lambda e: e.activation(
            row_f.ap[0:1, SOWN:2 * SOWN], row_f.ap[0:1, SOWN:2 * SOWN], AF.Ln),
            rd=[row_f], wr=[row_f])
        self.ev("scalar", lambda e: e.activation(
            row_f.ap[0:1, SOWN:2 * SOWN], row_f.ap[0:1, SOWN:2 * SOWN],
            AF.Exp, scale=-0.5),
            rd=[row_f], wr=[row_f])
        self.ev("vector", lambda e: e.tensor_copy(mr_bf.ap[0:1, :], row_f.ap[0:1, :]),
                rd=[row_f], wr=[mr_bf])

        for blk in range(2):
            self.gmm(Q3[:, 0:512], [b3[0]],
                     [(self.ones_bf[0:1, :], mr_bf.ap[0:1, blk * 512:(blk + 1) * 512])],
                     rd_slots=[mr_bf])
            self.gmm(Q3[:, 512:1024], [b3[1]],
                     [(self.ones_bf[0:1, :],
                       mr_bf.ap[0:1, SOWN + blk * 512:SOWN + (blk + 1) * 512])],
                     rd_slots=[mr_bf])
            for dc in range(DC):
                self.ev("vector", lambda e, d=dc, b=blk: e.tensor_tensor(
                    tmp.ap[:, :], self.xr_t[:, d, b * 512:(b + 1) * 512],
                    Q3[:, 0:512], AL.subtract),
                    rd=[self.xr, b3[0]], wr=[tmp])
                self.ev("vector", lambda e, d=dc, b=blk: e.tensor_tensor(
                    self.xl_t[:, d, b * 512:(b + 1) * 512], tmp.ap[:, :],
                    Q3[:, 512:1024], AL.mult),
                    rd=[tmp, b3[1]], wr=[self.xl])
        self.barrier()

    def p_attn_a(self):
        ARB, ARF = self.ARB, self.ARF
        DT = self.DT
        ARB.reset(self.arb_m0)
        bas_nat = ARB.salloc([64, D])
        w_nat = ARB.salloc([128, DC, HARM])
        basisT = {nm: ARB.salloc([128, DC, HARM]) for nm in ("q", "k", "v", "o")}
        wT = {nm: ARB.salloc([64, D]) for nm in ("q", "k", "v", "o")}
        res_q = ARB.salloc([64, SOWN])
        res_k = ARB.salloc([64, SEQ])
        res_v = ARB.salloc([64, SEQ])
        reso_sb = ARB.salloc([64, SOWN])
        dnr_bf = ARB.salloc([NH, SOWN])
        ARF.reset(self.arf_scr)
        ph_nat = ARF.salloc([128, HARM])
        am_nat = ARF.salloc([128, HARM])

        HS = Arena(self.arb_t, self.xp_base, DC * SOWN)
        mcell, mneg, dn = self.mcell, self.mneg, self.dn
        xlkeep = ARB.salloc([128, SOWN])
        self.ev("vector", lambda e: e.tensor_copy(xlkeep.ap[:, :], self.xl_t[:, 0, :]),
                rd=[self.xl], wr=[xlkeep])
        self.xlkeep = xlkeep

        for nm in ("q", "k", "v", "o"):
            self.dma("sync", self.dsy, lambda e, s=self.bas_bf[nm]:
                     e.dma_start(out=bas_nat.ap[:, :], in_=s[:, :]), wr=[bas_nat],
                     extra_waits=[(self.dgp, self.t_prep_bas)])
            for dc in range(DC):
                bk = self.bank[4]
                self.gmm(bk.ap[:, 0:64], [bk],
                         [(bas_nat.ap[:, dc * 128:(dc + 1) * 128],
                           self.id_bf[0:64, 0:64])],
                         rd_slots=[bas_nat])
                self.ev("vector", lambda e, q=bk.ap, d=dc, t=basisT[nm].ap:
                        e.tensor_copy(t[:, d, :], q[:, 0:64]),
                        rd=[bk], wr=[basisT[nm]])
            for oc in range(DC):
                self.dma("sync", self.dsy, lambda e, s=DT[f"phase_{nm}"], o=oc:
                         e.dma_start(out=ph_nat.ap[:, :], in_=s[o * 128:(o + 1) * 128, :]),
                         wr=[ph_nat], extra_waits=self.wc)
                self.dma("sync", self.dsy, lambda e, s=DT[f"amp_{nm}"], o=oc:
                         e.dma_start(out=am_nat.ap[:, :], in_=s[o * 128:(o + 1) * 128, :]),
                         wr=[am_nat])
                self.ev("scalar", lambda e:
                        e.activation(ph_nat.ap[:, :], ph_nat.ap[:, :], AF.Sin,
                                     bias=self.cvec[:, 0:1]),
                        rd=[ph_nat], wr=[ph_nat])
                self.ev("vector", lambda e, o=oc:
                        e.tensor_tensor(w_nat.ap[:, o, :], ph_nat.ap[:, :],
                                        am_nat.ap[:, :], AL.mult),
                        rd=[ph_nat, am_nat], wr=[w_nat])
            for oc in range(DC):
                bk = self.bank[5]
                self.gmm(bk.ap[0:64, 0:128], [bk],
                         [(w_nat.ap[:, oc, :], self.id_bf[:, :])], rd_slots=[w_nat])
                self.ev("vector", lambda e, q=bk.ap, o=oc, t=wT[nm].ap:
                        e.tensor_copy(t[:, o * 128:(o + 1) * 128], q[0:64, 0:128]),
                        rd=[bk], wr=[wT[nm]])

        def resonance(bt, src, n_tok, base, dst):
            for blk in range(n_tok // 512):
                bk = self.bank[6 + blk % 2]
                self.gmm(bk.ap[0:64, :], [bk],
                         [(bt.ap[:, dc, :], src.ap[:, dc, blk * 512:(blk + 1) * 512])
                          for dc in range(DC)],
                         rd_slots=[bt, src])
                self.ev("vector", lambda e, q=bk.ap, b=blk, d=dst.ap, bc=base:
                        e.tensor_copy(d[:, bc + b * 512:bc + (b + 1) * 512], q[0:64, :]),
                        rd=[bk], wr=[dst])
        resonance(basisT["q"], self.xl, SOWN, 0, res_q)
        resonance(basisT["k"], self.xl, SOWN, 0, res_k)
        resonance(basisT["k"], self.xp_ln, SOWN, SOWN, res_k)
        resonance(basisT["v"], self.xl, SOWN, 0, res_v)
        resonance(basisT["v"], self.xp_ln, SOWN, SOWN, res_v)

        kh = HS.salloc([128, SEQ])
        qh = HS.salloc([128, SOWN])
        vfm = HS.salloc([128, SEQ])
        vtm = HS.salloc([128, DC, 128])
        q2 = HS.salloc([128, SOWN])
        k2 = HS.salloc([128, SEQ])
        ptb = [HS.salloc([128, SOWN]) for _ in range(2)]
        brow = HS.salloc([NH, SOWN])
        qn2 = ARF.salloc([NH, SOWN])
        y_raw = self.xl

        for h in range(NH):
            def expand(dst, wt_s, res_s, ncol, qj, eng):
                Qx, bx = self.qpair(qj)
                for blk in range(ncol // 512):
                    bk = bx[blk % 2]
                    self.gmm(Qx[:, (blk % 2) * 512:(blk % 2) * 512 + 512], [bk],
                             [(wt_s.ap[:, h * 128:(h + 1) * 128],
                               res_s.ap[:, blk * 512:(blk + 1) * 512])],
                             rd_slots=[wt_s, res_s])
                    if eng == "scalar":
                        self.ev("scalar", lambda e, q=bk.ap, b=blk, d=dst.ap:
                                e.activation(d[:, b * 512:(b + 1) * 512], q, AF.Copy),
                                rd=[bk], wr=[dst])
                    else:
                        self.ev("vector", lambda e, q=bk.ap, b=blk, d=dst.ap:
                                e.tensor_copy(d[:, b * 512:(b + 1) * 512], q),
                                rd=[bk], wr=[dst])
            expand(qh, wT["q"], res_q, SOWN, 0, "scalar")
            expand(kh, wT["k"], res_k, SEQ, 1, "scalar")
            expand(vfm, wT["v"], res_v, SEQ, 0, "vector")
            for kc in range(DC):
                bk = self.bank[2 + kc % 2]
                self.gmm(bk.ap[:, 0:128], [bk],
                         [(vfm.ap[:, kc * 128:(kc + 1) * 128], self.id_bf[:, :])],
                         rd_slots=[vfm])
                self.ev("vector", lambda e, q=bk.ap, c=kc:
                        e.tensor_copy(vtm.ap[:, c, :], q[:, 0:128]),
                        rd=[bk], wr=[vtm])
            self.ev("scalar", lambda e: e.activation(q2.ap[:, :], qh.ap[:, :], AF.Square),
                    rd=[qh], wr=[q2])
            self.ev("scalar", lambda e: e.activation(k2.ap[:, :], kh.ap[:, :], AF.Square),
                    rd=[kh], wr=[k2])
            for blk in range(4):
                bk = self.bank[4]
                self.gmm(bk.ap[0:NH, :], [bk],
                         [(self.ones_bf[:, 0:NH],
                           k2.ap[:, blk * 512:(blk + 1) * 512])],
                         rd_slots=[k2])
                self.ev("vector", lambda e, b=blk, q=bk.ap:
                        e.reduce_max(mcell.ap[0:NH, b:b + 1],
                                     q[0:NH, 0:512], axis=AX.X),
                        rd=[bk], wr=[mcell])
            self.ev("vector", lambda e: e.reduce_max(
                mcell.ap[0:NH, 8:9], mcell.ap[0:NH, 0:4], axis=AX.X),
                rd=[mcell], wr=[mcell])
            self.ev("vector", lambda e: e.tensor_scalar(
                mcell.ap[0:NH, 9:10], mcell.ap[0:NH, 8:9], SM_SCALE * SM_SCALE,
                None, AL.mult),
                rd=[mcell], wr=[mcell])
            for blk in range(2):
                bk = self.bank[4]
                self.gmm(bk.ap[0:NH, :], [bk],
                         [(self.ones_bf[:, 0:NH],
                           q2.ap[:, blk * 512:(blk + 1) * 512])],
                         rd_slots=[q2])
                self.ev("vector", lambda e, b=blk, q=bk.ap: e.tensor_copy(
                    qn2.ap[0:NH, b * 512:(b + 1) * 512], q[0:NH, 0:512]),
                    rd=[bk], wr=[qn2])
            self.ev("vector", lambda e: e.tensor_scalar(
                qn2.ap[0:NH, :], qn2.ap[0:NH, :], mcell.ap[0:NH, 9:10],
                None, AL.mult),
                rd=[qn2, mcell], wr=[qn2])
            self.ev("scalar", lambda e: e.activation(
                qn2.ap[0:NH, :], qn2.ap[0:NH, :], AF.Ln),
                rd=[qn2], wr=[qn2])
            self.ev("scalar", lambda e: e.activation(
                qn2.ap[0:NH, :], qn2.ap[0:NH, :], AF.Exp, scale=0.5),
                rd=[qn2], wr=[qn2])
            self.ev("vector", lambda e: e.tensor_scalar(
                qn2.ap[0:NH, :], qn2.ap[0:NH, :], -1.0, None, AL.mult),
                rd=[qn2], wr=[qn2])
            self.ev("vector", lambda e: e.tensor_scalar(
                qn2.ap[0:NH, :], qn2.ap[0:NH, :], 78.0, None, AL.add),
                rd=[qn2], wr=[qn2])
            self.ev("vector", lambda e: e.tensor_scalar(
                brow.ap[0:NH, :], qn2.ap[0:NH, :], 1.0 / SM_SCALE, None, AL.mult),
                rd=[qn2], wr=[brow])

            av_b = [self.bank[4], self.bank[5]]
            dn_b = self.bank[6]
            for kc in range(DC):
                qj = kc % 2
                Qx, bx = self.qpair(qj)
                sw = (bx[0].wwait() + bx[1].wwait() + kh.rwait() + qh.rwait() +
                      brow.rwait())
                for qb in range(2):
                    self.t_mm(Qx[:, qb * 512:(qb + 1) * 512],
                              kh.ap[:, kc * 128:(kc + 1) * 128],
                              qh.ap[:, qb * 512:(qb + 1) * 512],
                              True, False, waits=sw if qb == 0 else ())
                    self.t_mm(Qx[:, qb * 512:(qb + 1) * 512],
                              self.ones_bf[0:1, :],
                              brow.ap[0:1, qb * 512:(qb + 1) * 512],
                              False, True)
                bx[0].note_w(self.pe)
                bx[1].note_w(self.pe)
                kh.note_r(self.pe)
                qh.note_r(self.pe)
                brow.note_r(self.pe)
                pt = ptb[kc % 2]
                self.ev("scalar", lambda e, s=Qx, p=pt.ap:
                        e.activation(p[:, :], s[:, 0:1024], AF.Exp, scale=SM_SCALE),
                        rd=[bx[0], bx[1]], wr=[pt])
                w_av = (av_b[0].wwait() + av_b[1].wwait() +
                        pt.rwait() + vtm.rwait()) if kc == 0 else pt.rwait()
                if h == 0 and kc == 0:
                    w_av = w_av + dn_b.wwait() + self.bank[7].wwait()
                for qb in range(2):
                    self.t_mm(av_b[qb].ap, vtm.ap[:, kc, :],
                              pt.ap[:, qb * 512:(qb + 1) * 512], kc == 0, kc == DC - 1,
                              waits=w_av if qb == 0 else ())
                    self.t_mm(self.Qt[3][0:NH, qb * 512:(qb + 1) * 512],
                              self.csel[:, h * NH:(h + 1) * NH],
                              pt.ap[:, qb * 512:(qb + 1) * 512],
                              h == 0 and kc == 0,
                              h == NH - 1 and kc == DC - 1)
                pt.note_r(self.pe)
            av_b[0].note_w(self.pe)
            av_b[1].note_w(self.pe)
            vtm.note_r(self.pe)
            for qb in range(2):
                self.ev("vector", lambda e, b=qb, hh=h: e.tensor_copy(
                    y_raw.ap[:, hh, b * 512:(b + 1) * 512], av_b[b].ap),
                    rd=[av_b[qb]], wr=[y_raw])

        dn_b.note_w(self.pe)
        self.bank[7].note_w(self.pe)
        self.ev("vector", lambda e:
                e.tensor_copy(dn.ap[0:NH, :], self.Qt[3][0:NH, 0:1024]),
                rd=[dn_b, self.bank[7]], wr=[dn])
        self.ev("vector", lambda e: e.reciprocal(dn.ap[:, :], dn.ap[:, :]),
                rd=[dn], wr=[dn])
        self.ev("vector", lambda e: e.tensor_copy(dnr_bf.ap[:, :], dn.ap[:, :]),
                rd=[dn], wr=[dnr_bf])

        ro_b = [self.bank[2], self.bank[3]]
        w_ro = ro_b[0].wwait() + ro_b[1].wwait()
        for h in range(NH):
            for qb in range(2):
                bk = self.bank[0]
                self.gmm(bk.ap, [bk],
                         [(self.hsel[:, h * 128:(h + 1) * 128],
                           dnr_bf.ap[0:NH, qb * 512:(qb + 1) * 512])],
                         rd_slots=[dnr_bf])
                self.ev("vector", lambda e, q=bk.ap, hh=h, b=qb: e.tensor_tensor(
                    y_raw.ap[:, hh, b * 512:(b + 1) * 512],
                    y_raw.ap[:, hh, b * 512:(b + 1) * 512], q, AL.mult),
                    rd=[bk, y_raw], wr=[y_raw])
                self.t_mm(self.Qt[1][0:64, qb * 512:(qb + 1) * 512],
                          basisT["o"].ap[:, h, :],
                          y_raw.ap[:, h, qb * 512:(qb + 1) * 512],
                          h == 0, h == NH - 1,
                          waits=(w_ro + y_raw.rwait()) if h == 0 else y_raw.rwait())
        ro_b[0].note_w(self.pe)
        ro_b[1].note_w(self.pe)
        y_raw.note_r(self.pe)
        for qb in range(2):
            self.ev("vector", lambda e, b=qb: e.tensor_copy(
                reso_sb.ap[:, b * 512:(b + 1) * 512],
                self.Qt[1][0:64, b * 512:(b + 1) * 512]),
                rd=[ro_b[qb]], wr=[reso_sb])

        for oc in range(OCN):
            for qb in range(2):
                bk = self.bank[4 + (oc * 2 + qb) % 2]
                self.gmm(bk.ap, [bk],
                         [(wT["o"].ap[:, oc * 128:(oc + 1) * 128],
                           reso_sb.ap[:, qb * 512:(qb + 1) * 512])],
                         rd_slots=[reso_sb])
                self.ev("vector", lambda e, q=bk.ap, o=oc, b=qb: e.tensor_tensor(
                    self.xr_t[:, o, b * 512:(b + 1) * 512], q,
                    self.xr_t[:, o, b * 512:(b + 1) * 512], AL.add),
                    rd=[bk, self.xr], wr=[self.xr])
        self.dbg = {
            "dn": (dn, 0, 0),
            "mcell": (mcell, 16, 0),
            "mneg": (mneg, 32, 0),
            "yraw0": (Slot(y_raw.ap[:, 0, :]), 160, 0),
            "resq": (res_q, 288, 0),
            "resk": (res_k, 352, 0),
            "qh": (qh, 416, 0),
            "kh": (kh, 544, 0),
            "wtq": (wT["q"], 672, 0),
            "dnrbf": (dnr_bf, 736, 0),
            "reso": (reso_sb, 752, 0),
            "basisTq": (Slot(basisT["q"].ap[:, 0, :]), 816, 0),
            "xl0": (self.xlkeep, 880, 0),
        }
        self.barrier()

    def p_dump(self):
        for nm, (slot, row, col) in self.dbg.items():
            ap = slot.ap
            np_, nf = ap.shape[0], 1
            sh = list(ap.shape)
            nf = 1
            for s_ in sh[1:]:
                nf *= s_
            self.dma("gpsimd", self.dgp, lambda e, a=ap, r=row, c2=col, p=sh[0], f=nf:
                     e.dma_start(out=self.y_out[r:r + p, c2:c2 + f], in_=a),
                     rd=[slot])
        self.emit("gpsimd", lambda e: e.nop(), waits=[(self.dgp, self.dgp.n)])

    def p_ffn(self):
        ARB, ARF = self.ARB, self.ARF
        DT = self.DT
        ARB.reset()
        ARF.reset(self.arf_scr)
        srep = ARB.salloc([128, E, 512])
        slab = [ARB.salloc([128, 1024]) for _ in range(2)]
        husk = ARB.salloc([128, 512])
        sco = ARB.salloc([E, SOWN])
        arb_gate = ARB.mark()
        t1 = ARB.salloc([E, SOWN])
        stb = ARB.salloc([128, E])
        wg1n = ARB.salloc([E, D])
        lg_f = ARF.salloc([E, SOWN])
        lgt = ARF.salloc([128, E])
        gsc = ARF.salloc([128, 8])
        bank = self.bank

        self.dma("gpsimd", self.dgp, lambda e:
                 e.dma_start(out=wg1n.ap[:, :], in_=DT["Wg1"][:, :]), wr=[wg1n])
        for dc in range(DC):
            bk = bank[0]
            self.gmm(bk.ap[:, 0:8], [bk],
                     [(wg1n.ap[:, dc * 128:(dc + 1) * 128], self.id_bf[0:8, 0:8])],
                     rd_slots=[wg1n])
            self.ev("vector", lambda e, q=bk.ap, d=dc:
                    e.tensor_copy(self.wg1T_sb[:, d, :], q[:, 0:8]),
                    rd=[bk], wr=[self.wg1T])
        self.dma("gpsimd", self.dgp, lambda e:
                 e.dma_start(out=stb.ap[0:8, 0:8], in_=DT["Wg2"][:, :]), wr=[stb])
        bk = bank[0]
        self.gmm(bk.ap[0:8, 0:8], [bk], [(stb.ap[0:8, 0:8], self.id_bf[0:8, 0:8])],
                 rd_slots=[stb])
        self.ev("vector", lambda e, q=bk.ap: e.tensor_copy(self.wg2T_sb[:, :],
                                                           q[0:8, 0:8]),
                rd=[bk], wr=[self.wg2T])

        for b in range(2):
            bk = bank[2 + b]
            w = bk.wwait() + self.wg1T.rwait() + self.xl.rwait()
            for dc in range(DC):
                self.t_mm(self.Qt[1][0:E, b * 512:(b + 1) * 512],
                          self.wg1T_sb[:, dc, :],
                          self.xl_t[:, dc, b * 512:(b + 1) * 512],
                          dc == 0, dc == DC - 1, waits=w if dc == 0 else ())
            bk.note_w(self.pe)
        self.wg1T.note_r(self.pe)
        self.xl.note_r(self.pe)
        self.ev("scalar", lambda e: e.activation(t1.ap[:, :], self.Qt[1][0:E, 0:1024],
                                                 AF.Tanh),
                rd=[bank[2], bank[3]], wr=[t1])
        for b in range(2):
            bk = bank[2 + b]
            self.gmm(self.Qt[1][0:E, b * 512:(b + 1) * 512], [bk],
                     [(self.wg2T_sb[:, :], t1.ap[:, b * 512:(b + 1) * 512])],
                     rd_slots=[self.wg2T, t1])
        self.ev("vector", lambda e: e.tensor_copy(lg_f.ap[:, :], self.Qt[1][0:E, 0:1024]),
                rd=[bank[2], bank[3]], wr=[lg_f])

        for tci in range(TC):
            bk = bank[0]
            self.gmm(bk.ap[:, 0:8], [bk],
                     [(lg_f.ap[:, tci * 128:(tci + 1) * 128], self.id_f32[0:8, 0:8])],
                     rd_slots=[lg_f])
            self.ev("vector", lambda e, q=bk.ap: e.tensor_copy(lgt.ap[:, :], q[:, 0:8]),
                    rd=[bk], wr=[lgt])
            self.ev("vector", lambda e: e.reduce_max(gsc.ap[:, 0:1], lgt.ap[:, :],
                                                     axis=AX.X),
                    rd=[lgt], wr=[gsc])
            self.ev("vector", lambda e: e.tensor_scalar(
                stb.ap[:, 0:E], lgt.ap[:, :], gsc.ap[:, 0:1], None, AL.is_ge),
                rd=[lgt, gsc], wr=[stb])
            self.ev("vector", lambda e: e.scalar_tensor_tensor(
                lgt.ap[:, :], stb.ap[:, 0:E], -1e30, lgt.ap[:, :], AL.mult, AL.add),
                rd=[stb, lgt], wr=[lgt])
            self.ev("vector", lambda e: e.reduce_max(gsc.ap[:, 1:2], lgt.ap[:, :],
                                                     axis=AX.X),
                    rd=[lgt], wr=[gsc])
            self.ev("vector", lambda e: e.tensor_scalar(
                lgt.ap[:, :], lgt.ap[:, :], gsc.ap[:, 1:2], None, AL.is_ge),
                rd=[lgt, gsc], wr=[lgt])
            self.ev("vector", lambda e: e.tensor_tensor(
                gsc.ap[:, 2:3], gsc.ap[:, 0:1], gsc.ap[:, 1:2], AL.subtract),
                rd=[gsc], wr=[gsc])
            self.ev("scalar", lambda e: e.activation(gsc.ap[:, 2:3], gsc.ap[:, 2:3],
                                                     AF.Sigmoid),
                    rd=[gsc], wr=[gsc])
            self.ev("vector", lambda e: e.tensor_scalar(
                gsc.ap[:, 3:4], gsc.ap[:, 2:3], -1.0, None, AL.mult),
                rd=[gsc], wr=[gsc])
            self.ev("vector", lambda e: e.tensor_scalar(
                gsc.ap[:, 3:4], gsc.ap[:, 3:4], 1.0, None, AL.add),
                rd=[gsc], wr=[gsc])
            self.ev("vector", lambda e: e.tensor_scalar(
                stb.ap[:, 0:E], stb.ap[:, 0:E], gsc.ap[:, 2:3], None, AL.mult),
                rd=[stb, gsc], wr=[stb])
            self.ev("vector", lambda e: e.scalar_tensor_tensor(
                stb.ap[:, 0:E], lgt.ap[:, :], gsc.ap[:, 3:4], stb.ap[:, 0:E],
                AL.mult, AL.add),
                rd=[lgt, gsc, stb], wr=[stb])
            bk = bank[1]
            self.gmm(bk.ap[0:8, 0:128], [bk], [(stb.ap[:, 0:E], self.id_bf[:, :])],
                     rd_slots=[stb])
            self.ev("vector", lambda e, q=bk.ap, i=tci:
                    e.tensor_copy(sco.ap[:, i * 128:(i + 1) * 128], q[0:8, 0:128]),
                    rd=[bk], wr=[sco])

        self.barrier()
        ARB.reset(arb_gate)
        hbuf = ARB.salloc([128, E, HCN, 512])
        for tt in range(2):
            for e_ in range(E):
                bk = bank[7]
                self.gmm(bk.ap, [bk],
                         [(self.hsel[0:8, e_ * 128:(e_ + 1) * 128],
                           sco.ap[0:8, tt * 512:(tt + 1) * 512])],
                         rd_slots=[sco])
                self.ev("vector", lambda e, q=bk.ap, i=e_:
                        e.tensor_copy(srep.ap[:, i, :], q),
                        rd=[bk], wr=[srep])
            for e_ in range(E):
                for dc in range(DC):
                    sla = slab[dc % 2]
                    self.dma("sync", self.dsy, lambda e, i=e_, d=dc, s=sla:
                             e.dma_start(out=s.ap[:, :],
                                         in_=self.wup_bf[i][:, d * 128:(d + 1) * 128],
                                         transpose=True),
                             wr=[sla], extra_waits=[(self.dgp, self.t_prep_up)])
                    for hc in range(HCN):
                        bk = bank[hc]
                        w = (bk.wwait() + sla.rwait() + self.xl.rwait()) if dc == 0 \
                            else (sla.rwait() if hc == 0 else ())
                        self.t_mm(bk.ap, sla.ap[:, hc * 128:(hc + 1) * 128],
                                  self.xl_t[:, dc, tt * 512:(tt + 1) * 512],
                                  dc == 0, dc == DC - 1, waits=w)
                        if dc == DC - 1:
                            bk.note_w(self.pe)
                    sla.note_r(self.pe)
                self.xl.note_r(self.pe)
                for hc in range(HCN):
                    bk = bank[hc]
                    self.ev("scalar", lambda e, q=bk.ap, i=e_, c=hc: e.activation(
                        husk.ap[:, :], q, AF.Silu,
                        bias=self.bup_sb[:, i * HCN + c:i * HCN + c + 1]),
                        rd=[bk], wr=[husk])
                    self.ev("vector", lambda e, i=e_, c=hc: e.tensor_tensor(
                        hbuf.ap[:, i, c, :], husk.ap[:, :], srep.ap[:, i, :], AL.mult),
                        rd=[husk, srep], wr=[hbuf])
            for og in range(2):
                for oc2 in range(8):
                    oc = og * 8 + oc2
                    bk = bank[oc2]
                    self.t_mm(bk.ap, self.bdown_sb[0:8, oc * 128:(oc + 1) * 128],
                              sco.ap[:, tt * 512:(tt + 1) * 512], True, False,
                              waits=bk.wwait() + sco.rwait() +
                              [(self.dgp, self.t_bdown)])
                sco.note_r(self.pe)
                for e_ in range(E):
                    for hc in range(HCN):
                        sla = slab[(e_ * HCN + hc) % 2]
                        self.dma("sync", self.dsy, lambda e, i=e_, c=hc, g=og, s=sla:
                                 e.dma_start(
                                     out=s.ap[:, :],
                                     in_=self.wdown_bf[i][g * 1024:(g + 1) * 1024,
                                                          c * 128:(c + 1) * 128],
                                     transpose=True),
                                 wr=[sla], extra_waits=[(self.dgp, self.t_prep_down)])
                        last = (e_ == E - 1 and hc == HCN - 1)
                        for oc2 in range(8):
                            self.t_mm(bank[oc2].ap, sla.ap[:, oc2 * 128:(oc2 + 1) * 128],
                                      hbuf.ap[:, e_, hc, :], False, last,
                                      waits=(sla.rwait() + hbuf.rwait())
                                      if oc2 == 0 else ())
                        sla.note_r(self.pe)
                hbuf.note_r(self.pe)
                for oc2 in range(8):
                    bank[oc2].note_w(self.pe)
                    oc = og * 8 + oc2
                    self.ev("vector", lambda e, q=bank[oc2].ap, o=oc, b=tt:
                            e.tensor_tensor(
                                self.xr_t[:, o, b * 512:(b + 1) * 512], q,
                                self.xr_t[:, o, b * 512:(b + 1) * 512], AL.add),
                            rd=[bank[oc2], self.xr], wr=[self.xr])
        self.dbg = {
            "sco": (sco, 0, 0),
            "lgf": (lg_f, 8, 0),
            "hbuf00": (Slot(hbuf.ap[:, 0, 0, :]), 16, 0),
            "xl0": (Slot(self.xl_t[:, 0, :]), 144, 0),
            "srep0": (Slot(srep.ap[:, 0, :]), 272, 0),
        }
        self.barrier()

    def p_attn_b(self):
        ARB, ARF = self.ARB, self.ARF
        ARB.reset()
        ARF.reset(self.arf_scr)
        phq = ARB.salloc([128, NH, SOWN])
        phk = ARB.salloc([128, NH, SOWN])
        arb_m1 = ARB.mark()
        PA = Arena(self.arb_t, arb_m1, self.ARB_W - arb_m1)
        pslab = [PA.salloc([128, 512]) for _ in range(2)]
        cgb = PA.salloc([NH, SOWN])
        cbf = PA.salloc([NH, SOWN])
        zrow = PA.salloc([NH, 128])
        pa_perm = PA.mark()
        exf = PA.salloc([128, 512])
        rbuf = [PA.salloc([128, SOWN]) for _ in range(2)]
        swb = PA.salloc([128, 512])
        ta = PA.salloc([128, 512])
        rot = ta
        beta = ARF.salloc([NH, SOWN])
        ctm = ARF.salloc([128, TC * NH])
        rctm = ARF.salloc([128, TC * NH])
        gcol = ARF.salloc([128, 1])
        oex = ARF.salloc([NH, TC])
        svf = ARF.salloc([128, 128])
        bank = self.bank
        dn = self.dn
        ynum = phk

        bkg = [bank[0], bank[1]]
        for b in range(2):
            w = bkg[b].wwait() + self.wg_s.rwait() + self.xl.rwait()
            for dc in range(DC):
                self.t_mm(self.Qt[0][0:NH, b * 512:(b + 1) * 512],
                          self.wg_sb[:, dc, :],
                          self.xl_t[:, dc, b * 512:(b + 1) * 512], dc == 0, dc == DC - 1,
                          waits=w if dc == 0 else ())
            bkg[b].note_w(self.pe)
        self.wg_s.note_r(self.pe)
        self.ev("scalar", lambda e: e.activation(beta.ap[:, :], self.Qt[0][0:NH, 0:1024],
                                                 AF.Sigmoid, bias=self.bg_sb[:, 0:1]),
                rd=[bkg[0], bkg[1]], wr=[beta], extra_waits=self.wc)
        self.ev("vector", lambda e: e.tensor_scalar(
            beta.ap[:, :], beta.ap[:, :], BETA_MAX, None, AL.min),
            rd=[beta], wr=[beta])
        self.ev("vector", lambda e: e.tensor_scalar(
            beta.ap[:, :], beta.ap[:, :], BETA_MIN, None, AL.max),
            rd=[beta], wr=[beta])
        self.ev("vector", lambda e: e.memset(zrow.ap[:, :], 0.0), wr=[zrow])
        for c in range(TC):
            self.ev("vector", lambda e, i=c: e.tensor_tensor_scan(
                beta.ap[:, i * 128:(i + 1) * 128], beta.ap[:, i * 128:(i + 1) * 128],
                zrow.ap[:, :], 1.0, AL.mult, AL.add),
                rd=[beta, zrow], wr=[beta])
        cvar = beta
        self.ev("vector", lambda e: e.tensor_copy(cbf.ap[:, :], cvar.ap[:, :]),
                rd=[cvar], wr=[cbf])
        for c in range(TC):
            bk = bank[2]
            self.gmm(bk.ap[:, 0:NH], [bk],
                     [(cvar.ap[:, c * 128:(c + 1) * 128], self.id_f32[0:NH, 0:NH])],
                     rd_slots=[cvar])
            self.ev("vector", lambda e, q=bk.ap, i=c:
                    e.tensor_copy(ctm.ap[:, i * NH:(i + 1) * NH], q[:, 0:NH]),
                    rd=[bk], wr=[ctm])
        self.ev("vector", lambda e: e.reciprocal(rctm.ap[:, :], ctm.ap[:, :]),
                rd=[ctm], wr=[rctm])
        self.ev("vector", lambda e: e.tensor_tensor_scan(
            oex.ap[:, 0:TC], cvar.ap[:, 127::128], zrow.ap[:, 0:TC],
            1.0, AL.mult, AL.add),
            rd=[cvar, zrow], wr=[oex])
        self.ev("vector", lambda e: e.tensor_copy(exf.ap[0:NH, 0:TC - 1],
                                                  oex.ap[:, 0:TC - 1]),
                rd=[oex], wr=[exf])
        self.ev("vector", lambda e: e.tensor_copy(oex.ap[:, 1:TC],
                                                  exf.ap[0:NH, 0:TC - 1]),
                rd=[exf], wr=[oex])
        self.ev("vector", lambda e: e.memset(oex.ap[:, 0:1], 1.0), wr=[oex])
        for c in range(TC):
            self.ev("vector", lambda e, i=c: e.tensor_scalar(
                cgb.ap[:, i * 128:(i + 1) * 128], cvar.ap[:, i * 128:(i + 1) * 128],
                oex.ap[:, i:i + 1], None, AL.mult),
                rd=[cvar, oex], wr=[cgb])

        def proj_rope(wdram, phi):
            for og in range(4):
                for dc in range(DC):
                    sla = pslab[dc % 2]
                    self.dma("scalar", self.dsc, lambda e, g=og, d=dc, s=sla:
                             e.dma_start(out=s.ap[:, :],
                                         in_=wdram[d * 128:(d + 1) * 128,
                                                   g * 512:(g + 1) * 512]),
                             wr=[sla], extra_waits=[(self.dgp, self.t_prep_pj)])
                    for oc4 in range(4):
                        Qx, bx = self.qpair(oc4)
                        w = (bx[0].wwait() + bx[1].wwait() + sla.rwait() +
                             self.xl.rwait()) if dc == 0 \
                            else (sla.rwait() if oc4 == 0 else ())
                        for b in range(2):
                            self.t_mm(Qx[:, b * 512:(b + 1) * 512],
                                      sla.ap[:, oc4 * 128:(oc4 + 1) * 128],
                                      self.xl_t[:, dc, b * 512:(b + 1) * 512],
                                      dc == 0, dc == DC - 1,
                                      waits=w if b == 0 else ())
                        if dc == DC - 1:
                            bx[0].note_w(self.pe)
                            bx[1].note_w(self.pe)
                    sla.note_r(self.pe)
                self.xl.note_r(self.pe)
                for pair in range(2):
                    for i2 in range(2):
                        oc4 = pair * 2 + i2
                        Qx, bx = self.qpair(oc4)
                        self.ev("scalar", lambda e, s=Qx, r=rbuf[i2].ap:
                                e.activation(r[:, :], s[:, 0:1024], AF.Copy),
                                rd=[bx[0], bx[1]], wr=[rbuf[i2]])
                    for i2 in range(2):
                        oc4 = pair * 2 + i2
                        h = og * 4 + oc4
                        rb = rbuf[i2]
                        Qx, bx = self.qpair(i2)
                        for b in range(2):
                            self.t_mm(Qx[:, b * 512:(b + 1) * 512], self.swap_bf[:, :],
                                      rb.ap[:, b * 512:(b + 1) * 512], True, True,
                                      waits=(bx[0].wwait() + bx[1].wwait() + rb.rwait())
                                      if b == 0 else ())
                        bx[0].note_w(self.pe)
                        bx[1].note_w(self.pe)
                        rb.note_r(self.pe)
                        for b in range(2):
                            sl = b * 512
                            self.ev("vector", lambda e, s=Qx, o=sl: e.tensor_copy(
                                swb.ap[:, :], s[:, o:o + 512]),
                                rd=[bx[b]], wr=[swb])
                            self.ev("vector", lambda e, r=rb.ap, o=sl: e.tensor_tensor(
                                ta.ap[:, :], r[:, o:o + 512],
                                self.rope_cos[:, o:o + 512], AL.mult),
                                rd=[rb], wr=[ta], extra_waits=self.wc)
                            self.ev("vector", lambda e, o=sl: e.tensor_tensor(
                                swb.ap[:, :], swb.ap[:, :],
                                self.rope_sin[:, o:o + 512], AL.mult),
                                rd=[swb], wr=[swb])
                            self.ev("vector", lambda e: e.tensor_tensor(
                                rot.ap[:, :], ta.ap[:, :], swb.ap[:, :], AL.add),
                                rd=[swb], wr=[rot])
                            self.ev("scalar", lambda e: e.activation(
                                exf.ap[:, :], rot.ap[:, :], AF.Exp),
                                rd=[rot], wr=[exf])
                            self.ev("scalar", lambda e: e.activation(
                                rot.ap[:, :], rot.ap[:, :], AF.Relu),
                                rd=[rot], wr=[rot])
                            self.ev("vector", lambda e, hh=h, p=phi, o=sl:
                                    e.scalar_tensor_tensor(
                                        p.ap[:, hh, o:o + 512], rot.ap[:, :], 1.0,
                                        exf.ap[:, :], AL.add, AL.min),
                                    rd=[rot, exf], wr=[phi])
        proj_rope(self.wpj_bf["q"], phq)
        proj_rope(self.wpj_bf["k"], phk)

        self.barrier()
        PA.reset(pa_perm)
        Abf = PA.salloc([128, 128])
        ktb = PA.salloc([128, 128])
        vtb = PA.salloc([128, 128])
        qgb = PA.salloc([128, 128])
        sbf = PA.salloc([128, 128])
        zrep = PA.salloc([128, NH])
        dtf = ARF.salloc([128, 128])
        pa_scan = PA.mark()
        sendb = PA.salloc([128, NH, 129])
        vgrp = PA.salloc([128, 4, SOWN])
        p_crep, p_A = bank[0], bank[1]
        p_kT, p_Sc = bank[2], bank[3]
        p_nm, p_z = bank[4], bank[5]
        p_dn, p_vT = bank[6], bank[7]

        self.ev("vector", lambda e: e.memset(dn.ap[:, :], DN_EPS), wr=[dn])

        for vg in range(4):
            for dc in range(DC):
                sla = pslab[dc % 2]
                self.dma("scalar", self.dsc, lambda e, g=vg, d=dc, s=sla:
                         e.dma_start(out=s.ap[:, :],
                                     in_=self.wpj_bf["v"][d * 128:(d + 1) * 128,
                                                          g * 512:(g + 1) * 512]),
                         wr=[sla], extra_waits=[(self.dgp, self.t_prep_pj)])
                for oc4 in range(4):
                    Qx, bx = self.qpair(oc4)
                    w = (bx[0].wwait() + bx[1].wwait() + sla.rwait() +
                         self.xl.rwait()) if dc == 0 \
                        else (sla.rwait() if oc4 == 0 else ())
                    for b in range(2):
                        self.t_mm(Qx[:, b * 512:(b + 1) * 512],
                                  sla.ap[:, oc4 * 128:(oc4 + 1) * 128],
                                  self.xl_t[:, dc, b * 512:(b + 1) * 512],
                                  dc == 0, dc == DC - 1, waits=w if b == 0 else ())
                    if dc == DC - 1:
                        bx[0].note_w(self.pe)
                        bx[1].note_w(self.pe)
                sla.note_r(self.pe)
            for oc4 in range(4):
                Qx, bx = self.qpair(oc4)
                self.ev("scalar", lambda e, s=Qx, i=oc4:
                        e.activation(vgrp.ap[:, i, :], s[:, 0:1024], AF.Copy),
                        rd=[bx[0], bx[1]], wr=[vgrp])

            for hl in range(4):
                h = vg * 4 + hl
                self.ev("vector", lambda e: e.memset(svf.ap[:, :], 0.0), wr=[svf])
                self.ev("vector", lambda e: e.memset(sbf.ap[:, :], 0.0), wr=[sbf])
                self.ev("vector", lambda e: e.memset(zrep.ap[:, :], 0.0), wr=[zrep])
                for c in range(TC):
                    self.gmm(p_crep.ap[:, 0:128], [p_crep],
                             [(self.hsel[:, h * 128:(h + 1) * 128],
                               cbf.ap[0:NH, c * 128:(c + 1) * 128])],
                             rd_slots=[cbf])
                    self.ev("vector", lambda e: e.tensor_copy(
                        gcol.ap[:, 0:1], p_crep.ap[:, 127:128]),
                        rd=[p_crep], wr=[gcol])
                    self.ev("vector", lambda e, hh=h, cc=c: e.scalar_tensor_tensor(
                        dtf.ap[:, :], p_crep.ap[:, 0:128],
                        rctm.ap[:, cc * NH + hh:cc * NH + hh + 1],
                        self.tri_f32[:, :], AL.mult, AL.mult),
                        rd=[p_crep, rctm], wr=[dtf])
                    self.gmm(p_A.ap[:, 0:128], [p_A],
                             [(phk.ap[:, h, c * 128:(c + 1) * 128],
                               phq.ap[:, h, c * 128:(c + 1) * 128])],
                             rd_slots=[phk, phq])
                    self.ev("vector", lambda e: e.tensor_tensor(
                        Abf.ap[:, :], p_A.ap[:, 0:128], dtf.ap[:, :], AL.mult),
                        rd=[p_A, dtf], wr=[Abf])
                    self.gmm(p_kT.ap[:, 0:128], [p_kT],
                             [(phk.ap[:, h, c * 128:(c + 1) * 128], self.id_bf[:, :])],
                             rd_slots=[phk])
                    self.ev("vector", lambda e: e.tensor_scalar(
                        ktb.ap[:, :], p_kT.ap[:, 0:128], dtf.ap[:, 127:128],
                        None, AL.mult),
                        rd=[p_kT, dtf], wr=[ktb])
                    self.gmm(p_vT.ap[:, 0:128], [p_vT],
                             [(vgrp.ap[:, hl, c * 128:(c + 1) * 128], self.id_bf[:, :])],
                             rd_slots=[vgrp])
                    self.ev("vector", lambda e: e.tensor_copy(vtb.ap[:, :],
                                                              p_vT.ap[:, 0:128]),
                            rd=[p_vT], wr=[vtb])
                    self.ev("vector", lambda e, hh=h, cc=c: e.tensor_tensor(
                        qgb.ap[:, :], phq.ap[:, hh, cc * 128:(cc + 1) * 128],
                        p_crep.ap[:, 0:128], AL.mult),
                        rd=[phq, p_crep], wr=[qgb])
                    self.gmm(p_nm.ap[:, 0:128], [p_nm],
                             [(vtb.ap[:, :], Abf.ap[:, :]),
                              (sbf.ap[:, :], qgb.ap[:, :])],
                             rd_slots=[vtb, Abf, sbf, qgb])
                    self.ev("vector", lambda e, hh=h, cc=c: e.tensor_copy(
                        ynum.ap[:, hh, cc * 128:(cc + 1) * 128], p_nm.ap[:, 0:128]),
                        rd=[p_nm], wr=[ynum])
                    self.gmm(p_dn.ap[0:NH, 0:128], [p_dn],
                             [(self.csel[:, h * NH:(h + 1) * NH], Abf.ap[:, :]),
                              (zrep.ap[:, :], qgb.ap[:, :])],
                             rd_slots=[Abf, qgb, zrep])
                    self.ev("vector", lambda e, cc=c: e.tensor_tensor(
                        dn.ap[0:NH, cc * 128:(cc + 1) * 128],
                        dn.ap[0:NH, cc * 128:(cc + 1) * 128],
                        p_dn.ap[0:NH, 0:128], AL.add),
                        rd=[p_dn, dn], wr=[dn])
                    self.gmm(p_Sc.ap[:, 0:128], [p_Sc],
                             [(ktb.ap[:, :], vtb.ap[:, :])],
                             rd_slots=[ktb, vtb])
                    self.ev("vector", lambda e: e.scalar_tensor_tensor(
                        svf.ap[:, :], svf.ap[:, :], gcol.ap[:, 0:1],
                        p_Sc.ap[:, 0:128], AL.mult, AL.add),
                        rd=[p_Sc, gcol, svf], wr=[svf])
                    if c == TC - 1:
                        self.ev("vector", lambda e, hh=h: e.tensor_copy(
                            sendb.ap[:, hh, 0:128], svf.ap[:, :]),
                            rd=[svf], wr=[sendb])
                    else:
                        self.ev("vector", lambda e: e.tensor_copy(sbf.ap[:, :],
                                                                  svf.ap[:, :]),
                                rd=[svf], wr=[sbf])
                    self.gmm(p_z.ap[:, 0:NH], [p_z],
                             [(ktb.ap[:, :], self.ones_bf[:, 0:NH])],
                             rd_slots=[ktb])
                    self.ev("vector", lambda e, hh=h: e.scalar_tensor_tensor(
                        zrep.ap[:, hh:hh + 1], zrep.ap[:, hh:hh + 1],
                        gcol.ap[:, 0:1], p_z.ap[:, 0:1], AL.mult, AL.add),
                        rd=[p_z, gcol, zrep], wr=[zrep])
                self.ev("vector", lambda e, hh=h: e.tensor_copy(
                    sendb.ap[:, hh, 128:129], zrep.ap[:, hh:hh + 1]),
                    rd=[zrep], wr=[sendb])

        cc_in3 = self.cc_in.rearrange("p (a b) -> p a b", a=NH)
        self.dma("scalar", self.dsc, lambda e:
                 e.dma_start(out=cc_in3, in_=sendb.ap), rd=[sendb])
        t_ccin = self.dsc.n
        self.ccs.tick()
        self.emit("gpsimd", lambda e: e.collective_compute(
            "AllGather", AL.bypass, replica_groups=[list(range(NCORES))],
            ins=[self.cc_in.opt()], outs=[self.cc_out.opt()]),
            waits=[(self.dsc, t_ccin)], inc=(self.ccs, 1))
        t_cc = self.ccs.n

        PA.reset(pa_scan)
        sinb = PA.salloc([128, NH, 128])
        zin_c = PA.salloc([128, NH])
        rkb = PA.salloc([128, NH, 129])
        dnr = cbf
        self.ev("vector", lambda e: e.memset(sinb.ap[:, :, :], 0.0), wr=[sinb],
                extra_waits=[(self.dsc, self.dsc.n)] + sendb.rd)
        self.ev("vector", lambda e: e.memset(zin_c.ap[:, :], 0.0), wr=[zin_c])
        for r in range(NCORES):
            rb = rkb
            self.dma("sync", self.dsy, lambda e, i=r, d=rb:
                     e.dma_start(out=d.ap[:, :, :], in_=self.cc_out[i].rearrange(
                         "p (a b) -> p a b", a=NH)),
                     wr=[rb], extra_waits=[(self.ccs, t_cc)])
            self.ev("vector", lambda e, i=r, s=rb.ap: e.scalar_tensor_tensor(
                sinb.ap[:, :, :], s[:, :, 0:128], self.selmask[:, i:i + 1],
                sinb.ap[:, :, :], AL.mult, AL.add),
                rd=[rb, sinb], wr=[sinb])
            self.ev("vector", lambda e, i=r, s=rb.ap: e.scalar_tensor_tensor(
                zin_c.ap[:, :], s[:, :, 128], self.selmask[:, i:i + 1],
                zin_c.ap[:, :], AL.mult, AL.add),
                rd=[rb, zin_c], wr=[zin_c])

        zinsel = PA.salloc([128, NH])
        for h in range(NH):
            self.ev("vector", lambda e: e.memset(zinsel.ap[:, :], 0.0), wr=[zinsel])
            self.ev("vector", lambda e, hh=h: e.tensor_copy(
                zinsel.ap[:, hh:hh + 1], zin_c.ap[:, hh:hh + 1]),
                rd=[zin_c], wr=[zinsel])
            for c in range(TC):
                self.gmm(p_crep.ap[:, 0:128], [p_crep],
                         [(self.hsel[:, h * 128:(h + 1) * 128],
                           cgb.ap[0:NH, c * 128:(c + 1) * 128])],
                         rd_slots=[cgb])
                self.ev("vector", lambda e, hh=h, cc=c: e.tensor_tensor(
                    qgb.ap[:, :], phq.ap[:, hh, cc * 128:(cc + 1) * 128],
                    p_crep.ap[:, 0:128], AL.mult),
                    rd=[phq, p_crep], wr=[qgb])
                self.gmm(p_nm.ap[:, 0:128], [p_nm],
                         [(sinb.ap[:, h, :], qgb.ap[:, :])],
                         rd_slots=[sinb, qgb])
                self.ev("vector", lambda e, hh=h, cc=c: e.tensor_tensor(
                    ynum.ap[:, hh, cc * 128:(cc + 1) * 128],
                    ynum.ap[:, hh, cc * 128:(cc + 1) * 128],
                    p_nm.ap[:, 0:128], AL.add),
                    rd=[p_nm, ynum], wr=[ynum])
                self.gmm(p_dn.ap[0:NH, 0:128], [p_dn],
                         [(zinsel.ap[:, :], qgb.ap[:, :])],
                         rd_slots=[zinsel, qgb])
                self.ev("vector", lambda e, cc=c: e.tensor_tensor(
                    dn.ap[0:NH, cc * 128:(cc + 1) * 128],
                    dn.ap[0:NH, cc * 128:(cc + 1) * 128],
                    p_dn.ap[0:NH, 0:128], AL.add),
                    rd=[p_dn, dn], wr=[dn])

        self.ev("vector", lambda e: e.reciprocal(dn.ap[:, :], dn.ap[:, :]),
                rd=[dn], wr=[dn])
        self.ev("vector", lambda e: e.tensor_copy(dnr.ap[:, :], dn.ap[:, :]),
                rd=[dn], wr=[dnr])
        for h in range(NH):
            for b in range(2):
                bk = bank[0]
                self.gmm(bk.ap, [bk],
                         [(self.hsel[:, h * 128:(h + 1) * 128],
                           dnr.ap[0:NH, b * 512:(b + 1) * 512])],
                         rd_slots=[dnr])
                self.ev("vector", lambda e, hh=h, bb=b: e.tensor_tensor(
                    ynum.ap[:, hh, bb * 512:(bb + 1) * 512],
                    ynum.ap[:, hh, bb * 512:(bb + 1) * 512],
                    self.Qt[0][:, 0:512], AL.mult),
                    rd=[bk, ynum], wr=[ynum])
        for og in range(4):
            for dc in range(DC):
                sla = pslab[dc % 2]
                self.dma("scalar", self.dsc, lambda e, g=og, d=dc, s=sla:
                         e.dma_start(out=s.ap[:, :],
                                     in_=self.wpj_bf["o"][d * 128:(d + 1) * 128,
                                                          g * 512:(g + 1) * 512]),
                         wr=[sla], extra_waits=[(self.dgp, self.t_prep_pj)])
                for oc4 in range(4):
                    Qx, bx = self.qpair(oc4)
                    w = (bx[0].wwait() + bx[1].wwait() + sla.rwait() + ynum.rwait()) \
                        if dc == 0 else (sla.rwait() if oc4 == 0 else ())
                    for b in range(2):
                        self.t_mm(Qx[:, b * 512:(b + 1) * 512],
                                  sla.ap[:, oc4 * 128:(oc4 + 1) * 128],
                                  ynum.ap[:, dc, b * 512:(b + 1) * 512],
                                  dc == 0, dc == DC - 1, waits=w if b == 0 else ())
                    if dc == DC - 1:
                        bx[0].note_w(self.pe)
                        bx[1].note_w(self.pe)
                sla.note_r(self.pe)
            ynum.note_r(self.pe)
            for oc4 in range(4):
                Qx, bx = self.qpair(oc4)
                oc = og * 4 + oc4
                for b in range(2):
                    self.ev("vector", lambda e, s=Qx, o=oc, bb=b:
                            e.scalar_tensor_tensor(
                                self.xr_t[:, o, bb * 512:(bb + 1) * 512],
                                s[:, bb * 512:(bb + 1) * 512],
                                self.bo_sb[:, o:o + 1],
                                self.xr_t[:, o, bb * 512:(bb + 1) * 512],
                                AL.add, AL.add),
                            rd=[bx[b], self.xr], wr=[self.xr])
        self.dbg = {
            "beta_c": (beta, 0, 0),
            "cbf": (cbf, 16, 0),
            "cgb": (cgb, 32, 0),
            "dn": (dn, 48, 0),
            "phq0": (Slot(phq.ap[:, 0, :]), 64, 0),
            "phk0": (Slot(phk.ap[:, 0, :]), 192, 0),
            "ynum0": (Slot(ynum.ap[:, 0, :]), 320, 0),
            "sinb0": (Slot(sinb.ap[:, 0, :]), 448, 0),
            "sendb0": (Slot(sendb.ap[:, 0, :]), 576, 0),
            "ynum1": (Slot(ynum.ap[:, 4, :]), 704, 0),
            "ynum8": (Slot(ynum.ap[:, 12, :]), 832, 0),
        }
        self.barrier()

    def p_finish(self):
        ARF = self.ARF
        ARF.reset(self.arf_scr)
        ot = ARF.salloc([128, D])
        for tci in range(TC):
            for dc in range(DC):
                bk = self.bank[dc % 4]
                self.gmm(bk.ap[:, 0:128], [bk],
                         [(self.xr_t[:, dc, tci * 128:(tci + 1) * 128],
                           self.id_f32[:, :])],
                         rd_slots=[self.xr])
                self.ev("vector", lambda e, q=bk.ap, d=dc:
                        e.tensor_copy(ot.ap[:, d * 128:(d + 1) * 128], q[:, 0:128]),
                        rd=[bk], wr=[ot])
            self.dma("sync", self.outs, lambda e, i=tci:
                     e.dma_start(out=self.y_out[i * 128:(i + 1) * 128, :],
                                 in_=ot.ap[:, :]),
                     rd=[ot])
        self.emit("sync", lambda e: e.nop(), waits=[(self.outs, self.outs.n)])

    def playback(self, stk):
        block = stk.enter_context(self.nc.Block())

        @block.tensor
        def _(e):
            for f in self.ops["tensor"]:
                f(e)

        @block.vector
        def _(e):
            for f in self.ops["vector"]:
                f(e)

        @block.scalar
        def _(e):
            for f in self.ops["scalar"]:
                f(e)

        @block.gpsimd
        def _(e):
            for f in self.ops["gpsimd"]:
                f(e)

        @block.sync
        def _(e):
            for f in self.ops["sync"]:
                f(e)



def _make_consts(core_id):
    j = core_id % 2
    I = np.eye(128)
    ones = np.ones((128, 128))
    swap = np.zeros((128, 128))
    for m in range(128):
        swap[(m + 64) % 128, m] = 1.0
    tri = (np.arange(128)[:, None] <= np.arange(128)[None, :]).astype(np.float64)
    pos = np.arange(j * SOWN, (j + 1) * SOWN, dtype=np.float64)
    inv_freq = 1.0 / (10000.0 ** (np.arange(64, dtype=np.float64) / 64.0))
    fr = inv_freq[:, None] * pos[None, :]
    cos = np.cos(fr)
    sin = np.sin(fr)
    rope_cos = np.concatenate([cos, cos], axis=0)
    rope_sin = np.concatenate([-sin, sin], axis=0)
    cv = np.zeros((128, 8))
    cv[:, 0] = np.pi / 2
    cv[:, 1] = LN_EPS
    sel = np.zeros((128, 8))
    if j == 1:
        sel[:, core_id - 1] = 1.0
    hsel = np.zeros((16, 16 * 128))
    for h in range(16):
        hsel[h, h * 128:(h + 1) * 128] = 1.0
    csel = np.zeros((128, 16 * 16))
    for h in range(16):
        csel[:, h * 16 + h] = 1.0
    return {
        "id_bf": I.astype(bfloat16), "id_f32": I.astype(np.float32),
        "ones_bf": ones.astype(bfloat16), "ones_f32": ones.astype(np.float32),
        "swap_bf": swap.astype(bfloat16), "tri_f32": tri.astype(np.float32),
        "rope_cos": rope_cos.astype(bfloat16), "rope_sin": rope_sin.astype(bfloat16),
        "cvec": cv.astype(np.float32), "selmask": sel.astype(np.float32),
        "hsel_bf": hsel.astype(bfloat16), "csel_bf": csel.astype(bfloat16),
    }


_NC_CACHE = {}


def _get_nc(stop_after):
    if stop_after not in _NC_CACHE:
        _NC_CACHE[stop_after] = MK(stop_after).build()
    return _NC_CACHE[stop_after]


def run_dev(inputs, stop_after="full"):
    nc = _get_nc(stop_after)
    x = np.ascontiguousarray(np.asarray(inputs["L0_attn_a_hidden_states"], np.float32))
    g = lambda n: np.ascontiguousarray(np.asarray(inputs[n], np.float32))
    shared = {}
    for nm in ("q", "k", "v", "o"):
        shared[f"basis_{nm}"] = g(f"L0_attn_a_basis_{nm}")
        shared[f"phase_{nm}"] = g(f"L0_attn_a_phase_{nm}")
        shared[f"amp_{nm}"] = g(f"L0_attn_a_amp_{nm}")
    for nm in ("Wq", "Wk", "Wv", "Wg", "Wo"):
        shared[nm] = g(f"L0_attn_b_{nm}")
    shared["bg"] = g("L0_attn_b_bg").reshape(NH, 1)
    shared["bo_pk"] = np.ascontiguousarray(
        g("L0_attn_b_bo").reshape(OCN, 128).T)
    shared["Wg1"] = g("L0_ffn_Wg1")
    shared["Wg2"] = g("L0_ffn_Wg2")
    shared["Wup"] = g("L0_ffn_Wup")
    shared["bup_pk"] = np.ascontiguousarray(
        g("L0_ffn_bup").reshape(E, HCN, 128).transpose(2, 0, 1).reshape(128, E * HCN))
    shared["Wdown"] = g("L0_ffn_Wdown")
    shared["bdown"] = g("L0_ffn_bdown")

    in_maps = []
    for c in range(NCORES):
        b, j = c // 2, c % 2
        m = dict(shared)
        m["x_own"] = np.ascontiguousarray(x[b, j * SOWN:(j + 1) * SOWN])
        m["x_part"] = np.ascontiguousarray(x[b, (1 - j) * SOWN:(2 - j) * SOWN])
        m.update(_make_consts(c))
        in_maps.append(m)

    res = run_bass_kernel_spmd(nc, in_maps, core_ids=list(range(NCORES)))
    out = np.empty((4, SEQ, D), np.float32)
    for c in range(NCORES):
        b, j = c // 2, c % 2
        out[b, j * SOWN:(j + 1) * SOWN] = res.results[c]["y_out"]
    return out


def kernel(**inputs):
    return run_dev(inputs, stop_after="full")


# version 03
# speedup vs baseline: 1.0692x; 1.0430x over previous
import math
from contextlib import ExitStack

import numpy as np
import ml_dtypes

import concourse.bass as bass
import concourse.mybir as mybir
from concourse.bass_utils import run_bass_kernel_spmd

F32 = mybir.dt.float32
BF16 = mybir.dt.bfloat16
AF = mybir.ActivationFunctionType
AL = mybir.AluOpType
AX = mybir.AxisListType
bfloat16 = ml_dtypes.bfloat16

NCORES = 8
D = 2048
NH = 16
DH = 128
HARM = 64
SEQ = 2048
SOWN = 1024
TC = 8
DC = 16
E = 8
HE = 1024
HCN = 8
OCN = 16
SM_SCALE = 1.0 / math.sqrt(DH)
LN_EPS = 1e-5
DN_EPS = 1e-6
BETA_MIN, BETA_MAX = 0.8, 0.9995


class Sem:
    def __init__(self, handle):
        self.h = handle
        self.n = 0

    def tick(self, k=1):
        self.n += k
        return self.n


class Slot:

    def __init__(self, ap=None):
        self.ap = ap
        self.wt = None
        self.rd = []

    def wwait(self):
        w = list(self.rd)
        if self.wt is not None:
            w.append(self.wt)
        return w

    def rwait(self):
        return [self.wt] if self.wt is not None else []

    def note_w(self, sem):
        self.wt = (sem, sem.n)
        self.rd = []

    def note_r(self, sem):
        self.rd.append((sem, sem.n))


def _dedup(waits):
    best = {}
    for (s, v) in waits:
        if s is None:
            continue
        if s not in best or v > best[s]:
            best[s] = v
    return [(s, v) for s, v in best.items()]


class Arena:
    def __init__(self, t, base, width):
        self.t = t
        self.base = base
        self.width = width
        self.top = base

    def alloc(self, shape):
        n = 1
        for s in shape[1:]:
            n *= s
        off = self.top
        self.top += n
        assert self.top <= self.base + self.width, \
            f"arena overflow: {self.top} > {self.base + self.width}"
        ap = self.t[0:shape[0], off:off + n]
        if len(shape) == 3:
            ap = ap.rearrange("p (a b) -> p a b", a=shape[1])
        elif len(shape) == 4:
            ap = ap.rearrange("p (a b c) -> p a b c", a=shape[1], b=shape[2])
        return ap

    def salloc(self, shape):
        return Slot(self.alloc(shape))

    def mark(self):
        return self.top

    def reset(self, mark=None):
        self.top = self.base if mark is None else mark


class MK:
    def __init__(self, stop_after="full"):
        self.stop_after = stop_after
        self.nc = bass.Bass("TRN2", target_bir_lowering=False)
        self.ops = {k: [] for k in ("tensor", "vector", "scalar", "gpsimd", "sync")}

    def emit(self, eng, fn, waits=(), inc=None):
        waits = tuple((s.h, v) for (s, v) in _dedup(waits))

        def thunk(e, fn=fn, waits=waits, inc=inc):
            for (h, v) in waits:
                e.wait_ge(h, v)
            ins = fn(e)
            if inc is not None:
                ins.then_inc(inc[0].h, inc[1])

        self.ops[eng].append(thunk)

    def t_mm(self, dst, lhsT, rhs, start, stop, waits=()):
        self.pe.tick()
        self.emit("tensor",
                  lambda e, d=dst, l=lhsT, r=rhs, st=start, sp=stop:
                  e.matmul(d, l, r, start=st, stop=sp),
                  waits=waits, inc=(self.pe, 1))
        return self.pe.n

    def gmm(self, dst, banks, parts, rd_slots=(), extra_waits=()):
        w = list(extra_waits)
        for b in banks:
            w += b.wwait()
        for s in rd_slots:
            w += s.rwait()
        n = len(parts)
        for i, (l, r) in enumerate(parts):
            self.t_mm(dst, l, r, i == 0, i == n - 1, waits=w if i == 0 else ())
        for b in banks:
            b.note_w(self.pe)
        for s in rd_slots:
            s.note_r(self.pe)
        return self.pe.n

    def ev(self, eng, fn, rd=(), wr=(), extra_waits=()):
        sem = {"vector": self.ve, "scalar": self.se, "gpsimd": self.ge}[eng]
        w = list(extra_waits)
        for s in rd:
            w += s.rwait()
        for s in wr:
            w += s.wwait()
        sem.tick()
        self.emit(eng, fn, waits=w, inc=(sem, 1))
        for s in rd:
            s.note_r(sem)
        for s in wr:
            s.note_w(sem)
        return sem.n

    def dma(self, eng, qsem, fn, rd=(), wr=(), extra_waits=()):
        w = list(extra_waits) + list(getattr(self, "fence", ()))
        for s in rd:
            w += s.rwait()
        for s in wr:
            w += s.wwait()
        qsem.tick(16)
        self.emit(eng, fn, waits=w, inc=(qsem, 16))
        for s in rd:
            s.note_r(qsem)
        for s in wr:
            s.note_w(qsem)
        return qsem.n

    def barrier(self):
        pts = [(self.pe, self.pe.n), (self.ve, self.ve.n), (self.se, self.se.n),
               (self.ge, self.ge.n)]
        for eng, own in (("tensor", self.pe), ("vector", self.ve),
                         ("scalar", self.se), ("gpsimd", self.ge)):
            w = [(s, v) for (s, v) in pts if s is not own]
            self.emit(eng, lambda e: e.nop(), waits=w)
        self.fence = pts

    def build(self):
        nc = self.nc
        self.DT = {}

        def din(name, shape, dtype=F32):
            self.DT[name] = nc.dram_tensor(name, list(shape), dtype,
                                           kind="ExternalInput").ap()

        din("x_own", [SOWN, D]); din("x_part", [SOWN, D])
        for nm in ("q", "k", "v", "o"):
            din(f"basis_{nm}", [HARM, D])
            din(f"phase_t_{nm}", [HARM, D])
            din(f"amp_t_{nm}", [HARM, D])
        for nm in ("Wq", "Wk", "Wv", "Wo"):
            din(nm, [D, D])
        din("Wg", [D, NH]); din("bg", [NH, 1]); din("bo_pk", [128, OCN])
        din("Wg1", [E, D]); din("Wg2", [E, E])
        din("Wup", [E, HE, D]); din("bup_pk", [128, E * HCN])
        din("Wdown", [E, D, HE]); din("bdown", [8, D])
        din("id_bf", [128, 128], BF16); din("id_f32", [128, 128], F32)
        din("ones_bf", [128, 128], BF16); din("ones_f32", [128, 128], F32)
        din("swap_bf", [128, 128], BF16); din("tri_f32", [128, 128], F32)
        din("rope_cos", [128, SOWN], BF16); din("rope_sin", [128, SOWN], BF16)
        din("cvec", [128, 8], F32); din("selmask", [128, 8], F32)
        din("hsel_bf", [NH, NH * 128], BF16)
        din("csel_bf", [128, NH * NH], BF16)

        self.y_out = nc.dram_tensor("y_out", [SOWN, D], F32, kind="ExternalOutput").ap()
        self.wup_bf = nc.dram_tensor("wup_bf", [E, HE, D], BF16).ap()
        self.wdown_bf = nc.dram_tensor("wdown_bf", [E, D, HE], BF16).ap()
        self.wpj_bf = {nm: nc.dram_tensor(f"w{nm}_bf", [D, D], BF16).ap()
                       for nm in ("q", "k", "v", "o")}
        self.bas_bf = {nm: nc.dram_tensor(f"bas{nm}_bf", [HARM, D], BF16).ap()
                       for nm in ("q", "k", "v", "o")}
        self.pha_bf = {nm: nc.dram_tensor(f"pha{nm}_bf", [HARM, D], BF16).ap()
                       for nm in ("q", "k", "v", "o")}
        self.amp_bf = {nm: nc.dram_tensor(f"amp{nm}_bf", [HARM, D], BF16).ap()
                       for nm in ("q", "k", "v", "o")}
        self.cc_in = nc.dram_tensor("cc_in", [128, NH * 129], BF16).ap()
        self.cc_out = nc.dram_tensor("cc_out", [NCORES, 128, NH * 129], BF16).ap()

        with ExitStack() as stk:
            sb = lambda n, sh, dt: stk.enter_context(nc.sbuf_tensor(n, list(sh), dt))
            psal = lambda n, sh: stk.enter_context(nc.psum_tensor(n, list(sh), F32))

            self.xr_t = sb("xr", [128, DC, SOWN], F32)
            self.xl_t = sb("xl", [128, DC, SOWN], BF16)
            self.id_bf = sb("c_id_bf", [128, 128], BF16)
            self.id_f32 = sb("c_id_f32", [128, 128], F32)
            self.ones_bf = sb("c_ones_bf", [128, 128], BF16)
            self.ones_f32 = sb("c_ones_f32", [128, 128], F32)
            self.swap_bf = sb("c_swap_bf", [128, 128], BF16)
            self.tri_f32 = sb("c_tri_f32", [128, 128], F32)
            self.rope_cos = sb("c_rope_cos", [128, SOWN], BF16)
            self.rope_sin = sb("c_rope_sin", [128, SOWN], BF16)
            self.cvec = sb("c_cvec", [128, 8], F32)
            self.selmask = sb("c_selmask", [128, 8], F32)
            self.bup_sb = sb("c_bup", [128, E * HCN], F32)
            self.bdown_sb = sb("c_bdown", [8, D], BF16)
            self.bo_sb = sb("c_bo", [128, OCN], F32)
            self.bg_sb = sb("c_bg", [NH, 1], F32)
            self.wg_sb = sb("c_wg", [128, DC, NH], BF16)
            self.wg1T_sb = sb("c_wg1t", [128, DC, E], BF16)
            self.wg2T_sb = sb("c_wg2t", [E, E], BF16)
            self.hsel = sb("c_hsel", [NH, NH * 128], BF16)
            self.csel = sb("c_csel", [128, NH * NH], BF16)

            self.ARB_W = 42864
            self.ARF_W = 3112
            self.arb_t = sb("arb", [128, self.ARB_W], BF16)
            self.arf_t = sb("arf", [128, self.ARF_W], F32)
            arb_addr = nc.lookup_mloc(self.arb_t).addr
            self.xf1_t = nc.alloc_sbuf_tensor_at(
                "xf1ov", [128, 2064], F32,
                offset=arb_addr + (self.ARB_W - 4128) * 2)

            self.Qt = [psal(f"q{i}", [128, 1024]) for i in range(4)]
            self.bank = [Slot(self.Qt[i // 2][:, (i % 2) * 512:(i % 2) * 512 + 512])
                         for i in range(8)]

            for nm in ("pe", "ve", "se", "ge", "dsy", "dsc", "dgp", "ccs", "outs"):
                setattr(self, nm, Sem(stk.enter_context(nc.semaphore(nm))))

            ARF = Arena(self.arf_t, 0, self.ARF_W)
            self.mcell = ARF.salloc([NH, 16])
            self.dn = ARF.salloc([NH, SOWN])
            self.mneg = ARF.salloc([128, NH])
            self.stat = ARF.salloc([128, 8])
            self.arf_scr = ARF.mark()
            self.ARF = ARF
            self.ARB = Arena(self.arb_t, 0, self.ARB_W)

            self.xr = Slot(self.xr_t)
            self.xl = Slot(self.xl_t)
            self.wg_s = Slot(self.wg_sb)
            self.wg1T = Slot(self.wg1T_sb)
            self.wg2T = Slot(self.wg2T_sb)

            self.p_prep()
            self.p_load_ln1()
            if self.stop_after == "ln1":
                self.p_finish(); self.playback(stk); return nc
            if self.stop_after == "ln1x":
                self.dbg = {
                    "xl0": (Slot(self.xl_t[:, 0, :]), 0, 0),
                    "xl1": (Slot(self.xl_t[:, 1, :]), 128, 0),
                    "xp0": (Slot(self.xp_ln.ap[:, 0, :]), 256, 0),
                    "st": (self.dbg_st, 384, 0),
                    "xlntm": (self.dbg_xlntm, 512, 0),
                    "xtm": (self.dbg_xtm, 640, 0),
                }
                self.p_dump(); self.playback(stk); return nc
            self.p_attn_a()
            if self.stop_after == "attn_a_dbg":
                self.p_dump(); self.playback(stk); return nc
            if self.stop_after != "attn_a":
                self.p_ln_fm()
                self.p_ffn()
                if self.stop_after == "ffn1_dbg":
                    self.p_dump(); self.playback(stk); return nc
                if self.stop_after != "ffn1":
                    self.p_ln_fm()
                    self.p_attn_b()
                    if self.stop_after == "attn_b_dbg":
                        self.p_dump(); self.playback(stk); return nc
                    if self.stop_after != "attn_b":
                        self.p_ln_fm()
                        self.p_ffn()
            self.p_finish()
            self.playback(stk)
            return nc

    def qpair(self, j):
        return self.Qt[j], [self.bank[2 * j], self.bank[2 * j + 1]]

    def p_prep(self):
        DT = self.DT
        for nm in ("q", "k", "v", "o"):
            self.dma("gpsimd", self.dgp, lambda e, s=DT[f"basis_{nm}"],
                     d=self.bas_bf[nm]:
                     e.dma_start(out=d[:, :], in_=s[:, :]))
        self.t_prep_bas = self.dgp.n
        for nm in ("q", "k", "v", "o"):
            self.dma("gpsimd", self.dgp, lambda e, s=DT[f"phase_t_{nm}"],
                     d=self.pha_bf[nm]:
                     e.dma_start(out=d[:, :], in_=s[:, :]))
            self.dma("gpsimd", self.dgp, lambda e, s=DT[f"amp_t_{nm}"],
                     d=self.amp_bf[nm]:
                     e.dma_start(out=d[:, :], in_=s[:, :]))
        self.t_prep_pa = self.dgp.n
        for dc in range(DC):
            self.dma("gpsimd", self.dgp, lambda e, d=dc:
                     e.dma_start(out=self.wg_sb[:, d, :],
                                 in_=DT["Wg"][d * 128:(d + 1) * 128, :]),
                     wr=[self.wg_s])
        self.dma("gpsimd", self.dgp, lambda e:
                 e.dma_start(out=self.bdown_sb[:, :], in_=DT["bdown"][:, :]))
        self.t_bdown = self.dgp.n
        for e_ in range(E):
            self.dma("gpsimd", self.dgp, lambda e, i=e_:
                     e.dma_start(out=self.wup_bf[i], in_=DT["Wup"][i]))
        self.t_prep_up = self.dgp.n
        for e_ in range(E):
            self.dma("gpsimd", self.dgp, lambda e, i=e_:
                     e.dma_start(out=self.wdown_bf[i], in_=DT["Wdown"][i]))
        self.t_prep_down = self.dgp.n
        for nm, key in (("Wq", "q"), ("Wk", "k"), ("Wv", "v"), ("Wo", "o")):
            for hh in range(2):
                self.dma("gpsimd", self.dgp, lambda e, s=DT[nm], d=self.wpj_bf[key], i=hh:
                         e.dma_start(out=d[i * 1024:(i + 1) * 1024, :],
                                     in_=s[i * 1024:(i + 1) * 1024, :]))
        self.t_prep_pj = self.dgp.n
        for nm, dst in (("id_bf", self.id_bf), ("id_f32", self.id_f32),
                        ("ones_bf", self.ones_bf), ("ones_f32", self.ones_f32),
                        ("swap_bf", self.swap_bf), ("tri_f32", self.tri_f32),
                        ("rope_cos", self.rope_cos), ("rope_sin", self.rope_sin),
                        ("cvec", self.cvec), ("selmask", self.selmask),
                        ("bup_pk", self.bup_sb), ("bo_pk", self.bo_sb),
                        ("bg", self.bg_sb), ("hsel_bf", self.hsel),
                        ("csel_bf", self.csel)):
            self.dma("scalar", self.dsc, lambda e, s=DT[nm], d=dst:
                     e.dma_start(out=d[:, :], in_=s[:, :]))
        self.t_consts = self.dsc.n
        self.wc = [(self.dsc, self.t_consts)]

    def p_load_ln1(self):
        DT = self.DT
        ARF, ARB = self.ARF, self.ARB
        ARF.reset(self.arf_scr)
        xf = [ARF.salloc([128, D]), Slot(self.xf1_t[0:128, 0:2048])]
        stp = [Slot(self.xf1_t[0:128, 2048:2056]),
               Slot(self.xf1_t[0:128, 2056:2064])]
        self.xp_base = ARB.mark()
        self.xp_ln = ARB.salloc([128, DC, SOWN])
        self.arb_m0 = ARB.mark()
        xnb = [ARB.salloc([128, D]) for _ in range(2)]
        xdump = ARB.salloc([128, D])
        self.dbg_xtm = xf[0]
        self.dbg_xlntm = xnb[0]
        self.dbg_st = stp[0]

        jobs = [("own", tci) for tci in range(TC)] + \
               [("part", tci) for tci in range(TC)]

        def s_dma(j):
            kind, tci = jobs[j]
            src = DT["x_own"] if kind == "own" else DT["x_part"]
            x = xf[j % 2]
            self.dma("sync", self.dsy, lambda e, i=tci, s=src, d=x.ap:
                     e.dma_start(out=d[:, :], in_=s[i * 128:(i + 1) * 128, :]),
                     wr=[x])

        def s_stats(j):
            x, st = xf[j % 2], stp[j % 2]
            xa, sa = x.ap, st.ap
            self.ev("scalar", lambda e, xa=xa, sa=sa:
                    e.activation(xdump.ap[:, :], xa[:, :], AF.Copy,
                                 accum_out=sa[:, 0:1]),
                    rd=[x], wr=[st])
            self.ev("scalar", lambda e, xa=xa, sa=sa:
                    e.activation(xdump.ap[:, :], xa[:, :], AF.Square,
                                 accum_out=sa[:, 1:2]),
                    rd=[x], wr=[st])
            self.ev("scalar", lambda e, sa=sa:
                    e.activation(sa[:, 2:3], sa[:, 0:1], AF.Copy, scale=1.0 / D),
                    rd=[st], wr=[st])
            self.ev("scalar", lambda e, sa=sa:
                    e.activation(sa[:, 3:4], sa[:, 0:1], AF.Square, scale=1.0 / D),
                    rd=[st], wr=[st])
            self.ev("vector", lambda e, sa=sa:
                    e.scalar_tensor_tensor(sa[:, 4:5], sa[:, 3:4], -1.0,
                                           self.cvec[:, 1:2], AL.mult, AL.add),
                    rd=[st], wr=[st],
                    extra_waits=self.wc if j == 0 else ())
            self.ev("scalar", lambda e, sa=sa:
                    e.activation(sa[:, 5:6], sa[:, 1:2], AF.Ln, scale=1.0 / D,
                                 bias=sa[:, 4:5]),
                    rd=[st], wr=[st])
            self.ev("scalar", lambda e, sa=sa:
                    e.activation(sa[:, 5:6], sa[:, 5:6], AF.Exp, scale=-0.5),
                    rd=[st], wr=[st])

        def s_norm_evac(j):
            kind, tci = jobs[j]
            x, st, xn = xf[j % 2], stp[j % 2], xnb[j % 2]
            xa, sa, na = x.ap, st.ap, xn.ap
            self.ev("vector", lambda e, xa=xa, sa=sa, na=na:
                    e.tensor_scalar(na[:, :], xa[:, :],
                                    sa[:, 2:3], sa[:, 5:6],
                                    AL.subtract, AL.mult),
                    rd=[x, st], wr=[xn])
            out_fm = self.xl if kind == "own" else self.xp_ln
            boff = 0 if kind == "own" else 4 * (j % 2)
            for g in range(4):
                bk = self.bank[boff + g]
                w = bk.wwait() + xn.rwait() + (self.wc if j == 0 and g == 0 else [])
                for dcl in range(4):
                    dc = g * 4 + dcl
                    self.t_mm(bk.ap[:, dcl * 128:(dcl + 1) * 128],
                              xn.ap[:, dc * 128:(dc + 1) * 128], self.id_bf[:, :],
                              True, True, waits=w if dcl == 0 else ())
                bk.note_w(self.pe)
                xn.note_r(self.pe)
                self.ev("vector", lambda e, q=bk.ap, g2=g, i=tci, o=out_fm:
                        e.tensor_copy(
                            o.ap[:, g2 * 4:(g2 + 1) * 4, i * 128:(i + 1) * 128],
                            q[:, 0:512].rearrange("p (a b) -> p a b", a=4)),
                        rd=[bk], wr=[out_fm])
            if kind == "own":
                for g in range(4):
                    bk = self.bank[4 + g]
                    w = bk.wwait() + x.rwait()
                    for dcl in range(4):
                        dc = g * 4 + dcl
                        self.t_mm(bk.ap[:, dcl * 128:(dcl + 1) * 128],
                                  x.ap[:, dc * 128:(dc + 1) * 128],
                                  self.id_f32[:, :],
                                  True, True, waits=w if dcl == 0 else ())
                    bk.note_w(self.pe)
                    x.note_r(self.pe)
                    self.ev("scalar", lambda e, q=bk.ap, g2=g, i=tci:
                            e.activation(
                                self.xr_t[:, g2 * 4:(g2 + 1) * 4,
                                          i * 128:(i + 1) * 128],
                                q[:, 0:512].rearrange("p (a b) -> p a b", a=4),
                                AF.Copy),
                            rd=[bk], wr=[self.xr])

        n = len(jobs)
        for t in range(n + 2):
            if t >= 2:
                s_norm_evac(t - 2)
            if t < n:
                s_dma(t)
            if 1 <= t <= n:
                s_stats(t - 1)
        self.barrier()

    def p_ln_fm(self):
        ARF, ARB = self.ARF, self.ARB
        ARF.reset(self.arf_scr)
        ARB.reset()
        xc = ARB.salloc([128, DC, SOWN])
        xq = ARB.salloc([128, DC, SOWN])
        mr_bf = ARB.salloc([1, 2 * SOWN])
        row_f = ARB.salloc([1, 2 * SOWN])
        msq = ARB.salloc([1, SOWN])
        tmp = ARF.salloc([128, 512])
        Q2, b2 = self.qpair(2)
        Q3, b3 = self.qpair(3)

        for dc in range(DC):
            self.ev("vector", lambda e, d=dc:
                    e.tensor_copy(xc.ap[:, d, :], self.xr_t[:, d, :]),
                    rd=[self.xr], wr=[xc])
            self.ev("scalar", lambda e, d=dc:
                    e.activation(xq.ap[:, d, :], xc.ap[:, d, :], AF.Square),
                    rd=[xc], wr=[xq])
        for Qrow, brow, srcs in ((Q2, b2, xc), (Q3, b3, xq)):
            for blk in range(2):
                bk = brow[blk]
                w = bk.wwait() + srcs.rwait()
                for dc in range(DC):
                    self.t_mm(Qrow[0:1, blk * 512:(blk + 1) * 512],
                              self.ones_bf[:, 0:1],
                              srcs.ap[:, dc, blk * 512:(blk + 1) * 512],
                              dc == 0, dc == DC - 1, waits=w if dc == 0 else ())
                bk.note_w(self.pe)
                srcs.note_r(self.pe)

        b2s = Slot(None)
        b2s.wt = b2[1].wt
        self.ev("vector", lambda e: e.tensor_scalar(
            row_f.ap[0:1, 0:SOWN], Q2[0:1, 0:1024], 1.0 / D, None, AL.mult),
            rd=[b2[0], b2[1]], wr=[row_f])
        self.ev("vector", lambda e: e.tensor_scalar(
            row_f.ap[0:1, SOWN:2 * SOWN], Q3[0:1, 0:1024], 1.0 / D, None, AL.mult),
            rd=[b3[0], b3[1]], wr=[row_f])
        self.ev("vector", lambda e: e.tensor_tensor(
            msq.ap[0:1, :], row_f.ap[0:1, 0:SOWN], row_f.ap[0:1, 0:SOWN], AL.mult),
            rd=[row_f], wr=[msq])
        self.ev("vector", lambda e: e.tensor_tensor(
            row_f.ap[0:1, SOWN:2 * SOWN], row_f.ap[0:1, SOWN:2 * SOWN],
            msq.ap[0:1, :], AL.subtract),
            rd=[msq, row_f], wr=[row_f])
        self.ev("vector", lambda e: e.tensor_scalar(
            row_f.ap[0:1, SOWN:2 * SOWN], row_f.ap[0:1, SOWN:2 * SOWN],
            LN_EPS, None, AL.add),
            rd=[row_f], wr=[row_f])
        self.ev("scalar", lambda e: e.activation(
            row_f.ap[0:1, SOWN:2 * SOWN], row_f.ap[0:1, SOWN:2 * SOWN], AF.Ln),
            rd=[row_f], wr=[row_f])
        self.ev("scalar", lambda e: e.activation(
            row_f.ap[0:1, SOWN:2 * SOWN], row_f.ap[0:1, SOWN:2 * SOWN],
            AF.Exp, scale=-0.5),
            rd=[row_f], wr=[row_f])
        self.ev("vector", lambda e: e.tensor_copy(mr_bf.ap[0:1, :], row_f.ap[0:1, :]),
                rd=[row_f], wr=[mr_bf])

        for blk in range(2):
            self.gmm(Q3[:, 0:512], [b3[0]],
                     [(self.ones_bf[0:1, :], mr_bf.ap[0:1, blk * 512:(blk + 1) * 512])],
                     rd_slots=[mr_bf])
            self.gmm(Q3[:, 512:1024], [b3[1]],
                     [(self.ones_bf[0:1, :],
                       mr_bf.ap[0:1, SOWN + blk * 512:SOWN + (blk + 1) * 512])],
                     rd_slots=[mr_bf])
            for dc in range(DC):
                self.ev("vector", lambda e, d=dc, b=blk: e.tensor_tensor(
                    tmp.ap[:, :], self.xr_t[:, d, b * 512:(b + 1) * 512],
                    Q3[:, 0:512], AL.subtract),
                    rd=[self.xr, b3[0]], wr=[tmp])
                self.ev("vector", lambda e, d=dc, b=blk: e.tensor_tensor(
                    self.xl_t[:, d, b * 512:(b + 1) * 512], tmp.ap[:, :],
                    Q3[:, 512:1024], AL.mult),
                    rd=[tmp, b3[1]], wr=[self.xl])
        self.barrier()

    def p_attn_a(self):
        ARB, ARF = self.ARB, self.ARF
        DT = self.DT
        ARB.reset(self.arb_m0)
        bas_nat = ARB.salloc([64, D])
        basisT = {nm: ARB.salloc([128, DC, HARM]) for nm in ("q", "k", "v", "o")}
        wT = {nm: ARB.salloc([64, D]) for nm in ("q", "k", "v", "o")}
        res_q = ARB.salloc([64, SOWN])
        res_k = ARB.salloc([64, SEQ])
        res_v = ARB.salloc([64, SEQ])
        reso_sb = ARB.salloc([64, SOWN])
        dnr_bf = ARB.salloc([NH, SOWN])
        ARF.reset(self.arf_scr)

        HS = Arena(self.arb_t, self.xp_base, DC * SOWN)
        mcell, mneg, dn = self.mcell, self.mneg, self.dn
        xlkeep = ARB.salloc([128, SOWN])
        self.ev("vector", lambda e: e.tensor_copy(xlkeep.ap[:, :], self.xl_t[:, 0, :]),
                rd=[self.xl], wr=[xlkeep])
        self.xlkeep = xlkeep

        amT = ARB.salloc([64, D])
        for nm in ("q", "k", "v", "o"):
            wt = wT[nm]
            self.dma("sync", self.dsy, lambda e, s=self.pha_bf[nm], d=wt.ap:
                     e.dma_start(out=d[:, :], in_=s[:, :]),
                     wr=[wt], extra_waits=[(self.dgp, self.t_prep_pa)])
            self.dma("scalar", self.dsc, lambda e, s=self.amp_bf[nm], d=amT.ap:
                     e.dma_start(out=d[:, :], in_=s[:, :]),
                     wr=[amT], extra_waits=[(self.dgp, self.t_prep_pa)])
            self.ev("scalar", lambda e, d=wt.ap:
                    e.activation(d[:, :], d[:, :], AF.Sin,
                                 bias=self.cvec[0:64, 0:1]),
                    rd=[wt], wr=[wt],
                    extra_waits=self.wc if nm == "q" else ())
            self.ev("vector", lambda e, d=wt.ap, a=amT.ap:
                    e.tensor_tensor(d[:, :], d[:, :], a[:, :], AL.mult),
                    rd=[wt, amT], wr=[wt])
            self.dma("sync", self.dsy, lambda e, s=self.bas_bf[nm]:
                     e.dma_start(out=bas_nat.ap[:, :], in_=s[:, :]), wr=[bas_nat],
                     extra_waits=[(self.dgp, self.t_prep_bas)])
            for dc in range(DC):
                bk = self.bank[4 + dc % 2]
                self.gmm(bk.ap[:, 0:64], [bk],
                         [(bas_nat.ap[:, dc * 128:(dc + 1) * 128],
                           self.id_bf[0:64, 0:64])],
                         rd_slots=[bas_nat])
                self.ev("vector", lambda e, q=bk.ap, d=dc, t=basisT[nm].ap:
                        e.tensor_copy(t[:, d, :], q[:, 0:64]),
                        rd=[bk], wr=[basisT[nm]])

        def resonance(bt, src, n_tok, base, dst):
            for blk in range(n_tok // 512):
                bk = self.bank[6 + blk % 2]
                self.gmm(bk.ap[0:64, :], [bk],
                         [(bt.ap[:, dc, :], src.ap[:, dc, blk * 512:(blk + 1) * 512])
                          for dc in range(DC)],
                         rd_slots=[bt, src])
                self.ev("vector", lambda e, q=bk.ap, b=blk, d=dst.ap, bc=base:
                        e.tensor_copy(d[:, bc + b * 512:bc + (b + 1) * 512], q[0:64, :]),
                        rd=[bk], wr=[dst])
        resonance(basisT["q"], self.xl, SOWN, 0, res_q)
        resonance(basisT["k"], self.xl, SOWN, 0, res_k)
        resonance(basisT["k"], self.xp_ln, SOWN, SOWN, res_k)
        resonance(basisT["v"], self.xl, SOWN, 0, res_v)
        resonance(basisT["v"], self.xp_ln, SOWN, SOWN, res_v)

        kh = HS.salloc([128, SEQ])
        qh = HS.salloc([128, SOWN])
        vfm = HS.salloc([128, SEQ])
        vtm = HS.salloc([128, DC, 128])
        q2 = HS.salloc([128, SOWN])
        k2 = HS.salloc([128, SEQ])
        ptb = [HS.salloc([128, SOWN]) for _ in range(2)]
        brow = HS.salloc([NH, SOWN])
        qn2 = ARF.salloc([NH, SOWN])
        y_raw = self.xl

        for h in range(NH):
            def expand(dst, wt_s, res_s, ncol, qj, eng):
                Qx, bx = self.qpair(qj)
                for blk in range(ncol // 512):
                    bk = bx[blk % 2]
                    self.gmm(Qx[:, (blk % 2) * 512:(blk % 2) * 512 + 512], [bk],
                             [(wt_s.ap[:, h * 128:(h + 1) * 128],
                               res_s.ap[:, blk * 512:(blk + 1) * 512])],
                             rd_slots=[wt_s, res_s])
                    if eng == "scalar":
                        self.ev("scalar", lambda e, q=bk.ap, b=blk, d=dst.ap:
                                e.activation(d[:, b * 512:(b + 1) * 512], q, AF.Copy),
                                rd=[bk], wr=[dst])
                    else:
                        self.ev("vector", lambda e, q=bk.ap, b=blk, d=dst.ap:
                                e.tensor_copy(d[:, b * 512:(b + 1) * 512], q),
                                rd=[bk], wr=[dst])
            expand(qh, wT["q"], res_q, SOWN, 0, "scalar")
            expand(kh, wT["k"], res_k, SEQ, 1, "scalar")
            expand(vfm, wT["v"], res_v, SEQ, 0, "vector")
            for kc in range(DC):
                bk = self.bank[2 + kc % 2]
                self.gmm(bk.ap[:, 0:128], [bk],
                         [(vfm.ap[:, kc * 128:(kc + 1) * 128], self.id_bf[:, :])],
                         rd_slots=[vfm])
                self.ev("vector", lambda e, q=bk.ap, c=kc:
                        e.tensor_copy(vtm.ap[:, c, :], q[:, 0:128]),
                        rd=[bk], wr=[vtm])
            self.ev("scalar", lambda e: e.activation(q2.ap[:, :], qh.ap[:, :], AF.Square),
                    rd=[qh], wr=[q2])
            self.ev("scalar", lambda e: e.activation(k2.ap[:, :], kh.ap[:, :], AF.Square),
                    rd=[kh], wr=[k2])
            for blk in range(4):
                bk = self.bank[4]
                self.gmm(bk.ap[0:NH, :], [bk],
                         [(self.ones_bf[:, 0:NH],
                           k2.ap[:, blk * 512:(blk + 1) * 512])],
                         rd_slots=[k2])
                self.ev("vector", lambda e, b=blk, q=bk.ap:
                        e.reduce_max(mcell.ap[0:NH, b:b + 1],
                                     q[0:NH, 0:512], axis=AX.X),
                        rd=[bk], wr=[mcell])
            self.ev("vector", lambda e: e.reduce_max(
                mcell.ap[0:NH, 8:9], mcell.ap[0:NH, 0:4], axis=AX.X),
                rd=[mcell], wr=[mcell])
            self.ev("vector", lambda e: e.tensor_scalar(
                mcell.ap[0:NH, 9:10], mcell.ap[0:NH, 8:9], SM_SCALE * SM_SCALE,
                None, AL.mult),
                rd=[mcell], wr=[mcell])
            for blk in range(2):
                bk = self.bank[4]
                self.gmm(bk.ap[0:NH, :], [bk],
                         [(self.ones_bf[:, 0:NH],
                           q2.ap[:, blk * 512:(blk + 1) * 512])],
                         rd_slots=[q2])
                self.ev("vector", lambda e, b=blk, q=bk.ap: e.tensor_copy(
                    qn2.ap[0:NH, b * 512:(b + 1) * 512], q[0:NH, 0:512]),
                    rd=[bk], wr=[qn2])
            self.ev("vector", lambda e: e.tensor_scalar(
                qn2.ap[0:NH, :], qn2.ap[0:NH, :], mcell.ap[0:NH, 9:10],
                None, AL.mult),
                rd=[qn2, mcell], wr=[qn2])
            self.ev("scalar", lambda e: e.activation(
                qn2.ap[0:NH, :], qn2.ap[0:NH, :], AF.Ln),
                rd=[qn2], wr=[qn2])
            self.ev("scalar", lambda e: e.activation(
                qn2.ap[0:NH, :], qn2.ap[0:NH, :], AF.Exp, scale=0.5),
                rd=[qn2], wr=[qn2])
            self.ev("vector", lambda e: e.tensor_scalar(
                qn2.ap[0:NH, :], qn2.ap[0:NH, :], -1.0, None, AL.mult),
                rd=[qn2], wr=[qn2])
            self.ev("vector", lambda e: e.tensor_scalar(
                qn2.ap[0:NH, :], qn2.ap[0:NH, :], 78.0, None, AL.add),
                rd=[qn2], wr=[qn2])
            self.ev("vector", lambda e: e.tensor_scalar(
                brow.ap[0:NH, :], qn2.ap[0:NH, :], 1.0 / SM_SCALE, None, AL.mult),
                rd=[qn2], wr=[brow])

            av_b = [self.bank[4], self.bank[5]]
            dn_b = self.bank[6]
            for kc in range(DC):
                qj = kc % 2
                Qx, bx = self.qpair(qj)
                sw = (bx[0].wwait() + bx[1].wwait() + kh.rwait() + qh.rwait() +
                      brow.rwait())
                for qb in range(2):
                    self.t_mm(Qx[:, qb * 512:(qb + 1) * 512],
                              kh.ap[:, kc * 128:(kc + 1) * 128],
                              qh.ap[:, qb * 512:(qb + 1) * 512],
                              True, False, waits=sw if qb == 0 else ())
                    self.t_mm(Qx[:, qb * 512:(qb + 1) * 512],
                              self.ones_bf[0:1, :],
                              brow.ap[0:1, qb * 512:(qb + 1) * 512],
                              False, True)
                bx[0].note_w(self.pe)
                bx[1].note_w(self.pe)
                kh.note_r(self.pe)
                qh.note_r(self.pe)
                brow.note_r(self.pe)
                pt = ptb[kc % 2]
                self.ev("scalar", lambda e, s=Qx, p=pt.ap:
                        e.activation(p[:, :], s[:, 0:1024], AF.Exp, scale=SM_SCALE),
                        rd=[bx[0], bx[1]], wr=[pt])
                w_av = (av_b[0].wwait() + av_b[1].wwait() +
                        pt.rwait() + vtm.rwait()) if kc == 0 else pt.rwait()
                if h == 0 and kc == 0:
                    w_av = w_av + dn_b.wwait() + self.bank[7].wwait()
                for qb in range(2):
                    self.t_mm(av_b[qb].ap, vtm.ap[:, kc, :],
                              pt.ap[:, qb * 512:(qb + 1) * 512], kc == 0, kc == DC - 1,
                              waits=w_av if qb == 0 else ())
                    self.t_mm(self.Qt[3][0:NH, qb * 512:(qb + 1) * 512],
                              self.csel[:, h * NH:(h + 1) * NH],
                              pt.ap[:, qb * 512:(qb + 1) * 512],
                              h == 0 and kc == 0,
                              h == NH - 1 and kc == DC - 1)
                pt.note_r(self.pe)
            av_b[0].note_w(self.pe)
            av_b[1].note_w(self.pe)
            vtm.note_r(self.pe)
            for qb in range(2):
                self.ev("vector", lambda e, b=qb, hh=h: e.tensor_copy(
                    y_raw.ap[:, hh, b * 512:(b + 1) * 512], av_b[b].ap),
                    rd=[av_b[qb]], wr=[y_raw])

        dn_b.note_w(self.pe)
        self.bank[7].note_w(self.pe)
        self.ev("vector", lambda e:
                e.tensor_copy(dn.ap[0:NH, :], self.Qt[3][0:NH, 0:1024]),
                rd=[dn_b, self.bank[7]], wr=[dn])
        self.ev("vector", lambda e: e.reciprocal(dn.ap[:, :], dn.ap[:, :]),
                rd=[dn], wr=[dn])
        self.ev("vector", lambda e: e.tensor_copy(dnr_bf.ap[:, :], dn.ap[:, :]),
                rd=[dn], wr=[dnr_bf])

        ro_b = [self.bank[2], self.bank[3]]
        w_ro = ro_b[0].wwait() + ro_b[1].wwait()
        for h in range(NH):
            for qb in range(2):
                bk = self.bank[0]
                self.gmm(bk.ap, [bk],
                         [(self.hsel[:, h * 128:(h + 1) * 128],
                           dnr_bf.ap[0:NH, qb * 512:(qb + 1) * 512])],
                         rd_slots=[dnr_bf])
                self.ev("vector", lambda e, q=bk.ap, hh=h, b=qb: e.tensor_tensor(
                    y_raw.ap[:, hh, b * 512:(b + 1) * 512],
                    y_raw.ap[:, hh, b * 512:(b + 1) * 512], q, AL.mult),
                    rd=[bk, y_raw], wr=[y_raw])
                self.t_mm(self.Qt[1][0:64, qb * 512:(qb + 1) * 512],
                          basisT["o"].ap[:, h, :],
                          y_raw.ap[:, h, qb * 512:(qb + 1) * 512],
                          h == 0, h == NH - 1,
                          waits=(w_ro + y_raw.rwait()) if h == 0 else y_raw.rwait())
        ro_b[0].note_w(self.pe)
        ro_b[1].note_w(self.pe)
        y_raw.note_r(self.pe)
        for qb in range(2):
            self.ev("vector", lambda e, b=qb: e.tensor_copy(
                reso_sb.ap[:, b * 512:(b + 1) * 512],
                self.Qt[1][0:64, b * 512:(b + 1) * 512]),
                rd=[ro_b[qb]], wr=[reso_sb])

        for oc in range(OCN):
            for qb in range(2):
                bk = self.bank[4 + (oc * 2 + qb) % 2]
                self.gmm(bk.ap, [bk],
                         [(wT["o"].ap[:, oc * 128:(oc + 1) * 128],
                           reso_sb.ap[:, qb * 512:(qb + 1) * 512])],
                         rd_slots=[reso_sb])
                self.ev("vector", lambda e, q=bk.ap, o=oc, b=qb: e.tensor_tensor(
                    self.xr_t[:, o, b * 512:(b + 1) * 512], q,
                    self.xr_t[:, o, b * 512:(b + 1) * 512], AL.add),
                    rd=[bk, self.xr], wr=[self.xr])
        self.dbg = {
            "dn": (dn, 0, 0),
            "mcell": (mcell, 16, 0),
            "mneg": (mneg, 32, 0),
            "yraw0": (Slot(y_raw.ap[:, 0, :]), 160, 0),
            "resq": (res_q, 288, 0),
            "resk": (res_k, 352, 0),
            "qh": (qh, 416, 0),
            "kh": (kh, 544, 0),
            "wtq": (wT["q"], 672, 0),
            "dnrbf": (dnr_bf, 736, 0),
            "reso": (reso_sb, 752, 0),
            "basisTq": (Slot(basisT["q"].ap[:, 0, :]), 816, 0),
            "xl0": (self.xlkeep, 880, 0),
        }
        self.barrier()

    def p_dump(self):
        for nm, (slot, row, col) in self.dbg.items():
            ap = slot.ap
            np_, nf = ap.shape[0], 1
            sh = list(ap.shape)
            nf = 1
            for s_ in sh[1:]:
                nf *= s_
            self.dma("gpsimd", self.dgp, lambda e, a=ap, r=row, c2=col, p=sh[0], f=nf:
                     e.dma_start(out=self.y_out[r:r + p, c2:c2 + f], in_=a),
                     rd=[slot])
        self.emit("gpsimd", lambda e: e.nop(), waits=[(self.dgp, self.dgp.n)])

    def p_ffn(self):
        ARB, ARF = self.ARB, self.ARF
        DT = self.DT
        ARB.reset()
        ARF.reset(self.arf_scr)
        srep = ARB.salloc([128, E, 512])
        slab = [ARB.salloc([128, 1024]) for _ in range(2)]
        husk = ARB.salloc([128, 512])
        sco = ARB.salloc([E, SOWN])
        arb_gate = ARB.mark()
        t1 = ARB.salloc([E, SOWN])
        stb = ARB.salloc([128, E])
        wg1n = ARB.salloc([E, D])
        lg_f = ARF.salloc([E, SOWN])
        lgt = ARF.salloc([128, E])
        gsc = ARF.salloc([128, 8])
        bank = self.bank

        self.dma("gpsimd", self.dgp, lambda e:
                 e.dma_start(out=wg1n.ap[:, :], in_=DT["Wg1"][:, :]), wr=[wg1n])
        for dc in range(DC):
            bk = bank[0]
            self.gmm(bk.ap[:, 0:8], [bk],
                     [(wg1n.ap[:, dc * 128:(dc + 1) * 128], self.id_bf[0:8, 0:8])],
                     rd_slots=[wg1n])
            self.ev("vector", lambda e, q=bk.ap, d=dc:
                    e.tensor_copy(self.wg1T_sb[:, d, :], q[:, 0:8]),
                    rd=[bk], wr=[self.wg1T])
        self.dma("gpsimd", self.dgp, lambda e:
                 e.dma_start(out=stb.ap[0:8, 0:8], in_=DT["Wg2"][:, :]), wr=[stb])
        bk = bank[0]
        self.gmm(bk.ap[0:8, 0:8], [bk], [(stb.ap[0:8, 0:8], self.id_bf[0:8, 0:8])],
                 rd_slots=[stb])
        self.ev("vector", lambda e, q=bk.ap: e.tensor_copy(self.wg2T_sb[:, :],
                                                           q[0:8, 0:8]),
                rd=[bk], wr=[self.wg2T])

        for b in range(2):
            bk = bank[2 + b]
            w = bk.wwait() + self.wg1T.rwait() + self.xl.rwait()
            for dc in range(DC):
                self.t_mm(self.Qt[1][0:E, b * 512:(b + 1) * 512],
                          self.wg1T_sb[:, dc, :],
                          self.xl_t[:, dc, b * 512:(b + 1) * 512],
                          dc == 0, dc == DC - 1, waits=w if dc == 0 else ())
            bk.note_w(self.pe)
        self.wg1T.note_r(self.pe)
        self.xl.note_r(self.pe)
        self.ev("scalar", lambda e: e.activation(t1.ap[:, :], self.Qt[1][0:E, 0:1024],
                                                 AF.Tanh),
                rd=[bank[2], bank[3]], wr=[t1])
        for b in range(2):
            bk = bank[2 + b]
            self.gmm(self.Qt[1][0:E, b * 512:(b + 1) * 512], [bk],
                     [(self.wg2T_sb[:, :], t1.ap[:, b * 512:(b + 1) * 512])],
                     rd_slots=[self.wg2T, t1])
        self.ev("vector", lambda e: e.tensor_copy(lg_f.ap[:, :], self.Qt[1][0:E, 0:1024]),
                rd=[bank[2], bank[3]], wr=[lg_f])

        for tci in range(TC):
            bk = bank[0]
            self.gmm(bk.ap[:, 0:8], [bk],
                     [(lg_f.ap[:, tci * 128:(tci + 1) * 128], self.id_f32[0:8, 0:8])],
                     rd_slots=[lg_f])
            self.ev("vector", lambda e, q=bk.ap: e.tensor_copy(lgt.ap[:, :], q[:, 0:8]),
                    rd=[bk], wr=[lgt])
            self.ev("vector", lambda e: e.reduce_max(gsc.ap[:, 0:1], lgt.ap[:, :],
                                                     axis=AX.X),
                    rd=[lgt], wr=[gsc])
            self.ev("vector", lambda e: e.tensor_scalar(
                stb.ap[:, 0:E], lgt.ap[:, :], gsc.ap[:, 0:1], None, AL.is_ge),
                rd=[lgt, gsc], wr=[stb])
            self.ev("vector", lambda e: e.scalar_tensor_tensor(
                lgt.ap[:, :], stb.ap[:, 0:E], -1e30, lgt.ap[:, :], AL.mult, AL.add),
                rd=[stb, lgt], wr=[lgt])
            self.ev("vector", lambda e: e.reduce_max(gsc.ap[:, 1:2], lgt.ap[:, :],
                                                     axis=AX.X),
                    rd=[lgt], wr=[gsc])
            self.ev("vector", lambda e: e.tensor_scalar(
                lgt.ap[:, :], lgt.ap[:, :], gsc.ap[:, 1:2], None, AL.is_ge),
                rd=[lgt, gsc], wr=[lgt])
            self.ev("vector", lambda e: e.tensor_tensor(
                gsc.ap[:, 2:3], gsc.ap[:, 0:1], gsc.ap[:, 1:2], AL.subtract),
                rd=[gsc], wr=[gsc])
            self.ev("scalar", lambda e: e.activation(gsc.ap[:, 2:3], gsc.ap[:, 2:3],
                                                     AF.Sigmoid),
                    rd=[gsc], wr=[gsc])
            self.ev("vector", lambda e: e.tensor_scalar(
                gsc.ap[:, 3:4], gsc.ap[:, 2:3], -1.0, None, AL.mult),
                rd=[gsc], wr=[gsc])
            self.ev("vector", lambda e: e.tensor_scalar(
                gsc.ap[:, 3:4], gsc.ap[:, 3:4], 1.0, None, AL.add),
                rd=[gsc], wr=[gsc])
            self.ev("vector", lambda e: e.tensor_scalar(
                stb.ap[:, 0:E], stb.ap[:, 0:E], gsc.ap[:, 2:3], None, AL.mult),
                rd=[stb, gsc], wr=[stb])
            self.ev("vector", lambda e: e.scalar_tensor_tensor(
                stb.ap[:, 0:E], lgt.ap[:, :], gsc.ap[:, 3:4], stb.ap[:, 0:E],
                AL.mult, AL.add),
                rd=[lgt, gsc, stb], wr=[stb])
            bk = bank[1]
            self.gmm(bk.ap[0:8, 0:128], [bk], [(stb.ap[:, 0:E], self.id_bf[:, :])],
                     rd_slots=[stb])
            self.ev("vector", lambda e, q=bk.ap, i=tci:
                    e.tensor_copy(sco.ap[:, i * 128:(i + 1) * 128], q[0:8, 0:128]),
                    rd=[bk], wr=[sco])

        self.barrier()
        ARB.reset(arb_gate)
        hbuf = ARB.salloc([128, E, HCN, 512])
        for tt in range(2):
            for e_ in range(E):
                bk = bank[7]
                self.gmm(bk.ap, [bk],
                         [(self.hsel[0:8, e_ * 128:(e_ + 1) * 128],
                           sco.ap[0:8, tt * 512:(tt + 1) * 512])],
                         rd_slots=[sco])
                self.ev("vector", lambda e, q=bk.ap, i=e_:
                        e.tensor_copy(srep.ap[:, i, :], q),
                        rd=[bk], wr=[srep])
            for e_ in range(E):
                for dc in range(DC):
                    sla = slab[dc % 2]
                    self.dma("sync", self.dsy, lambda e, i=e_, d=dc, s=sla:
                             e.dma_start(out=s.ap[:, :],
                                         in_=self.wup_bf[i][:, d * 128:(d + 1) * 128],
                                         transpose=True),
                             wr=[sla], extra_waits=[(self.dgp, self.t_prep_up)])
                    for hc in range(HCN):
                        bk = bank[hc]
                        w = (bk.wwait() + sla.rwait() + self.xl.rwait()) if dc == 0 \
                            else (sla.rwait() if hc == 0 else ())
                        self.t_mm(bk.ap, sla.ap[:, hc * 128:(hc + 1) * 128],
                                  self.xl_t[:, dc, tt * 512:(tt + 1) * 512],
                                  dc == 0, dc == DC - 1, waits=w)
                        if dc == DC - 1:
                            bk.note_w(self.pe)
                    sla.note_r(self.pe)
                self.xl.note_r(self.pe)
                for hc in range(HCN):
                    bk = bank[hc]
                    self.ev("scalar", lambda e, q=bk.ap, i=e_, c=hc: e.activation(
                        husk.ap[:, :], q, AF.Silu,
                        bias=self.bup_sb[:, i * HCN + c:i * HCN + c + 1]),
                        rd=[bk], wr=[husk])
                    self.ev("vector", lambda e, i=e_, c=hc: e.tensor_tensor(
                        hbuf.ap[:, i, c, :], husk.ap[:, :], srep.ap[:, i, :], AL.mult),
                        rd=[husk, srep], wr=[hbuf])
            for og in range(2):
                for oc2 in range(8):
                    oc = og * 8 + oc2
                    bk = bank[oc2]
                    self.t_mm(bk.ap, self.bdown_sb[0:8, oc * 128:(oc + 1) * 128],
                              sco.ap[:, tt * 512:(tt + 1) * 512], True, False,
                              waits=bk.wwait() + sco.rwait() +
                              [(self.dgp, self.t_bdown)])
                sco.note_r(self.pe)
                for e_ in range(E):
                    for hc in range(HCN):
                        sla = slab[(e_ * HCN + hc) % 2]
                        self.dma("sync", self.dsy, lambda e, i=e_, c=hc, g=og, s=sla:
                                 e.dma_start(
                                     out=s.ap[:, :],
                                     in_=self.wdown_bf[i][g * 1024:(g + 1) * 1024,
                                                          c * 128:(c + 1) * 128],
                                     transpose=True),
                                 wr=[sla], extra_waits=[(self.dgp, self.t_prep_down)])
                        last = (e_ == E - 1 and hc == HCN - 1)
                        for oc2 in range(8):
                            self.t_mm(bank[oc2].ap, sla.ap[:, oc2 * 128:(oc2 + 1) * 128],
                                      hbuf.ap[:, e_, hc, :], False, last,
                                      waits=(sla.rwait() + hbuf.rwait())
                                      if oc2 == 0 else ())
                        sla.note_r(self.pe)
                hbuf.note_r(self.pe)
                for oc2 in range(8):
                    bank[oc2].note_w(self.pe)
                    oc = og * 8 + oc2
                    self.ev("vector", lambda e, q=bank[oc2].ap, o=oc, b=tt:
                            e.tensor_tensor(
                                self.xr_t[:, o, b * 512:(b + 1) * 512], q,
                                self.xr_t[:, o, b * 512:(b + 1) * 512], AL.add),
                            rd=[bank[oc2], self.xr], wr=[self.xr])
        self.dbg = {
            "sco": (sco, 0, 0),
            "lgf": (lg_f, 8, 0),
            "hbuf00": (Slot(hbuf.ap[:, 0, 0, :]), 16, 0),
            "xl0": (Slot(self.xl_t[:, 0, :]), 144, 0),
            "srep0": (Slot(srep.ap[:, 0, :]), 272, 0),
        }
        self.barrier()

    def p_attn_b(self):
        ARB, ARF = self.ARB, self.ARF
        ARB.reset()
        ARF.reset(self.arf_scr)
        phq = ARB.salloc([128, NH, SOWN])
        phk = ARB.salloc([128, NH, SOWN])
        arb_m1 = ARB.mark()
        PA = Arena(self.arb_t, arb_m1, self.ARB_W - arb_m1)
        pslab = [PA.salloc([128, 512]) for _ in range(2)]
        cgb = PA.salloc([NH, SOWN])
        cbf = PA.salloc([NH, SOWN])
        zrow = PA.salloc([NH, 128])
        pa_perm = PA.mark()
        exf = PA.salloc([128, 512])
        rbuf = [PA.salloc([128, SOWN]) for _ in range(2)]
        swb = PA.salloc([128, 512])
        ta = PA.salloc([128, 512])
        rot = ta
        beta = ARF.salloc([NH, SOWN])
        ctm = ARF.salloc([128, TC * NH])
        rctm = ARF.salloc([128, TC * NH])
        gcol = ARF.salloc([128, 1])
        oex = ARF.salloc([NH, TC])
        svf = ARF.salloc([128, 128])
        bank = self.bank
        dn = self.dn
        ynum = phk

        bkg = [bank[0], bank[1]]
        for b in range(2):
            w = bkg[b].wwait() + self.wg_s.rwait() + self.xl.rwait()
            for dc in range(DC):
                self.t_mm(self.Qt[0][0:NH, b * 512:(b + 1) * 512],
                          self.wg_sb[:, dc, :],
                          self.xl_t[:, dc, b * 512:(b + 1) * 512], dc == 0, dc == DC - 1,
                          waits=w if dc == 0 else ())
            bkg[b].note_w(self.pe)
        self.wg_s.note_r(self.pe)
        self.ev("scalar", lambda e: e.activation(beta.ap[:, :], self.Qt[0][0:NH, 0:1024],
                                                 AF.Sigmoid, bias=self.bg_sb[:, 0:1]),
                rd=[bkg[0], bkg[1]], wr=[beta], extra_waits=self.wc)
        self.ev("vector", lambda e: e.tensor_scalar(
            beta.ap[:, :], beta.ap[:, :], BETA_MAX, None, AL.min),
            rd=[beta], wr=[beta])
        self.ev("vector", lambda e: e.tensor_scalar(
            beta.ap[:, :], beta.ap[:, :], BETA_MIN, None, AL.max),
            rd=[beta], wr=[beta])
        self.ev("vector", lambda e: e.memset(zrow.ap[:, :], 0.0), wr=[zrow])
        for c in range(TC):
            self.ev("vector", lambda e, i=c: e.tensor_tensor_scan(
                beta.ap[:, i * 128:(i + 1) * 128], beta.ap[:, i * 128:(i + 1) * 128],
                zrow.ap[:, :], 1.0, AL.mult, AL.add),
                rd=[beta, zrow], wr=[beta])
        cvar = beta
        self.ev("vector", lambda e: e.tensor_copy(cbf.ap[:, :], cvar.ap[:, :]),
                rd=[cvar], wr=[cbf])
        for c in range(TC):
            bk = bank[2]
            self.gmm(bk.ap[:, 0:NH], [bk],
                     [(cvar.ap[:, c * 128:(c + 1) * 128], self.id_f32[0:NH, 0:NH])],
                     rd_slots=[cvar])
            self.ev("vector", lambda e, q=bk.ap, i=c:
                    e.tensor_copy(ctm.ap[:, i * NH:(i + 1) * NH], q[:, 0:NH]),
                    rd=[bk], wr=[ctm])
        self.ev("vector", lambda e: e.reciprocal(rctm.ap[:, :], ctm.ap[:, :]),
                rd=[ctm], wr=[rctm])
        self.ev("vector", lambda e: e.tensor_tensor_scan(
            oex.ap[:, 0:TC], cvar.ap[:, 127::128], zrow.ap[:, 0:TC],
            1.0, AL.mult, AL.add),
            rd=[cvar, zrow], wr=[oex])
        self.ev("vector", lambda e: e.tensor_copy(exf.ap[0:NH, 0:TC - 1],
                                                  oex.ap[:, 0:TC - 1]),
                rd=[oex], wr=[exf])
        self.ev("vector", lambda e: e.tensor_copy(oex.ap[:, 1:TC],
                                                  exf.ap[0:NH, 0:TC - 1]),
                rd=[exf], wr=[oex])
        self.ev("vector", lambda e: e.memset(oex.ap[:, 0:1], 1.0), wr=[oex])
        for c in range(TC):
            self.ev("vector", lambda e, i=c: e.tensor_scalar(
                cgb.ap[:, i * 128:(i + 1) * 128], cvar.ap[:, i * 128:(i + 1) * 128],
                oex.ap[:, i:i + 1], None, AL.mult),
                rd=[cvar, oex], wr=[cgb])

        def proj_rope(wdram, phi):
            for og in range(4):
                for dc in range(DC):
                    sla = pslab[dc % 2]
                    self.dma("scalar", self.dsc, lambda e, g=og, d=dc, s=sla:
                             e.dma_start(out=s.ap[:, :],
                                         in_=wdram[d * 128:(d + 1) * 128,
                                                   g * 512:(g + 1) * 512]),
                             wr=[sla], extra_waits=[(self.dgp, self.t_prep_pj)])
                    for oc4 in range(4):
                        Qx, bx = self.qpair(oc4)
                        w = (bx[0].wwait() + bx[1].wwait() + sla.rwait() +
                             self.xl.rwait()) if dc == 0 \
                            else (sla.rwait() if oc4 == 0 else ())
                        for b in range(2):
                            self.t_mm(Qx[:, b * 512:(b + 1) * 512],
                                      sla.ap[:, oc4 * 128:(oc4 + 1) * 128],
                                      self.xl_t[:, dc, b * 512:(b + 1) * 512],
                                      dc == 0, dc == DC - 1,
                                      waits=w if b == 0 else ())
                        if dc == DC - 1:
                            bx[0].note_w(self.pe)
                            bx[1].note_w(self.pe)
                    sla.note_r(self.pe)
                self.xl.note_r(self.pe)
                for pair in range(2):
                    for i2 in range(2):
                        oc4 = pair * 2 + i2
                        Qx, bx = self.qpair(oc4)
                        self.ev("scalar", lambda e, s=Qx, r=rbuf[i2].ap:
                                e.activation(r[:, :], s[:, 0:1024], AF.Copy),
                                rd=[bx[0], bx[1]], wr=[rbuf[i2]])
                    for i2 in range(2):
                        oc4 = pair * 2 + i2
                        h = og * 4 + oc4
                        rb = rbuf[i2]
                        Qx, bx = self.qpair(i2)
                        for b in range(2):
                            self.t_mm(Qx[:, b * 512:(b + 1) * 512], self.swap_bf[:, :],
                                      rb.ap[:, b * 512:(b + 1) * 512], True, True,
                                      waits=(bx[0].wwait() + bx[1].wwait() + rb.rwait())
                                      if b == 0 else ())
                        bx[0].note_w(self.pe)
                        bx[1].note_w(self.pe)
                        rb.note_r(self.pe)
                        for b in range(2):
                            sl = b * 512
                            self.ev("vector", lambda e, s=Qx, o=sl: e.tensor_copy(
                                swb.ap[:, :], s[:, o:o + 512]),
                                rd=[bx[b]], wr=[swb])
                            self.ev("vector", lambda e, r=rb.ap, o=sl: e.tensor_tensor(
                                ta.ap[:, :], r[:, o:o + 512],
                                self.rope_cos[:, o:o + 512], AL.mult),
                                rd=[rb], wr=[ta], extra_waits=self.wc)
                            self.ev("vector", lambda e, o=sl: e.tensor_tensor(
                                swb.ap[:, :], swb.ap[:, :],
                                self.rope_sin[:, o:o + 512], AL.mult),
                                rd=[swb], wr=[swb])
                            self.ev("vector", lambda e: e.tensor_tensor(
                                rot.ap[:, :], ta.ap[:, :], swb.ap[:, :], AL.add),
                                rd=[swb], wr=[rot])
                            self.ev("scalar", lambda e: e.activation(
                                exf.ap[:, :], rot.ap[:, :], AF.Exp),
                                rd=[rot], wr=[exf])
                            self.ev("scalar", lambda e: e.activation(
                                rot.ap[:, :], rot.ap[:, :], AF.Relu),
                                rd=[rot], wr=[rot])
                            self.ev("vector", lambda e, hh=h, p=phi, o=sl:
                                    e.scalar_tensor_tensor(
                                        p.ap[:, hh, o:o + 512], rot.ap[:, :], 1.0,
                                        exf.ap[:, :], AL.add, AL.min),
                                    rd=[rot, exf], wr=[phi])
        proj_rope(self.wpj_bf["q"], phq)
        proj_rope(self.wpj_bf["k"], phk)

        self.barrier()
        PA.reset(pa_perm)
        Abf = PA.salloc([128, 128])
        ktb = PA.salloc([128, 128])
        vtb = PA.salloc([128, 128])
        qgb = PA.salloc([128, 128])
        sbf = PA.salloc([128, 128])
        zrep = PA.salloc([128, NH])
        dtf = ARF.salloc([128, 128])
        pa_scan = PA.mark()
        sendb = PA.salloc([128, NH, 129])
        vgrp = PA.salloc([128, 4, SOWN])
        p_crep, p_A = bank[0], bank[1]
        p_kT, p_Sc = bank[2], bank[3]
        p_nm, p_z = bank[4], bank[5]
        p_dn, p_vT = bank[6], bank[7]

        self.ev("vector", lambda e: e.memset(dn.ap[:, :], DN_EPS), wr=[dn])

        for vg in range(4):
            for dc in range(DC):
                sla = pslab[dc % 2]
                self.dma("scalar", self.dsc, lambda e, g=vg, d=dc, s=sla:
                         e.dma_start(out=s.ap[:, :],
                                     in_=self.wpj_bf["v"][d * 128:(d + 1) * 128,
                                                          g * 512:(g + 1) * 512]),
                         wr=[sla], extra_waits=[(self.dgp, self.t_prep_pj)])
                for oc4 in range(4):
                    Qx, bx = self.qpair(oc4)
                    w = (bx[0].wwait() + bx[1].wwait() + sla.rwait() +
                         self.xl.rwait()) if dc == 0 \
                        else (sla.rwait() if oc4 == 0 else ())
                    for b in range(2):
                        self.t_mm(Qx[:, b * 512:(b + 1) * 512],
                                  sla.ap[:, oc4 * 128:(oc4 + 1) * 128],
                                  self.xl_t[:, dc, b * 512:(b + 1) * 512],
                                  dc == 0, dc == DC - 1, waits=w if b == 0 else ())
                    if dc == DC - 1:
                        bx[0].note_w(self.pe)
                        bx[1].note_w(self.pe)
                sla.note_r(self.pe)
            for oc4 in range(4):
                Qx, bx = self.qpair(oc4)
                self.ev("scalar", lambda e, s=Qx, i=oc4:
                        e.activation(vgrp.ap[:, i, :], s[:, 0:1024], AF.Copy),
                        rd=[bx[0], bx[1]], wr=[vgrp])

            for hl in range(4):
                h = vg * 4 + hl
                self.ev("vector", lambda e: e.memset(svf.ap[:, :], 0.0), wr=[svf])
                self.ev("vector", lambda e: e.memset(sbf.ap[:, :], 0.0), wr=[sbf])
                self.ev("vector", lambda e: e.memset(zrep.ap[:, :], 0.0), wr=[zrep])
                for c in range(TC):
                    self.gmm(p_crep.ap[:, 0:128], [p_crep],
                             [(self.hsel[:, h * 128:(h + 1) * 128],
                               cbf.ap[0:NH, c * 128:(c + 1) * 128])],
                             rd_slots=[cbf])
                    self.ev("vector", lambda e: e.tensor_copy(
                        gcol.ap[:, 0:1], p_crep.ap[:, 127:128]),
                        rd=[p_crep], wr=[gcol])
                    self.ev("vector", lambda e, hh=h, cc=c: e.scalar_tensor_tensor(
                        dtf.ap[:, :], p_crep.ap[:, 0:128],
                        rctm.ap[:, cc * NH + hh:cc * NH + hh + 1],
                        self.tri_f32[:, :], AL.mult, AL.mult),
                        rd=[p_crep, rctm], wr=[dtf])
                    self.gmm(p_A.ap[:, 0:128], [p_A],
                             [(phk.ap[:, h, c * 128:(c + 1) * 128],
                               phq.ap[:, h, c * 128:(c + 1) * 128])],
                             rd_slots=[phk, phq])
                    self.ev("vector", lambda e: e.tensor_tensor(
                        Abf.ap[:, :], p_A.ap[:, 0:128], dtf.ap[:, :], AL.mult),
                        rd=[p_A, dtf], wr=[Abf])
                    self.gmm(p_kT.ap[:, 0:128], [p_kT],
                             [(phk.ap[:, h, c * 128:(c + 1) * 128], self.id_bf[:, :])],
                             rd_slots=[phk])
                    self.ev("vector", lambda e: e.tensor_scalar(
                        ktb.ap[:, :], p_kT.ap[:, 0:128], dtf.ap[:, 127:128],
                        None, AL.mult),
                        rd=[p_kT, dtf], wr=[ktb])
                    self.gmm(p_vT.ap[:, 0:128], [p_vT],
                             [(vgrp.ap[:, hl, c * 128:(c + 1) * 128], self.id_bf[:, :])],
                             rd_slots=[vgrp])
                    self.ev("vector", lambda e: e.tensor_copy(vtb.ap[:, :],
                                                              p_vT.ap[:, 0:128]),
                            rd=[p_vT], wr=[vtb])
                    self.ev("vector", lambda e, hh=h, cc=c: e.tensor_tensor(
                        qgb.ap[:, :], phq.ap[:, hh, cc * 128:(cc + 1) * 128],
                        p_crep.ap[:, 0:128], AL.mult),
                        rd=[phq, p_crep], wr=[qgb])
                    self.gmm(p_nm.ap[:, 0:128], [p_nm],
                             [(vtb.ap[:, :], Abf.ap[:, :]),
                              (sbf.ap[:, :], qgb.ap[:, :])],
                             rd_slots=[vtb, Abf, sbf, qgb])
                    self.ev("vector", lambda e, hh=h, cc=c: e.tensor_copy(
                        ynum.ap[:, hh, cc * 128:(cc + 1) * 128], p_nm.ap[:, 0:128]),
                        rd=[p_nm], wr=[ynum])
                    self.gmm(p_dn.ap[0:NH, 0:128], [p_dn],
                             [(self.csel[:, h * NH:(h + 1) * NH], Abf.ap[:, :]),
                              (zrep.ap[:, :], qgb.ap[:, :])],
                             rd_slots=[Abf, qgb, zrep])
                    self.ev("vector", lambda e, cc=c: e.tensor_tensor(
                        dn.ap[0:NH, cc * 128:(cc + 1) * 128],
                        dn.ap[0:NH, cc * 128:(cc + 1) * 128],
                        p_dn.ap[0:NH, 0:128], AL.add),
                        rd=[p_dn, dn], wr=[dn])
                    self.gmm(p_Sc.ap[:, 0:128], [p_Sc],
                             [(ktb.ap[:, :], vtb.ap[:, :])],
                             rd_slots=[ktb, vtb])
                    self.ev("vector", lambda e: e.scalar_tensor_tensor(
                        svf.ap[:, :], svf.ap[:, :], gcol.ap[:, 0:1],
                        p_Sc.ap[:, 0:128], AL.mult, AL.add),
                        rd=[p_Sc, gcol, svf], wr=[svf])
                    if c == TC - 1:
                        self.ev("vector", lambda e, hh=h: e.tensor_copy(
                            sendb.ap[:, hh, 0:128], svf.ap[:, :]),
                            rd=[svf], wr=[sendb])
                    else:
                        self.ev("vector", lambda e: e.tensor_copy(sbf.ap[:, :],
                                                                  svf.ap[:, :]),
                                rd=[svf], wr=[sbf])
                    self.gmm(p_z.ap[:, 0:NH], [p_z],
                             [(ktb.ap[:, :], self.ones_bf[:, 0:NH])],
                             rd_slots=[ktb])
                    self.ev("vector", lambda e, hh=h: e.scalar_tensor_tensor(
                        zrep.ap[:, hh:hh + 1], zrep.ap[:, hh:hh + 1],
                        gcol.ap[:, 0:1], p_z.ap[:, 0:1], AL.mult, AL.add),
                        rd=[p_z, gcol, zrep], wr=[zrep])
                self.ev("vector", lambda e, hh=h: e.tensor_copy(
                    sendb.ap[:, hh, 128:129], zrep.ap[:, hh:hh + 1]),
                    rd=[zrep], wr=[sendb])

        cc_in3 = self.cc_in.rearrange("p (a b) -> p a b", a=NH)
        self.dma("scalar", self.dsc, lambda e:
                 e.dma_start(out=cc_in3, in_=sendb.ap), rd=[sendb])
        t_ccin = self.dsc.n
        self.ccs.tick()
        self.emit("gpsimd", lambda e: e.collective_compute(
            "AllGather", AL.bypass, replica_groups=[list(range(NCORES))],
            ins=[self.cc_in.opt()], outs=[self.cc_out.opt()]),
            waits=[(self.dsc, t_ccin)], inc=(self.ccs, 1))
        t_cc = self.ccs.n

        PA.reset(pa_scan)
        sinb = PA.salloc([128, NH, 128])
        zin_c = PA.salloc([128, NH])
        rkb = PA.salloc([128, NH, 129])
        dnr = cbf
        self.ev("vector", lambda e: e.memset(sinb.ap[:, :, :], 0.0), wr=[sinb],
                extra_waits=[(self.dsc, self.dsc.n)] + sendb.rd)
        self.ev("vector", lambda e: e.memset(zin_c.ap[:, :], 0.0), wr=[zin_c])
        for r in range(NCORES):
            rb = rkb
            self.dma("sync", self.dsy, lambda e, i=r, d=rb:
                     e.dma_start(out=d.ap[:, :, :], in_=self.cc_out[i].rearrange(
                         "p (a b) -> p a b", a=NH)),
                     wr=[rb], extra_waits=[(self.ccs, t_cc)])
            self.ev("vector", lambda e, i=r, s=rb.ap: e.scalar_tensor_tensor(
                sinb.ap[:, :, :], s[:, :, 0:128], self.selmask[:, i:i + 1],
                sinb.ap[:, :, :], AL.mult, AL.add),
                rd=[rb, sinb], wr=[sinb])
            self.ev("vector", lambda e, i=r, s=rb.ap: e.scalar_tensor_tensor(
                zin_c.ap[:, :], s[:, :, 128], self.selmask[:, i:i + 1],
                zin_c.ap[:, :], AL.mult, AL.add),
                rd=[rb, zin_c], wr=[zin_c])

        zinsel = PA.salloc([128, NH])
        for h in range(NH):
            self.ev("vector", lambda e: e.memset(zinsel.ap[:, :], 0.0), wr=[zinsel])
            self.ev("vector", lambda e, hh=h: e.tensor_copy(
                zinsel.ap[:, hh:hh + 1], zin_c.ap[:, hh:hh + 1]),
                rd=[zin_c], wr=[zinsel])
            for c in range(TC):
                self.gmm(p_crep.ap[:, 0:128], [p_crep],
                         [(self.hsel[:, h * 128:(h + 1) * 128],
                           cgb.ap[0:NH, c * 128:(c + 1) * 128])],
                         rd_slots=[cgb])
                self.ev("vector", lambda e, hh=h, cc=c: e.tensor_tensor(
                    qgb.ap[:, :], phq.ap[:, hh, cc * 128:(cc + 1) * 128],
                    p_crep.ap[:, 0:128], AL.mult),
                    rd=[phq, p_crep], wr=[qgb])
                self.gmm(p_nm.ap[:, 0:128], [p_nm],
                         [(sinb.ap[:, h, :], qgb.ap[:, :])],
                         rd_slots=[sinb, qgb])
                self.ev("vector", lambda e, hh=h, cc=c: e.tensor_tensor(
                    ynum.ap[:, hh, cc * 128:(cc + 1) * 128],
                    ynum.ap[:, hh, cc * 128:(cc + 1) * 128],
                    p_nm.ap[:, 0:128], AL.add),
                    rd=[p_nm, ynum], wr=[ynum])
                self.gmm(p_dn.ap[0:NH, 0:128], [p_dn],
                         [(zinsel.ap[:, :], qgb.ap[:, :])],
                         rd_slots=[zinsel, qgb])
                self.ev("vector", lambda e, cc=c: e.tensor_tensor(
                    dn.ap[0:NH, cc * 128:(cc + 1) * 128],
                    dn.ap[0:NH, cc * 128:(cc + 1) * 128],
                    p_dn.ap[0:NH, 0:128], AL.add),
                    rd=[p_dn, dn], wr=[dn])

        self.ev("vector", lambda e: e.reciprocal(dn.ap[:, :], dn.ap[:, :]),
                rd=[dn], wr=[dn])
        self.ev("vector", lambda e: e.tensor_copy(dnr.ap[:, :], dn.ap[:, :]),
                rd=[dn], wr=[dnr])
        for h in range(NH):
            for b in range(2):
                bk = bank[0]
                self.gmm(bk.ap, [bk],
                         [(self.hsel[:, h * 128:(h + 1) * 128],
                           dnr.ap[0:NH, b * 512:(b + 1) * 512])],
                         rd_slots=[dnr])
                self.ev("vector", lambda e, hh=h, bb=b: e.tensor_tensor(
                    ynum.ap[:, hh, bb * 512:(bb + 1) * 512],
                    ynum.ap[:, hh, bb * 512:(bb + 1) * 512],
                    self.Qt[0][:, 0:512], AL.mult),
                    rd=[bk, ynum], wr=[ynum])
        for og in range(4):
            for dc in range(DC):
                sla = pslab[dc % 2]
                self.dma("scalar", self.dsc, lambda e, g=og, d=dc, s=sla:
                         e.dma_start(out=s.ap[:, :],
                                     in_=self.wpj_bf["o"][d * 128:(d + 1) * 128,
                                                          g * 512:(g + 1) * 512]),
                         wr=[sla], extra_waits=[(self.dgp, self.t_prep_pj)])
                for oc4 in range(4):
                    Qx, bx = self.qpair(oc4)
                    w = (bx[0].wwait() + bx[1].wwait() + sla.rwait() + ynum.rwait()) \
                        if dc == 0 else (sla.rwait() if oc4 == 0 else ())
                    for b in range(2):
                        self.t_mm(Qx[:, b * 512:(b + 1) * 512],
                                  sla.ap[:, oc4 * 128:(oc4 + 1) * 128],
                                  ynum.ap[:, dc, b * 512:(b + 1) * 512],
                                  dc == 0, dc == DC - 1, waits=w if b == 0 else ())
                    if dc == DC - 1:
                        bx[0].note_w(self.pe)
                        bx[1].note_w(self.pe)
                sla.note_r(self.pe)
            ynum.note_r(self.pe)
            for oc4 in range(4):
                Qx, bx = self.qpair(oc4)
                oc = og * 4 + oc4
                for b in range(2):
                    self.ev("vector", lambda e, s=Qx, o=oc, bb=b:
                            e.scalar_tensor_tensor(
                                self.xr_t[:, o, bb * 512:(bb + 1) * 512],
                                s[:, bb * 512:(bb + 1) * 512],
                                self.bo_sb[:, o:o + 1],
                                self.xr_t[:, o, bb * 512:(bb + 1) * 512],
                                AL.add, AL.add),
                            rd=[bx[b], self.xr], wr=[self.xr])
        self.dbg = {
            "beta_c": (beta, 0, 0),
            "cbf": (cbf, 16, 0),
            "cgb": (cgb, 32, 0),
            "dn": (dn, 48, 0),
            "phq0": (Slot(phq.ap[:, 0, :]), 64, 0),
            "phk0": (Slot(phk.ap[:, 0, :]), 192, 0),
            "ynum0": (Slot(ynum.ap[:, 0, :]), 320, 0),
            "sinb0": (Slot(sinb.ap[:, 0, :]), 448, 0),
            "sendb0": (Slot(sendb.ap[:, 0, :]), 576, 0),
            "ynum1": (Slot(ynum.ap[:, 4, :]), 704, 0),
            "ynum8": (Slot(ynum.ap[:, 12, :]), 832, 0),
        }
        self.barrier()

    def p_finish(self):
        ARF = self.ARF
        ARF.reset(self.arf_scr)
        ot = ARF.salloc([128, D])
        for tci in range(TC):
            for dc in range(DC):
                bk = self.bank[dc % 4]
                self.gmm(bk.ap[:, 0:128], [bk],
                         [(self.xr_t[:, dc, tci * 128:(tci + 1) * 128],
                           self.id_f32[:, :])],
                         rd_slots=[self.xr])
                self.ev("vector", lambda e, q=bk.ap, d=dc:
                        e.tensor_copy(ot.ap[:, d * 128:(d + 1) * 128], q[:, 0:128]),
                        rd=[bk], wr=[ot])
            self.dma("sync", self.outs, lambda e, i=tci:
                     e.dma_start(out=self.y_out[i * 128:(i + 1) * 128, :],
                                 in_=ot.ap[:, :]),
                     rd=[ot])
        self.emit("sync", lambda e: e.nop(), waits=[(self.outs, self.outs.n)])

    def playback(self, stk):
        block = stk.enter_context(self.nc.Block())

        @block.tensor
        def _(e):
            for f in self.ops["tensor"]:
                f(e)

        @block.vector
        def _(e):
            for f in self.ops["vector"]:
                f(e)

        @block.scalar
        def _(e):
            for f in self.ops["scalar"]:
                f(e)

        @block.gpsimd
        def _(e):
            for f in self.ops["gpsimd"]:
                f(e)

        @block.sync
        def _(e):
            for f in self.ops["sync"]:
                f(e)



def _make_consts(core_id):
    j = core_id % 2
    I = np.eye(128)
    ones = np.ones((128, 128))
    swap = np.zeros((128, 128))
    for m in range(128):
        swap[(m + 64) % 128, m] = 1.0
    tri = (np.arange(128)[:, None] <= np.arange(128)[None, :]).astype(np.float64)
    pos = np.arange(j * SOWN, (j + 1) * SOWN, dtype=np.float64)
    inv_freq = 1.0 / (10000.0 ** (np.arange(64, dtype=np.float64) / 64.0))
    fr = inv_freq[:, None] * pos[None, :]
    cos = np.cos(fr)
    sin = np.sin(fr)
    rope_cos = np.concatenate([cos, cos], axis=0)
    rope_sin = np.concatenate([-sin, sin], axis=0)
    cv = np.zeros((128, 8))
    cv[:, 0] = np.pi / 2
    cv[:, 1] = LN_EPS
    sel = np.zeros((128, 8))
    if j == 1:
        sel[:, core_id - 1] = 1.0
    hsel = np.zeros((16, 16 * 128))
    for h in range(16):
        hsel[h, h * 128:(h + 1) * 128] = 1.0
    csel = np.zeros((128, 16 * 16))
    for h in range(16):
        csel[:, h * 16 + h] = 1.0
    return {
        "id_bf": I.astype(bfloat16), "id_f32": I.astype(np.float32),
        "ones_bf": ones.astype(bfloat16), "ones_f32": ones.astype(np.float32),
        "swap_bf": swap.astype(bfloat16), "tri_f32": tri.astype(np.float32),
        "rope_cos": rope_cos.astype(bfloat16), "rope_sin": rope_sin.astype(bfloat16),
        "cvec": cv.astype(np.float32), "selmask": sel.astype(np.float32),
        "hsel_bf": hsel.astype(bfloat16), "csel_bf": csel.astype(bfloat16),
    }


_NC_CACHE = {}


def _get_nc(stop_after):
    if stop_after not in _NC_CACHE:
        _NC_CACHE[stop_after] = MK(stop_after).build()
    return _NC_CACHE[stop_after]


def run_dev(inputs, stop_after="full"):
    nc = _get_nc(stop_after)
    x = np.ascontiguousarray(np.asarray(inputs["L0_attn_a_hidden_states"], np.float32))
    g = lambda n: np.ascontiguousarray(np.asarray(inputs[n], np.float32))
    shared = {}
    for nm in ("q", "k", "v", "o"):
        shared[f"basis_{nm}"] = g(f"L0_attn_a_basis_{nm}")
        shared[f"phase_t_{nm}"] = np.ascontiguousarray(
            g(f"L0_attn_a_phase_{nm}").T)
        shared[f"amp_t_{nm}"] = np.ascontiguousarray(
            g(f"L0_attn_a_amp_{nm}").T)
    for nm in ("Wq", "Wk", "Wv", "Wg", "Wo"):
        shared[nm] = g(f"L0_attn_b_{nm}")
    shared["bg"] = g("L0_attn_b_bg").reshape(NH, 1)
    shared["bo_pk"] = np.ascontiguousarray(
        g("L0_attn_b_bo").reshape(OCN, 128).T)
    shared["Wg1"] = g("L0_ffn_Wg1")
    shared["Wg2"] = g("L0_ffn_Wg2")
    shared["Wup"] = g("L0_ffn_Wup")
    shared["bup_pk"] = np.ascontiguousarray(
        g("L0_ffn_bup").reshape(E, HCN, 128).transpose(2, 0, 1).reshape(128, E * HCN))
    shared["Wdown"] = g("L0_ffn_Wdown")
    shared["bdown"] = g("L0_ffn_bdown")

    in_maps = []
    for c in range(NCORES):
        b, j = c // 2, c % 2
        m = dict(shared)
        m["x_own"] = np.ascontiguousarray(x[b, j * SOWN:(j + 1) * SOWN])
        m["x_part"] = np.ascontiguousarray(x[b, (1 - j) * SOWN:(2 - j) * SOWN])
        m.update(_make_consts(c))
        in_maps.append(m)

    res = run_bass_kernel_spmd(nc, in_maps, core_ids=list(range(NCORES)))
    out = np.empty((4, SEQ, D), np.float32)
    for c in range(NCORES):
        b, j = c // 2, c % 2
        out[b, j * SOWN:(j + 1) * SOWN] = res.results[c]["y_out"]
    return out


def kernel(**inputs):
    return run_dev(inputs, stop_after="full")


# version 04
# speedup vs baseline: 1.1944x; 1.1171x over previous
import math
from contextlib import ExitStack

import numpy as np
import ml_dtypes

import concourse.bass as bass
import concourse.mybir as mybir
from concourse.bass_utils import run_bass_kernel_spmd

F32 = mybir.dt.float32
BF16 = mybir.dt.bfloat16
AF = mybir.ActivationFunctionType
AL = mybir.AluOpType
AX = mybir.AxisListType
bfloat16 = ml_dtypes.bfloat16

NCORES = 8
D = 2048
NH = 16
DH = 128
HARM = 64
SEQ = 2048
SOWN = 1024
TC = 8
DC = 16
E = 8
HE = 1024
HCN = 8
OCN = 16
SM_SCALE = 1.0 / math.sqrt(DH)
LN_EPS = 1e-5
DN_EPS = 1e-6
BETA_MIN, BETA_MAX = 0.8, 0.9995


class Sem:
    def __init__(self, handle):
        self.h = handle
        self.n = 0

    def tick(self, k=1):
        self.n += k
        return self.n


class Slot:

    def __init__(self, ap=None):
        self.ap = ap
        self.wt = None
        self.rd = []

    def wwait(self):
        w = list(self.rd)
        if self.wt is not None:
            w.append(self.wt)
        return w

    def rwait(self):
        return [self.wt] if self.wt is not None else []

    def note_w(self, sem):
        self.wt = (sem, sem.n)
        self.rd = []

    def note_r(self, sem):
        self.rd.append((sem, sem.n))


def _dedup(waits):
    best = {}
    for (s, v) in waits:
        if s is None:
            continue
        if s not in best or v > best[s]:
            best[s] = v
    return [(s, v) for s, v in best.items()]


class Arena:
    def __init__(self, t, base, width):
        self.t = t
        self.base = base
        self.width = width
        self.top = base

    def alloc(self, shape):
        n = 1
        for s in shape[1:]:
            n *= s
        off = self.top
        self.top += n
        assert self.top <= self.base + self.width, \
            f"arena overflow: {self.top} > {self.base + self.width}"
        ap = self.t[0:shape[0], off:off + n]
        if len(shape) == 3:
            ap = ap.rearrange("p (a b) -> p a b", a=shape[1])
        elif len(shape) == 4:
            ap = ap.rearrange("p (a b c) -> p a b c", a=shape[1], b=shape[2])
        return ap

    def salloc(self, shape):
        return Slot(self.alloc(shape))

    def mark(self):
        return self.top

    def reset(self, mark=None):
        self.top = self.base if mark is None else mark


class MK:
    def __init__(self, stop_after="full"):
        self.stop_after = stop_after
        self.nc = bass.Bass("TRN2", target_bir_lowering=False)
        self.ops = {k: [] for k in ("tensor", "vector", "scalar", "gpsimd", "sync")}

    def emit(self, eng, fn, waits=(), inc=None):
        waits = tuple((s.h, v) for (s, v) in _dedup(waits))

        def thunk(e, fn=fn, waits=waits, inc=inc):
            for (h, v) in waits:
                e.wait_ge(h, v)
            ins = fn(e)
            if inc is not None:
                ins.then_inc(inc[0].h, inc[1])

        self.ops[eng].append(thunk)

    def t_mm(self, dst, lhsT, rhs, start, stop, waits=()):
        self.pe.tick()
        self.emit("tensor",
                  lambda e, d=dst, l=lhsT, r=rhs, st=start, sp=stop:
                  e.matmul(d, l, r, start=st, stop=sp),
                  waits=waits, inc=(self.pe, 1))
        return self.pe.n

    def gmm(self, dst, banks, parts, rd_slots=(), extra_waits=()):
        w = list(extra_waits)
        for b in banks:
            w += b.wwait()
        for s in rd_slots:
            w += s.rwait()
        n = len(parts)
        for i, (l, r) in enumerate(parts):
            self.t_mm(dst, l, r, i == 0, i == n - 1, waits=w if i == 0 else ())
        for b in banks:
            b.note_w(self.pe)
        for s in rd_slots:
            s.note_r(self.pe)
        return self.pe.n

    def ev(self, eng, fn, rd=(), wr=(), extra_waits=()):
        sem = {"vector": self.ve, "scalar": self.se, "gpsimd": self.ge}[eng]
        w = list(extra_waits)
        for s in rd:
            w += s.rwait()
        for s in wr:
            w += s.wwait()
        sem.tick()
        self.emit(eng, fn, waits=w, inc=(sem, 1))
        for s in rd:
            s.note_r(sem)
        for s in wr:
            s.note_w(sem)
        return sem.n

    def dma(self, eng, qsem, fn, rd=(), wr=(), extra_waits=()):
        w = list(extra_waits) + list(getattr(self, "fence", ()))
        for s in rd:
            w += s.rwait()
        for s in wr:
            w += s.wwait()
        qsem.tick(16)
        self.emit(eng, fn, waits=w, inc=(qsem, 16))
        for s in rd:
            s.note_r(qsem)
        for s in wr:
            s.note_w(qsem)
        return qsem.n

    def barrier(self):
        pts = [(self.pe, self.pe.n), (self.ve, self.ve.n), (self.se, self.se.n),
               (self.ge, self.ge.n)]
        for eng, own in (("tensor", self.pe), ("vector", self.ve),
                         ("scalar", self.se), ("gpsimd", self.ge)):
            w = [(s, v) for (s, v) in pts if s is not own]
            self.emit(eng, lambda e: e.nop(), waits=w)
        self.fence = pts

    def build(self):
        nc = self.nc
        self.DT = {}

        def din(name, shape, dtype=F32):
            self.DT[name] = nc.dram_tensor(name, list(shape), dtype,
                                           kind="ExternalInput").ap()

        din("x_own", [SOWN, D]); din("x_part", [SOWN, D])
        for nm in ("q", "k", "v", "o"):
            din(f"basis_{nm}", [HARM, D])
            din(f"phase_t_{nm}", [HARM, D])
            din(f"amp_t_{nm}", [HARM, D])
        for nm in ("Wq", "Wk", "Wv", "Wo"):
            din(nm, [D, D])
        din("Wg", [D, NH]); din("bg", [NH, 1]); din("bo_pk", [128, OCN])
        din("Wg1", [E, D]); din("Wg2", [E, E])
        din("Wup", [E, HE, D]); din("bup_pk", [128, E * HCN])
        din("Wdown", [E, D, HE]); din("bdown", [8, D])
        din("id_bf", [128, 128], BF16); din("id_f32", [128, 128], F32)
        din("ones_bf", [128, 128], BF16); din("ones_f32", [128, 128], F32)
        din("swap_bf", [128, 128], BF16); din("tri_f32", [128, 128], F32)
        din("rope_cos", [128, SOWN], BF16); din("rope_sin", [128, SOWN], BF16)
        din("cvec", [128, 8], F32); din("selmask", [128, 8], F32)
        din("hsel_bf", [NH, NH * 128], BF16)
        din("csel_bf", [128, NH * NH], BF16)

        self.y_out = nc.dram_tensor("y_out", [SOWN, D], F32, kind="ExternalOutput").ap()
        self.wup_bf = nc.dram_tensor("wup_bf", [E, HE, D], BF16).ap()
        self.wdown_bf = nc.dram_tensor("wdown_bf", [E, D, HE], BF16).ap()
        self.wpj_bf = {nm: nc.dram_tensor(f"w{nm}_bf", [D, D], BF16).ap()
                       for nm in ("q", "k", "v", "o")}
        self.bas_bf = {nm: nc.dram_tensor(f"bas{nm}_bf", [HARM, D], BF16).ap()
                       for nm in ("q", "k", "v", "o")}
        self.pha_bf = {nm: nc.dram_tensor(f"pha{nm}_bf", [HARM, D], BF16).ap()
                       for nm in ("q", "k", "v", "o")}
        self.amp_bf = {nm: nc.dram_tensor(f"amp{nm}_bf", [HARM, D], BF16).ap()
                       for nm in ("q", "k", "v", "o")}
        self.cc_in = nc.dram_tensor("cc_in", [128, NH * 129], BF16).ap()
        self.cc_out = nc.dram_tensor("cc_out", [NCORES, 128, NH * 129], BF16).ap()

        with ExitStack() as stk:
            sb = lambda n, sh, dt: stk.enter_context(nc.sbuf_tensor(n, list(sh), dt))
            psal = lambda n, sh: stk.enter_context(nc.psum_tensor(n, list(sh), F32))

            self.xr_t = sb("xr", [128, DC, SOWN], F32)
            self.xl_t = sb("xl", [128, DC, SOWN], BF16)
            self.id_bf = sb("c_id_bf", [128, 128], BF16)
            self.id_f32 = sb("c_id_f32", [128, 128], F32)
            self.ones_bf = sb("c_ones_bf", [128, 128], BF16)
            self.ones_f32 = sb("c_ones_f32", [128, 128], F32)
            self.swap_bf = sb("c_swap_bf", [128, 128], BF16)
            self.tri_f32 = sb("c_tri_f32", [128, 128], F32)
            self.rope_cos = sb("c_rope_cos", [128, SOWN], BF16)
            self.rope_sin = sb("c_rope_sin", [128, SOWN], BF16)
            self.cvec = sb("c_cvec", [128, 8], F32)
            self.selmask = sb("c_selmask", [128, 8], F32)
            self.bup_sb = sb("c_bup", [128, E * HCN], F32)
            self.bdown_sb = sb("c_bdown", [8, D], BF16)
            self.bo_sb = sb("c_bo", [128, OCN], F32)
            self.bg_sb = sb("c_bg", [NH, 1], F32)
            self.wg_sb = sb("c_wg", [128, DC, NH], BF16)
            self.wg1T_sb = sb("c_wg1t", [128, DC, E], BF16)
            self.wg2T_sb = sb("c_wg2t", [E, E], BF16)
            self.hsel = sb("c_hsel", [NH, NH * 128], BF16)
            self.csel = sb("c_csel", [128, NH * NH], BF16)

            self.ARB_W = 42864
            self.ARF_W = 3112
            self.arb_t = sb("arb", [128, self.ARB_W], BF16)
            self.arf_t = sb("arf", [128, self.ARF_W], F32)
            arb_addr = nc.lookup_mloc(self.arb_t).addr
            self.xf1_t = nc.alloc_sbuf_tensor_at(
                "xf1ov", [128, 2064], F32,
                offset=arb_addr + (self.ARB_W - 4128) * 2)

            self.Qt = [psal(f"q{i}", [128, 1024]) for i in range(4)]
            self.bank = [Slot(self.Qt[i // 2][:, (i % 2) * 512:(i % 2) * 512 + 512])
                         for i in range(8)]

            for nm in ("pe", "ve", "se", "ge", "dsy", "dsc", "dgp", "ccs", "outs"):
                setattr(self, nm, Sem(stk.enter_context(nc.semaphore(nm))))

            ARF = Arena(self.arf_t, 0, self.ARF_W)
            self.mcell = ARF.salloc([NH, 16])
            self.dn = ARF.salloc([NH, SOWN])
            self.mneg = ARF.salloc([128, NH])
            self.stat = ARF.salloc([128, 8])
            self.arf_scr = ARF.mark()
            self.ARF = ARF
            self.ARB = Arena(self.arb_t, 0, self.ARB_W)

            self.xr = Slot(self.xr_t)
            self.xl = Slot(self.xl_t)
            self.wg_s = Slot(self.wg_sb)
            self.wg1T = Slot(self.wg1T_sb)
            self.wg2T = Slot(self.wg2T_sb)

            self.p_prep()
            self.p_load_ln1()
            if self.stop_after == "ln1":
                self.p_finish(); self.playback(stk); return nc
            if self.stop_after == "ln1x":
                self.dbg = {
                    "xl0": (Slot(self.xl_t[:, 0, :]), 0, 0),
                    "xl1": (Slot(self.xl_t[:, 1, :]), 128, 0),
                    "xp0": (Slot(self.xp_ln.ap[:, 0, :]), 256, 0),
                    "st": (self.dbg_st, 384, 0),
                    "xlntm": (self.dbg_xlntm, 512, 0),
                    "xtm": (self.dbg_xtm, 640, 0),
                }
                self.p_dump(); self.playback(stk); return nc
            self.p_attn_a()
            if self.stop_after == "attn_a_dbg":
                self.p_dump(); self.playback(stk); return nc
            if self.stop_after != "attn_a":
                self.p_ln_fm()
                self.p_ffn()
                if self.stop_after == "ffn1_dbg":
                    self.p_dump(); self.playback(stk); return nc
                if self.stop_after != "ffn1":
                    self.p_ln_fm()
                    self.p_attn_b()
                    if self.stop_after == "attn_b_dbg":
                        self.p_dump(); self.playback(stk); return nc
                    if self.stop_after != "attn_b":
                        self.p_ln_fm()
                        self.p_ffn()
            self.p_finish()
            self.playback(stk)
            return nc

    def qpair(self, j):
        return self.Qt[j], [self.bank[2 * j], self.bank[2 * j + 1]]

    def p_prep(self):
        DT = self.DT
        for nm in ("q", "k", "v", "o"):
            self.dma("gpsimd", self.dgp, lambda e, s=DT[f"basis_{nm}"],
                     d=self.bas_bf[nm]:
                     e.dma_start(out=d[:, :], in_=s[:, :]))
        self.t_prep_bas = self.dgp.n
        for nm in ("q", "k", "v", "o"):
            self.dma("gpsimd", self.dgp, lambda e, s=DT[f"phase_t_{nm}"],
                     d=self.pha_bf[nm]:
                     e.dma_start(out=d[:, :], in_=s[:, :]))
            self.dma("gpsimd", self.dgp, lambda e, s=DT[f"amp_t_{nm}"],
                     d=self.amp_bf[nm]:
                     e.dma_start(out=d[:, :], in_=s[:, :]))
        self.t_prep_pa = self.dgp.n
        for dc in range(DC):
            self.dma("gpsimd", self.dgp, lambda e, d=dc:
                     e.dma_start(out=self.wg_sb[:, d, :],
                                 in_=DT["Wg"][d * 128:(d + 1) * 128, :]),
                     wr=[self.wg_s])
        self.dma("gpsimd", self.dgp, lambda e:
                 e.dma_start(out=self.bdown_sb[:, :], in_=DT["bdown"][:, :]))
        self.t_bdown = self.dgp.n
        for e_ in range(E):
            self.dma("gpsimd", self.dgp, lambda e, i=e_:
                     e.dma_start(out=self.wup_bf[i], in_=DT["Wup"][i]))
        self.t_prep_up = self.dgp.n
        for e_ in range(E):
            self.dma("gpsimd", self.dgp, lambda e, i=e_:
                     e.dma_start(out=self.wdown_bf[i], in_=DT["Wdown"][i]))
        self.t_prep_down = self.dgp.n
        for nm, key in (("Wq", "q"), ("Wk", "k"), ("Wv", "v"), ("Wo", "o")):
            for hh in range(2):
                self.dma("gpsimd", self.dgp, lambda e, s=DT[nm], d=self.wpj_bf[key], i=hh:
                         e.dma_start(out=d[i * 1024:(i + 1) * 1024, :],
                                     in_=s[i * 1024:(i + 1) * 1024, :]))
        self.t_prep_pj = self.dgp.n
        for nm, dst in (("id_bf", self.id_bf), ("id_f32", self.id_f32),
                        ("ones_bf", self.ones_bf), ("ones_f32", self.ones_f32),
                        ("swap_bf", self.swap_bf), ("tri_f32", self.tri_f32),
                        ("rope_cos", self.rope_cos), ("rope_sin", self.rope_sin),
                        ("cvec", self.cvec), ("selmask", self.selmask),
                        ("bup_pk", self.bup_sb), ("bo_pk", self.bo_sb),
                        ("bg", self.bg_sb), ("hsel_bf", self.hsel),
                        ("csel_bf", self.csel)):
            self.dma("scalar", self.dsc, lambda e, s=DT[nm], d=dst:
                     e.dma_start(out=d[:, :], in_=s[:, :]))
        self.t_consts = self.dsc.n
        self.wc = [(self.dsc, self.t_consts)]

    def p_load_ln1(self):
        DT = self.DT
        ARF, ARB = self.ARF, self.ARB
        ARF.reset(self.arf_scr)
        xf = [ARF.salloc([128, D]), Slot(self.xf1_t[0:128, 0:2048])]
        stp = [Slot(self.xf1_t[0:128, 2048:2056]),
               Slot(self.xf1_t[0:128, 2056:2064])]
        self.xp_base = ARB.mark()
        self.xp_ln = ARB.salloc([128, DC, SOWN])
        self.arb_m0 = ARB.mark()
        xnb = [ARB.salloc([128, D]) for _ in range(2)]
        xdump = ARB.salloc([128, D])
        self.dbg_xtm = xf[0]
        self.dbg_xlntm = xnb[0]
        self.dbg_st = stp[0]

        jobs = [("own", tci) for tci in range(TC)] + \
               [("part", tci) for tci in range(TC)]

        def s_dma(j):
            kind, tci = jobs[j]
            src = DT["x_own"] if kind == "own" else DT["x_part"]
            x = xf[j % 2]
            self.dma("sync", self.dsy, lambda e, i=tci, s=src, d=x.ap:
                     e.dma_start(out=d[:, :], in_=s[i * 128:(i + 1) * 128, :]),
                     wr=[x])

        def s_stats(j):
            x, st = xf[j % 2], stp[j % 2]
            xa, sa = x.ap, st.ap
            self.ev("scalar", lambda e, xa=xa, sa=sa:
                    e.activation(xdump.ap[:, :], xa[:, :], AF.Copy,
                                 accum_out=sa[:, 0:1]),
                    rd=[x], wr=[st])
            self.ev("scalar", lambda e, xa=xa, sa=sa:
                    e.activation(xdump.ap[:, :], xa[:, :], AF.Square,
                                 accum_out=sa[:, 1:2]),
                    rd=[x], wr=[st])
            self.ev("scalar", lambda e, sa=sa:
                    e.activation(sa[:, 2:3], sa[:, 0:1], AF.Copy, scale=1.0 / D),
                    rd=[st], wr=[st])
            self.ev("scalar", lambda e, sa=sa:
                    e.activation(sa[:, 3:4], sa[:, 0:1], AF.Square, scale=1.0 / D),
                    rd=[st], wr=[st])
            self.ev("vector", lambda e, sa=sa:
                    e.scalar_tensor_tensor(sa[:, 4:5], sa[:, 3:4], -1.0,
                                           self.cvec[:, 1:2], AL.mult, AL.add),
                    rd=[st], wr=[st],
                    extra_waits=self.wc if j == 0 else ())
            self.ev("scalar", lambda e, sa=sa:
                    e.activation(sa[:, 5:6], sa[:, 1:2], AF.Ln, scale=1.0 / D,
                                 bias=sa[:, 4:5]),
                    rd=[st], wr=[st])
            self.ev("scalar", lambda e, sa=sa:
                    e.activation(sa[:, 5:6], sa[:, 5:6], AF.Exp, scale=-0.5),
                    rd=[st], wr=[st])

        def s_norm_evac(j):
            kind, tci = jobs[j]
            x, st, xn = xf[j % 2], stp[j % 2], xnb[j % 2]
            xa, sa, na = x.ap, st.ap, xn.ap
            self.ev("vector", lambda e, xa=xa, sa=sa, na=na:
                    e.tensor_scalar(na[:, :], xa[:, :],
                                    sa[:, 2:3], sa[:, 5:6],
                                    AL.subtract, AL.mult),
                    rd=[x, st], wr=[xn])
            out_fm = self.xl if kind == "own" else self.xp_ln
            boff = 0 if kind == "own" else 4 * (j % 2)
            for g in range(4):
                bk = self.bank[boff + g]
                w = bk.wwait() + xn.rwait() + (self.wc if j == 0 and g == 0 else [])
                for dcl in range(4):
                    dc = g * 4 + dcl
                    self.t_mm(bk.ap[:, dcl * 128:(dcl + 1) * 128],
                              xn.ap[:, dc * 128:(dc + 1) * 128], self.id_bf[:, :],
                              True, True, waits=w if dcl == 0 else ())
                bk.note_w(self.pe)
                xn.note_r(self.pe)
                self.ev("vector", lambda e, q=bk.ap, g2=g, i=tci, o=out_fm:
                        e.tensor_copy(
                            o.ap[:, g2 * 4:(g2 + 1) * 4, i * 128:(i + 1) * 128],
                            q[:, 0:512].rearrange("p (a b) -> p a b", a=4)),
                        rd=[bk], wr=[out_fm])
            if kind == "own":
                for g in range(4):
                    bk = self.bank[4 + g]
                    w = bk.wwait() + x.rwait()
                    for dcl in range(4):
                        dc = g * 4 + dcl
                        self.t_mm(bk.ap[:, dcl * 128:(dcl + 1) * 128],
                                  x.ap[:, dc * 128:(dc + 1) * 128],
                                  self.id_f32[:, :],
                                  True, True, waits=w if dcl == 0 else ())
                    bk.note_w(self.pe)
                    x.note_r(self.pe)
                    self.ev("scalar", lambda e, q=bk.ap, g2=g, i=tci:
                            e.activation(
                                self.xr_t[:, g2 * 4:(g2 + 1) * 4,
                                          i * 128:(i + 1) * 128],
                                q[:, 0:512].rearrange("p (a b) -> p a b", a=4),
                                AF.Copy),
                            rd=[bk], wr=[self.xr])

        n = len(jobs)
        for t in range(n + 2):
            if t >= 2:
                s_norm_evac(t - 2)
            if t < n:
                s_dma(t)
            if 1 <= t <= n:
                s_stats(t - 1)
        self.barrier()

    def p_ln_fm(self):
        ARF, ARB = self.ARF, self.ARB
        ARF.reset(self.arf_scr)
        ARB.reset()
        xc = ARB.salloc([128, DC, SOWN])
        xq = ARB.salloc([128, DC, SOWN])
        mr_bf = ARB.salloc([1, 2 * SOWN])
        row_f = ARB.salloc([1, 2 * SOWN])
        msq = ARB.salloc([1, SOWN])
        tmp = ARF.salloc([128, 512])
        Q2, b2 = self.qpair(2)
        Q3, b3 = self.qpair(3)

        for dc in range(DC):
            self.ev("vector", lambda e, d=dc:
                    e.tensor_copy(xc.ap[:, d, :], self.xr_t[:, d, :]),
                    rd=[self.xr], wr=[xc])
            self.ev("scalar", lambda e, d=dc:
                    e.activation(xq.ap[:, d, :], xc.ap[:, d, :], AF.Square),
                    rd=[xc], wr=[xq])
        for Qrow, brow, srcs in ((Q2, b2, xc), (Q3, b3, xq)):
            for blk in range(2):
                bk = brow[blk]
                w = bk.wwait() + srcs.rwait()
                for dc in range(DC):
                    self.t_mm(Qrow[0:1, blk * 512:(blk + 1) * 512],
                              self.ones_bf[:, 0:1],
                              srcs.ap[:, dc, blk * 512:(blk + 1) * 512],
                              dc == 0, dc == DC - 1, waits=w if dc == 0 else ())
                bk.note_w(self.pe)
                srcs.note_r(self.pe)

        b2s = Slot(None)
        b2s.wt = b2[1].wt
        self.ev("vector", lambda e: e.tensor_scalar(
            row_f.ap[0:1, 0:SOWN], Q2[0:1, 0:1024], 1.0 / D, None, AL.mult),
            rd=[b2[0], b2[1]], wr=[row_f])
        self.ev("vector", lambda e: e.tensor_scalar(
            row_f.ap[0:1, SOWN:2 * SOWN], Q3[0:1, 0:1024], 1.0 / D, None, AL.mult),
            rd=[b3[0], b3[1]], wr=[row_f])
        self.ev("vector", lambda e: e.tensor_tensor(
            msq.ap[0:1, :], row_f.ap[0:1, 0:SOWN], row_f.ap[0:1, 0:SOWN], AL.mult),
            rd=[row_f], wr=[msq])
        self.ev("vector", lambda e: e.tensor_tensor(
            row_f.ap[0:1, SOWN:2 * SOWN], row_f.ap[0:1, SOWN:2 * SOWN],
            msq.ap[0:1, :], AL.subtract),
            rd=[msq, row_f], wr=[row_f])
        self.ev("vector", lambda e: e.tensor_scalar(
            row_f.ap[0:1, SOWN:2 * SOWN], row_f.ap[0:1, SOWN:2 * SOWN],
            LN_EPS, None, AL.add),
            rd=[row_f], wr=[row_f])
        self.ev("scalar", lambda e: e.activation(
            row_f.ap[0:1, SOWN:2 * SOWN], row_f.ap[0:1, SOWN:2 * SOWN], AF.Ln),
            rd=[row_f], wr=[row_f])
        self.ev("scalar", lambda e: e.activation(
            row_f.ap[0:1, SOWN:2 * SOWN], row_f.ap[0:1, SOWN:2 * SOWN],
            AF.Exp, scale=-0.5),
            rd=[row_f], wr=[row_f])
        self.ev("vector", lambda e: e.tensor_copy(mr_bf.ap[0:1, :], row_f.ap[0:1, :]),
                rd=[row_f], wr=[mr_bf])

        for blk in range(2):
            self.gmm(Q3[:, 0:512], [b3[0]],
                     [(self.ones_bf[0:1, :], mr_bf.ap[0:1, blk * 512:(blk + 1) * 512])],
                     rd_slots=[mr_bf])
            self.gmm(Q3[:, 512:1024], [b3[1]],
                     [(self.ones_bf[0:1, :],
                       mr_bf.ap[0:1, SOWN + blk * 512:SOWN + (blk + 1) * 512])],
                     rd_slots=[mr_bf])
            for dc in range(DC):
                self.ev("vector", lambda e, d=dc, b=blk: e.tensor_tensor(
                    tmp.ap[:, :], self.xr_t[:, d, b * 512:(b + 1) * 512],
                    Q3[:, 0:512], AL.subtract),
                    rd=[self.xr, b3[0]], wr=[tmp])
                self.ev("vector", lambda e, d=dc, b=blk: e.tensor_tensor(
                    self.xl_t[:, d, b * 512:(b + 1) * 512], tmp.ap[:, :],
                    Q3[:, 512:1024], AL.mult),
                    rd=[tmp, b3[1]], wr=[self.xl])
        self.barrier()

    def p_attn_a(self):
        ARB, ARF = self.ARB, self.ARF
        DT = self.DT
        ARB.reset(self.arb_m0)
        bas_nat = ARB.salloc([64, D])
        basisT = {nm: ARB.salloc([128, DC, HARM]) for nm in ("q", "k", "v", "o")}
        wT = {nm: ARB.salloc([64, D]) for nm in ("q", "k", "v", "o")}
        res_q = ARB.salloc([64, SOWN])
        res_k = ARB.salloc([64, SEQ])
        res_v = ARB.salloc([64, SEQ])
        reso_sb = ARB.salloc([64, SOWN])
        dnr_bf = ARB.salloc([NH, SOWN])
        ARF.reset(self.arf_scr)

        HS = Arena(self.arb_t, self.xp_base, DC * SOWN)
        mcell, mneg, dn = self.mcell, self.mneg, self.dn
        xlkeep = ARB.salloc([128, SOWN])
        self.ev("vector", lambda e: e.tensor_copy(xlkeep.ap[:, :], self.xl_t[:, 0, :]),
                rd=[self.xl], wr=[xlkeep])
        self.xlkeep = xlkeep

        amT = ARB.salloc([64, D])
        for nm in ("q", "k", "v", "o"):
            wt = wT[nm]
            self.dma("sync", self.dsy, lambda e, s=self.pha_bf[nm], d=wt.ap:
                     e.dma_start(out=d[:, :], in_=s[:, :]),
                     wr=[wt], extra_waits=[(self.dgp, self.t_prep_pa)])
            self.dma("scalar", self.dsc, lambda e, s=self.amp_bf[nm], d=amT.ap:
                     e.dma_start(out=d[:, :], in_=s[:, :]),
                     wr=[amT], extra_waits=[(self.dgp, self.t_prep_pa)])
            self.ev("scalar", lambda e, d=wt.ap:
                    e.activation(d[:, :], d[:, :], AF.Sin,
                                 bias=self.cvec[0:64, 0:1]),
                    rd=[wt], wr=[wt],
                    extra_waits=self.wc if nm == "q" else ())
            self.ev("vector", lambda e, d=wt.ap, a=amT.ap:
                    e.tensor_tensor(d[:, :], d[:, :], a[:, :], AL.mult),
                    rd=[wt, amT], wr=[wt])
            self.dma("sync", self.dsy, lambda e, s=self.bas_bf[nm]:
                     e.dma_start(out=bas_nat.ap[:, :], in_=s[:, :]), wr=[bas_nat],
                     extra_waits=[(self.dgp, self.t_prep_bas)])
            for dc in range(DC):
                bk = self.bank[4 + dc % 2]
                self.gmm(bk.ap[:, 0:64], [bk],
                         [(bas_nat.ap[:, dc * 128:(dc + 1) * 128],
                           self.id_bf[0:64, 0:64])],
                         rd_slots=[bas_nat])
                self.ev("vector", lambda e, q=bk.ap, d=dc, t=basisT[nm].ap:
                        e.tensor_copy(t[:, d, :], q[:, 0:64]),
                        rd=[bk], wr=[basisT[nm]])

        def resonance(bt, src, n_tok, base, dst):
            for blk in range(n_tok // 512):
                bk = self.bank[6 + blk % 2]
                self.gmm(bk.ap[0:64, :], [bk],
                         [(bt.ap[:, dc, :], src.ap[:, dc, blk * 512:(blk + 1) * 512])
                          for dc in range(DC)],
                         rd_slots=[bt, src])
                self.ev("vector", lambda e, q=bk.ap, b=blk, d=dst.ap, bc=base:
                        e.tensor_copy(d[:, bc + b * 512:bc + (b + 1) * 512], q[0:64, :]),
                        rd=[bk], wr=[dst])
        resonance(basisT["q"], self.xl, SOWN, 0, res_q)
        resonance(basisT["k"], self.xl, SOWN, 0, res_k)
        resonance(basisT["k"], self.xp_ln, SOWN, SOWN, res_k)
        resonance(basisT["v"], self.xl, SOWN, 0, res_v)
        resonance(basisT["v"], self.xp_ln, SOWN, SOWN, res_v)

        kh = HS.salloc([128, SEQ])
        qh = HS.salloc([128, SOWN])
        vfm = HS.salloc([128, SEQ])
        vtm = HS.salloc([128, DC, 128])
        q2 = HS.salloc([128, SOWN])
        k2 = HS.salloc([128, SEQ])
        ptb = [HS.salloc([128, SOWN]) for _ in range(2)]
        brow = HS.salloc([NH, SOWN])
        qn2 = ARF.salloc([NH, SOWN])
        y_raw = self.xl

        for h in range(NH):
            def expand(dst, wt_s, res_s, ncol, qj, eng):
                Qx, bx = self.qpair(qj)
                for blk in range(ncol // 512):
                    bk = bx[blk % 2]
                    self.gmm(Qx[:, (blk % 2) * 512:(blk % 2) * 512 + 512], [bk],
                             [(wt_s.ap[:, h * 128:(h + 1) * 128],
                               res_s.ap[:, blk * 512:(blk + 1) * 512])],
                             rd_slots=[wt_s, res_s])
                    if eng == "scalar":
                        self.ev("scalar", lambda e, q=bk.ap, b=blk, d=dst.ap:
                                e.activation(d[:, b * 512:(b + 1) * 512], q, AF.Copy),
                                rd=[bk], wr=[dst])
                    else:
                        self.ev("vector", lambda e, q=bk.ap, b=blk, d=dst.ap:
                                e.tensor_copy(d[:, b * 512:(b + 1) * 512], q),
                                rd=[bk], wr=[dst])
            expand(qh, wT["q"], res_q, SOWN, 0, "scalar")
            expand(kh, wT["k"], res_k, SEQ, 1, "scalar")
            expand(vfm, wT["v"], res_v, SEQ, 0, "vector")
            for kc in range(DC):
                bk = self.bank[2 + kc % 2]
                self.gmm(bk.ap[:, 0:128], [bk],
                         [(vfm.ap[:, kc * 128:(kc + 1) * 128], self.id_bf[:, :])],
                         rd_slots=[vfm])
                self.ev("vector", lambda e, q=bk.ap, c=kc:
                        e.tensor_copy(vtm.ap[:, c, :], q[:, 0:128]),
                        rd=[bk], wr=[vtm])
            self.ev("scalar", lambda e: e.activation(q2.ap[:, :], qh.ap[:, :], AF.Square),
                    rd=[qh], wr=[q2])
            self.ev("scalar", lambda e: e.activation(k2.ap[:, :], kh.ap[:, :], AF.Square),
                    rd=[kh], wr=[k2])
            for blk in range(4):
                bk = self.bank[4]
                self.gmm(bk.ap[0:NH, :], [bk],
                         [(self.ones_bf[:, 0:NH],
                           k2.ap[:, blk * 512:(blk + 1) * 512])],
                         rd_slots=[k2])
                self.ev("vector", lambda e, b=blk, q=bk.ap:
                        e.reduce_max(mcell.ap[0:NH, b:b + 1],
                                     q[0:NH, 0:512], axis=AX.X),
                        rd=[bk], wr=[mcell])
            self.ev("vector", lambda e: e.reduce_max(
                mcell.ap[0:NH, 8:9], mcell.ap[0:NH, 0:4], axis=AX.X),
                rd=[mcell], wr=[mcell])
            self.ev("vector", lambda e: e.tensor_scalar(
                mcell.ap[0:NH, 9:10], mcell.ap[0:NH, 8:9], SM_SCALE * SM_SCALE,
                None, AL.mult),
                rd=[mcell], wr=[mcell])
            for blk in range(2):
                bk = self.bank[4]
                self.gmm(bk.ap[0:NH, :], [bk],
                         [(self.ones_bf[:, 0:NH],
                           q2.ap[:, blk * 512:(blk + 1) * 512])],
                         rd_slots=[q2])
                self.ev("vector", lambda e, b=blk, q=bk.ap: e.tensor_copy(
                    qn2.ap[0:NH, b * 512:(b + 1) * 512], q[0:NH, 0:512]),
                    rd=[bk], wr=[qn2])
            self.ev("vector", lambda e: e.tensor_scalar(
                qn2.ap[0:NH, :], qn2.ap[0:NH, :], mcell.ap[0:NH, 9:10],
                None, AL.mult),
                rd=[qn2, mcell], wr=[qn2])
            self.ev("scalar", lambda e: e.activation(
                qn2.ap[0:NH, :], qn2.ap[0:NH, :], AF.Ln),
                rd=[qn2], wr=[qn2])
            self.ev("scalar", lambda e: e.activation(
                qn2.ap[0:NH, :], qn2.ap[0:NH, :], AF.Exp, scale=0.5),
                rd=[qn2], wr=[qn2])
            self.ev("vector", lambda e: e.tensor_scalar(
                qn2.ap[0:NH, :], qn2.ap[0:NH, :], -1.0, None, AL.mult),
                rd=[qn2], wr=[qn2])
            self.ev("vector", lambda e: e.tensor_scalar(
                qn2.ap[0:NH, :], qn2.ap[0:NH, :], 78.0, None, AL.add),
                rd=[qn2], wr=[qn2])
            self.ev("vector", lambda e: e.tensor_scalar(
                brow.ap[0:NH, :], qn2.ap[0:NH, :], 1.0 / SM_SCALE, None, AL.mult),
                rd=[qn2], wr=[brow])

            av_b = [self.bank[4], self.bank[5]]
            dn_b = self.bank[6]
            for kc in range(DC):
                qj = kc % 2
                Qx, bx = self.qpair(qj)
                sw = (bx[0].wwait() + bx[1].wwait() + kh.rwait() + qh.rwait() +
                      brow.rwait())
                for qb in range(2):
                    self.t_mm(Qx[:, qb * 512:(qb + 1) * 512],
                              kh.ap[:, kc * 128:(kc + 1) * 128],
                              qh.ap[:, qb * 512:(qb + 1) * 512],
                              True, False, waits=sw if qb == 0 else ())
                    self.t_mm(Qx[:, qb * 512:(qb + 1) * 512],
                              self.ones_bf[0:1, :],
                              brow.ap[0:1, qb * 512:(qb + 1) * 512],
                              False, True)
                bx[0].note_w(self.pe)
                bx[1].note_w(self.pe)
                kh.note_r(self.pe)
                qh.note_r(self.pe)
                brow.note_r(self.pe)
                pt = ptb[kc % 2]
                self.ev("scalar", lambda e, s=Qx, p=pt.ap:
                        e.activation(p[:, :], s[:, 0:1024], AF.Exp, scale=SM_SCALE),
                        rd=[bx[0], bx[1]], wr=[pt])
                w_av = (av_b[0].wwait() + av_b[1].wwait() +
                        pt.rwait() + vtm.rwait()) if kc == 0 else pt.rwait()
                if h == 0 and kc == 0:
                    w_av = w_av + dn_b.wwait() + self.bank[7].wwait()
                for qb in range(2):
                    self.t_mm(av_b[qb].ap, vtm.ap[:, kc, :],
                              pt.ap[:, qb * 512:(qb + 1) * 512], kc == 0, kc == DC - 1,
                              waits=w_av if qb == 0 else ())
                    self.t_mm(self.Qt[3][0:NH, qb * 512:(qb + 1) * 512],
                              self.csel[:, h * NH:(h + 1) * NH],
                              pt.ap[:, qb * 512:(qb + 1) * 512],
                              h == 0 and kc == 0,
                              h == NH - 1 and kc == DC - 1)
                pt.note_r(self.pe)
            av_b[0].note_w(self.pe)
            av_b[1].note_w(self.pe)
            vtm.note_r(self.pe)
            for qb in range(2):
                self.ev("vector", lambda e, b=qb, hh=h: e.tensor_copy(
                    y_raw.ap[:, hh, b * 512:(b + 1) * 512], av_b[b].ap),
                    rd=[av_b[qb]], wr=[y_raw])

        dn_b.note_w(self.pe)
        self.bank[7].note_w(self.pe)
        self.ev("vector", lambda e:
                e.tensor_copy(dn.ap[0:NH, :], self.Qt[3][0:NH, 0:1024]),
                rd=[dn_b, self.bank[7]], wr=[dn])
        self.ev("vector", lambda e: e.reciprocal(dn.ap[:, :], dn.ap[:, :]),
                rd=[dn], wr=[dn])
        self.ev("vector", lambda e: e.tensor_copy(dnr_bf.ap[:, :], dn.ap[:, :]),
                rd=[dn], wr=[dnr_bf])

        ro_b = [self.bank[2], self.bank[3]]
        w_ro = ro_b[0].wwait() + ro_b[1].wwait()
        for h in range(NH):
            for qb in range(2):
                bk = self.bank[0]
                self.gmm(bk.ap, [bk],
                         [(self.hsel[:, h * 128:(h + 1) * 128],
                           dnr_bf.ap[0:NH, qb * 512:(qb + 1) * 512])],
                         rd_slots=[dnr_bf])
                self.ev("vector", lambda e, q=bk.ap, hh=h, b=qb: e.tensor_tensor(
                    y_raw.ap[:, hh, b * 512:(b + 1) * 512],
                    y_raw.ap[:, hh, b * 512:(b + 1) * 512], q, AL.mult),
                    rd=[bk, y_raw], wr=[y_raw])
                self.t_mm(self.Qt[1][0:64, qb * 512:(qb + 1) * 512],
                          basisT["o"].ap[:, h, :],
                          y_raw.ap[:, h, qb * 512:(qb + 1) * 512],
                          h == 0, h == NH - 1,
                          waits=(w_ro + y_raw.rwait()) if h == 0 else y_raw.rwait())
        ro_b[0].note_w(self.pe)
        ro_b[1].note_w(self.pe)
        y_raw.note_r(self.pe)
        for qb in range(2):
            self.ev("vector", lambda e, b=qb: e.tensor_copy(
                reso_sb.ap[:, b * 512:(b + 1) * 512],
                self.Qt[1][0:64, b * 512:(b + 1) * 512]),
                rd=[ro_b[qb]], wr=[reso_sb])

        for oc in range(OCN):
            for qb in range(2):
                bk = self.bank[4 + (oc * 2 + qb) % 2]
                self.gmm(bk.ap, [bk],
                         [(wT["o"].ap[:, oc * 128:(oc + 1) * 128],
                           reso_sb.ap[:, qb * 512:(qb + 1) * 512])],
                         rd_slots=[reso_sb])
                self.ev("vector", lambda e, q=bk.ap, o=oc, b=qb: e.tensor_tensor(
                    self.xr_t[:, o, b * 512:(b + 1) * 512], q,
                    self.xr_t[:, o, b * 512:(b + 1) * 512], AL.add),
                    rd=[bk, self.xr], wr=[self.xr])
        self.dbg = {
            "dn": (dn, 0, 0),
            "mcell": (mcell, 16, 0),
            "mneg": (mneg, 32, 0),
            "yraw0": (Slot(y_raw.ap[:, 0, :]), 160, 0),
            "resq": (res_q, 288, 0),
            "resk": (res_k, 352, 0),
            "qh": (qh, 416, 0),
            "kh": (kh, 544, 0),
            "wtq": (wT["q"], 672, 0),
            "dnrbf": (dnr_bf, 736, 0),
            "reso": (reso_sb, 752, 0),
            "basisTq": (Slot(basisT["q"].ap[:, 0, :]), 816, 0),
            "xl0": (self.xlkeep, 880, 0),
        }
        self.barrier()

    def p_dump(self):
        for nm, (slot, row, col) in self.dbg.items():
            ap = slot.ap
            np_, nf = ap.shape[0], 1
            sh = list(ap.shape)
            nf = 1
            for s_ in sh[1:]:
                nf *= s_
            self.dma("gpsimd", self.dgp, lambda e, a=ap, r=row, c2=col, p=sh[0], f=nf:
                     e.dma_start(out=self.y_out[r:r + p, c2:c2 + f], in_=a),
                     rd=[slot])
        self.emit("gpsimd", lambda e: e.nop(), waits=[(self.dgp, self.dgp.n)])

    def p_ffn(self):
        ARB, ARF = self.ARB, self.ARF
        DT = self.DT
        ARB.reset()
        ARF.reset(self.arf_scr)
        srep = ARB.salloc([128, E, 512])
        slab = [ARB.salloc([128, 1024]) for _ in range(4)]
        husk = ARB.salloc([128, 512])
        sco = ARB.salloc([E, SOWN])
        arb_gate = ARB.mark()
        t1 = ARB.salloc([E, SOWN])
        stb = ARB.salloc([128, E])
        wg1n = ARB.salloc([E, D])
        lg_f = ARF.salloc([E, SOWN])
        lgt = ARF.salloc([128, E])
        gsc = ARF.salloc([128, 8])
        bank = self.bank

        self.dma("gpsimd", self.dgp, lambda e:
                 e.dma_start(out=wg1n.ap[:, :], in_=DT["Wg1"][:, :]), wr=[wg1n])
        for dc in range(DC):
            bk = bank[0]
            self.gmm(bk.ap[:, 0:8], [bk],
                     [(wg1n.ap[:, dc * 128:(dc + 1) * 128], self.id_bf[0:8, 0:8])],
                     rd_slots=[wg1n])
            self.ev("vector", lambda e, q=bk.ap, d=dc:
                    e.tensor_copy(self.wg1T_sb[:, d, :], q[:, 0:8]),
                    rd=[bk], wr=[self.wg1T])
        self.dma("gpsimd", self.dgp, lambda e:
                 e.dma_start(out=stb.ap[0:8, 0:8], in_=DT["Wg2"][:, :]), wr=[stb])
        bk = bank[0]
        self.gmm(bk.ap[0:8, 0:8], [bk], [(stb.ap[0:8, 0:8], self.id_bf[0:8, 0:8])],
                 rd_slots=[stb])
        self.ev("vector", lambda e, q=bk.ap: e.tensor_copy(self.wg2T_sb[:, :],
                                                           q[0:8, 0:8]),
                rd=[bk], wr=[self.wg2T])

        for b in range(2):
            bk = bank[2 + b]
            w = bk.wwait() + self.wg1T.rwait() + self.xl.rwait()
            for dc in range(DC):
                self.t_mm(self.Qt[1][0:E, b * 512:(b + 1) * 512],
                          self.wg1T_sb[:, dc, :],
                          self.xl_t[:, dc, b * 512:(b + 1) * 512],
                          dc == 0, dc == DC - 1, waits=w if dc == 0 else ())
            bk.note_w(self.pe)
        self.wg1T.note_r(self.pe)
        self.xl.note_r(self.pe)
        self.ev("scalar", lambda e: e.activation(t1.ap[:, :], self.Qt[1][0:E, 0:1024],
                                                 AF.Tanh),
                rd=[bank[2], bank[3]], wr=[t1])
        for b in range(2):
            bk = bank[2 + b]
            self.gmm(self.Qt[1][0:E, b * 512:(b + 1) * 512], [bk],
                     [(self.wg2T_sb[:, :], t1.ap[:, b * 512:(b + 1) * 512])],
                     rd_slots=[self.wg2T, t1])
        self.ev("vector", lambda e: e.tensor_copy(lg_f.ap[:, :], self.Qt[1][0:E, 0:1024]),
                rd=[bank[2], bank[3]], wr=[lg_f])

        for tci in range(TC):
            bk = bank[0]
            self.gmm(bk.ap[:, 0:8], [bk],
                     [(lg_f.ap[:, tci * 128:(tci + 1) * 128], self.id_f32[0:8, 0:8])],
                     rd_slots=[lg_f])
            self.ev("vector", lambda e, q=bk.ap: e.tensor_copy(lgt.ap[:, :], q[:, 0:8]),
                    rd=[bk], wr=[lgt])
            self.ev("vector", lambda e: e.reduce_max(gsc.ap[:, 0:1], lgt.ap[:, :],
                                                     axis=AX.X),
                    rd=[lgt], wr=[gsc])
            self.ev("vector", lambda e: e.tensor_scalar(
                stb.ap[:, 0:E], lgt.ap[:, :], gsc.ap[:, 0:1], None, AL.is_ge),
                rd=[lgt, gsc], wr=[stb])
            self.ev("vector", lambda e: e.scalar_tensor_tensor(
                lgt.ap[:, :], stb.ap[:, 0:E], -1e30, lgt.ap[:, :], AL.mult, AL.add),
                rd=[stb, lgt], wr=[lgt])
            self.ev("vector", lambda e: e.reduce_max(gsc.ap[:, 1:2], lgt.ap[:, :],
                                                     axis=AX.X),
                    rd=[lgt], wr=[gsc])
            self.ev("vector", lambda e: e.tensor_scalar(
                lgt.ap[:, :], lgt.ap[:, :], gsc.ap[:, 1:2], None, AL.is_ge),
                rd=[lgt, gsc], wr=[lgt])
            self.ev("vector", lambda e: e.tensor_tensor(
                gsc.ap[:, 2:3], gsc.ap[:, 0:1], gsc.ap[:, 1:2], AL.subtract),
                rd=[gsc], wr=[gsc])
            self.ev("scalar", lambda e: e.activation(gsc.ap[:, 2:3], gsc.ap[:, 2:3],
                                                     AF.Sigmoid),
                    rd=[gsc], wr=[gsc])
            self.ev("vector", lambda e: e.tensor_scalar(
                gsc.ap[:, 3:4], gsc.ap[:, 2:3], -1.0, None, AL.mult),
                rd=[gsc], wr=[gsc])
            self.ev("vector", lambda e: e.tensor_scalar(
                gsc.ap[:, 3:4], gsc.ap[:, 3:4], 1.0, None, AL.add),
                rd=[gsc], wr=[gsc])
            self.ev("vector", lambda e: e.tensor_scalar(
                stb.ap[:, 0:E], stb.ap[:, 0:E], gsc.ap[:, 2:3], None, AL.mult),
                rd=[stb, gsc], wr=[stb])
            self.ev("vector", lambda e: e.scalar_tensor_tensor(
                stb.ap[:, 0:E], lgt.ap[:, :], gsc.ap[:, 3:4], stb.ap[:, 0:E],
                AL.mult, AL.add),
                rd=[lgt, gsc, stb], wr=[stb])
            bk = bank[1]
            self.gmm(bk.ap[0:8, 0:128], [bk], [(stb.ap[:, 0:E], self.id_bf[:, :])],
                     rd_slots=[stb])
            self.ev("vector", lambda e, q=bk.ap, i=tci:
                    e.tensor_copy(sco.ap[:, i * 128:(i + 1) * 128], q[0:8, 0:128]),
                    rd=[bk], wr=[sco])

        self.barrier()
        ARB.reset(arb_gate)
        hbuf = ARB.salloc([128, E, HCN, 512])
        for tt in range(2):
            for e_ in range(E):
                bk = bank[7]
                self.gmm(bk.ap, [bk],
                         [(self.hsel[0:8, e_ * 128:(e_ + 1) * 128],
                           sco.ap[0:8, tt * 512:(tt + 1) * 512])],
                         rd_slots=[sco])
                self.ev("vector", lambda e, q=bk.ap, i=e_:
                        e.tensor_copy(srep.ap[:, i, :], q),
                        rd=[bk], wr=[srep])
            for e_ in range(E):
                for dc in range(DC):
                    si = (e_ * DC + dc) % 4
                    sla = slab[si]
                    qeng, qsem = (("sync", self.dsy),
                                  ("scalar", self.dsc))[si % 2]
                    self.dma(qeng, qsem, lambda e, i=e_, d=dc, s=sla:
                             e.dma_start(out=s.ap[:, :],
                                         in_=self.wup_bf[i][:, d * 128:(d + 1) * 128],
                                         transpose=True),
                             wr=[sla], extra_waits=[(self.dgp, self.t_prep_up)])
                    for hc in range(HCN):
                        bk = bank[hc]
                        w = (bk.wwait() + sla.rwait() + self.xl.rwait()) if dc == 0 \
                            else (sla.rwait() if hc == 0 else ())
                        self.t_mm(bk.ap, sla.ap[:, hc * 128:(hc + 1) * 128],
                                  self.xl_t[:, dc, tt * 512:(tt + 1) * 512],
                                  dc == 0, dc == DC - 1, waits=w)
                        if dc == DC - 1:
                            bk.note_w(self.pe)
                    sla.note_r(self.pe)
                self.xl.note_r(self.pe)
                for hc in range(HCN):
                    bk = bank[hc]
                    self.ev("scalar", lambda e, q=bk.ap, i=e_, c=hc: e.activation(
                        husk.ap[:, :], q, AF.Silu,
                        bias=self.bup_sb[:, i * HCN + c:i * HCN + c + 1]),
                        rd=[bk], wr=[husk])
                    self.ev("vector", lambda e, i=e_, c=hc: e.tensor_tensor(
                        hbuf.ap[:, i, c, :], husk.ap[:, :], srep.ap[:, i, :], AL.mult),
                        rd=[husk, srep], wr=[hbuf])
            for og in range(2):
                for oc2 in range(8):
                    oc = og * 8 + oc2
                    bk = bank[oc2]
                    self.t_mm(bk.ap, self.bdown_sb[0:8, oc * 128:(oc + 1) * 128],
                              sco.ap[:, tt * 512:(tt + 1) * 512], True, False,
                              waits=bk.wwait() + sco.rwait() +
                              [(self.dgp, self.t_bdown)])
                sco.note_r(self.pe)
                for e_ in range(E):
                    for hc in range(HCN):
                        si = (e_ * HCN + hc) % 4
                        sla = slab[si]
                        qeng, qsem = (("sync", self.dsy),
                                      ("scalar", self.dsc))[si % 2]
                        self.dma(qeng, qsem, lambda e, i=e_, c=hc, g=og, s=sla:
                                 e.dma_start(
                                     out=s.ap[:, :],
                                     in_=self.wdown_bf[i][g * 1024:(g + 1) * 1024,
                                                          c * 128:(c + 1) * 128],
                                     transpose=True),
                                 wr=[sla], extra_waits=[(self.dgp, self.t_prep_down)])
                        last = (e_ == E - 1 and hc == HCN - 1)
                        for oc2 in range(8):
                            self.t_mm(bank[oc2].ap, sla.ap[:, oc2 * 128:(oc2 + 1) * 128],
                                      hbuf.ap[:, e_, hc, :], False, last,
                                      waits=(sla.rwait() + hbuf.rwait())
                                      if oc2 == 0 else ())
                        sla.note_r(self.pe)
                hbuf.note_r(self.pe)
                for oc2 in range(8):
                    bank[oc2].note_w(self.pe)
                    oc = og * 8 + oc2
                    self.ev("vector", lambda e, q=bank[oc2].ap, o=oc, b=tt:
                            e.tensor_tensor(
                                self.xr_t[:, o, b * 512:(b + 1) * 512], q,
                                self.xr_t[:, o, b * 512:(b + 1) * 512], AL.add),
                            rd=[bank[oc2], self.xr], wr=[self.xr])
        self.dbg = {
            "sco": (sco, 0, 0),
            "lgf": (lg_f, 8, 0),
            "hbuf00": (Slot(hbuf.ap[:, 0, 0, :]), 16, 0),
            "xl0": (Slot(self.xl_t[:, 0, :]), 144, 0),
            "srep0": (Slot(srep.ap[:, 0, :]), 272, 0),
        }
        self.barrier()

    def p_attn_b(self):
        ARB, ARF = self.ARB, self.ARF
        ARB.reset()
        ARF.reset(self.arf_scr)
        phq = ARB.salloc([128, NH, SOWN])
        phk = ARB.salloc([128, NH, SOWN])
        arb_m1 = ARB.mark()
        PA = Arena(self.arb_t, arb_m1, self.ARB_W - arb_m1)
        pslab = [PA.salloc([128, 512]) for _ in range(2)]
        cgb = PA.salloc([NH, SOWN])
        cbf = PA.salloc([NH, SOWN])
        zrow = PA.salloc([NH, 128])
        pa_perm = PA.mark()
        exf = PA.salloc([128, 512])
        rbuf = [PA.salloc([128, SOWN]) for _ in range(2)]
        swb = PA.salloc([128, 512])
        ta = PA.salloc([128, 512])
        rot = ta
        beta = ARF.salloc([NH, SOWN])
        ctm = ARF.salloc([128, TC * NH])
        rctm = ARF.salloc([128, TC * NH])
        gcol = ARF.salloc([128, 1])
        oex = ARF.salloc([NH, TC])
        svf = ARF.salloc([128, 128])
        bank = self.bank
        dn = self.dn
        ynum = phk

        bkg = [bank[0], bank[1]]
        for b in range(2):
            w = bkg[b].wwait() + self.wg_s.rwait() + self.xl.rwait()
            for dc in range(DC):
                self.t_mm(self.Qt[0][0:NH, b * 512:(b + 1) * 512],
                          self.wg_sb[:, dc, :],
                          self.xl_t[:, dc, b * 512:(b + 1) * 512], dc == 0, dc == DC - 1,
                          waits=w if dc == 0 else ())
            bkg[b].note_w(self.pe)
        self.wg_s.note_r(self.pe)
        self.ev("scalar", lambda e: e.activation(beta.ap[:, :], self.Qt[0][0:NH, 0:1024],
                                                 AF.Sigmoid, bias=self.bg_sb[:, 0:1]),
                rd=[bkg[0], bkg[1]], wr=[beta], extra_waits=self.wc)
        self.ev("vector", lambda e: e.tensor_scalar(
            beta.ap[:, :], beta.ap[:, :], BETA_MAX, None, AL.min),
            rd=[beta], wr=[beta])
        self.ev("vector", lambda e: e.tensor_scalar(
            beta.ap[:, :], beta.ap[:, :], BETA_MIN, None, AL.max),
            rd=[beta], wr=[beta])
        self.ev("vector", lambda e: e.memset(zrow.ap[:, :], 0.0), wr=[zrow])
        for c in range(TC):
            self.ev("vector", lambda e, i=c: e.tensor_tensor_scan(
                beta.ap[:, i * 128:(i + 1) * 128], beta.ap[:, i * 128:(i + 1) * 128],
                zrow.ap[:, :], 1.0, AL.mult, AL.add),
                rd=[beta, zrow], wr=[beta])
        cvar = beta
        self.ev("vector", lambda e: e.tensor_copy(cbf.ap[:, :], cvar.ap[:, :]),
                rd=[cvar], wr=[cbf])
        for c in range(TC):
            bk = bank[2]
            self.gmm(bk.ap[:, 0:NH], [bk],
                     [(cvar.ap[:, c * 128:(c + 1) * 128], self.id_f32[0:NH, 0:NH])],
                     rd_slots=[cvar])
            self.ev("vector", lambda e, q=bk.ap, i=c:
                    e.tensor_copy(ctm.ap[:, i * NH:(i + 1) * NH], q[:, 0:NH]),
                    rd=[bk], wr=[ctm])
        self.ev("vector", lambda e: e.reciprocal(rctm.ap[:, :], ctm.ap[:, :]),
                rd=[ctm], wr=[rctm])
        self.ev("vector", lambda e: e.tensor_tensor_scan(
            oex.ap[:, 0:TC], cvar.ap[:, 127::128], zrow.ap[:, 0:TC],
            1.0, AL.mult, AL.add),
            rd=[cvar, zrow], wr=[oex])
        self.ev("vector", lambda e: e.tensor_copy(exf.ap[0:NH, 0:TC - 1],
                                                  oex.ap[:, 0:TC - 1]),
                rd=[oex], wr=[exf])
        self.ev("vector", lambda e: e.tensor_copy(oex.ap[:, 1:TC],
                                                  exf.ap[0:NH, 0:TC - 1]),
                rd=[exf], wr=[oex])
        self.ev("vector", lambda e: e.memset(oex.ap[:, 0:1], 1.0), wr=[oex])
        for c in range(TC):
            self.ev("vector", lambda e, i=c: e.tensor_scalar(
                cgb.ap[:, i * 128:(i + 1) * 128], cvar.ap[:, i * 128:(i + 1) * 128],
                oex.ap[:, i:i + 1], None, AL.mult),
                rd=[cvar, oex], wr=[cgb])

        def proj_rope(wdram, phi):
            for og in range(4):
                for dc in range(DC):
                    sla = pslab[dc % 2]
                    self.dma("scalar", self.dsc, lambda e, g=og, d=dc, s=sla:
                             e.dma_start(out=s.ap[:, :],
                                         in_=wdram[d * 128:(d + 1) * 128,
                                                   g * 512:(g + 1) * 512]),
                             wr=[sla], extra_waits=[(self.dgp, self.t_prep_pj)])
                    for oc4 in range(4):
                        Qx, bx = self.qpair(oc4)
                        w = (bx[0].wwait() + bx[1].wwait() + sla.rwait() +
                             self.xl.rwait()) if dc == 0 \
                            else (sla.rwait() if oc4 == 0 else ())
                        for b in range(2):
                            self.t_mm(Qx[:, b * 512:(b + 1) * 512],
                                      sla.ap[:, oc4 * 128:(oc4 + 1) * 128],
                                      self.xl_t[:, dc, b * 512:(b + 1) * 512],
                                      dc == 0, dc == DC - 1,
                                      waits=w if b == 0 else ())
                        if dc == DC - 1:
                            bx[0].note_w(self.pe)
                            bx[1].note_w(self.pe)
                    sla.note_r(self.pe)
                self.xl.note_r(self.pe)
                for pair in range(2):
                    for i2 in range(2):
                        oc4 = pair * 2 + i2
                        Qx, bx = self.qpair(oc4)
                        self.ev("scalar", lambda e, s=Qx, r=rbuf[i2].ap:
                                e.activation(r[:, :], s[:, 0:1024], AF.Copy),
                                rd=[bx[0], bx[1]], wr=[rbuf[i2]])
                    for i2 in range(2):
                        oc4 = pair * 2 + i2
                        h = og * 4 + oc4
                        rb = rbuf[i2]
                        Qx, bx = self.qpair(i2)
                        for b in range(2):
                            self.t_mm(Qx[:, b * 512:(b + 1) * 512], self.swap_bf[:, :],
                                      rb.ap[:, b * 512:(b + 1) * 512], True, True,
                                      waits=(bx[0].wwait() + bx[1].wwait() + rb.rwait())
                                      if b == 0 else ())
                        bx[0].note_w(self.pe)
                        bx[1].note_w(self.pe)
                        rb.note_r(self.pe)
                        for b in range(2):
                            sl = b * 512
                            self.ev("vector", lambda e, s=Qx, o=sl: e.tensor_copy(
                                swb.ap[:, :], s[:, o:o + 512]),
                                rd=[bx[b]], wr=[swb])
                            self.ev("vector", lambda e, r=rb.ap, o=sl: e.tensor_tensor(
                                ta.ap[:, :], r[:, o:o + 512],
                                self.rope_cos[:, o:o + 512], AL.mult),
                                rd=[rb], wr=[ta], extra_waits=self.wc)
                            self.ev("vector", lambda e, o=sl: e.tensor_tensor(
                                swb.ap[:, :], swb.ap[:, :],
                                self.rope_sin[:, o:o + 512], AL.mult),
                                rd=[swb], wr=[swb])
                            self.ev("vector", lambda e: e.tensor_tensor(
                                rot.ap[:, :], ta.ap[:, :], swb.ap[:, :], AL.add),
                                rd=[swb], wr=[rot])
                            self.ev("scalar", lambda e: e.activation(
                                exf.ap[:, :], rot.ap[:, :], AF.Exp),
                                rd=[rot], wr=[exf])
                            self.ev("scalar", lambda e: e.activation(
                                rot.ap[:, :], rot.ap[:, :], AF.Relu),
                                rd=[rot], wr=[rot])
                            self.ev("vector", lambda e, hh=h, p=phi, o=sl:
                                    e.scalar_tensor_tensor(
                                        p.ap[:, hh, o:o + 512], rot.ap[:, :], 1.0,
                                        exf.ap[:, :], AL.add, AL.min),
                                    rd=[rot, exf], wr=[phi])
        proj_rope(self.wpj_bf["q"], phq)
        proj_rope(self.wpj_bf["k"], phk)

        self.barrier()
        PA.reset(pa_perm)
        Abf = PA.salloc([128, 128])
        ktb = PA.salloc([128, 128])
        vtb = PA.salloc([128, 128])
        qgb = PA.salloc([128, 128])
        sbf = PA.salloc([128, 128])
        zrep = PA.salloc([128, NH])
        dtf = ARF.salloc([128, 128])
        pa_scan = PA.mark()
        sendb = PA.salloc([128, NH, 129])
        vgrp = PA.salloc([128, 4, SOWN])
        p_crep, p_A = bank[0], bank[1]
        p_kT, p_Sc = bank[2], bank[3]
        p_nm, p_z = bank[4], bank[5]
        p_dn, p_vT = bank[6], bank[7]

        self.ev("vector", lambda e: e.memset(dn.ap[:, :], DN_EPS), wr=[dn])

        for vg in range(4):
            for dc in range(DC):
                sla = pslab[dc % 2]
                self.dma("scalar", self.dsc, lambda e, g=vg, d=dc, s=sla:
                         e.dma_start(out=s.ap[:, :],
                                     in_=self.wpj_bf["v"][d * 128:(d + 1) * 128,
                                                          g * 512:(g + 1) * 512]),
                         wr=[sla], extra_waits=[(self.dgp, self.t_prep_pj)])
                for oc4 in range(4):
                    Qx, bx = self.qpair(oc4)
                    w = (bx[0].wwait() + bx[1].wwait() + sla.rwait() +
                         self.xl.rwait()) if dc == 0 \
                        else (sla.rwait() if oc4 == 0 else ())
                    for b in range(2):
                        self.t_mm(Qx[:, b * 512:(b + 1) * 512],
                                  sla.ap[:, oc4 * 128:(oc4 + 1) * 128],
                                  self.xl_t[:, dc, b * 512:(b + 1) * 512],
                                  dc == 0, dc == DC - 1, waits=w if b == 0 else ())
                    if dc == DC - 1:
                        bx[0].note_w(self.pe)
                        bx[1].note_w(self.pe)
                sla.note_r(self.pe)
            for oc4 in range(4):
                Qx, bx = self.qpair(oc4)
                self.ev("scalar", lambda e, s=Qx, i=oc4:
                        e.activation(vgrp.ap[:, i, :], s[:, 0:1024], AF.Copy),
                        rd=[bx[0], bx[1]], wr=[vgrp])

            for hl in range(4):
                h = vg * 4 + hl
                self.ev("vector", lambda e: e.memset(svf.ap[:, :], 0.0), wr=[svf])
                self.ev("vector", lambda e: e.memset(sbf.ap[:, :], 0.0), wr=[sbf])
                self.ev("vector", lambda e: e.memset(zrep.ap[:, :], 0.0), wr=[zrep])
                for c in range(TC):
                    self.gmm(p_crep.ap[:, 0:128], [p_crep],
                             [(self.hsel[:, h * 128:(h + 1) * 128],
                               cbf.ap[0:NH, c * 128:(c + 1) * 128])],
                             rd_slots=[cbf])
                    self.ev("vector", lambda e: e.tensor_copy(
                        gcol.ap[:, 0:1], p_crep.ap[:, 127:128]),
                        rd=[p_crep], wr=[gcol])
                    self.ev("vector", lambda e, hh=h, cc=c: e.scalar_tensor_tensor(
                        dtf.ap[:, :], p_crep.ap[:, 0:128],
                        rctm.ap[:, cc * NH + hh:cc * NH + hh + 1],
                        self.tri_f32[:, :], AL.mult, AL.mult),
                        rd=[p_crep, rctm], wr=[dtf])
                    self.gmm(p_A.ap[:, 0:128], [p_A],
                             [(phk.ap[:, h, c * 128:(c + 1) * 128],
                               phq.ap[:, h, c * 128:(c + 1) * 128])],
                             rd_slots=[phk, phq])
                    self.ev("vector", lambda e: e.tensor_tensor(
                        Abf.ap[:, :], p_A.ap[:, 0:128], dtf.ap[:, :], AL.mult),
                        rd=[p_A, dtf], wr=[Abf])
                    self.gmm(p_kT.ap[:, 0:128], [p_kT],
                             [(phk.ap[:, h, c * 128:(c + 1) * 128], self.id_bf[:, :])],
                             rd_slots=[phk])
                    self.ev("vector", lambda e: e.tensor_scalar(
                        ktb.ap[:, :], p_kT.ap[:, 0:128], dtf.ap[:, 127:128],
                        None, AL.mult),
                        rd=[p_kT, dtf], wr=[ktb])
                    self.gmm(p_vT.ap[:, 0:128], [p_vT],
                             [(vgrp.ap[:, hl, c * 128:(c + 1) * 128], self.id_bf[:, :])],
                             rd_slots=[vgrp])
                    self.ev("vector", lambda e: e.tensor_copy(vtb.ap[:, :],
                                                              p_vT.ap[:, 0:128]),
                            rd=[p_vT], wr=[vtb])
                    self.ev("vector", lambda e, hh=h, cc=c: e.tensor_tensor(
                        qgb.ap[:, :], phq.ap[:, hh, cc * 128:(cc + 1) * 128],
                        p_crep.ap[:, 0:128], AL.mult),
                        rd=[phq, p_crep], wr=[qgb])
                    self.gmm(p_nm.ap[:, 0:128], [p_nm],
                             [(vtb.ap[:, :], Abf.ap[:, :]),
                              (sbf.ap[:, :], qgb.ap[:, :])],
                             rd_slots=[vtb, Abf, sbf, qgb])
                    self.ev("vector", lambda e, hh=h, cc=c: e.tensor_copy(
                        ynum.ap[:, hh, cc * 128:(cc + 1) * 128], p_nm.ap[:, 0:128]),
                        rd=[p_nm], wr=[ynum])
                    self.gmm(p_dn.ap[0:NH, 0:128], [p_dn],
                             [(self.csel[:, h * NH:(h + 1) * NH], Abf.ap[:, :]),
                              (zrep.ap[:, :], qgb.ap[:, :])],
                             rd_slots=[Abf, qgb, zrep])
                    self.ev("vector", lambda e, cc=c: e.tensor_tensor(
                        dn.ap[0:NH, cc * 128:(cc + 1) * 128],
                        dn.ap[0:NH, cc * 128:(cc + 1) * 128],
                        p_dn.ap[0:NH, 0:128], AL.add),
                        rd=[p_dn, dn], wr=[dn])
                    self.gmm(p_Sc.ap[:, 0:128], [p_Sc],
                             [(ktb.ap[:, :], vtb.ap[:, :])],
                             rd_slots=[ktb, vtb])
                    self.ev("vector", lambda e: e.scalar_tensor_tensor(
                        svf.ap[:, :], svf.ap[:, :], gcol.ap[:, 0:1],
                        p_Sc.ap[:, 0:128], AL.mult, AL.add),
                        rd=[p_Sc, gcol, svf], wr=[svf])
                    if c == TC - 1:
                        self.ev("vector", lambda e, hh=h: e.tensor_copy(
                            sendb.ap[:, hh, 0:128], svf.ap[:, :]),
                            rd=[svf], wr=[sendb])
                    else:
                        self.ev("vector", lambda e: e.tensor_copy(sbf.ap[:, :],
                                                                  svf.ap[:, :]),
                                rd=[svf], wr=[sbf])
                    self.gmm(p_z.ap[:, 0:NH], [p_z],
                             [(ktb.ap[:, :], self.ones_bf[:, 0:NH])],
                             rd_slots=[ktb])
                    self.ev("vector", lambda e, hh=h: e.scalar_tensor_tensor(
                        zrep.ap[:, hh:hh + 1], zrep.ap[:, hh:hh + 1],
                        gcol.ap[:, 0:1], p_z.ap[:, 0:1], AL.mult, AL.add),
                        rd=[p_z, gcol, zrep], wr=[zrep])
                self.ev("vector", lambda e, hh=h: e.tensor_copy(
                    sendb.ap[:, hh, 128:129], zrep.ap[:, hh:hh + 1]),
                    rd=[zrep], wr=[sendb])

        cc_in3 = self.cc_in.rearrange("p (a b) -> p a b", a=NH)
        self.dma("scalar", self.dsc, lambda e:
                 e.dma_start(out=cc_in3, in_=sendb.ap), rd=[sendb])
        t_ccin = self.dsc.n
        self.ccs.tick()
        self.emit("gpsimd", lambda e: e.collective_compute(
            "AllGather", AL.bypass, replica_groups=[list(range(NCORES))],
            ins=[self.cc_in.opt()], outs=[self.cc_out.opt()]),
            waits=[(self.dsc, t_ccin)], inc=(self.ccs, 1))
        t_cc = self.ccs.n

        PA.reset(pa_scan)
        sinb = PA.salloc([128, NH, 128])
        zin_c = PA.salloc([128, NH])
        rkb = PA.salloc([128, NH, 129])
        dnr = cbf
        self.ev("vector", lambda e: e.memset(sinb.ap[:, :, :], 0.0), wr=[sinb],
                extra_waits=[(self.dsc, self.dsc.n)] + sendb.rd)
        self.ev("vector", lambda e: e.memset(zin_c.ap[:, :], 0.0), wr=[zin_c])
        for r in range(NCORES):
            rb = rkb
            self.dma("sync", self.dsy, lambda e, i=r, d=rb:
                     e.dma_start(out=d.ap[:, :, :], in_=self.cc_out[i].rearrange(
                         "p (a b) -> p a b", a=NH)),
                     wr=[rb], extra_waits=[(self.ccs, t_cc)])
            self.ev("vector", lambda e, i=r, s=rb.ap: e.scalar_tensor_tensor(
                sinb.ap[:, :, :], s[:, :, 0:128], self.selmask[:, i:i + 1],
                sinb.ap[:, :, :], AL.mult, AL.add),
                rd=[rb, sinb], wr=[sinb])
            self.ev("vector", lambda e, i=r, s=rb.ap: e.scalar_tensor_tensor(
                zin_c.ap[:, :], s[:, :, 128], self.selmask[:, i:i + 1],
                zin_c.ap[:, :], AL.mult, AL.add),
                rd=[rb, zin_c], wr=[zin_c])

        zinsel = PA.salloc([128, NH])
        for h in range(NH):
            self.ev("vector", lambda e: e.memset(zinsel.ap[:, :], 0.0), wr=[zinsel])
            self.ev("vector", lambda e, hh=h: e.tensor_copy(
                zinsel.ap[:, hh:hh + 1], zin_c.ap[:, hh:hh + 1]),
                rd=[zin_c], wr=[zinsel])
            for c in range(TC):
                self.gmm(p_crep.ap[:, 0:128], [p_crep],
                         [(self.hsel[:, h * 128:(h + 1) * 128],
                           cgb.ap[0:NH, c * 128:(c + 1) * 128])],
                         rd_slots=[cgb])
                self.ev("vector", lambda e, hh=h, cc=c: e.tensor_tensor(
                    qgb.ap[:, :], phq.ap[:, hh, cc * 128:(cc + 1) * 128],
                    p_crep.ap[:, 0:128], AL.mult),
                    rd=[phq, p_crep], wr=[qgb])
                self.gmm(p_nm.ap[:, 0:128], [p_nm],
                         [(sinb.ap[:, h, :], qgb.ap[:, :])],
                         rd_slots=[sinb, qgb])
                self.ev("vector", lambda e, hh=h, cc=c: e.tensor_tensor(
                    ynum.ap[:, hh, cc * 128:(cc + 1) * 128],
                    ynum.ap[:, hh, cc * 128:(cc + 1) * 128],
                    p_nm.ap[:, 0:128], AL.add),
                    rd=[p_nm, ynum], wr=[ynum])
                self.gmm(p_dn.ap[0:NH, 0:128], [p_dn],
                         [(zinsel.ap[:, :], qgb.ap[:, :])],
                         rd_slots=[zinsel, qgb])
                self.ev("vector", lambda e, cc=c: e.tensor_tensor(
                    dn.ap[0:NH, cc * 128:(cc + 1) * 128],
                    dn.ap[0:NH, cc * 128:(cc + 1) * 128],
                    p_dn.ap[0:NH, 0:128], AL.add),
                    rd=[p_dn, dn], wr=[dn])

        self.ev("vector", lambda e: e.reciprocal(dn.ap[:, :], dn.ap[:, :]),
                rd=[dn], wr=[dn])
        self.ev("vector", lambda e: e.tensor_copy(dnr.ap[:, :], dn.ap[:, :]),
                rd=[dn], wr=[dnr])
        for h in range(NH):
            for b in range(2):
                bk = bank[0]
                self.gmm(bk.ap, [bk],
                         [(self.hsel[:, h * 128:(h + 1) * 128],
                           dnr.ap[0:NH, b * 512:(b + 1) * 512])],
                         rd_slots=[dnr])
                self.ev("vector", lambda e, hh=h, bb=b: e.tensor_tensor(
                    ynum.ap[:, hh, bb * 512:(bb + 1) * 512],
                    ynum.ap[:, hh, bb * 512:(bb + 1) * 512],
                    self.Qt[0][:, 0:512], AL.mult),
                    rd=[bk, ynum], wr=[ynum])
        for og in range(4):
            for dc in range(DC):
                sla = pslab[dc % 2]
                self.dma("scalar", self.dsc, lambda e, g=og, d=dc, s=sla:
                         e.dma_start(out=s.ap[:, :],
                                     in_=self.wpj_bf["o"][d * 128:(d + 1) * 128,
                                                          g * 512:(g + 1) * 512]),
                         wr=[sla], extra_waits=[(self.dgp, self.t_prep_pj)])
                for oc4 in range(4):
                    Qx, bx = self.qpair(oc4)
                    w = (bx[0].wwait() + bx[1].wwait() + sla.rwait() + ynum.rwait()) \
                        if dc == 0 else (sla.rwait() if oc4 == 0 else ())
                    for b in range(2):
                        self.t_mm(Qx[:, b * 512:(b + 1) * 512],
                                  sla.ap[:, oc4 * 128:(oc4 + 1) * 128],
                                  ynum.ap[:, dc, b * 512:(b + 1) * 512],
                                  dc == 0, dc == DC - 1, waits=w if b == 0 else ())
                    if dc == DC - 1:
                        bx[0].note_w(self.pe)
                        bx[1].note_w(self.pe)
                sla.note_r(self.pe)
            ynum.note_r(self.pe)
            for oc4 in range(4):
                Qx, bx = self.qpair(oc4)
                oc = og * 4 + oc4
                for b in range(2):
                    self.ev("vector", lambda e, s=Qx, o=oc, bb=b:
                            e.scalar_tensor_tensor(
                                self.xr_t[:, o, bb * 512:(bb + 1) * 512],
                                s[:, bb * 512:(bb + 1) * 512],
                                self.bo_sb[:, o:o + 1],
                                self.xr_t[:, o, bb * 512:(bb + 1) * 512],
                                AL.add, AL.add),
                            rd=[bx[b], self.xr], wr=[self.xr])
        self.dbg = {
            "beta_c": (beta, 0, 0),
            "cbf": (cbf, 16, 0),
            "cgb": (cgb, 32, 0),
            "dn": (dn, 48, 0),
            "phq0": (Slot(phq.ap[:, 0, :]), 64, 0),
            "phk0": (Slot(phk.ap[:, 0, :]), 192, 0),
            "ynum0": (Slot(ynum.ap[:, 0, :]), 320, 0),
            "sinb0": (Slot(sinb.ap[:, 0, :]), 448, 0),
            "sendb0": (Slot(sendb.ap[:, 0, :]), 576, 0),
            "ynum1": (Slot(ynum.ap[:, 4, :]), 704, 0),
            "ynum8": (Slot(ynum.ap[:, 12, :]), 832, 0),
        }
        self.barrier()

    def p_finish(self):
        ARF = self.ARF
        ARF.reset(self.arf_scr)
        ot = ARF.salloc([128, D])
        for tci in range(TC):
            for dc in range(DC):
                bk = self.bank[dc % 4]
                self.gmm(bk.ap[:, 0:128], [bk],
                         [(self.xr_t[:, dc, tci * 128:(tci + 1) * 128],
                           self.id_f32[:, :])],
                         rd_slots=[self.xr])
                self.ev("vector", lambda e, q=bk.ap, d=dc:
                        e.tensor_copy(ot.ap[:, d * 128:(d + 1) * 128], q[:, 0:128]),
                        rd=[bk], wr=[ot])
            self.dma("sync", self.outs, lambda e, i=tci:
                     e.dma_start(out=self.y_out[i * 128:(i + 1) * 128, :],
                                 in_=ot.ap[:, :]),
                     rd=[ot])
        self.emit("sync", lambda e: e.nop(), waits=[(self.outs, self.outs.n)])

    def playback(self, stk):
        block = stk.enter_context(self.nc.Block())

        @block.tensor
        def _(e):
            for f in self.ops["tensor"]:
                f(e)

        @block.vector
        def _(e):
            for f in self.ops["vector"]:
                f(e)

        @block.scalar
        def _(e):
            for f in self.ops["scalar"]:
                f(e)

        @block.gpsimd
        def _(e):
            for f in self.ops["gpsimd"]:
                f(e)

        @block.sync
        def _(e):
            for f in self.ops["sync"]:
                f(e)



def _make_consts(core_id):
    j = core_id % 2
    I = np.eye(128)
    ones = np.ones((128, 128))
    swap = np.zeros((128, 128))
    for m in range(128):
        swap[(m + 64) % 128, m] = 1.0
    tri = (np.arange(128)[:, None] <= np.arange(128)[None, :]).astype(np.float64)
    pos = np.arange(j * SOWN, (j + 1) * SOWN, dtype=np.float64)
    inv_freq = 1.0 / (10000.0 ** (np.arange(64, dtype=np.float64) / 64.0))
    fr = inv_freq[:, None] * pos[None, :]
    cos = np.cos(fr)
    sin = np.sin(fr)
    rope_cos = np.concatenate([cos, cos], axis=0)
    rope_sin = np.concatenate([-sin, sin], axis=0)
    cv = np.zeros((128, 8))
    cv[:, 0] = np.pi / 2
    cv[:, 1] = LN_EPS
    sel = np.zeros((128, 8))
    if j == 1:
        sel[:, core_id - 1] = 1.0
    hsel = np.zeros((16, 16 * 128))
    for h in range(16):
        hsel[h, h * 128:(h + 1) * 128] = 1.0
    csel = np.zeros((128, 16 * 16))
    for h in range(16):
        csel[:, h * 16 + h] = 1.0
    return {
        "id_bf": I.astype(bfloat16), "id_f32": I.astype(np.float32),
        "ones_bf": ones.astype(bfloat16), "ones_f32": ones.astype(np.float32),
        "swap_bf": swap.astype(bfloat16), "tri_f32": tri.astype(np.float32),
        "rope_cos": rope_cos.astype(bfloat16), "rope_sin": rope_sin.astype(bfloat16),
        "cvec": cv.astype(np.float32), "selmask": sel.astype(np.float32),
        "hsel_bf": hsel.astype(bfloat16), "csel_bf": csel.astype(bfloat16),
    }


_NC_CACHE = {}


def _get_nc(stop_after):
    if stop_after not in _NC_CACHE:
        _NC_CACHE[stop_after] = MK(stop_after).build()
    return _NC_CACHE[stop_after]


def run_dev(inputs, stop_after="full"):
    nc = _get_nc(stop_after)
    x = np.ascontiguousarray(np.asarray(inputs["L0_attn_a_hidden_states"], np.float32))
    g = lambda n: np.ascontiguousarray(np.asarray(inputs[n], np.float32))
    shared = {}
    for nm in ("q", "k", "v", "o"):
        shared[f"basis_{nm}"] = g(f"L0_attn_a_basis_{nm}")
        shared[f"phase_t_{nm}"] = np.ascontiguousarray(
            g(f"L0_attn_a_phase_{nm}").T)
        shared[f"amp_t_{nm}"] = np.ascontiguousarray(
            g(f"L0_attn_a_amp_{nm}").T)
    for nm in ("Wq", "Wk", "Wv", "Wg", "Wo"):
        shared[nm] = g(f"L0_attn_b_{nm}")
    shared["bg"] = g("L0_attn_b_bg").reshape(NH, 1)
    shared["bo_pk"] = np.ascontiguousarray(
        g("L0_attn_b_bo").reshape(OCN, 128).T)
    shared["Wg1"] = g("L0_ffn_Wg1")
    shared["Wg2"] = g("L0_ffn_Wg2")
    shared["Wup"] = g("L0_ffn_Wup")
    shared["bup_pk"] = np.ascontiguousarray(
        g("L0_ffn_bup").reshape(E, HCN, 128).transpose(2, 0, 1).reshape(128, E * HCN))
    shared["Wdown"] = g("L0_ffn_Wdown")
    shared["bdown"] = g("L0_ffn_bdown")

    in_maps = []
    for c in range(NCORES):
        b, j = c // 2, c % 2
        m = dict(shared)
        m["x_own"] = np.ascontiguousarray(x[b, j * SOWN:(j + 1) * SOWN])
        m["x_part"] = np.ascontiguousarray(x[b, (1 - j) * SOWN:(2 - j) * SOWN])
        m.update(_make_consts(c))
        in_maps.append(m)

    res = run_bass_kernel_spmd(nc, in_maps, core_ids=list(range(NCORES)))
    out = np.empty((4, SEQ, D), np.float32)
    for c in range(NCORES):
        b, j = c // 2, c % 2
        out[b, j * SOWN:(j + 1) * SOWN] = res.results[c]["y_out"]
    return out


def kernel(**inputs):
    return run_dev(inputs, stop_after="full")


# version 05 (final)
# speedup vs baseline: 1.2104x; 1.0134x over previous
import math
from contextlib import ExitStack

import numpy as np
import ml_dtypes

import concourse.bass as bass
import concourse.mybir as mybir
from concourse.bass_utils import run_bass_kernel_spmd

F32 = mybir.dt.float32
BF16 = mybir.dt.bfloat16
AF = mybir.ActivationFunctionType
AL = mybir.AluOpType
AX = mybir.AxisListType
bfloat16 = ml_dtypes.bfloat16

NCORES = 8
D = 2048
NH = 16
DH = 128
HARM = 64
SEQ = 2048
SOWN = 1024
TC = 8
DC = 16
E = 8
HE = 1024
HCN = 8
OCN = 16
SM_SCALE = 1.0 / math.sqrt(DH)
LN_EPS = 1e-5
DN_EPS = 1e-6
BETA_MIN, BETA_MAX = 0.8, 0.9995


class Sem:
    def __init__(self, handle):
        self.h = handle
        self.n = 0

    def tick(self, k=1):
        self.n += k
        return self.n


class Slot:

    def __init__(self, ap=None):
        self.ap = ap
        self.wt = None
        self.rd = []

    def wwait(self):
        w = list(self.rd)
        if self.wt is not None:
            w.append(self.wt)
        return w

    def rwait(self):
        return [self.wt] if self.wt is not None else []

    def note_w(self, sem):
        self.wt = (sem, sem.n)
        self.rd = []

    def note_r(self, sem):
        self.rd.append((sem, sem.n))


def _dedup(waits):
    best = {}
    for (s, v) in waits:
        if s is None:
            continue
        if s not in best or v > best[s]:
            best[s] = v
    return [(s, v) for s, v in best.items()]


class Arena:
    def __init__(self, t, base, width):
        self.t = t
        self.base = base
        self.width = width
        self.top = base

    def alloc(self, shape):
        n = 1
        for s in shape[1:]:
            n *= s
        off = self.top
        self.top += n
        assert self.top <= self.base + self.width, \
            f"arena overflow: {self.top} > {self.base + self.width}"
        ap = self.t[0:shape[0], off:off + n]
        if len(shape) == 3:
            ap = ap.rearrange("p (a b) -> p a b", a=shape[1])
        elif len(shape) == 4:
            ap = ap.rearrange("p (a b c) -> p a b c", a=shape[1], b=shape[2])
        return ap

    def salloc(self, shape):
        return Slot(self.alloc(shape))

    def mark(self):
        return self.top

    def reset(self, mark=None):
        self.top = self.base if mark is None else mark


class MK:
    def __init__(self, stop_after="full"):
        self.stop_after = stop_after
        self.nc = bass.Bass("TRN2", target_bir_lowering=False)
        self.ops = {k: [] for k in ("tensor", "vector", "scalar", "gpsimd", "sync")}

    def emit(self, eng, fn, waits=(), inc=None):
        waits = tuple((s.h, v) for (s, v) in _dedup(waits))

        def thunk(e, fn=fn, waits=waits, inc=inc):
            for (h, v) in waits:
                e.wait_ge(h, v)
            ins = fn(e)
            if inc is not None:
                ins.then_inc(inc[0].h, inc[1])

        self.ops[eng].append(thunk)

    def t_mm(self, dst, lhsT, rhs, start, stop, waits=()):
        self.pe.tick()
        self.emit("tensor",
                  lambda e, d=dst, l=lhsT, r=rhs, st=start, sp=stop:
                  e.matmul(d, l, r, start=st, stop=sp),
                  waits=waits, inc=(self.pe, 1))
        return self.pe.n

    def gmm(self, dst, banks, parts, rd_slots=(), extra_waits=()):
        w = list(extra_waits)
        for b in banks:
            w += b.wwait()
        for s in rd_slots:
            w += s.rwait()
        n = len(parts)
        for i, (l, r) in enumerate(parts):
            self.t_mm(dst, l, r, i == 0, i == n - 1, waits=w if i == 0 else ())
        for b in banks:
            b.note_w(self.pe)
        for s in rd_slots:
            s.note_r(self.pe)
        return self.pe.n

    def ev(self, eng, fn, rd=(), wr=(), extra_waits=()):
        sem = {"vector": self.ve, "scalar": self.se, "gpsimd": self.ge}[eng]
        w = list(extra_waits)
        for s in rd:
            w += s.rwait()
        for s in wr:
            w += s.wwait()
        sem.tick()
        self.emit(eng, fn, waits=w, inc=(sem, 1))
        for s in rd:
            s.note_r(sem)
        for s in wr:
            s.note_w(sem)
        return sem.n

    def dma(self, eng, qsem, fn, rd=(), wr=(), extra_waits=()):
        w = list(extra_waits) + list(getattr(self, "fence", ()))
        for s in rd:
            w += s.rwait()
        for s in wr:
            w += s.wwait()
        qsem.tick(16)
        self.emit(eng, fn, waits=w, inc=(qsem, 16))
        for s in rd:
            s.note_r(qsem)
        for s in wr:
            s.note_w(qsem)
        return qsem.n

    def barrier(self):
        pts = [(self.pe, self.pe.n), (self.ve, self.ve.n), (self.se, self.se.n),
               (self.ge, self.ge.n)]
        for eng, own in (("tensor", self.pe), ("vector", self.ve),
                         ("scalar", self.se), ("gpsimd", self.ge)):
            w = [(s, v) for (s, v) in pts if s is not own]
            self.emit(eng, lambda e: e.nop(), waits=w)
        self.fence = pts

    def build(self):
        nc = self.nc
        self.DT = {}

        def din(name, shape, dtype=F32):
            self.DT[name] = nc.dram_tensor(name, list(shape), dtype,
                                           kind="ExternalInput").ap()

        din("x_own", [SOWN, D]); din("x_part", [SOWN, D])
        for nm in ("q", "k", "v", "o"):
            din(f"basis_{nm}", [HARM, D])
            din(f"phase_t_{nm}", [HARM, D])
            din(f"amp_t_{nm}", [HARM, D])
        for nm in ("Wq", "Wk", "Wv", "Wo"):
            din(nm, [D, D])
        din("Wg", [D, NH]); din("bg", [NH, 1]); din("bo_pk", [128, OCN])
        din("Wg1", [E, D]); din("Wg2", [E, E])
        din("Wup", [E, HE, D]); din("bup_pk", [128, E * HCN])
        din("Wdown", [E, D, HE]); din("bdown", [8, D])
        din("id_bf", [128, 128], BF16); din("id_f32", [128, 128], F32)
        din("ones_bf", [128, 128], BF16); din("ones_f32", [128, 128], F32)
        din("swap_bf", [128, 128], BF16); din("tri_f32", [128, 128], F32)
        din("rope_cos", [128, SOWN], BF16); din("rope_sin", [128, SOWN], BF16)
        din("cvec", [128, 8], F32); din("selmask", [128, 8], F32)
        din("hsel_bf", [NH, NH * 128], BF16)
        din("csel_bf", [128, NH * NH], BF16)

        self.y_out = nc.dram_tensor("y_out", [SOWN, D], F32, kind="ExternalOutput").ap()
        self.wup_bf = nc.dram_tensor("wup_bf", [E, HE, D], BF16).ap()
        self.wdown_bf = nc.dram_tensor("wdown_bf", [E, D, HE], BF16).ap()
        self.wpj_bf = {nm: nc.dram_tensor(f"w{nm}_bf", [D, D], BF16).ap()
                       for nm in ("q", "k", "v", "o")}
        self.bas_bf = {nm: nc.dram_tensor(f"bas{nm}_bf", [HARM, D], BF16).ap()
                       for nm in ("q", "k", "v", "o")}
        self.pha_bf = {nm: nc.dram_tensor(f"pha{nm}_bf", [HARM, D], BF16).ap()
                       for nm in ("q", "k", "v", "o")}
        self.amp_bf = {nm: nc.dram_tensor(f"amp{nm}_bf", [HARM, D], BF16).ap()
                       for nm in ("q", "k", "v", "o")}
        self.cc_in = nc.dram_tensor("cc_in", [128, NH * 129], BF16).ap()
        self.cc_out = nc.dram_tensor("cc_out", [NCORES, 128, NH * 129], BF16).ap()

        with ExitStack() as stk:
            sb = lambda n, sh, dt: stk.enter_context(nc.sbuf_tensor(n, list(sh), dt))
            psal = lambda n, sh: stk.enter_context(nc.psum_tensor(n, list(sh), F32))

            self.xr_t = sb("xr", [128, DC, SOWN], F32)
            self.xl_t = sb("xl", [128, DC, SOWN], BF16)
            self.id_bf = sb("c_id_bf", [128, 128], BF16)
            self.id_f32 = sb("c_id_f32", [128, 128], F32)
            self.ones_bf = sb("c_ones_bf", [128, 128], BF16)
            self.ones_f32 = sb("c_ones_f32", [128, 128], F32)
            self.swap_bf = sb("c_swap_bf", [128, 128], BF16)
            self.tri_f32 = sb("c_tri_f32", [128, 128], F32)
            self.rope_cos = sb("c_rope_cos", [128, SOWN], BF16)
            self.rope_sin = sb("c_rope_sin", [128, SOWN], BF16)
            self.cvec = sb("c_cvec", [128, 8], F32)
            self.selmask = sb("c_selmask", [128, 8], F32)
            self.bup_sb = sb("c_bup", [128, E * HCN], F32)
            self.bdown_sb = sb("c_bdown", [8, D], BF16)
            self.bo_sb = sb("c_bo", [128, OCN], F32)
            self.bg_sb = sb("c_bg", [NH, 1], F32)
            self.wg_sb = sb("c_wg", [128, DC, NH], BF16)
            self.wg1T_sb = sb("c_wg1t", [128, DC, E], BF16)
            self.wg2T_sb = sb("c_wg2t", [E, E], BF16)
            self.hsel = sb("c_hsel", [NH, NH * 128], BF16)
            self.csel = sb("c_csel", [128, NH * NH], BF16)

            self.ARB_W = 42864
            self.ARF_W = 3112
            self.arb_t = sb("arb", [128, self.ARB_W], BF16)
            self.arf_t = sb("arf", [128, self.ARF_W], F32)
            arb_addr = nc.lookup_mloc(self.arb_t).addr
            self.xf1_t = nc.alloc_sbuf_tensor_at(
                "xf1ov", [128, 2064], F32,
                offset=arb_addr + (self.ARB_W - 4128) * 2)

            self.Qt = [psal(f"q{i}", [128, 1024]) for i in range(4)]
            self.bank = [Slot(self.Qt[i // 2][:, (i % 2) * 512:(i % 2) * 512 + 512])
                         for i in range(8)]

            for nm in ("pe", "ve", "se", "ge", "dsy", "dsc", "dgp", "ccs", "outs"):
                setattr(self, nm, Sem(stk.enter_context(nc.semaphore(nm))))

            ARF = Arena(self.arf_t, 0, self.ARF_W)
            self.mcell = ARF.salloc([NH, 16])
            self.dn = ARF.salloc([NH, SOWN])
            self.mneg = ARF.salloc([128, NH])
            self.stat = ARF.salloc([128, 8])
            self.arf_scr = ARF.mark()
            self.ARF = ARF
            self.ARB = Arena(self.arb_t, 0, self.ARB_W)

            self.xr = Slot(self.xr_t)
            self.xl = Slot(self.xl_t)
            self.wg_s = Slot(self.wg_sb)
            self.wg1T = Slot(self.wg1T_sb)
            self.wg2T = Slot(self.wg2T_sb)

            self.p_prep()
            self.p_load_ln1()
            if self.stop_after == "ln1":
                self.p_finish(); self.playback(stk); return nc
            if self.stop_after == "ln1x":
                self.dbg = {
                    "xl0": (Slot(self.xl_t[:, 0, :]), 0, 0),
                    "xl1": (Slot(self.xl_t[:, 1, :]), 128, 0),
                    "xp0": (Slot(self.xp_ln.ap[:, 0, :]), 256, 0),
                    "st": (self.dbg_st, 384, 0),
                    "xlntm": (self.dbg_xlntm, 512, 0),
                    "xtm": (self.dbg_xtm, 640, 0),
                }
                self.p_dump(); self.playback(stk); return nc
            self.p_attn_a()
            if self.stop_after == "attn_a_dbg":
                self.p_dump(); self.playback(stk); return nc
            if self.stop_after != "attn_a":
                self.p_ln_fm()
                self.p_ffn()
                if self.stop_after == "ffn1_dbg":
                    self.p_dump(); self.playback(stk); return nc
                if self.stop_after != "ffn1":
                    self.p_ln_fm()
                    self.p_attn_b()
                    if self.stop_after == "attn_b_dbg":
                        self.p_dump(); self.playback(stk); return nc
                    if self.stop_after != "attn_b":
                        self.p_ln_fm()
                        self.p_ffn()
            self.p_finish()
            self.playback(stk)
            return nc

    def qpair(self, j):
        return self.Qt[j], [self.bank[2 * j], self.bank[2 * j + 1]]

    def p_prep(self):
        DT = self.DT
        for nm in ("q", "k", "v", "o"):
            self.dma("gpsimd", self.dgp, lambda e, s=DT[f"basis_{nm}"],
                     d=self.bas_bf[nm]:
                     e.dma_start(out=d[:, :], in_=s[:, :]))
        self.t_prep_bas = self.dgp.n
        for nm in ("q", "k", "v", "o"):
            self.dma("gpsimd", self.dgp, lambda e, s=DT[f"phase_t_{nm}"],
                     d=self.pha_bf[nm]:
                     e.dma_start(out=d[:, :], in_=s[:, :]))
            self.dma("gpsimd", self.dgp, lambda e, s=DT[f"amp_t_{nm}"],
                     d=self.amp_bf[nm]:
                     e.dma_start(out=d[:, :], in_=s[:, :]))
        self.t_prep_pa = self.dgp.n
        for dc in range(DC):
            self.dma("gpsimd", self.dgp, lambda e, d=dc:
                     e.dma_start(out=self.wg_sb[:, d, :],
                                 in_=DT["Wg"][d * 128:(d + 1) * 128, :]),
                     wr=[self.wg_s])
        self.dma("gpsimd", self.dgp, lambda e:
                 e.dma_start(out=self.bdown_sb[:, :], in_=DT["bdown"][:, :]))
        self.t_bdown = self.dgp.n
        for e_ in range(E):
            self.dma("gpsimd", self.dgp, lambda e, i=e_:
                     e.dma_start(out=self.wup_bf[i], in_=DT["Wup"][i]))
        self.t_prep_up = self.dgp.n
        for e_ in range(E):
            self.dma("gpsimd", self.dgp, lambda e, i=e_:
                     e.dma_start(out=self.wdown_bf[i], in_=DT["Wdown"][i]))
        self.t_prep_down = self.dgp.n
        for nm, key in (("Wq", "q"), ("Wk", "k"), ("Wv", "v"), ("Wo", "o")):
            for hh in range(2):
                self.dma("gpsimd", self.dgp, lambda e, s=DT[nm], d=self.wpj_bf[key], i=hh:
                         e.dma_start(out=d[i * 1024:(i + 1) * 1024, :],
                                     in_=s[i * 1024:(i + 1) * 1024, :]))
        self.t_prep_pj = self.dgp.n
        for nm, dst in (("id_bf", self.id_bf), ("id_f32", self.id_f32),
                        ("ones_bf", self.ones_bf), ("ones_f32", self.ones_f32),
                        ("swap_bf", self.swap_bf), ("tri_f32", self.tri_f32),
                        ("rope_cos", self.rope_cos), ("rope_sin", self.rope_sin),
                        ("cvec", self.cvec), ("selmask", self.selmask),
                        ("bup_pk", self.bup_sb), ("bo_pk", self.bo_sb),
                        ("bg", self.bg_sb), ("hsel_bf", self.hsel),
                        ("csel_bf", self.csel)):
            self.dma("scalar", self.dsc, lambda e, s=DT[nm], d=dst:
                     e.dma_start(out=d[:, :], in_=s[:, :]))
        self.t_consts = self.dsc.n
        self.wc = [(self.dsc, self.t_consts)]

    def p_load_ln1(self):
        DT = self.DT
        ARF, ARB = self.ARF, self.ARB
        ARF.reset(self.arf_scr)
        xf = [ARF.salloc([128, D]), Slot(self.xf1_t[0:128, 0:2048])]
        stp = [Slot(self.xf1_t[0:128, 2048:2056]),
               Slot(self.xf1_t[0:128, 2056:2064])]
        self.xp_base = ARB.mark()
        self.xp_ln = ARB.salloc([128, DC, SOWN])
        self.arb_m0 = ARB.mark()
        xnb = [ARB.salloc([128, D]) for _ in range(2)]
        xdump = ARB.salloc([128, D])
        self.dbg_xtm = xf[0]
        self.dbg_xlntm = xnb[0]
        self.dbg_st = stp[0]

        jobs = [("own", tci) for tci in range(TC)] + \
               [("part", tci) for tci in range(TC)]

        def s_dma(j):
            kind, tci = jobs[j]
            src = DT["x_own"] if kind == "own" else DT["x_part"]
            x = xf[j % 2]
            self.dma("sync", self.dsy, lambda e, i=tci, s=src, d=x.ap:
                     e.dma_start(out=d[:, :], in_=s[i * 128:(i + 1) * 128, :]),
                     wr=[x])

        def s_stats(j):
            x, st = xf[j % 2], stp[j % 2]
            xa, sa = x.ap, st.ap
            self.ev("scalar", lambda e, xa=xa, sa=sa:
                    e.activation(xdump.ap[:, :], xa[:, :], AF.Copy,
                                 accum_out=sa[:, 0:1]),
                    rd=[x], wr=[st])
            self.ev("scalar", lambda e, xa=xa, sa=sa:
                    e.activation(xdump.ap[:, :], xa[:, :], AF.Square,
                                 accum_out=sa[:, 1:2]),
                    rd=[x], wr=[st])
            self.ev("scalar", lambda e, sa=sa:
                    e.activation(sa[:, 2:3], sa[:, 0:1], AF.Copy, scale=1.0 / D),
                    rd=[st], wr=[st])
            self.ev("scalar", lambda e, sa=sa:
                    e.activation(sa[:, 3:4], sa[:, 0:1], AF.Square, scale=1.0 / D),
                    rd=[st], wr=[st])
            self.ev("vector", lambda e, sa=sa:
                    e.scalar_tensor_tensor(sa[:, 4:5], sa[:, 3:4], -1.0,
                                           self.cvec[:, 1:2], AL.mult, AL.add),
                    rd=[st], wr=[st],
                    extra_waits=self.wc if j == 0 else ())
            self.ev("scalar", lambda e, sa=sa:
                    e.activation(sa[:, 5:6], sa[:, 1:2], AF.Ln, scale=1.0 / D,
                                 bias=sa[:, 4:5]),
                    rd=[st], wr=[st])
            self.ev("scalar", lambda e, sa=sa:
                    e.activation(sa[:, 5:6], sa[:, 5:6], AF.Exp, scale=-0.5),
                    rd=[st], wr=[st])

        def s_norm_evac(j):
            kind, tci = jobs[j]
            x, st, xn = xf[j % 2], stp[j % 2], xnb[j % 2]
            xa, sa, na = x.ap, st.ap, xn.ap
            self.ev("vector", lambda e, xa=xa, sa=sa, na=na:
                    e.tensor_scalar(na[:, :], xa[:, :],
                                    sa[:, 2:3], sa[:, 5:6],
                                    AL.subtract, AL.mult),
                    rd=[x, st], wr=[xn])
            out_fm = self.xl if kind == "own" else self.xp_ln
            boff = 0 if kind == "own" else 4 * (j % 2)
            for g in range(4):
                bk = self.bank[boff + g]
                w = bk.wwait() + xn.rwait() + (self.wc if j == 0 and g == 0 else [])
                for dcl in range(4):
                    dc = g * 4 + dcl
                    self.t_mm(bk.ap[:, dcl * 128:(dcl + 1) * 128],
                              xn.ap[:, dc * 128:(dc + 1) * 128], self.id_bf[:, :],
                              True, True, waits=w if dcl == 0 else ())
                bk.note_w(self.pe)
                xn.note_r(self.pe)
                self.ev("vector", lambda e, q=bk.ap, g2=g, i=tci, o=out_fm:
                        e.tensor_copy(
                            o.ap[:, g2 * 4:(g2 + 1) * 4, i * 128:(i + 1) * 128],
                            q[:, 0:512].rearrange("p (a b) -> p a b", a=4)),
                        rd=[bk], wr=[out_fm])
            if kind == "own":
                for g in range(4):
                    bk = self.bank[4 + g]
                    w = bk.wwait() + x.rwait()
                    for dcl in range(4):
                        dc = g * 4 + dcl
                        self.t_mm(bk.ap[:, dcl * 128:(dcl + 1) * 128],
                                  x.ap[:, dc * 128:(dc + 1) * 128],
                                  self.id_f32[:, :],
                                  True, True, waits=w if dcl == 0 else ())
                    bk.note_w(self.pe)
                    x.note_r(self.pe)
                    self.ev("scalar", lambda e, q=bk.ap, g2=g, i=tci:
                            e.activation(
                                self.xr_t[:, g2 * 4:(g2 + 1) * 4,
                                          i * 128:(i + 1) * 128],
                                q[:, 0:512].rearrange("p (a b) -> p a b", a=4),
                                AF.Copy),
                            rd=[bk], wr=[self.xr])

        n = len(jobs)
        for t in range(n + 2):
            if t >= 2:
                s_norm_evac(t - 2)
            if t < n:
                s_dma(t)
            if 1 <= t <= n:
                s_stats(t - 1)
        self.barrier()

    def p_ln_fm(self):
        ARF, ARB = self.ARF, self.ARB
        ARF.reset(self.arf_scr)
        ARB.reset()
        xc = ARB.salloc([128, DC, SOWN])
        xq = ARB.salloc([128, DC, SOWN])
        mr_bf = ARB.salloc([1, 2 * SOWN])
        row_f = ARB.salloc([1, 2 * SOWN])
        msq = ARB.salloc([1, SOWN])
        tmp = ARF.salloc([128, 512])
        Q2, b2 = self.qpair(2)
        Q3, b3 = self.qpair(3)

        self.ev("vector", lambda e:
                e.tensor_copy(xc.ap[:, :, :], self.xr_t[:, :, :]),
                rd=[self.xr], wr=[xc])
        self.ev("scalar", lambda e:
                e.activation(xq.ap[:, :, :], xc.ap[:, :, :], AF.Square),
                rd=[xc], wr=[xq])
        for Qrow, brow, srcs in ((Q2, b2, xc), (Q3, b3, xq)):
            for blk in range(2):
                bk = brow[blk]
                w = bk.wwait() + srcs.rwait()
                for dc in range(DC):
                    self.t_mm(Qrow[0:1, blk * 512:(blk + 1) * 512],
                              self.ones_bf[:, 0:1],
                              srcs.ap[:, dc, blk * 512:(blk + 1) * 512],
                              dc == 0, dc == DC - 1, waits=w if dc == 0 else ())
                bk.note_w(self.pe)
                srcs.note_r(self.pe)

        b2s = Slot(None)
        b2s.wt = b2[1].wt
        self.ev("vector", lambda e: e.tensor_scalar(
            row_f.ap[0:1, 0:SOWN], Q2[0:1, 0:1024], 1.0 / D, None, AL.mult),
            rd=[b2[0], b2[1]], wr=[row_f])
        self.ev("vector", lambda e: e.tensor_scalar(
            row_f.ap[0:1, SOWN:2 * SOWN], Q3[0:1, 0:1024], 1.0 / D, None, AL.mult),
            rd=[b3[0], b3[1]], wr=[row_f])
        self.ev("vector", lambda e: e.tensor_tensor(
            msq.ap[0:1, :], row_f.ap[0:1, 0:SOWN], row_f.ap[0:1, 0:SOWN], AL.mult),
            rd=[row_f], wr=[msq])
        self.ev("vector", lambda e: e.tensor_tensor(
            row_f.ap[0:1, SOWN:2 * SOWN], row_f.ap[0:1, SOWN:2 * SOWN],
            msq.ap[0:1, :], AL.subtract),
            rd=[msq, row_f], wr=[row_f])
        self.ev("vector", lambda e: e.tensor_scalar(
            row_f.ap[0:1, SOWN:2 * SOWN], row_f.ap[0:1, SOWN:2 * SOWN],
            LN_EPS, None, AL.add),
            rd=[row_f], wr=[row_f])
        self.ev("scalar", lambda e: e.activation(
            row_f.ap[0:1, SOWN:2 * SOWN], row_f.ap[0:1, SOWN:2 * SOWN], AF.Ln),
            rd=[row_f], wr=[row_f])
        self.ev("scalar", lambda e: e.activation(
            row_f.ap[0:1, SOWN:2 * SOWN], row_f.ap[0:1, SOWN:2 * SOWN],
            AF.Exp, scale=-0.5),
            rd=[row_f], wr=[row_f])
        self.ev("vector", lambda e: e.tensor_copy(mr_bf.ap[0:1, :], row_f.ap[0:1, :]),
                rd=[row_f], wr=[mr_bf])

        def bcast(ap2d):
            return bass.AP(ap2d.tensor, ap2d.offset,
                           mybir.VecI64Pair([list(ap2d.ap[0]), [0, DC],
                                             list(ap2d.ap[1])]))
        rbf = ARB.salloc([128, 512])
        for blk in range(2):
            self.gmm(Q3[:, 0:512], [b3[0]],
                     [(self.ones_bf[0:1, :], mr_bf.ap[0:1, blk * 512:(blk + 1) * 512])],
                     rd_slots=[mr_bf])
            self.gmm(Q3[:, 512:1024], [b3[1]],
                     [(self.ones_bf[0:1, :],
                       mr_bf.ap[0:1, SOWN + blk * 512:SOWN + (blk + 1) * 512])],
                     rd_slots=[mr_bf])
            self.ev("vector", lambda e: e.tensor_copy(rbf.ap[:, :], Q3[:, 512:1024]),
                    rd=[b3[1]], wr=[rbf])
            self.ev("vector", lambda e, b=blk: e.tensor_tensor(
                xq.ap[:, :, b * 512:(b + 1) * 512],
                self.xr_t[:, :, b * 512:(b + 1) * 512],
                bcast(Q3[:, 0:512]), AL.subtract),
                rd=[self.xr, b3[0]], wr=[xq])
            self.ev("vector", lambda e, b=blk: e.tensor_tensor(
                self.xl_t[:, :, b * 512:(b + 1) * 512],
                xq.ap[:, :, b * 512:(b + 1) * 512],
                bcast(rbf.ap[:, :]), AL.mult),
                rd=[xq, rbf], wr=[self.xl])
        self.barrier()

    def p_attn_a(self):
        ARB, ARF = self.ARB, self.ARF
        DT = self.DT
        ARB.reset(self.arb_m0)
        bas_nat = ARB.salloc([64, D])
        basisT = {nm: ARB.salloc([128, DC, HARM]) for nm in ("q", "k", "v", "o")}
        wT = {nm: ARB.salloc([64, D]) for nm in ("q", "k", "v", "o")}
        res_q = ARB.salloc([64, SOWN])
        res_k = ARB.salloc([64, SEQ])
        res_v = ARB.salloc([64, SEQ])
        reso_sb = ARB.salloc([64, SOWN])
        dnr_bf = ARB.salloc([NH, SOWN])
        ARF.reset(self.arf_scr)

        HS = Arena(self.arb_t, self.xp_base, DC * SOWN)
        mcell, mneg, dn = self.mcell, self.mneg, self.dn
        xlkeep = ARB.salloc([128, SOWN])
        self.ev("vector", lambda e: e.tensor_copy(xlkeep.ap[:, :], self.xl_t[:, 0, :]),
                rd=[self.xl], wr=[xlkeep])
        self.xlkeep = xlkeep

        amT = ARB.salloc([64, D])
        for nm in ("q", "k", "v", "o"):
            wt = wT[nm]
            self.dma("sync", self.dsy, lambda e, s=self.pha_bf[nm], d=wt.ap:
                     e.dma_start(out=d[:, :], in_=s[:, :]),
                     wr=[wt], extra_waits=[(self.dgp, self.t_prep_pa)])
            self.dma("scalar", self.dsc, lambda e, s=self.amp_bf[nm], d=amT.ap:
                     e.dma_start(out=d[:, :], in_=s[:, :]),
                     wr=[amT], extra_waits=[(self.dgp, self.t_prep_pa)])
            self.ev("scalar", lambda e, d=wt.ap:
                    e.activation(d[:, :], d[:, :], AF.Sin,
                                 bias=self.cvec[0:64, 0:1]),
                    rd=[wt], wr=[wt],
                    extra_waits=self.wc if nm == "q" else ())
            self.ev("vector", lambda e, d=wt.ap, a=amT.ap:
                    e.tensor_tensor(d[:, :], d[:, :], a[:, :], AL.mult),
                    rd=[wt, amT], wr=[wt])
            self.dma("sync", self.dsy, lambda e, s=self.bas_bf[nm]:
                     e.dma_start(out=bas_nat.ap[:, :], in_=s[:, :]), wr=[bas_nat],
                     extra_waits=[(self.dgp, self.t_prep_bas)])
            for dc in range(DC):
                bk = self.bank[4 + dc % 2]
                self.gmm(bk.ap[:, 0:64], [bk],
                         [(bas_nat.ap[:, dc * 128:(dc + 1) * 128],
                           self.id_bf[0:64, 0:64])],
                         rd_slots=[bas_nat])
                self.ev("vector", lambda e, q=bk.ap, d=dc, t=basisT[nm].ap:
                        e.tensor_copy(t[:, d, :], q[:, 0:64]),
                        rd=[bk], wr=[basisT[nm]])

        def resonance(bt, src, n_tok, base, dst):
            for blk in range(n_tok // 512):
                bk = self.bank[6 + blk % 2]
                self.gmm(bk.ap[0:64, :], [bk],
                         [(bt.ap[:, dc, :], src.ap[:, dc, blk * 512:(blk + 1) * 512])
                          for dc in range(DC)],
                         rd_slots=[bt, src])
                self.ev("vector", lambda e, q=bk.ap, b=blk, d=dst.ap, bc=base:
                        e.tensor_copy(d[:, bc + b * 512:bc + (b + 1) * 512], q[0:64, :]),
                        rd=[bk], wr=[dst])
        resonance(basisT["q"], self.xl, SOWN, 0, res_q)
        resonance(basisT["k"], self.xl, SOWN, 0, res_k)
        resonance(basisT["k"], self.xp_ln, SOWN, SOWN, res_k)
        resonance(basisT["v"], self.xl, SOWN, 0, res_v)
        resonance(basisT["v"], self.xp_ln, SOWN, SOWN, res_v)

        kh = HS.salloc([128, SEQ])
        qh = HS.salloc([128, SOWN])
        vfm = HS.salloc([128, SEQ])
        vtm = HS.salloc([128, DC, 128])
        q2 = HS.salloc([128, SOWN])
        k2 = HS.salloc([128, SEQ])
        ptb = [HS.salloc([128, SOWN]) for _ in range(2)]
        brow = HS.salloc([NH, SOWN])
        qn2 = ARF.salloc([NH, SOWN])
        y_raw = self.xl

        for h in range(NH):
            def expand(dst, wt_s, res_s, ncol, qj, eng):
                Qx, bx = self.qpair(qj)
                for blk in range(ncol // 512):
                    bk = bx[blk % 2]
                    self.gmm(Qx[:, (blk % 2) * 512:(blk % 2) * 512 + 512], [bk],
                             [(wt_s.ap[:, h * 128:(h + 1) * 128],
                               res_s.ap[:, blk * 512:(blk + 1) * 512])],
                             rd_slots=[wt_s, res_s])
                    if eng == "scalar":
                        self.ev("scalar", lambda e, q=bk.ap, b=blk, d=dst.ap:
                                e.activation(d[:, b * 512:(b + 1) * 512], q, AF.Copy),
                                rd=[bk], wr=[dst])
                    else:
                        self.ev("vector", lambda e, q=bk.ap, b=blk, d=dst.ap:
                                e.tensor_copy(d[:, b * 512:(b + 1) * 512], q),
                                rd=[bk], wr=[dst])
            expand(qh, wT["q"], res_q, SOWN, 0, "scalar")
            expand(kh, wT["k"], res_k, SEQ, 1, "scalar")
            expand(vfm, wT["v"], res_v, SEQ, 0, "vector")
            for kc in range(DC):
                bk = self.bank[2 + kc % 2]
                self.gmm(bk.ap[:, 0:128], [bk],
                         [(vfm.ap[:, kc * 128:(kc + 1) * 128], self.id_bf[:, :])],
                         rd_slots=[vfm])
                self.ev("vector", lambda e, q=bk.ap, c=kc:
                        e.tensor_copy(vtm.ap[:, c, :], q[:, 0:128]),
                        rd=[bk], wr=[vtm])
            self.ev("scalar", lambda e: e.activation(q2.ap[:, :], qh.ap[:, :], AF.Square),
                    rd=[qh], wr=[q2])
            self.ev("scalar", lambda e: e.activation(k2.ap[:, :], kh.ap[:, :], AF.Square),
                    rd=[kh], wr=[k2])
            for blk in range(4):
                bk = self.bank[4]
                self.gmm(bk.ap[0:NH, :], [bk],
                         [(self.ones_bf[:, 0:NH],
                           k2.ap[:, blk * 512:(blk + 1) * 512])],
                         rd_slots=[k2])
                self.ev("vector", lambda e, b=blk, q=bk.ap:
                        e.reduce_max(mcell.ap[0:NH, b:b + 1],
                                     q[0:NH, 0:512], axis=AX.X),
                        rd=[bk], wr=[mcell])
            self.ev("vector", lambda e: e.reduce_max(
                mcell.ap[0:NH, 8:9], mcell.ap[0:NH, 0:4], axis=AX.X),
                rd=[mcell], wr=[mcell])
            self.ev("vector", lambda e: e.tensor_scalar(
                mcell.ap[0:NH, 9:10], mcell.ap[0:NH, 8:9], SM_SCALE * SM_SCALE,
                None, AL.mult),
                rd=[mcell], wr=[mcell])
            for blk in range(2):
                bk = self.bank[4]
                self.gmm(bk.ap[0:NH, :], [bk],
                         [(self.ones_bf[:, 0:NH],
                           q2.ap[:, blk * 512:(blk + 1) * 512])],
                         rd_slots=[q2])
                self.ev("vector", lambda e, b=blk, q=bk.ap: e.tensor_copy(
                    qn2.ap[0:NH, b * 512:(b + 1) * 512], q[0:NH, 0:512]),
                    rd=[bk], wr=[qn2])
            self.ev("vector", lambda e: e.tensor_scalar(
                qn2.ap[0:NH, :], qn2.ap[0:NH, :], mcell.ap[0:NH, 9:10],
                None, AL.mult),
                rd=[qn2, mcell], wr=[qn2])
            self.ev("scalar", lambda e: e.activation(
                qn2.ap[0:NH, :], qn2.ap[0:NH, :], AF.Ln),
                rd=[qn2], wr=[qn2])
            self.ev("scalar", lambda e: e.activation(
                qn2.ap[0:NH, :], qn2.ap[0:NH, :], AF.Exp, scale=0.5),
                rd=[qn2], wr=[qn2])
            self.ev("vector", lambda e: e.tensor_scalar(
                qn2.ap[0:NH, :], qn2.ap[0:NH, :], -1.0, None, AL.mult),
                rd=[qn2], wr=[qn2])
            self.ev("vector", lambda e: e.tensor_scalar(
                qn2.ap[0:NH, :], qn2.ap[0:NH, :], 78.0, None, AL.add),
                rd=[qn2], wr=[qn2])
            self.ev("vector", lambda e: e.tensor_scalar(
                brow.ap[0:NH, :], qn2.ap[0:NH, :], 1.0 / SM_SCALE, None, AL.mult),
                rd=[qn2], wr=[brow])

            av_b = [self.bank[4], self.bank[5]]
            dn_b = self.bank[6]
            for kc in range(DC):
                qj = kc % 2
                Qx, bx = self.qpair(qj)
                sw = (bx[0].wwait() + bx[1].wwait() + kh.rwait() + qh.rwait() +
                      brow.rwait())
                for qb in range(2):
                    self.t_mm(Qx[:, qb * 512:(qb + 1) * 512],
                              kh.ap[:, kc * 128:(kc + 1) * 128],
                              qh.ap[:, qb * 512:(qb + 1) * 512],
                              True, False, waits=sw if qb == 0 else ())
                    self.t_mm(Qx[:, qb * 512:(qb + 1) * 512],
                              self.ones_bf[0:1, :],
                              brow.ap[0:1, qb * 512:(qb + 1) * 512],
                              False, True)
                bx[0].note_w(self.pe)
                bx[1].note_w(self.pe)
                kh.note_r(self.pe)
                qh.note_r(self.pe)
                brow.note_r(self.pe)
                pt = ptb[kc % 2]
                self.ev("scalar", lambda e, s=Qx, p=pt.ap:
                        e.activation(p[:, :], s[:, 0:1024], AF.Exp, scale=SM_SCALE),
                        rd=[bx[0], bx[1]], wr=[pt])
                w_av = (av_b[0].wwait() + av_b[1].wwait() +
                        pt.rwait() + vtm.rwait()) if kc == 0 else pt.rwait()
                if h == 0 and kc == 0:
                    w_av = w_av + dn_b.wwait() + self.bank[7].wwait()
                for qb in range(2):
                    self.t_mm(av_b[qb].ap, vtm.ap[:, kc, :],
                              pt.ap[:, qb * 512:(qb + 1) * 512], kc == 0, kc == DC - 1,
                              waits=w_av if qb == 0 else ())
                    self.t_mm(self.Qt[3][0:NH, qb * 512:(qb + 1) * 512],
                              self.csel[:, h * NH:(h + 1) * NH],
                              pt.ap[:, qb * 512:(qb + 1) * 512],
                              h == 0 and kc == 0,
                              h == NH - 1 and kc == DC - 1)
                pt.note_r(self.pe)
            av_b[0].note_w(self.pe)
            av_b[1].note_w(self.pe)
            vtm.note_r(self.pe)
            for qb in range(2):
                self.ev("vector", lambda e, b=qb, hh=h: e.tensor_copy(
                    y_raw.ap[:, hh, b * 512:(b + 1) * 512], av_b[b].ap),
                    rd=[av_b[qb]], wr=[y_raw])

        dn_b.note_w(self.pe)
        self.bank[7].note_w(self.pe)
        self.ev("vector", lambda e:
                e.tensor_copy(dn.ap[0:NH, :], self.Qt[3][0:NH, 0:1024]),
                rd=[dn_b, self.bank[7]], wr=[dn])
        self.ev("vector", lambda e: e.reciprocal(dn.ap[:, :], dn.ap[:, :]),
                rd=[dn], wr=[dn])
        self.ev("vector", lambda e: e.tensor_copy(dnr_bf.ap[:, :], dn.ap[:, :]),
                rd=[dn], wr=[dnr_bf])

        ro_b = [self.bank[2], self.bank[3]]
        w_ro = ro_b[0].wwait() + ro_b[1].wwait()
        for h in range(NH):
            for qb in range(2):
                bk = self.bank[0]
                self.gmm(bk.ap, [bk],
                         [(self.hsel[:, h * 128:(h + 1) * 128],
                           dnr_bf.ap[0:NH, qb * 512:(qb + 1) * 512])],
                         rd_slots=[dnr_bf])
                self.ev("vector", lambda e, q=bk.ap, hh=h, b=qb: e.tensor_tensor(
                    y_raw.ap[:, hh, b * 512:(b + 1) * 512],
                    y_raw.ap[:, hh, b * 512:(b + 1) * 512], q, AL.mult),
                    rd=[bk, y_raw], wr=[y_raw])
                self.t_mm(self.Qt[1][0:64, qb * 512:(qb + 1) * 512],
                          basisT["o"].ap[:, h, :],
                          y_raw.ap[:, h, qb * 512:(qb + 1) * 512],
                          h == 0, h == NH - 1,
                          waits=(w_ro + y_raw.rwait()) if h == 0 else y_raw.rwait())
        ro_b[0].note_w(self.pe)
        ro_b[1].note_w(self.pe)
        y_raw.note_r(self.pe)
        for qb in range(2):
            self.ev("vector", lambda e, b=qb: e.tensor_copy(
                reso_sb.ap[:, b * 512:(b + 1) * 512],
                self.Qt[1][0:64, b * 512:(b + 1) * 512]),
                rd=[ro_b[qb]], wr=[reso_sb])

        for oc in range(OCN):
            for qb in range(2):
                bk = self.bank[4 + (oc * 2 + qb) % 2]
                self.gmm(bk.ap, [bk],
                         [(wT["o"].ap[:, oc * 128:(oc + 1) * 128],
                           reso_sb.ap[:, qb * 512:(qb + 1) * 512])],
                         rd_slots=[reso_sb])
                self.ev("vector", lambda e, q=bk.ap, o=oc, b=qb: e.tensor_tensor(
                    self.xr_t[:, o, b * 512:(b + 1) * 512], q,
                    self.xr_t[:, o, b * 512:(b + 1) * 512], AL.add),
                    rd=[bk, self.xr], wr=[self.xr])
        self.dbg = {
            "dn": (dn, 0, 0),
            "mcell": (mcell, 16, 0),
            "mneg": (mneg, 32, 0),
            "yraw0": (Slot(y_raw.ap[:, 0, :]), 160, 0),
            "resq": (res_q, 288, 0),
            "resk": (res_k, 352, 0),
            "qh": (qh, 416, 0),
            "kh": (kh, 544, 0),
            "wtq": (wT["q"], 672, 0),
            "dnrbf": (dnr_bf, 736, 0),
            "reso": (reso_sb, 752, 0),
            "basisTq": (Slot(basisT["q"].ap[:, 0, :]), 816, 0),
            "xl0": (self.xlkeep, 880, 0),
        }
        self.barrier()

    def p_dump(self):
        for nm, (slot, row, col) in self.dbg.items():
            ap = slot.ap
            np_, nf = ap.shape[0], 1
            sh = list(ap.shape)
            nf = 1
            for s_ in sh[1:]:
                nf *= s_
            self.dma("gpsimd", self.dgp, lambda e, a=ap, r=row, c2=col, p=sh[0], f=nf:
                     e.dma_start(out=self.y_out[r:r + p, c2:c2 + f], in_=a),
                     rd=[slot])
        self.emit("gpsimd", lambda e: e.nop(), waits=[(self.dgp, self.dgp.n)])

    def p_ffn(self):
        ARB, ARF = self.ARB, self.ARF
        DT = self.DT
        ARB.reset()
        ARF.reset(self.arf_scr)
        srep = ARB.salloc([128, E, 512])
        slab = [ARB.salloc([128, 1024]) for _ in range(4)]
        husk = ARB.salloc([128, 512])
        sco = ARB.salloc([E, SOWN])
        arb_gate = ARB.mark()
        t1 = ARB.salloc([E, SOWN])
        stb = ARB.salloc([128, E])
        wg1n = ARB.salloc([E, D])
        lg_f = ARF.salloc([E, SOWN])
        lgt = ARF.salloc([128, E])
        gsc = ARF.salloc([128, 8])
        bank = self.bank

        self.dma("gpsimd", self.dgp, lambda e:
                 e.dma_start(out=wg1n.ap[:, :], in_=DT["Wg1"][:, :]), wr=[wg1n])
        for dc in range(DC):
            bk = bank[0]
            self.gmm(bk.ap[:, 0:8], [bk],
                     [(wg1n.ap[:, dc * 128:(dc + 1) * 128], self.id_bf[0:8, 0:8])],
                     rd_slots=[wg1n])
            self.ev("vector", lambda e, q=bk.ap, d=dc:
                    e.tensor_copy(self.wg1T_sb[:, d, :], q[:, 0:8]),
                    rd=[bk], wr=[self.wg1T])
        self.dma("gpsimd", self.dgp, lambda e:
                 e.dma_start(out=stb.ap[0:8, 0:8], in_=DT["Wg2"][:, :]), wr=[stb])
        bk = bank[0]
        self.gmm(bk.ap[0:8, 0:8], [bk], [(stb.ap[0:8, 0:8], self.id_bf[0:8, 0:8])],
                 rd_slots=[stb])
        self.ev("vector", lambda e, q=bk.ap: e.tensor_copy(self.wg2T_sb[:, :],
                                                           q[0:8, 0:8]),
                rd=[bk], wr=[self.wg2T])

        for b in range(2):
            bk = bank[2 + b]
            w = bk.wwait() + self.wg1T.rwait() + self.xl.rwait()
            for dc in range(DC):
                self.t_mm(self.Qt[1][0:E, b * 512:(b + 1) * 512],
                          self.wg1T_sb[:, dc, :],
                          self.xl_t[:, dc, b * 512:(b + 1) * 512],
                          dc == 0, dc == DC - 1, waits=w if dc == 0 else ())
            bk.note_w(self.pe)
        self.wg1T.note_r(self.pe)
        self.xl.note_r(self.pe)
        self.ev("scalar", lambda e: e.activation(t1.ap[:, :], self.Qt[1][0:E, 0:1024],
                                                 AF.Tanh),
                rd=[bank[2], bank[3]], wr=[t1])
        for b in range(2):
            bk = bank[2 + b]
            self.gmm(self.Qt[1][0:E, b * 512:(b + 1) * 512], [bk],
                     [(self.wg2T_sb[:, :], t1.ap[:, b * 512:(b + 1) * 512])],
                     rd_slots=[self.wg2T, t1])
        self.ev("vector", lambda e: e.tensor_copy(lg_f.ap[:, :], self.Qt[1][0:E, 0:1024]),
                rd=[bank[2], bank[3]], wr=[lg_f])

        for tci in range(TC):
            bk = bank[0]
            self.gmm(bk.ap[:, 0:8], [bk],
                     [(lg_f.ap[:, tci * 128:(tci + 1) * 128], self.id_f32[0:8, 0:8])],
                     rd_slots=[lg_f])
            self.ev("vector", lambda e, q=bk.ap: e.tensor_copy(lgt.ap[:, :], q[:, 0:8]),
                    rd=[bk], wr=[lgt])
            self.ev("vector", lambda e: e.reduce_max(gsc.ap[:, 0:1], lgt.ap[:, :],
                                                     axis=AX.X),
                    rd=[lgt], wr=[gsc])
            self.ev("vector", lambda e: e.tensor_scalar(
                stb.ap[:, 0:E], lgt.ap[:, :], gsc.ap[:, 0:1], None, AL.is_ge),
                rd=[lgt, gsc], wr=[stb])
            self.ev("vector", lambda e: e.scalar_tensor_tensor(
                lgt.ap[:, :], stb.ap[:, 0:E], -1e30, lgt.ap[:, :], AL.mult, AL.add),
                rd=[stb, lgt], wr=[lgt])
            self.ev("vector", lambda e: e.reduce_max(gsc.ap[:, 1:2], lgt.ap[:, :],
                                                     axis=AX.X),
                    rd=[lgt], wr=[gsc])
            self.ev("vector", lambda e: e.tensor_scalar(
                lgt.ap[:, :], lgt.ap[:, :], gsc.ap[:, 1:2], None, AL.is_ge),
                rd=[lgt, gsc], wr=[lgt])
            self.ev("vector", lambda e: e.tensor_tensor(
                gsc.ap[:, 2:3], gsc.ap[:, 0:1], gsc.ap[:, 1:2], AL.subtract),
                rd=[gsc], wr=[gsc])
            self.ev("scalar", lambda e: e.activation(gsc.ap[:, 2:3], gsc.ap[:, 2:3],
                                                     AF.Sigmoid),
                    rd=[gsc], wr=[gsc])
            self.ev("vector", lambda e: e.tensor_scalar(
                gsc.ap[:, 3:4], gsc.ap[:, 2:3], -1.0, None, AL.mult),
                rd=[gsc], wr=[gsc])
            self.ev("vector", lambda e: e.tensor_scalar(
                gsc.ap[:, 3:4], gsc.ap[:, 3:4], 1.0, None, AL.add),
                rd=[gsc], wr=[gsc])
            self.ev("vector", lambda e: e.tensor_scalar(
                stb.ap[:, 0:E], stb.ap[:, 0:E], gsc.ap[:, 2:3], None, AL.mult),
                rd=[stb, gsc], wr=[stb])
            self.ev("vector", lambda e: e.scalar_tensor_tensor(
                stb.ap[:, 0:E], lgt.ap[:, :], gsc.ap[:, 3:4], stb.ap[:, 0:E],
                AL.mult, AL.add),
                rd=[lgt, gsc, stb], wr=[stb])
            bk = bank[1]
            self.gmm(bk.ap[0:8, 0:128], [bk], [(stb.ap[:, 0:E], self.id_bf[:, :])],
                     rd_slots=[stb])
            self.ev("vector", lambda e, q=bk.ap, i=tci:
                    e.tensor_copy(sco.ap[:, i * 128:(i + 1) * 128], q[0:8, 0:128]),
                    rd=[bk], wr=[sco])

        self.barrier()
        ARB.reset(arb_gate)
        hbuf = ARB.salloc([128, E, HCN, 512])
        for tt in range(2):
            for e_ in range(E):
                bk = bank[7]
                self.gmm(bk.ap, [bk],
                         [(self.hsel[0:8, e_ * 128:(e_ + 1) * 128],
                           sco.ap[0:8, tt * 512:(tt + 1) * 512])],
                         rd_slots=[sco])
                self.ev("vector", lambda e, q=bk.ap, i=e_:
                        e.tensor_copy(srep.ap[:, i, :], q),
                        rd=[bk], wr=[srep])
            for e_ in range(E):
                for dc in range(DC):
                    si = (e_ * DC + dc) % 4
                    sla = slab[si]
                    qeng, qsem = (("sync", self.dsy),
                                  ("scalar", self.dsc))[si % 2]
                    self.dma(qeng, qsem, lambda e, i=e_, d=dc, s=sla:
                             e.dma_start(out=s.ap[:, :],
                                         in_=self.wup_bf[i][:, d * 128:(d + 1) * 128],
                                         transpose=True),
                             wr=[sla], extra_waits=[(self.dgp, self.t_prep_up)])
                    for hc in range(HCN):
                        bk = bank[hc]
                        w = (bk.wwait() + sla.rwait() + self.xl.rwait()) if dc == 0 \
                            else (sla.rwait() if hc == 0 else ())
                        self.t_mm(bk.ap, sla.ap[:, hc * 128:(hc + 1) * 128],
                                  self.xl_t[:, dc, tt * 512:(tt + 1) * 512],
                                  dc == 0, dc == DC - 1, waits=w)
                        if dc == DC - 1:
                            bk.note_w(self.pe)
                    sla.note_r(self.pe)
                self.xl.note_r(self.pe)
                for hc in range(HCN):
                    bk = bank[hc]
                    self.ev("scalar", lambda e, q=bk.ap, i=e_, c=hc: e.activation(
                        husk.ap[:, :], q, AF.Silu,
                        bias=self.bup_sb[:, i * HCN + c:i * HCN + c + 1]),
                        rd=[bk], wr=[husk])
                    self.ev("vector", lambda e, i=e_, c=hc: e.tensor_tensor(
                        hbuf.ap[:, i, c, :], husk.ap[:, :], srep.ap[:, i, :], AL.mult),
                        rd=[husk, srep], wr=[hbuf])
            for og in range(2):
                for oc2 in range(8):
                    oc = og * 8 + oc2
                    bk = bank[oc2]
                    self.t_mm(bk.ap, self.bdown_sb[0:8, oc * 128:(oc + 1) * 128],
                              sco.ap[:, tt * 512:(tt + 1) * 512], True, False,
                              waits=bk.wwait() + sco.rwait() +
                              [(self.dgp, self.t_bdown)])
                sco.note_r(self.pe)
                for e_ in range(E):
                    for hc in range(HCN):
                        si = (e_ * HCN + hc) % 4
                        sla = slab[si]
                        qeng, qsem = (("sync", self.dsy),
                                      ("scalar", self.dsc))[si % 2]
                        self.dma(qeng, qsem, lambda e, i=e_, c=hc, g=og, s=sla:
                                 e.dma_start(
                                     out=s.ap[:, :],
                                     in_=self.wdown_bf[i][g * 1024:(g + 1) * 1024,
                                                          c * 128:(c + 1) * 128],
                                     transpose=True),
                                 wr=[sla], extra_waits=[(self.dgp, self.t_prep_down)])
                        last = (e_ == E - 1 and hc == HCN - 1)
                        for oc2 in range(8):
                            self.t_mm(bank[oc2].ap, sla.ap[:, oc2 * 128:(oc2 + 1) * 128],
                                      hbuf.ap[:, e_, hc, :], False, last,
                                      waits=(sla.rwait() + hbuf.rwait())
                                      if oc2 == 0 else ())
                        sla.note_r(self.pe)
                hbuf.note_r(self.pe)
                for oc2 in range(8):
                    bank[oc2].note_w(self.pe)
                    oc = og * 8 + oc2
                    self.ev("vector", lambda e, q=bank[oc2].ap, o=oc, b=tt:
                            e.tensor_tensor(
                                self.xr_t[:, o, b * 512:(b + 1) * 512], q,
                                self.xr_t[:, o, b * 512:(b + 1) * 512], AL.add),
                            rd=[bank[oc2], self.xr], wr=[self.xr])
        self.dbg = {
            "sco": (sco, 0, 0),
            "lgf": (lg_f, 8, 0),
            "hbuf00": (Slot(hbuf.ap[:, 0, 0, :]), 16, 0),
            "xl0": (Slot(self.xl_t[:, 0, :]), 144, 0),
            "srep0": (Slot(srep.ap[:, 0, :]), 272, 0),
        }
        self.barrier()

    def p_attn_b(self):
        ARB, ARF = self.ARB, self.ARF
        ARB.reset()
        ARF.reset(self.arf_scr)
        phq = ARB.salloc([128, NH, SOWN])
        phk = ARB.salloc([128, NH, SOWN])
        arb_m1 = ARB.mark()
        PA = Arena(self.arb_t, arb_m1, self.ARB_W - arb_m1)
        pslab = [PA.salloc([128, 512]) for _ in range(2)]
        cgb = PA.salloc([NH, SOWN])
        cbf = PA.salloc([NH, SOWN])
        zrow = PA.salloc([NH, 128])
        pa_perm = PA.mark()
        exf = PA.salloc([128, 512])
        rbuf = [PA.salloc([128, SOWN]) for _ in range(2)]
        swb = PA.salloc([128, 512])
        ta = PA.salloc([128, 512])
        rot = ta
        beta = ARF.salloc([NH, SOWN])
        ctm = ARF.salloc([128, TC * NH])
        rctm = ARF.salloc([128, TC * NH])
        gcol = ARF.salloc([128, 1])
        oex = ARF.salloc([NH, TC])
        svf = ARF.salloc([128, 128])
        bank = self.bank
        dn = self.dn
        ynum = phk

        bkg = [bank[0], bank[1]]
        for b in range(2):
            w = bkg[b].wwait() + self.wg_s.rwait() + self.xl.rwait()
            for dc in range(DC):
                self.t_mm(self.Qt[0][0:NH, b * 512:(b + 1) * 512],
                          self.wg_sb[:, dc, :],
                          self.xl_t[:, dc, b * 512:(b + 1) * 512], dc == 0, dc == DC - 1,
                          waits=w if dc == 0 else ())
            bkg[b].note_w(self.pe)
        self.wg_s.note_r(self.pe)
        self.ev("scalar", lambda e: e.activation(beta.ap[:, :], self.Qt[0][0:NH, 0:1024],
                                                 AF.Sigmoid, bias=self.bg_sb[:, 0:1]),
                rd=[bkg[0], bkg[1]], wr=[beta], extra_waits=self.wc)
        self.ev("vector", lambda e: e.tensor_scalar(
            beta.ap[:, :], beta.ap[:, :], BETA_MAX, None, AL.min),
            rd=[beta], wr=[beta])
        self.ev("vector", lambda e: e.tensor_scalar(
            beta.ap[:, :], beta.ap[:, :], BETA_MIN, None, AL.max),
            rd=[beta], wr=[beta])
        self.ev("vector", lambda e: e.memset(zrow.ap[:, :], 0.0), wr=[zrow])
        for c in range(TC):
            self.ev("vector", lambda e, i=c: e.tensor_tensor_scan(
                beta.ap[:, i * 128:(i + 1) * 128], beta.ap[:, i * 128:(i + 1) * 128],
                zrow.ap[:, :], 1.0, AL.mult, AL.add),
                rd=[beta, zrow], wr=[beta])
        cvar = beta
        self.ev("vector", lambda e: e.tensor_copy(cbf.ap[:, :], cvar.ap[:, :]),
                rd=[cvar], wr=[cbf])
        for c in range(TC):
            bk = bank[2]
            self.gmm(bk.ap[:, 0:NH], [bk],
                     [(cvar.ap[:, c * 128:(c + 1) * 128], self.id_f32[0:NH, 0:NH])],
                     rd_slots=[cvar])
            self.ev("vector", lambda e, q=bk.ap, i=c:
                    e.tensor_copy(ctm.ap[:, i * NH:(i + 1) * NH], q[:, 0:NH]),
                    rd=[bk], wr=[ctm])
        self.ev("vector", lambda e: e.reciprocal(rctm.ap[:, :], ctm.ap[:, :]),
                rd=[ctm], wr=[rctm])
        self.ev("vector", lambda e: e.tensor_tensor_scan(
            oex.ap[:, 0:TC], cvar.ap[:, 127::128], zrow.ap[:, 0:TC],
            1.0, AL.mult, AL.add),
            rd=[cvar, zrow], wr=[oex])
        self.ev("vector", lambda e: e.tensor_copy(exf.ap[0:NH, 0:TC - 1],
                                                  oex.ap[:, 0:TC - 1]),
                rd=[oex], wr=[exf])
        self.ev("vector", lambda e: e.tensor_copy(oex.ap[:, 1:TC],
                                                  exf.ap[0:NH, 0:TC - 1]),
                rd=[exf], wr=[oex])
        self.ev("vector", lambda e: e.memset(oex.ap[:, 0:1], 1.0), wr=[oex])
        for c in range(TC):
            self.ev("vector", lambda e, i=c: e.tensor_scalar(
                cgb.ap[:, i * 128:(i + 1) * 128], cvar.ap[:, i * 128:(i + 1) * 128],
                oex.ap[:, i:i + 1], None, AL.mult),
                rd=[cvar, oex], wr=[cgb])

        def proj_rope(wdram, phi):
            for og in range(4):
                for dc in range(DC):
                    sla = pslab[dc % 2]
                    self.dma("scalar", self.dsc, lambda e, g=og, d=dc, s=sla:
                             e.dma_start(out=s.ap[:, :],
                                         in_=wdram[d * 128:(d + 1) * 128,
                                                   g * 512:(g + 1) * 512]),
                             wr=[sla], extra_waits=[(self.dgp, self.t_prep_pj)])
                    for oc4 in range(4):
                        Qx, bx = self.qpair(oc4)
                        w = (bx[0].wwait() + bx[1].wwait() + sla.rwait() +
                             self.xl.rwait()) if dc == 0 \
                            else (sla.rwait() if oc4 == 0 else ())
                        for b in range(2):
                            self.t_mm(Qx[:, b * 512:(b + 1) * 512],
                                      sla.ap[:, oc4 * 128:(oc4 + 1) * 128],
                                      self.xl_t[:, dc, b * 512:(b + 1) * 512],
                                      dc == 0, dc == DC - 1,
                                      waits=w if b == 0 else ())
                        if dc == DC - 1:
                            bx[0].note_w(self.pe)
                            bx[1].note_w(self.pe)
                    sla.note_r(self.pe)
                self.xl.note_r(self.pe)
                for pair in range(2):
                    for i2 in range(2):
                        oc4 = pair * 2 + i2
                        Qx, bx = self.qpair(oc4)
                        self.ev("scalar", lambda e, s=Qx, r=rbuf[i2].ap:
                                e.activation(r[:, :], s[:, 0:1024], AF.Copy),
                                rd=[bx[0], bx[1]], wr=[rbuf[i2]])
                    for i2 in range(2):
                        oc4 = pair * 2 + i2
                        h = og * 4 + oc4
                        rb = rbuf[i2]
                        Qx, bx = self.qpair(i2)
                        for b in range(2):
                            self.t_mm(Qx[:, b * 512:(b + 1) * 512], self.swap_bf[:, :],
                                      rb.ap[:, b * 512:(b + 1) * 512], True, True,
                                      waits=(bx[0].wwait() + bx[1].wwait() + rb.rwait())
                                      if b == 0 else ())
                        bx[0].note_w(self.pe)
                        bx[1].note_w(self.pe)
                        rb.note_r(self.pe)
                        for b in range(2):
                            sl = b * 512
                            self.ev("vector", lambda e, s=Qx, o=sl: e.tensor_copy(
                                swb.ap[:, :], s[:, o:o + 512]),
                                rd=[bx[b]], wr=[swb])
                            self.ev("vector", lambda e, r=rb.ap, o=sl: e.tensor_tensor(
                                ta.ap[:, :], r[:, o:o + 512],
                                self.rope_cos[:, o:o + 512], AL.mult),
                                rd=[rb], wr=[ta], extra_waits=self.wc)
                            self.ev("vector", lambda e, o=sl: e.tensor_tensor(
                                swb.ap[:, :], swb.ap[:, :],
                                self.rope_sin[:, o:o + 512], AL.mult),
                                rd=[swb], wr=[swb])
                            self.ev("vector", lambda e: e.tensor_tensor(
                                rot.ap[:, :], ta.ap[:, :], swb.ap[:, :], AL.add),
                                rd=[swb], wr=[rot])
                            self.ev("scalar", lambda e: e.activation(
                                exf.ap[:, :], rot.ap[:, :], AF.Exp),
                                rd=[rot], wr=[exf])
                            self.ev("scalar", lambda e: e.activation(
                                rot.ap[:, :], rot.ap[:, :], AF.Relu),
                                rd=[rot], wr=[rot])
                            self.ev("vector", lambda e, hh=h, p=phi, o=sl:
                                    e.scalar_tensor_tensor(
                                        p.ap[:, hh, o:o + 512], rot.ap[:, :], 1.0,
                                        exf.ap[:, :], AL.add, AL.min),
                                    rd=[rot, exf], wr=[phi])
        proj_rope(self.wpj_bf["q"], phq)
        proj_rope(self.wpj_bf["k"], phk)

        self.barrier()
        PA.reset(pa_perm)
        Abf = PA.salloc([128, 128])
        ktb = PA.salloc([128, 128])
        vtb = PA.salloc([128, 128])
        qgb = PA.salloc([128, 128])
        sbf = PA.salloc([128, 128])
        zrep = PA.salloc([128, NH])
        dtf = ARF.salloc([128, 128])
        pa_scan = PA.mark()
        sendb = PA.salloc([128, NH, 129])
        vgrp = PA.salloc([128, 4, SOWN])
        p_crep, p_A = bank[0], bank[1]
        p_kT, p_Sc = bank[2], bank[3]
        p_nm, p_z = bank[4], bank[5]
        p_dn, p_vT = bank[6], bank[7]

        self.ev("vector", lambda e: e.memset(dn.ap[:, :], DN_EPS), wr=[dn])

        for vg in range(4):
            for dc in range(DC):
                sla = pslab[dc % 2]
                self.dma("scalar", self.dsc, lambda e, g=vg, d=dc, s=sla:
                         e.dma_start(out=s.ap[:, :],
                                     in_=self.wpj_bf["v"][d * 128:(d + 1) * 128,
                                                          g * 512:(g + 1) * 512]),
                         wr=[sla], extra_waits=[(self.dgp, self.t_prep_pj)])
                for oc4 in range(4):
                    Qx, bx = self.qpair(oc4)
                    w = (bx[0].wwait() + bx[1].wwait() + sla.rwait() +
                         self.xl.rwait()) if dc == 0 \
                        else (sla.rwait() if oc4 == 0 else ())
                    for b in range(2):
                        self.t_mm(Qx[:, b * 512:(b + 1) * 512],
                                  sla.ap[:, oc4 * 128:(oc4 + 1) * 128],
                                  self.xl_t[:, dc, b * 512:(b + 1) * 512],
                                  dc == 0, dc == DC - 1, waits=w if b == 0 else ())
                    if dc == DC - 1:
                        bx[0].note_w(self.pe)
                        bx[1].note_w(self.pe)
                sla.note_r(self.pe)
            for oc4 in range(4):
                Qx, bx = self.qpair(oc4)
                self.ev("scalar", lambda e, s=Qx, i=oc4:
                        e.activation(vgrp.ap[:, i, :], s[:, 0:1024], AF.Copy),
                        rd=[bx[0], bx[1]], wr=[vgrp])

            for hl in range(4):
                h = vg * 4 + hl
                self.ev("vector", lambda e: e.memset(svf.ap[:, :], 0.0), wr=[svf])
                self.ev("vector", lambda e: e.memset(sbf.ap[:, :], 0.0), wr=[sbf])
                self.ev("vector", lambda e: e.memset(zrep.ap[:, :], 0.0), wr=[zrep])
                for c in range(TC):
                    self.gmm(p_crep.ap[:, 0:128], [p_crep],
                             [(self.hsel[:, h * 128:(h + 1) * 128],
                               cbf.ap[0:NH, c * 128:(c + 1) * 128])],
                             rd_slots=[cbf])
                    self.ev("vector", lambda e: e.tensor_copy(
                        gcol.ap[:, 0:1], p_crep.ap[:, 127:128]),
                        rd=[p_crep], wr=[gcol])
                    self.ev("vector", lambda e, hh=h, cc=c: e.scalar_tensor_tensor(
                        dtf.ap[:, :], p_crep.ap[:, 0:128],
                        rctm.ap[:, cc * NH + hh:cc * NH + hh + 1],
                        self.tri_f32[:, :], AL.mult, AL.mult),
                        rd=[p_crep, rctm], wr=[dtf])
                    self.gmm(p_A.ap[:, 0:128], [p_A],
                             [(phk.ap[:, h, c * 128:(c + 1) * 128],
                               phq.ap[:, h, c * 128:(c + 1) * 128])],
                             rd_slots=[phk, phq])
                    self.ev("vector", lambda e: e.tensor_tensor(
                        Abf.ap[:, :], p_A.ap[:, 0:128], dtf.ap[:, :], AL.mult),
                        rd=[p_A, dtf], wr=[Abf])
                    self.gmm(p_kT.ap[:, 0:128], [p_kT],
                             [(phk.ap[:, h, c * 128:(c + 1) * 128], self.id_bf[:, :])],
                             rd_slots=[phk])
                    self.ev("vector", lambda e: e.tensor_scalar(
                        ktb.ap[:, :], p_kT.ap[:, 0:128], dtf.ap[:, 127:128],
                        None, AL.mult),
                        rd=[p_kT, dtf], wr=[ktb])
                    self.gmm(p_vT.ap[:, 0:128], [p_vT],
                             [(vgrp.ap[:, hl, c * 128:(c + 1) * 128], self.id_bf[:, :])],
                             rd_slots=[vgrp])
                    self.ev("vector", lambda e: e.tensor_copy(vtb.ap[:, :],
                                                              p_vT.ap[:, 0:128]),
                            rd=[p_vT], wr=[vtb])
                    self.ev("vector", lambda e, hh=h, cc=c: e.tensor_tensor(
                        qgb.ap[:, :], phq.ap[:, hh, cc * 128:(cc + 1) * 128],
                        p_crep.ap[:, 0:128], AL.mult),
                        rd=[phq, p_crep], wr=[qgb])
                    self.gmm(p_nm.ap[:, 0:128], [p_nm],
                             [(vtb.ap[:, :], Abf.ap[:, :]),
                              (sbf.ap[:, :], qgb.ap[:, :])],
                             rd_slots=[vtb, Abf, sbf, qgb])
                    self.ev("vector", lambda e, hh=h, cc=c: e.tensor_copy(
                        ynum.ap[:, hh, cc * 128:(cc + 1) * 128], p_nm.ap[:, 0:128]),
                        rd=[p_nm], wr=[ynum])
                    self.gmm(p_dn.ap[0:NH, 0:128], [p_dn],
                             [(self.csel[:, h * NH:(h + 1) * NH], Abf.ap[:, :]),
                              (zrep.ap[:, :], qgb.ap[:, :])],
                             rd_slots=[Abf, qgb, zrep])
                    self.ev("vector", lambda e, cc=c: e.tensor_tensor(
                        dn.ap[0:NH, cc * 128:(cc + 1) * 128],
                        dn.ap[0:NH, cc * 128:(cc + 1) * 128],
                        p_dn.ap[0:NH, 0:128], AL.add),
                        rd=[p_dn, dn], wr=[dn])
                    self.gmm(p_Sc.ap[:, 0:128], [p_Sc],
                             [(ktb.ap[:, :], vtb.ap[:, :])],
                             rd_slots=[ktb, vtb])
                    self.ev("vector", lambda e: e.scalar_tensor_tensor(
                        svf.ap[:, :], svf.ap[:, :], gcol.ap[:, 0:1],
                        p_Sc.ap[:, 0:128], AL.mult, AL.add),
                        rd=[p_Sc, gcol, svf], wr=[svf])
                    if c == TC - 1:
                        self.ev("vector", lambda e, hh=h: e.tensor_copy(
                            sendb.ap[:, hh, 0:128], svf.ap[:, :]),
                            rd=[svf], wr=[sendb])
                    else:
                        self.ev("vector", lambda e: e.tensor_copy(sbf.ap[:, :],
                                                                  svf.ap[:, :]),
                                rd=[svf], wr=[sbf])
                    self.gmm(p_z.ap[:, 0:NH], [p_z],
                             [(ktb.ap[:, :], self.ones_bf[:, 0:NH])],
                             rd_slots=[ktb])
                    self.ev("vector", lambda e, hh=h: e.scalar_tensor_tensor(
                        zrep.ap[:, hh:hh + 1], zrep.ap[:, hh:hh + 1],
                        gcol.ap[:, 0:1], p_z.ap[:, 0:1], AL.mult, AL.add),
                        rd=[p_z, gcol, zrep], wr=[zrep])
                self.ev("vector", lambda e, hh=h: e.tensor_copy(
                    sendb.ap[:, hh, 128:129], zrep.ap[:, hh:hh + 1]),
                    rd=[zrep], wr=[sendb])

        cc_in3 = self.cc_in.rearrange("p (a b) -> p a b", a=NH)
        self.dma("scalar", self.dsc, lambda e:
                 e.dma_start(out=cc_in3, in_=sendb.ap), rd=[sendb])
        t_ccin = self.dsc.n
        self.ccs.tick()
        self.emit("gpsimd", lambda e: e.collective_compute(
            "AllGather", AL.bypass, replica_groups=[list(range(NCORES))],
            ins=[self.cc_in.opt()], outs=[self.cc_out.opt()]),
            waits=[(self.dsc, t_ccin)], inc=(self.ccs, 1))
        t_cc = self.ccs.n

        PA.reset(pa_scan)
        sinb = PA.salloc([128, NH, 128])
        zin_c = PA.salloc([128, NH])
        rkb = PA.salloc([128, NH, 129])
        dnr = cbf
        self.ev("vector", lambda e: e.memset(sinb.ap[:, :, :], 0.0), wr=[sinb],
                extra_waits=[(self.dsc, self.dsc.n)] + sendb.rd)
        self.ev("vector", lambda e: e.memset(zin_c.ap[:, :], 0.0), wr=[zin_c])
        for r in range(NCORES):
            rb = rkb
            self.dma("sync", self.dsy, lambda e, i=r, d=rb:
                     e.dma_start(out=d.ap[:, :, :], in_=self.cc_out[i].rearrange(
                         "p (a b) -> p a b", a=NH)),
                     wr=[rb], extra_waits=[(self.ccs, t_cc)])
            self.ev("vector", lambda e, i=r, s=rb.ap: e.scalar_tensor_tensor(
                sinb.ap[:, :, :], s[:, :, 0:128], self.selmask[:, i:i + 1],
                sinb.ap[:, :, :], AL.mult, AL.add),
                rd=[rb, sinb], wr=[sinb])
            self.ev("vector", lambda e, i=r, s=rb.ap: e.scalar_tensor_tensor(
                zin_c.ap[:, :], s[:, :, 128], self.selmask[:, i:i + 1],
                zin_c.ap[:, :], AL.mult, AL.add),
                rd=[rb, zin_c], wr=[zin_c])

        zinsel = PA.salloc([128, NH])
        for h in range(NH):
            self.ev("vector", lambda e: e.memset(zinsel.ap[:, :], 0.0), wr=[zinsel])
            self.ev("vector", lambda e, hh=h: e.tensor_copy(
                zinsel.ap[:, hh:hh + 1], zin_c.ap[:, hh:hh + 1]),
                rd=[zin_c], wr=[zinsel])
            for c in range(TC):
                self.gmm(p_crep.ap[:, 0:128], [p_crep],
                         [(self.hsel[:, h * 128:(h + 1) * 128],
                           cgb.ap[0:NH, c * 128:(c + 1) * 128])],
                         rd_slots=[cgb])
                self.ev("vector", lambda e, hh=h, cc=c: e.tensor_tensor(
                    qgb.ap[:, :], phq.ap[:, hh, cc * 128:(cc + 1) * 128],
                    p_crep.ap[:, 0:128], AL.mult),
                    rd=[phq, p_crep], wr=[qgb])
                self.gmm(p_nm.ap[:, 0:128], [p_nm],
                         [(sinb.ap[:, h, :], qgb.ap[:, :])],
                         rd_slots=[sinb, qgb])
                self.ev("vector", lambda e, hh=h, cc=c: e.tensor_tensor(
                    ynum.ap[:, hh, cc * 128:(cc + 1) * 128],
                    ynum.ap[:, hh, cc * 128:(cc + 1) * 128],
                    p_nm.ap[:, 0:128], AL.add),
                    rd=[p_nm, ynum], wr=[ynum])
                self.gmm(p_dn.ap[0:NH, 0:128], [p_dn],
                         [(zinsel.ap[:, :], qgb.ap[:, :])],
                         rd_slots=[zinsel, qgb])
                self.ev("vector", lambda e, cc=c: e.tensor_tensor(
                    dn.ap[0:NH, cc * 128:(cc + 1) * 128],
                    dn.ap[0:NH, cc * 128:(cc + 1) * 128],
                    p_dn.ap[0:NH, 0:128], AL.add),
                    rd=[p_dn, dn], wr=[dn])

        self.ev("vector", lambda e: e.reciprocal(dn.ap[:, :], dn.ap[:, :]),
                rd=[dn], wr=[dn])
        self.ev("vector", lambda e: e.tensor_copy(dnr.ap[:, :], dn.ap[:, :]),
                rd=[dn], wr=[dnr])
        for h in range(NH):
            for b in range(2):
                bk = bank[0]
                self.gmm(bk.ap, [bk],
                         [(self.hsel[:, h * 128:(h + 1) * 128],
                           dnr.ap[0:NH, b * 512:(b + 1) * 512])],
                         rd_slots=[dnr])
                self.ev("vector", lambda e, hh=h, bb=b: e.tensor_tensor(
                    ynum.ap[:, hh, bb * 512:(bb + 1) * 512],
                    ynum.ap[:, hh, bb * 512:(bb + 1) * 512],
                    self.Qt[0][:, 0:512], AL.mult),
                    rd=[bk, ynum], wr=[ynum])
        for og in range(4):
            for dc in range(DC):
                sla = pslab[dc % 2]
                self.dma("scalar", self.dsc, lambda e, g=og, d=dc, s=sla:
                         e.dma_start(out=s.ap[:, :],
                                     in_=self.wpj_bf["o"][d * 128:(d + 1) * 128,
                                                          g * 512:(g + 1) * 512]),
                         wr=[sla], extra_waits=[(self.dgp, self.t_prep_pj)])
                for oc4 in range(4):
                    Qx, bx = self.qpair(oc4)
                    w = (bx[0].wwait() + bx[1].wwait() + sla.rwait() + ynum.rwait()) \
                        if dc == 0 else (sla.rwait() if oc4 == 0 else ())
                    for b in range(2):
                        self.t_mm(Qx[:, b * 512:(b + 1) * 512],
                                  sla.ap[:, oc4 * 128:(oc4 + 1) * 128],
                                  ynum.ap[:, dc, b * 512:(b + 1) * 512],
                                  dc == 0, dc == DC - 1, waits=w if b == 0 else ())
                    if dc == DC - 1:
                        bx[0].note_w(self.pe)
                        bx[1].note_w(self.pe)
                sla.note_r(self.pe)
            ynum.note_r(self.pe)
            for oc4 in range(4):
                Qx, bx = self.qpair(oc4)
                oc = og * 4 + oc4
                for b in range(2):
                    self.ev("vector", lambda e, s=Qx, o=oc, bb=b:
                            e.scalar_tensor_tensor(
                                self.xr_t[:, o, bb * 512:(bb + 1) * 512],
                                s[:, bb * 512:(bb + 1) * 512],
                                self.bo_sb[:, o:o + 1],
                                self.xr_t[:, o, bb * 512:(bb + 1) * 512],
                                AL.add, AL.add),
                            rd=[bx[b], self.xr], wr=[self.xr])
        self.dbg = {
            "beta_c": (beta, 0, 0),
            "cbf": (cbf, 16, 0),
            "cgb": (cgb, 32, 0),
            "dn": (dn, 48, 0),
            "phq0": (Slot(phq.ap[:, 0, :]), 64, 0),
            "phk0": (Slot(phk.ap[:, 0, :]), 192, 0),
            "ynum0": (Slot(ynum.ap[:, 0, :]), 320, 0),
            "sinb0": (Slot(sinb.ap[:, 0, :]), 448, 0),
            "sendb0": (Slot(sendb.ap[:, 0, :]), 576, 0),
            "ynum1": (Slot(ynum.ap[:, 4, :]), 704, 0),
            "ynum8": (Slot(ynum.ap[:, 12, :]), 832, 0),
        }
        self.barrier()

    def p_finish(self):
        ARF = self.ARF
        ARF.reset(self.arf_scr)
        ot = ARF.salloc([128, D])
        for tci in range(TC):
            for dc in range(DC):
                bk = self.bank[dc % 4]
                self.gmm(bk.ap[:, 0:128], [bk],
                         [(self.xr_t[:, dc, tci * 128:(tci + 1) * 128],
                           self.id_f32[:, :])],
                         rd_slots=[self.xr])
                self.ev("vector", lambda e, q=bk.ap, d=dc:
                        e.tensor_copy(ot.ap[:, d * 128:(d + 1) * 128], q[:, 0:128]),
                        rd=[bk], wr=[ot])
            self.dma("sync", self.outs, lambda e, i=tci:
                     e.dma_start(out=self.y_out[i * 128:(i + 1) * 128, :],
                                 in_=ot.ap[:, :]),
                     rd=[ot])
        self.emit("sync", lambda e: e.nop(), waits=[(self.outs, self.outs.n)])

    def playback(self, stk):
        block = stk.enter_context(self.nc.Block())

        @block.tensor
        def _(e):
            for f in self.ops["tensor"]:
                f(e)

        @block.vector
        def _(e):
            for f in self.ops["vector"]:
                f(e)

        @block.scalar
        def _(e):
            for f in self.ops["scalar"]:
                f(e)

        @block.gpsimd
        def _(e):
            for f in self.ops["gpsimd"]:
                f(e)

        @block.sync
        def _(e):
            for f in self.ops["sync"]:
                f(e)



def _make_consts(core_id):
    j = core_id % 2
    I = np.eye(128)
    ones = np.ones((128, 128))
    swap = np.zeros((128, 128))
    for m in range(128):
        swap[(m + 64) % 128, m] = 1.0
    tri = (np.arange(128)[:, None] <= np.arange(128)[None, :]).astype(np.float64)
    pos = np.arange(j * SOWN, (j + 1) * SOWN, dtype=np.float64)
    inv_freq = 1.0 / (10000.0 ** (np.arange(64, dtype=np.float64) / 64.0))
    fr = inv_freq[:, None] * pos[None, :]
    cos = np.cos(fr)
    sin = np.sin(fr)
    rope_cos = np.concatenate([cos, cos], axis=0)
    rope_sin = np.concatenate([-sin, sin], axis=0)
    cv = np.zeros((128, 8))
    cv[:, 0] = np.pi / 2
    cv[:, 1] = LN_EPS
    sel = np.zeros((128, 8))
    if j == 1:
        sel[:, core_id - 1] = 1.0
    hsel = np.zeros((16, 16 * 128))
    for h in range(16):
        hsel[h, h * 128:(h + 1) * 128] = 1.0
    csel = np.zeros((128, 16 * 16))
    for h in range(16):
        csel[:, h * 16 + h] = 1.0
    return {
        "id_bf": I.astype(bfloat16), "id_f32": I.astype(np.float32),
        "ones_bf": ones.astype(bfloat16), "ones_f32": ones.astype(np.float32),
        "swap_bf": swap.astype(bfloat16), "tri_f32": tri.astype(np.float32),
        "rope_cos": rope_cos.astype(bfloat16), "rope_sin": rope_sin.astype(bfloat16),
        "cvec": cv.astype(np.float32), "selmask": sel.astype(np.float32),
        "hsel_bf": hsel.astype(bfloat16), "csel_bf": csel.astype(bfloat16),
    }


_NC_CACHE = {}


def _get_nc(stop_after):
    if stop_after not in _NC_CACHE:
        _NC_CACHE[stop_after] = MK(stop_after).build()
    return _NC_CACHE[stop_after]


def run_dev(inputs, stop_after="full"):
    nc = _get_nc(stop_after)
    x = np.ascontiguousarray(np.asarray(inputs["L0_attn_a_hidden_states"], np.float32))
    g = lambda n: np.ascontiguousarray(np.asarray(inputs[n], np.float32))
    shared = {}
    for nm in ("q", "k", "v", "o"):
        shared[f"basis_{nm}"] = g(f"L0_attn_a_basis_{nm}")
        shared[f"phase_t_{nm}"] = np.ascontiguousarray(
            g(f"L0_attn_a_phase_{nm}").T)
        shared[f"amp_t_{nm}"] = np.ascontiguousarray(
            g(f"L0_attn_a_amp_{nm}").T)
    for nm in ("Wq", "Wk", "Wv", "Wg", "Wo"):
        shared[nm] = g(f"L0_attn_b_{nm}")
    shared["bg"] = g("L0_attn_b_bg").reshape(NH, 1)
    shared["bo_pk"] = np.ascontiguousarray(
        g("L0_attn_b_bo").reshape(OCN, 128).T)
    shared["Wg1"] = g("L0_ffn_Wg1")
    shared["Wg2"] = g("L0_ffn_Wg2")
    shared["Wup"] = g("L0_ffn_Wup")
    shared["bup_pk"] = np.ascontiguousarray(
        g("L0_ffn_bup").reshape(E, HCN, 128).transpose(2, 0, 1).reshape(128, E * HCN))
    shared["Wdown"] = g("L0_ffn_Wdown")
    shared["bdown"] = g("L0_ffn_bdown")

    in_maps = []
    for c in range(NCORES):
        b, j = c // 2, c % 2
        m = dict(shared)
        m["x_own"] = np.ascontiguousarray(x[b, j * SOWN:(j + 1) * SOWN])
        m["x_part"] = np.ascontiguousarray(x[b, (1 - j) * SOWN:(2 - j) * SOWN])
        m.update(_make_consts(c))
        in_maps.append(m)

    res = run_bass_kernel_spmd(nc, in_maps, core_ids=list(range(NCORES)))
    out = np.empty((4, SEQ, D), np.float32)
    for c in range(NCORES):
        b, j = c // 2, c % 2
        out[b, j * SOWN:(j + 1) * SOWN] = res.results[c]["y_out"]
    return out


def kernel(**inputs):
    return run_dev(inputs, stop_after="full")
